# version 27
# baseline (speedup 1.0000x reference)
import numpy as np
import ml_dtypes
import concourse.bass as bass
import concourse.mybir as mybir
from concourse.bass_utils import run_bass_kernel_spmd
from contextlib import ExitStack

F32 = mybir.dt.float32
BF16 = mybir.dt.bfloat16
I32 = mybir.dt.int32
AF = mybir.ActivationFunctionType
ALU = mybir.AluOpType
AX = mybir.AxisListType

MAGIC = 1.5 * 2.0 ** 23
T = 2048
D = 1024
NT = 16
ND = 8
H0 = 16
HD = 64
NC0 = 32
F1 = 4096
H1F = 3072
EPS_LN = 1e-5
EPS_Q = 1e-8

ENGINES = ("tensor", "vector", "scalar", "gpsimd", "sync")
NQSEM = 8
TT_DROP = True
PREP = True
PREP_W1 = True
PREP_TR = True
PREP_W2 = True
PREP_QK = False
PREP_W2T = True
PSUM_WIDE = True


def _ap_interval(ap):
    tname = type(ap.tensor).__name__
    if PSUM_WIDE and "PSum" in tname:
        return (ap.tensor.name, 0, 1 << 60)
    esz = mybir.dt.size(ap.dtype)
    lo = ap.offset
    hi = ap.offset
    if "DRam" in tname:
        for step, count in ap.ap:
            if count <= 1:
                continue
            ext = step * (count - 1)
            if ext >= 0:
                hi += ext
            else:
                lo += ext
    else:
        tshape = ap.tensor.shape
        pbytes = mybir.dt.size(ap.tensor.dtype)
        for s in tshape[1:]:
            pbytes *= s
        pitch = pbytes // esz
        lo = hi = ap.offset % pitch
        for step, count in ap.ap:
            if count <= 1 or abs(step) >= pitch:
                continue
            ext = step * (count - 1)
            if ext >= 0:
                hi += ext
            else:
                lo += ext
    return (ap.tensor.name, lo * esz, (hi + 1) * esz)


class Op:
    __slots__ = ("engine", "emit", "kind", "queue", "deps", "idx", "inc_id",
                 "flagged", "label", "qidx", "ownsem", "_maxdep", "tr")

    def __init__(self, engine, emit, kind, queue, label):
        self.tr = False
        self.engine = engine
        self.emit = emit
        self.kind = kind
        self.queue = queue
        self.deps = []
        self.idx = -1
        self.qidx = -1
        self.inc_id = None
        self.flagged = False
        self.ownsem = None
        self.label = label


class Asm:

    def __init__(self, nc):
        self.nc = nc
        self.ops = []
        self.eng_ops = {e: [] for e in ENGINES}
        self.queues = {"sq": "sync", "aq": "scalar", "gq": "gpsimd"}
        self.q_ops = {q: [] for q in self.queues}
        self.seg = {}
        self.n_ccsem = 0

    def _touch(self, op, aps, is_write):
        for ap in aps:
            name, lo, hi = _ap_interval(ap)
            segs = self.seg.setdefault(name, [])
            new = []
            deps = op.deps
            if is_write:
                keep = []
                for s in segs:
                    if s[1] <= lo or s[0] >= hi:
                        keep.append(s)
                        continue
                    if s[2] is not None:
                        deps.append(s[2])
                    deps.extend(s[3])
                    if s[0] < lo:
                        keep.append([s[0], lo, s[2], list(s[3])])
                    if s[1] > hi:
                        keep.append([hi, s[1], s[2], list(s[3])])
                keep.append([lo, hi, op, []])
                self.seg[name] = keep
            else:
                for s in segs:
                    if s[1] <= lo or s[0] >= hi:
                        continue
                    if s[2] is not None:
                        deps.append(s[2])
                    s[3].append(op)
                if not segs:
                    segs.append([lo, hi, None, [op]])

    def op(self, engine, emit, r=(), w=(), label="", tr=False):
        o = Op(engine, emit, "c", None, label)
        o.tr = tr
        self._touch(o, r, False)
        self._touch(o, w, True)
        self.ops.append(o)
        self.eng_ops[engine].append(o)
        return o

    def dma(self, queue, out, in_, label="", **kw):
        eng = self.queues[queue]
        o = Op(eng, lambda e, out=out, in_=in_, kw=kw: e.dma_start(out=out, in_=in_, **kw),
               "d", queue, label)
        self._touch(o, [in_], False)
        self._touch(o, [out], True)
        self.ops.append(o)
        self.eng_ops[eng].append(o)
        self.q_ops[queue].append(o)
        return o

    def dmat(self, queue, out, in_, label=""):
        eng = self.queues[queue]
        assert eng in ("sync", "scalar")
        o = Op(eng, lambda e, out=out, in_=in_: e.dma_start_transpose(out, in_),
               "d", queue, label)
        self._touch(o, [in_], False)
        self._touch(o, [out], True)
        last = getattr(self, "_last_dmat", None)
        if last is not None:
            o.deps.append(last)
        self._last_dmat = o
        self.ops.append(o)
        self.eng_ops[eng].append(o)
        self.q_ops[queue].append(o)
        return o

    def cc(self, emit, r=(), w=(), label="cc"):
        o = Op("gpsimd", emit, "cc", None, label)
        self._touch(o, r, False)
        self._touch(o, w, True)
        self.ops.append(o)
        self.eng_ops["gpsimd"].append(o)
        self.n_ccsem += 1
        return o

    def build(self, block, sems):
        for e in ENGINES:
            for i, o in enumerate(self.eng_ops[e]):
                o.idx = i
        for q in self.q_ops:
            for i, o in enumerate(self.q_ops[q]):
                o.qidx = i
        tforce = {}
        prev = None
        for o in self.eng_ops["tensor"]:
            if prev is not None and (prev.tr or o.tr):
                tforce[o.idx] = prev.idx
            prev = o
        ccs = []
        for o in self.ops:
            if o.kind == "cc":
                o.ownsem = sems["cc"][len(ccs)]
                ccs.append(o)
            maxdep = {}
            for d in o.deps:
                if d is o:
                    continue
                if d.kind in ("d", "cc"):
                    continue
                if d.engine == o.engine == "gpsimd":
                    continue
                if (d.engine == o.engine == "tensor" and TT_DROP and
                        "|" in o.label and o.label.split("|")[0] == d.label.split("|")[0]):
                    continue
                if maxdep.get(d.engine, -1) < d.idx:
                    maxdep[d.engine] = d.idx
            if o.engine == "tensor" and o.idx in tforce:
                if maxdep.get("tensor", -1) < tforce[o.idx]:
                    maxdep["tensor"] = tforce[o.idx]
            o._maxdep = maxdep
            for e2, i2 in maxdep.items():
                self.eng_ops[e2][i2].flagged = True
        for e in ENGINES:
            n = 0
            for o in self.eng_ops[e]:
                if o.kind == "c" and o.flagged:
                    n += 1
                    o.inc_id = n
        plans = {e: [] for e in ENGINES}
        waited = {}
        drained = {e: 0 for e in ENGINES}
        for o in self.ops:
            waits = []
            need_drain = False
            for d in o.deps:
                if d is o:
                    continue
                if d.kind == "d":
                    key = ("q", d.queue, d.qidx % NQSEM)
                    v = 16 * (d.qidx // NQSEM + 1)
                    if waited.get((o.engine, key), 0) < v:
                        waits.append((key, v))
                elif d.kind == "cc":
                    key = ("cc", id(d.ownsem))
                    if waited.get((o.engine, key), 0) < 1:
                        waits.append(((key[0], key[1], d.ownsem), 1))
                elif d.engine == o.engine == "gpsimd":
                    if d.idx >= drained[o.engine]:
                        need_drain = True
            for e2, i2 in o._maxdep.items():
                d = self.eng_ops[e2][i2]
                key = ("e", e2)
                v = d.inc_id
                assert v is not None
                if waited.get((o.engine, key), 0) < v:
                    waits.append((key, v))
            if o.kind == "d" and o.qidx >= NQSEM:
                key = ("q", o.queue, o.qidx % NQSEM)
                v = 16 * (o.qidx // NQSEM)
                if waited.get((o.engine, key), 0) < v:
                    waits.append((key, v))
            final_waits = []
            seen = set()
            for key, v in waits:
                k2 = key[:3] if key[0] != "cc" else key[:2]
                if waited.get((o.engine, k2), 0) < v and (k2, v) not in seen:
                    waited[(o.engine, k2)] = v
                    seen.add((k2, v))
                    final_waits.append((key, v))
            if need_drain:
                drained[o.engine] = o.idx
            plans[o.engine].append((final_waits, need_drain, o))
        self._plans = plans
        self._sems = sems

        nc = self.nc
        qsem = sems["q"]
        esem = sems["e"]

        def run_engine(engobj, ename):
            for waits, need_drain, o in plans[ename]:
                for key, v in waits:
                    if key[0] == "q":
                        engobj.wait_ge(qsem[(key[1], key[2])], v)
                    elif key[0] == "e":
                        engobj.wait_ge(esem[key[1]], v)
                    else:
                        engobj.wait_ge(key[2], v)
                if need_drain:
                    engobj.drain()
                inst = o.emit(engobj)
                if o.kind == "d":
                    inst.then_inc(qsem[(o.queue, o.qidx % NQSEM)], 16)
                elif o.kind == "cc":
                    inst.then_inc(o.ownsem, 1)
                elif o.flagged:
                    if ename == "gpsimd":
                        engobj.drain().then_inc(esem[ename], 1)
                    else:
                        inst.then_inc(esem[ename], 1)

        @block.tensor
        def _(e):
            run_engine(e, "tensor")

        @block.vector
        def _(e):
            run_engine(e, "vector")

        @block.scalar
        def _(e):
            run_engine(e, "scalar")

        @block.gpsimd
        def _(e):
            run_engine(e, "gpsimd")

        @block.sync
        def _(e):
            run_engine(e, "sync")



def host_constants():
    c = {}
    c["hc_ident"] = np.eye(128, dtype=ml_dtypes.bfloat16)
    ends = np.arange(0, T, 64)
    starts = np.clip(ends - 63, 0, None)
    lens = (ends - starts + 1).astype(np.float32)
    A = np.zeros((T, NC0), np.float32)
    for ch in range(NC0):
        A[starts[ch]:ends[ch] + 1, ch] = 1.0 / lens[ch]
    c["hc_A"] = A.astype(ml_dtypes.bfloat16)
    msk = np.where(ends[None, :] <= np.arange(T)[:, None], 0.0, -1e30).astype(np.float32)
    c["hc_mask"] = msk
    j = np.arange(128)[:, None]
    i = np.arange(128)[None, :]
    c["hc_mdec"] = np.where(i >= j, (i - j).astype(np.float32), 1.0e4)
    c["hc_gq"] = np.broadcast_to((i + 1).astype(np.float32), (128, 128)).copy()
    c["hc_gk"] = (127 - np.arange(128, dtype=np.float32))[:, None].copy()
    ia = np.zeros((128, 64), ml_dtypes.bfloat16)
    ia[0:64] = np.eye(64)
    ib = np.zeros((128, 64), ml_dtypes.bfloat16)
    ib[64:128] = np.eye(64)
    c["hc_idA"] = ia
    c["hc_idB"] = ib
    return c



class MK:
    def __init__(self, nc, ctx, upto):
        self.nc = nc
        self.ctx = ctx
        self.upto = upto
        e = ctx.enter_context
        self.A = None

        dram = nc.dram_tensor
        self.x_in = dram("x", [T, D], F32, kind="ExternalInput").ap()
        self.y_out = dram("y", [T, D], F32, kind="ExternalOutput").ap()
        names = [("Wq", [D, D]), ("bq", [D]), ("Wk", [D, D]), ("bk", [D]),
                 ("Wv", [D, D]), ("bv", [D]), ("Wo", [D, D]), ("bo", [D]),
                 ("w1", [F1, D]), ("b1", [F1]), ("w2", [D, F1]), ("b2", [D]),
                 ("wqkv", [3 * D, D]), ("bqkv", [3 * D]), ("outw", [D, D]),
                 ("outb", [D]), ("gamma", [16])]
        for n, shp in names:
            setattr(self, n, dram(n, shp, F32, kind="ExternalInput").ap())
        for n, shp in [("wg", [H1F, D]), ("wu", [H1F, D]), ("wd", [D, H1F])]:
            setattr(self, n, dram(n, shp, I32, kind="ExternalInput").ap())
        for n, shp, dt in [("hc_ident", [128, 128], BF16), ("hc_A", [T, NC0], BF16),
                           ("hc_mask", [T, NC0], F32), ("hc_mdec", [128, 128], F32),
                           ("hc_gq", [128, 128], F32), ("hc_gk", [128, 1], F32),
                           ("hc_idA", [128, 64], BF16), ("hc_idB", [128, 64], BF16)]:
            setattr(self, n, dram(n, shp, dt, kind="ExternalInput").ap())

        self.xlnT_hbm = dram("xlnT_hbm", [T, D], BF16).ap()
        self.w1q_hbm = dram("w1q_hbm", [F1, D], BF16).ap()
        self.w1qT_hbm = dram("w1qT_hbm", [D, F1], BF16).ap()
        self.w2q_hbm = dram("w2q_hbm", [D, F1], BF16).ap()
        self.w2qT_hbm = dram("w2qT_hbm", [F1, D], BF16).ap()
        self.h_hbm = dram("h_hbm", [F1, T], BF16).ap()
        self.wqkv_bf = dram("wqkv_bf", [3 * D, D], BF16).ap()
        self.wqkvT_hbm = dram("wqkvT_hbm", [D, 3 * D], BF16).ap()
        self.outw_bf = dram("outw_bf", [D, D], BF16).ap()
        self.wg_bf = dram("wg_bf", [H1F, D], BF16).ap()
        self.wu_bf = dram("wu_bf", [H1F, D], BF16).ap()
        self.wd_bf = dram("wd_bf", [D, H1F], BF16).ap()
        self.wgT_sl = dram("wgT_sl", [24, 128, ND, 128], BF16).ap()
        self.wuT_sl = dram("wuT_sl", [24, 128, ND, 128], BF16).ap()
        self.w1qT_sl = dram("w1qT_sl", [32, 128, ND, 128], BF16).ap()
        self.wqkv_sl = dram("wqkv_sl", [16, 128, ND, 128], BF16).ap()
        self.wq_sl = dram("wq_sl", [8, 128, ND, 128], BF16).ap()
        self.wk_sl = dram("wk_sl", [8, 128, ND, 128], BF16).ap()
        self.wv_sl = dram("wv_sl", [8, 128, ND, 128], BF16).ap()
        self.wdT_hbm = dram("wdT_hbm", [H1F, D], BF16).ap()
        self.h2_hbm = dram("h2_hbm", [H1F, T], BF16).ap()
        self.row_hbm = dram("row_hbm", [8, D], F32).ap()
        self.cc_a = [dram(f"cc_a{i}", [1, 128], F32).ap() for i in range(2)]
        self.cc_b = [dram(f"cc_b{i}", [1, 128], F32).ap() for i in range(2)]

        sb = nc.sbuf_tensor
        self.res = e(sb("res", [128, NT * D], F32))
        self.xfm = e(sb("xfm", [128, ND * T], BF16))
        self.big0 = e(sb("big0", [128, NT * D], BF16))
        self.big1 = e(sb("big1", [128, 10240], BF16))
        self.big2 = e(sb("big2", [128, 8192], BF16))
        self.scr32 = e(sb("scr32", [128, 2048], F32))
        self.tile = e(sb("tile", [128, 4096], BF16))
        self.rowb = e(sb("rowb", [128, 3072], BF16))
        self.cons = e(sb("cons", [128, 6144], BF16))
        self.con32 = e(sb("con32", [128, 800], F32))
        self.stat = e(sb("stat", [128, 256], F32))
        self.pf = [e(nc.psum_tensor(f"pf{i}", [128, 512], F32)) for i in range(7)]
        self.pb = e(nc.psum_tensor("pb", [128, 1024], BF16))

        self.sems = {
            "e": {en: e(nc.semaphore(f"se_{en}")) for en in ENGINES},
            "q": {(q, i): e(nc.semaphore(f"sq_{q}{i}"))
                  for q in ("sq", "aq", "gq") for i in range(NQSEM)},
            "cc": [e(nc.semaphore(f"scc{i}")) for i in range(2)],
        }
        self.A = Asm(nc)

    def resT(self, tc):
        return self.res[:, tc * D:(tc + 1) * D]

    def xfmS(self, dk, t0, t1):
        return self.xfm[:, dk * T + t0: dk * T + t1]

    def ln_into(self, src_of_tc, dst_of_tc, label, nt=NT, track_max=None):
        A, st = self.A, self.stat
        for tc in range(nt):
            src = src_of_tc(tc)
            bn6 = st[:, 32:44]
            for g in range(2):
                A.op("vector", lambda e, s=src, o=bn6, g=g: e.bn_stats(
                    out=o[:, 6 * g:6 * g + 6], in_=s[:, 512 * g:512 * g + 512]),
                    r=[src], w=[bn6[:, 6 * g:6 * g + 6]], label=f"{label}_bnst{tc}_{g}")
            mv = st[:, 44 + 2 * tc: 46 + 2 * tc]
            A.op("vector", lambda e, i=bn6, o=mv: e.bn_aggr(out=o, in_=i),
                 r=[bn6], w=[mv], label=f"{label}_bnag{tc}")
        mvall = st[:, 44:44 + 2 * nt]
        var_v = mvall.rearrange("p (t two) -> p t two", two=2)[:, :, 1:2]
        vareps = st[:, 80:80 + nt]
        A.op("vector", lambda e, i=var_v, o=vareps: e.tensor_scalar(
            out=o, in0=i.rearrange("p t one -> p (t one)"), scalar1=EPS_LN, scalar2=None, op0=ALU.add),
            r=[mvall], w=[vareps], label=f"{label}_veps")
        std = st[:, 96:96 + nt]
        A.op("scalar", lambda e, i=vareps, o=std: e.activation(out=o, in_=i, func=AF.Ln),
             r=[vareps], w=[std], label=f"{label}_ln")
        rstd = st[:, 112:112 + nt]
        A.op("scalar", lambda e, i=std, o=rstd: e.activation(out=o, in_=i, func=AF.Exp, scale=-0.5),
             r=[std], w=[rstd], label=f"{label}_rexp")
        for tc in range(nt):
            src = src_of_tc(tc)
            dst = dst_of_tc(tc)
            m = mvall[:, 2 * tc:2 * tc + 1]
            r = rstd[:, tc:tc + 1]
            A.op("vector", lambda e, s=src, o=dst, m=m, r=r: e.tensor_scalar(
                out=o, in0=s, scalar1=m, scalar2=r, op0=ALU.subtract, op1=ALU.mult),
                r=[src, mvall, rstd], w=[dst], label=f"{label}_nrm{tc}")

    def transpose_roundtrip(self, tm_of_tc, label):
        A = self.A
        hb = self.xlnT_hbm
        for tc in range(NT):
            src = tm_of_tc(tc)
            A.dma("aq", hb[tc * 128:(tc + 1) * 128, :], src, label=f"{label}_st{tc}")
        for dk in range(ND):
            A.dmat("aq", self.xfmS(dk, 0, T), hb[:, dk * 128:(dk + 1) * 128],
                   label=f"{label}_tr{dk}")

    def postln_residual(self, psum_list, tc, bias_row=None, label="",
                        scale_row=None, scale_const=None, then=None):
        A, st = self.A, self.stat
        ao = self.scr32[:, 0:1024]
        sqs = st[:, 42:44]
        for i, ps in enumerate(psum_list):
            dst = ao[:, i * 512:(i + 1) * 512]
            acc = sqs[:, i:i + 1]
            if scale_row is not None:
                sr = scale_row[:, i * 512:(i + 1) * 512]
                A.op("vector", lambda e, p=ps, s=sr, o=dst:
                     e.tensor_tensor(out=o, in0=p[:, :], in1=s, op=ALU.mult),
                     r=[ps[:, :], scale_row], w=[dst], label=f"{label}_s{i}")
                if bias_row is not None:
                    br = bias_row[:, i * 512:(i + 1) * 512]
                    A.op("vector", lambda e, b=br, o=dst, a=acc:
                         e.scalar_tensor_tensor(out=o, in0=o, scalar=1.0, in1=b,
                                                op0=ALU.mult, op1=ALU.add, accum_out=a),
                         r=[dst, bias_row], w=[dst, acc], label=f"{label}_b{i}")
                else:
                    A.op("vector", lambda e, o=dst, a=acc:
                         e.tensor_scalar(out=o, in0=o, scalar1=1.0, scalar2=0.0,
                                         op0=ALU.mult, op1=ALU.add, accum_out=a),
                         r=[dst], w=[dst, acc], label=f"{label}_a{i}")
            elif scale_const is not None:
                A.op("vector", lambda e, p=ps, o=dst, c=scale_const, a=acc:
                     e.tensor_scalar(out=o, in0=p[:, :], scalar1=c, scalar2=0.0,
                                     op0=ALU.mult, op1=ALU.add, accum_out=a),
                     r=[ps[:, :]], w=[dst, acc], label=f"{label}_c{i}")
            elif bias_row is not None:
                br = bias_row[:, i * 512:(i + 1) * 512]
                A.op("vector", lambda e, p=ps, b=br, o=dst, a=acc:
                     e.scalar_tensor_tensor(out=o, in0=p[:, :], scalar=1.0, in1=b,
                                            op0=ALU.mult, op1=ALU.add, accum_out=a),
                     r=[ps[:, :], bias_row], w=[dst, acc], label=f"{label}_b{i}")
            else:
                A.op("vector", lambda e, p=ps, o=dst, a=acc:
                     e.tensor_scalar(out=o, in0=p[:, :], scalar1=1.0, scalar2=0.0,
                                     op0=ALU.mult, op1=ALU.add, accum_out=a),
                     r=[ps[:, :]], w=[dst, acc], label=f"{label}_cp{i}")
        sums = st[:, 40:42]
        sqscr = self.scr32[:, 1792:2048].bitcast(BF16)
        for g in range(2):
            A.op("scalar", lambda e, s=ao, g=g, a=sums[:, g:g + 1], o=sqscr:
                 e.activation(out=o, in_=s[:, 512 * g:512 * g + 512], func=AF.Square,
                              accum_out=a),
                 r=[ao], w=[sums[:, g:g + 1], sqscr], label=f"{label}_sq{g}")
        mv = st[:, 44:46]
        A.op("vector", lambda e, i=sums, o=mv: e.reduce_sum(out=o[:, 1:2], in_=i, axis=AX.X),
             r=[sums], w=[mv[:, 1:2]], label=f"{label}_sqsum")
        A.op("vector", lambda e, i=sqs, o=mv: e.reduce_sum(out=o[:, 0:1], in_=i, axis=AX.X),
             r=[sqs], w=[mv[:, 0:1]], label=f"{label}_msum")
        A.op("vector", lambda e, o=mv: e.tensor_scalar(
            out=o, in0=o, scalar1=1.0 / D, scalar2=None, op0=ALU.mult),
            r=[mv], w=[mv], label=f"{label}_mdiv")
        ve = st[:, 46:47]
        A.op("vector", lambda e, m=mv, o=ve: e.scalar_tensor_tensor(
            out=o, in0=m[:, 0:1], scalar=-1.0, in1=m[:, 0:1], op0=ALU.mult, op1=ALU.mult),
            r=[mv], w=[ve], label=f"{label}_msq")
        A.op("vector", lambda e, m=mv, o=ve: e.tensor_tensor(
            out=o, in0=o, in1=m[:, 1:2], op=ALU.add),
            r=[ve, mv], w=[ve], label=f"{label}_var")
        A.op("vector", lambda e, o=ve: e.tensor_scalar(
            out=o, in0=o, scalar1=0.0, scalar2=EPS_LN, op0=ALU.max, op1=ALU.add),
            r=[ve], w=[ve], label=f"{label}_veps")
        sd = st[:, 47:48]
        A.op("scalar", lambda e, i=ve, o=sd: e.activation(out=o, in_=i, func=AF.Ln),
             r=[ve], w=[sd], label=f"{label}_ln")
        rs = st[:, 48:49]
        A.op("scalar", lambda e, i=sd, o=rs: e.activation(out=o, in_=i, func=AF.Exp, scale=-0.5),
             r=[sd], w=[rs], label=f"{label}_rexp")
        mr = st[:, 49:50]
        A.op("vector", lambda e, m=mv, r=rs, o=mr: e.tensor_tensor(
            out=o, in0=m[:, 0:1], in1=r, op=ALU.mult),
            r=[mv, rs], w=[mr], label=f"{label}_mr")
        dst = self.resT(tc)
        A.op("vector", lambda e, a=ao, r=rs, d=dst: e.scalar_tensor_tensor(
            out=d, in0=a, scalar=r, in1=d, op0=ALU.mult, op1=ALU.add),
            r=[ao, rs, dst], w=[dst], label=f"{label}_acc")
        A.op("vector", lambda e, d=dst, m=mr: e.tensor_scalar(
            out=d, in0=d, scalar1=m, scalar2=None, op0=ALU.subtract),
            r=[dst, mr], w=[dst], label=f"{label}_sub")
        if then is not None:
            then(tc)

    def drip(self, n=2):
        while n > 0 and self._prep_units:
            self._prep_units.pop(0)()
            n -= 1

    def drip_flush(self):
        while self._prep_units:
            self._prep_units.pop(0)()

    def build(self):
        self.p0_load_consts()
        self._prep_units = []
        if PREP:
            self.p_prep_build_units()
        self.p1_l0_attn()
        self.drip_flush()
        if self.upto >= 2:
            self.p2_l0_ffn()
        if self.upto >= 3:
            self.p3_l1_attn()
        if self.upto >= 4:
            self.p4_l1_ffn()
        self.p5_store_out()

    def p0_load_consts(self):
        A = self.A
        xs = self.x_in.rearrange("(a p) d -> p a d", p=128)
        rd = self.res[:, :].rearrange("p (a d) -> p a d", d=D)
        for tc in range(NT):
            A.dma("sq", rd[:, tc:tc + 1, :], xs[:, tc:tc + 1, :], label=f"ld_x{tc}")
        self.ident = self.cons[:, 0:128]
        A.dma("sq", self.ident, self.hc_ident, label="ld_ident")
        self.A_sb = self.cons[:, 128:128 + NT * NC0]
        A.dma("sq", self.A_sb.rearrange("p (a c) -> p a c", c=NC0),
              self.hc_A.rearrange("(a p) c -> p a c", p=128), label="ld_A")
        self.idA = self.cons[:, 640:704]
        self.idB = self.cons[:, 704:768]
        A.dma("sq", self.idA, self.hc_idA, label="ld_idA")
        A.dma("sq", self.idB, self.hc_idB, label="ld_idB")
        self.mask_sb = self.con32[:, 0:NT * NC0]
        A.dma("sq", self.mask_sb.rearrange("p (a c) -> p a c", c=NC0),
              self.hc_mask.rearrange("(a p) c -> p a c", p=128), label="ld_mask")
        self.mdec = self.con32[:, 512:640]
        A.dma("sq", self.mdec, self.hc_mdec, label="ld_mdec")
        self.gq_in = self.con32[:, 640:768]
        A.dma("sq", self.gq_in, self.hc_gq, label="ld_gq")
        self.gk_in = self.con32[:, 768:769]
        A.dma("sq", self.gk_in, self.hc_gk, label="ld_gk")
        self.nmag = self.con32[:, 769:770]
        A.op("vector", lambda e, o=self.nmag: e.memset(o, -MAGIC), w=[self.nmag], label="nmag")


    def p_prep_build_units(self):
        A, st = self.A, self.stat
        b1 = self.big1
        U = self._prep_units
        fsl = [b1[:, 4096:6144].bitcast(F32), b1[:, 6144:8192].bitcast(F32)]
        osl = [b1[:, 8192:9216], b1[:, 9216:10240]]
        NF1 = F1 // 128

        sw1 = st[:, 0:32]
        sw2 = st[:, 224:232]
        am = self.con32[:, 770:771]
        rc = self.con32[:, 771:772]
        a4 = self.con32[:, 772:773]

        def w1_load(fc):
            A.dma("sq", fsl[fc % 2], self.w1[fc * 128:(fc + 1) * 128, :], label=f"w1l{fc}")

        def w1_comp(fc):
            fslot = fsl[fc % 2]
            oslot = osl[fc % 2]
            A.op("vector", lambda e, i=fslot, o=am: e.reduce_max(
                out=o, in_=i, axis=AX.X, apply_absolute_value=True),
                r=[fslot], w=[am], label=f"w1am{fc}")
            A.op("vector", lambda e, o=am: e.tensor_scalar(
                out=o, in0=o, scalar1=EPS_Q, scalar2=None, op0=ALU.max),
                r=[am], w=[am], label=f"w1amc{fc}")
            A.op("vector", lambda e, i=am, o=rc: e.reciprocal(out=o, in_=i),
                 r=[am], w=[rc], label=f"w1rc{fc}")
            A.op("vector", lambda e, o=rc: e.tensor_scalar(
                out=o, in0=o, scalar1=127.0, scalar2=None, op0=ALU.mult),
                r=[rc], w=[rc], label=f"w1rc2{fc}")
            A.op("vector", lambda e, i=am, o=sw1[:, fc:fc + 1]: e.tensor_scalar(
                out=o, in0=i, scalar1=1.0 / 127.0, scalar2=None, op0=ALU.mult),
                r=[am], w=[sw1[:, fc:fc + 1]], label=f"w1sw{fc}")
            self.magic_round(oslot, fslot, rc[:, 0:1], False, f"w1q{fc}", tmp=fslot)
            A.dma("sq", self.w1q_hbm[fc * 128:(fc + 1) * 128, :], oslot, label=f"w1s{fc}")

        U.append(lambda: (w1_load(0), w1_load(1)))
        for fc in range(NF1):
            U.append(lambda fc=fc: (w1_comp(fc),
                                    w1_load(fc + 2) if fc + 2 < NF1 else None))

        def w1t_unit(dk, hh):
            stg2 = b1[:, 4096 + (hh % 2) * 2048: 4096 + (hh % 2) * 2048 + 2048]
            A.dmat("sq", stg2, self.w1q_hbm[hh * 2048:(hh + 1) * 2048,
                                            dk * 128:(dk + 1) * 128], label=f"w1t{dk}_{hh}")
            A.dma("gq", self.w1qT_sl[hh * 16:(hh + 1) * 16, :, dk, :]
                  .rearrange("fc p f -> p fc f"),
                  stg2.rearrange("p (fc f) -> p fc f", f=128), label=f"w1ts{dk}_{hh}")

        for dk in range(ND):
            for hh in range(2):
                U.append(lambda dk=dk, hh=hh: w1t_unit(dk, hh))

        def w2_maxpass(dc):
            for q4 in range(4):
                fslot = fsl[q4 % 2]
                A.dma("aq", fslot, self.w2[dc * 128:(dc + 1) * 128, q4 * 1024:(q4 + 1) * 1024],
                      label=f"w2l{dc}_{q4}")
                A.op("vector", lambda e, i=fslot, o=a4: e.reduce_max(
                    out=o, in_=i, axis=AX.X, apply_absolute_value=True),
                    r=[fslot], w=[a4], label=f"w2a{dc}_{q4}")
                if q4 == 0:
                    A.op("vector", lambda e, i=a4, o=am: e.tensor_copy(out=o, in_=i),
                         r=[a4], w=[am], label=f"w2c{dc}")
                else:
                    A.op("vector", lambda e, i=a4, o=am: e.tensor_tensor(
                        out=o, in0=o, in1=i, op=ALU.max), r=[a4, am], w=[am], label=f"w2m{dc}_{q4}")
            A.op("vector", lambda e, o=am: e.tensor_scalar(
                out=o, in0=o, scalar1=EPS_Q, scalar2=None, op0=ALU.max),
                r=[am], w=[am], label=f"w2mc{dc}")
            A.op("vector", lambda e, i=am, o=rc: e.reciprocal(out=o, in_=i),
                 r=[am], w=[rc], label=f"w2rc{dc}")
            A.op("vector", lambda e, o=rc: e.tensor_scalar(
                out=o, in0=o, scalar1=127.0, scalar2=None, op0=ALU.mult),
                r=[rc], w=[rc], label=f"w2rc2{dc}")
            A.op("vector", lambda e, i=am, o=sw2[:, dc:dc + 1]: e.tensor_scalar(
                out=o, in0=i, scalar1=1.0 / 127.0, scalar2=None, op0=ALU.mult),
                r=[am], w=[sw2[:, dc:dc + 1]], label=f"w2sw{dc}")

        def w2_quant(dc, q4):
            fslot = fsl[q4 % 2]
            oslot = osl[q4 % 2]
            A.dma("aq", fslot, self.w2[dc * 128:(dc + 1) * 128, q4 * 1024:(q4 + 1) * 1024],
                  label=f"w2l2{dc}_{q4}")
            self.magic_round(oslot, fslot, rc[:, 0:1], False, f"w2q{dc}_{q4}", tmp=fslot)
            A.dma("aq", self.w2q_hbm[dc * 128:(dc + 1) * 128, q4 * 1024:(q4 + 1) * 1024],
                  oslot, label=f"w2s{dc}_{q4}")

        for dc in range(ND):
            U.append(lambda dc=dc: w2_maxpass(dc))
            U.append(lambda dc=dc: (w2_quant(dc, 0), w2_quant(dc, 1)))
            U.append(lambda dc=dc: (w2_quant(dc, 2), w2_quant(dc, 3)))

        def w2t_unit(fc):
            stg2 = b1[:, 4096 + (fc % 2) * 2048: 4096 + (fc % 2) * 2048 + 1024]
            A.dmat("aq", stg2, self.w2q_hbm[:, fc * 128:(fc + 1) * 128], label=f"w2t{fc}")
            A.dma("gq", self.w2qT_hbm[fc * 128:(fc + 1) * 128, :], stg2, label=f"w2ts{fc}")

        for fc in range(NF1):
            U.append(lambda fc=fc: w2t_unit(fc))

        def cast_unit(nm, dst, src_, c0, c1):
            for c in range(c0, c1):
                A.dma("gq", dst[c * 128:(c + 1) * 128, :], src_[c * 128:(c + 1) * 128, :],
                      label=f"{nm}_cast{c}")

        for c in range(0, 24, 4):
            U.append(lambda c=c: cast_unit("qk", self.wqkv_bf, self.wqkv, c, c + 4))

        def qkt_unit(dk, hh):
            stg2 = b1[:, 4096 + (hh % 2) * 2048: 4096 + (hh % 2) * 2048 + 1536]
            A.dmat("aq", stg2, self.wqkv_bf[hh * 1536:(hh + 1) * 1536,
                                            dk * 128:(dk + 1) * 128], label=f"qkt{dk}_{hh}")
            A.dma("gq", self.wqkvT_hbm[dk * 128:(dk + 1) * 128, hh * 1536:(hh + 1) * 1536],
                  stg2, label=f"qkts{dk}_{hh}")
            nsl = 12 if hh == 0 else 4
            A.dma("gq", self.wqkv_sl[hh * 12:hh * 12 + nsl, :, dk, :]
                  .rearrange("fc p f -> p fc f"),
                  stg2[:, 0:nsl * 128].rearrange("p (fc f) -> p fc f", f=128),
                  label=f"qksl{dk}_{hh}")

        for dk in range(ND):
            for hh in range(2):
                U.append(lambda dk=dk, hh=hh: qkt_unit(dk, hh))
        for c in range(0, 8, 4):
            U.append(lambda c=c: cast_unit("ow", self.outw_bf, self.outw, c, c + 4))
        for c in range(0, 24, 4):
            U.append(lambda c=c: cast_unit("wg", self.wg_bf, self.wg, c, c + 4))
        for c in range(0, 24, 4):
            U.append(lambda c=c: cast_unit("wu", self.wu_bf, self.wu, c, c + 4))
        for c in range(0, 8, 4):
            U.append(lambda c=c: cast_unit("wd", self.wd_bf, self.wd, c, c + 4))

    def p_prep_late(self):
        A = self.A
        b1 = self.big1
        for nm, w_bf, w_sl in (("wg", self.wg_bf, self.wgT_sl),
                               ("wu", self.wu_bf, self.wuT_sl)):
            for dk in range(ND):
                for hh in range(2):
                    stg2 = b1[:, 8192:9728] if hh == 0 else b1[:, 8704:10240]
                    A.dmat("sq", stg2, w_bf[hh * 1536:(hh + 1) * 1536,
                                            dk * 128:(dk + 1) * 128], label=f"{nm}t{dk}_{hh}")
                    A.dma("sq", w_sl[hh * 12:(hh + 1) * 12, :, dk, :]
                          .rearrange("fc p f -> p fc f"),
                          stg2.rearrange("p (fc f) -> p fc f", f=128),
                          label=f"{nm}ts{dk}_{hh}")

    def p1_l0_attn(self):
        A, nc = self.A, self.nc
        xln = self.big0

        def xlnT(tc):
            return xln[:, tc * D:(tc + 1) * D]

        self.ln_into(self.resT, xlnT, "ln1")
        self.transpose_roundtrip(xlnT, "tr1")

        ps = self.pf[0]
        summ_ps = ps[:, 0:256].rearrange("p (dk c) -> p dk c", c=NC0)
        for dk in range(ND):
            self.drip(1)
            for tc in range(NT):
                lhsT = xlnT(tc)[:, dk * 128:(dk + 1) * 128]
                rhs = self.A_sb[:, tc * NC0:(tc + 1) * NC0]
                out = summ_ps[:, dk, :]
                A.op("tensor", lambda e, o=out, l=lhsT, r=rhs, s=(tc == 0), p=(tc == NT - 1):
                     e.matmul(o, l, r, start=s, stop=p),
                     r=[lhsT, rhs], w=[ps[:, dk * 32:(dk + 1) * 32]], label=f"summ_{dk}|{tc}")
        summ = self.tile[:, 0:256]
        A.op("vector", lambda e, i=ps[:, 0:256], o=summ: e.tensor_copy(out=o, in_=i),
             r=[ps[:, 0:256]], w=[summ], label="summ_ev")

        bq_s = self.stat[:, 128:136]
        bk_s = self.stat[:, 136:144]
        A.dma("gq", bq_s, self.bq.rearrange("(c p) -> p c", p=128), label="ld_bq", allow_slow_non_contiguous=True)
        A.dma("gq", bk_s, self.bk.rearrange("(c p) -> p c", p=128), label="ld_bk", allow_slow_non_contiguous=True)
        bk8 = self.stat[:, 144:152]
        A.op("vector", lambda e, i=bk_s, o=bk8: e.tensor_scalar(
            out=o, in0=i, scalar1=0.125, scalar2=None, op0=ALU.mult),
            r=[bk_s], w=[bk8], label="bk8")
        bv_row = self.rowb[:, 0:1024]
        bo_row = self.rowb[:, 1024:2048]
        A.dma("gq", bv_row, self.bv.rearrange("(one d) -> one d", one=1).broadcast_to([128, D]), label="ld_bv")
        A.dma("gq", bo_row, self.bo.rearrange("(one d) -> one d", one=1).broadcast_to([128, D]), label="ld_bo")

        kfm = self.tile[:, 256:512]
        wk_sl = self.big1[:, 0:4096]
        self._p1slot = 0

        def wslot():
            s = self._p1slot % 4
            self._p1slot += 1
            return wk_sl[:, s * 1024:(s + 1) * 1024]

        for fc in range(ND):
            self.drip(2)
            buf = wslot()
            A.dma("gq", buf.rearrange("p (dk f) -> p dk f", f=128),
                  self.Wk[:, fc * 128:(fc + 1) * 128].rearrange("(dk p) f -> p dk f", p=128),
                  label=f"ld_wk{fc}")
            out = self.pf[1][:, 0:32]
            for dk in range(ND):
                l = buf[:, dk * 128:(dk + 1) * 128]
                r = summ[:, dk * 32:(dk + 1) * 32]
                A.op("tensor", lambda e, o=out, l=l, r=r, s=(dk == 0), p=(dk == ND - 1):
                     e.matmul(o, l, r, start=s, stop=p),
                     r=[l, r], w=[out], label=f"kmm{fc}|{dk}")
            o2 = kfm[:, fc * 32:(fc + 1) * 32]
            A.op("scalar", lambda e, i=out, o=o2, b=bk8[:, fc:fc + 1]: e.activation(
                out=o, in_=i, func=AF.Identity, bias=b, scale=0.125),
                r=[out, bk8], w=[o2], label=f"kev{fc}")

        vpad = self.tile[:, 512:1536]
        A.op("vector", lambda e, o=vpad: e.memset(o, 0.0), w=[vpad], label="vpad0")
        vbufs = []
        for fc in range(ND):
            self.drip(2)
            buf = wslot()
            vbufs.append(buf)
            A.dma("gq", buf.rearrange("p (dk f) -> p dk f", f=128),
                  self.Wv[:, fc * 128:(fc + 1) * 128].rearrange("(dk p) f -> p dk f", p=128),
                  label=f"ld_wv{fc}")
            for hh in range(2):
                h = 2 * fc + hh
                b = 32 * (h % 4)
                pcol = (h % 8) * 64
                out = self.pf[2][b:b + 32, pcol:pcol + 64]
                for dk in range(ND):
                    l = summ[:, dk * 32:(dk + 1) * 32]
                    r = buf[:, dk * 128 + hh * 64: dk * 128 + hh * 64 + 64]
                    A.op("tensor", lambda e, o=out, l=l, r=r, s=(dk == 0), p=(dk == ND - 1),
                         tp=(0, b): e.matmul(o, l, r, start=s, stop=p, tile_position=tp),
                         r=[l, r], w=[self.pf[2][:, pcol:pcol + 64]], label=f"vmm{h}|{dk}")
                dst = vpad[:, h * 64:(h + 1) * 64][b:b + 32, :]
                bvs = bv_row[b:b + 32, h * 64:(h + 1) * 64]
                A.op("vector", lambda e, i=out, o=dst, b2=bvs: e.tensor_tensor(
                    out=o, in0=i, in1=b2, op=ALU.add),
                    r=[self.pf[2][:, pcol:pcol + 64], bv_row], w=[dst], label=f"vev{h}")

        q = self.big0
        for fc in range(ND):
            self.drip(2)
            buf = wslot()
            A.dma("gq", buf.rearrange("p (dk f) -> p dk f", f=128),
                  self.Wq[:, fc * 128:(fc + 1) * 128].rearrange("(dk p) f -> p dk f", p=128),
                  label=f"ld_wq{fc}")
            for tt in range(4):
                self.drip(1)
                ps_t = self.pf[3 + (tt % 2)]
                for dk in range(ND):
                    l = buf[:, dk * 128:(dk + 1) * 128]
                    r = self.xfmS(dk, tt * 512, (tt + 1) * 512)
                    A.op("tensor", lambda e, o=ps_t[:, :], l=l, r=r, s=(dk == 0), p=(dk == ND - 1):
                         e.matmul(o, l, r, start=s, stop=p),
                         r=[l, r], w=[ps_t[:, :]], label=f"qmm{fc}_{tt}|{dk}")
                dst = q[:, fc * T + tt * 512: fc * T + (tt + 1) * 512]
                A.op("scalar", lambda e, i=ps_t, o=dst, b=bq_s[:, fc:fc + 1]: e.activation(
                    out=o, in_=i[:, :], func=AF.Identity, bias=b),
                    r=[ps_t[:, :], bq_s], w=[dst], label=f"qev{fc}_{tt}")

        wo_sb = self.big2[:, 0:8192]
        A.dma("gq", wo_sb.rearrange("p (dk f) -> p dk f", f=D),
              self.Wo.rearrange("(dk p) f -> p dk f", p=128), label="ld_wo")

        at_b = [self.tile[:, 1536:2048], self.tile[:, 3584:4096]]
        atT = self.tile[:, 2048:2560]
        ytile = self.tile[:, 2560:3584]

        def emit_s1(tc):
            ps_sc = self.pf[5] if tc % 2 == 0 else self.pf[3]
            for h in range(H0):
                b = 64 * (h % 2)
                l = q[:, (h // 2) * T + tc * 128:(h // 2) * T + tc * 128 + 128][b:b + 64, :]
                r = kfm[:, (h // 2) * 32:(h // 2) * 32 + 32][b:b + 64, :]
                out = ps_sc[:, h * 32:(h + 1) * 32]
                A.op("tensor", lambda e, o=out, l=l, r=r: e.matmul(o, l, r, start=True, stop=True),
                     r=[l, r], w=[out], label=f"sc{tc}_{h}")
            scf = self.scr32[:, 1024:1536]
            mk = self.mask_sb[:, tc * NC0:(tc + 1) * NC0]
            A.op("vector", lambda e, i=ps_sc, m=mk, o=scf: e.tensor_tensor(
                out=o.rearrange("p (h c) -> p h c", c=NC0),
                in0=i[:, :].rearrange("p (h c) -> p h c", c=NC0),
                in1=m.rearrange("p (one c) -> p one c", one=1).broadcast_to([128, H0, NC0]), op=ALU.add),
                r=[ps_sc[:, :], self.mask_sb], w=[scf], label=f"msk{tc}")
            mx = self.stat[:, 160:176]
            A.op("vector", lambda e, i=scf, o=mx: e.tensor_reduce(
                out=o, in_=i.rearrange("p (h c) -> p h c", c=NC0), axis=AX.X, op=ALU.max),
                r=[scf], w=[mx], label=f"mx{tc}")
            A.op("vector", lambda e, i=scf, m=mx, o=scf: e.tensor_tensor(
                out=o.rearrange("p (h c) -> p h c", c=NC0),
                in0=i.rearrange("p (h c) -> p h c", c=NC0),
                in1=m.rearrange("p (h one) -> p h one", one=1).broadcast_to([128, H0, NC0]), op=ALU.subtract),
                r=[scf, mx], w=[scf], label=f"sub{tc}")
            A.op("scalar", lambda e, i=scf, o=scf: e.activation(out=o, in_=i, func=AF.Exp),
                 r=[scf], w=[scf], label=f"exp{tc}")
            sm = self.stat[:, 176:192]
            A.op("vector", lambda e, i=scf, o=sm: e.tensor_reduce(
                out=o, in_=i.rearrange("p (h c) -> p h c", c=NC0), axis=AX.X, op=ALU.add),
                r=[scf], w=[sm], label=f"sm{tc}")
            rs = self.stat[:, 192:208]
            A.op("vector", lambda e, i=sm, o=rs: e.reciprocal(out=o, in_=i),
                 r=[sm], w=[rs], label=f"rs{tc}")
            at = at_b[tc % 2]
            A.op("vector", lambda e, i=scf, r2=rs, o=at: e.tensor_tensor(
                out=o.rearrange("p (h c) -> p h c", c=NC0),
                in0=i.rearrange("p (h c) -> p h c", c=NC0),
                in1=r2.rearrange("p (h one) -> p h one", one=1).broadcast_to([128, H0, NC0]), op=ALU.mult),
                r=[scf, rs], w=[at], label=f"at{tc}")

        def emit_s2(tc):
            at = at_b[tc % 2]
            for qd in range(4):
                src = at[:, qd * 128:(qd + 1) * 128]
                out = self.pb[:, qd * 128:(qd + 1) * 128]
                A.op("tensor", lambda e, o=out, s=src: e.transpose(o, s, self.ident),
                     r=[src, self.ident], w=[out], label=f"atT{tc}_{qd}", tr=True)
            A.op("scalar", lambda e, i=self.pb[:, 0:512], o=atT: e.copy(out=o, in_=i),
                 r=[self.pb[:, 0:512]], w=[atT], label=f"atTev{tc}")
            ps_y = [self.pf[6], self.pf[0]]
            for h in range(H0):
                b = 64 * (h % 2)
                l = vpad[:, h * 64:(h + 1) * 64]
                r = atT[:, (h // 4) * 128:(h // 4) * 128 + 128]
                bank = ps_y[(h // 2) % 2]
                col = ((h // 2) // 2) * 128
                out = bank[b:b + 64, col:col + 128]
                A.op("tensor", lambda e, o=out, l=l, r=r, tp=(0, b): e.matmul(
                    o, l, r, start=True, stop=True, tile_position=tp),
                    r=[l, r], w=[bank[:, col:col + 128]], label=f"av{tc}_{h}")
            for i2 in range(2):
                A.op("scalar", lambda e, i=ps_y[i2], o=ytile[:, i2 * 512:(i2 + 1) * 512]:
                     e.copy(out=o, in_=i[:, :]),
                     r=[ps_y[i2][:, :]], w=[ytile[:, i2 * 512:(i2 + 1) * 512]], label=f"yev{tc}_{i2}")
            ps_o = [self.pf[1], self.pf[2]]
            for k in range(ND):
                l = ytile[:, (k % 2) * 512 + (k // 2) * 128: (k % 2) * 512 + (k // 2) * 128 + 128]
                for half in range(2):
                    r = wo_sb[:, k * D + half * 512: k * D + (half + 1) * 512]
                    out = ps_o[half]
                    A.op("tensor", lambda e, o=out, l=l, r=r, s=(k == 0), p=(k == ND - 1):
                         e.matmul(o[:, :], l, r, start=s, stop=p),
                         r=[l, r], w=[out[:, :]], label=f"op{tc}_{half}|{k}")
            self.postln_residual(ps_o, tc, bo_row, f"pl1_{tc}",
                                 then=lambda tc2: self.ln_tc(
                                     tc2, self.xfm[:, tc2 * D:(tc2 + 1) * D], "ln2",
                                     colmax=self.stat[:, 208:224]))

        for tc in range(NT):
            emit_s1(tc)
            self.drip(2)
            if tc > 0:
                emit_s2(tc - 1)
                self.drip(2)
        emit_s2(NT - 1)

    def global_max_collective(self, m128, idx, out_col, label):
        A = self.A
        cca, ccb = self.cc_a[idx], self.cc_b[idx]
        A.dma("gq", cca.rearrange("one p -> p one"), m128, label=f"{label}_st")
        A.cc(lambda e, a=cca, b=ccb: e.collective_compute(
            "AllReduce", ALU.max, replica_groups=[list(range(8))],
            ins=[a], outs=[b]), r=[cca], w=[ccb], label=f"{label}_cc")
        gall = self.scr32[:, 0:128]
        A.dma("gq", gall, ccb[0:1, :].broadcast_to([128, 128]), label=f"{label}_ld")
        A.op("vector", lambda e, i=gall, o=out_col: e.reduce_max(out=o, in_=i, axis=AX.X),
             r=[gall], w=[out_col], label=f"{label}_mx")

    def magic_round(self, out_bf, in_ap, inv_ap, relu, label, tmp=None):
        A = self.A
        if tmp is None:
            tmp = self.scr32[:, 0:in_ap.shape[-1]]
        A.op("vector", lambda e, i=in_ap, o=tmp, s=inv_ap: e.tensor_scalar(
            out=o, in0=i, scalar1=s, scalar2=MAGIC, op0=ALU.mult, op1=ALU.add),
            r=[in_ap, inv_ap], w=[tmp], label=f"{label}_a")
        fn = AF.Relu if relu else AF.Identity
        A.op("scalar", lambda e, i=tmp, o=out_bf, fn=fn, b=self.nmag: e.activation(
            out=o, in_=i, func=fn, bias=b),
            r=[tmp, self.nmag], w=[out_bf], label=f"{label}_b")

    def ln_tc(self, tc, dst, label, colmax=None):
        A, st = self.A, self.stat
        src = self.resT(tc)
        bn6 = st[:, 64 + (tc % 2) * 16:64 + (tc % 2) * 16 + 12]
        for g in range(2):
            A.op("vector", lambda e, s=src, o=bn6, g=g: e.bn_stats(
                out=o[:, 6 * g:6 * g + 6], in_=s[:, 512 * g:512 * g + 512]),
                r=[src], w=[bn6[:, 6 * g:6 * g + 6]], label=f"{label}_bnst{tc}_{g}")
        mv = st[:, 96 + (tc % 2) * 2:96 + (tc % 2) * 2 + 2]
        A.op("vector", lambda e, i=bn6, o=mv: e.bn_aggr(out=o, in_=i),
             r=[bn6], w=[mv], label=f"{label}_bnag{tc}")
        ve = st[:, 100 + (tc % 2):101 + (tc % 2)]
        A.op("vector", lambda e, i=mv, o=ve: e.tensor_scalar(
            out=o, in0=i[:, 1:2], scalar1=EPS_LN, scalar2=None, op0=ALU.add),
            r=[mv], w=[ve], label=f"{label}_veps{tc}")
        A.op("scalar", lambda e, o=ve: e.activation(out=o, in_=o, func=AF.Ln),
             r=[ve], w=[ve], label=f"{label}_ln{tc}")
        rstd = st[:, 102 + (tc % 2):103 + (tc % 2)]
        A.op("scalar", lambda e, i=ve, o=rstd: e.activation(out=o, in_=i, func=AF.Exp, scale=-0.5),
             r=[ve], w=[rstd], label=f"{label}_rexp{tc}")
        A.op("vector", lambda e, s=src, o=dst, m=mv[:, 0:1], r2=rstd: e.tensor_scalar(
            out=o, in0=s, scalar1=m, scalar2=r2, op0=ALU.subtract, op1=ALU.mult),
            r=[src, mv, rstd], w=[dst], label=f"{label}_nrm{tc}")
        A.dma("aq", self.xlnT_hbm[tc * 128:(tc + 1) * 128, :], dst, label=f"{label}_st{tc}")
        if colmax is not None:
            A.op("vector", lambda e, i=dst, o=colmax[:, tc:tc + 1]:
                 e.reduce_max(out=o, in_=i, axis=AX.X),
                 r=[dst], w=[colmax[:, tc:tc + 1]], label=f"{label}_cm{tc}")

    def p2_l0_ffn(self):
        A, st = self.A, self.stat
        xln = self.big0

        def xlnT(tc):
            return xln[:, tc * D:(tc + 1) * D]

        for dk in range(ND):
            A.dmat("aq", self.xfmS(dk, 0, T), self.xlnT_hbm[:, dk * 128:(dk + 1) * 128],
                   label=f"tr2_tr{dk}")
        colmax = st[:, 208:224]
        m128 = st[:, 232:233]
        A.op("vector", lambda e, i=colmax, o=m128: e.reduce_max(out=o, in_=i, axis=AX.X),
             r=[colmax], w=[m128], label="m128")
        A.op("vector", lambda e, o=m128: e.tensor_scalar(
            out=o, in0=o, scalar1=EPS_Q, scalar2=None, op0=ALU.max),
            r=[m128], w=[m128], label="m128c")
        gmax = st[:, 233:234]
        self.global_max_collective(m128, 0, gmax, "gmx0")
        inv_s = st[:, 234:235]
        s_q = st[:, 235:236]
        A.op("vector", lambda e, i=gmax, o=inv_s: e.reciprocal(out=o, in_=i),
             r=[gmax], w=[inv_s], label="invs0")
        A.op("vector", lambda e, o=inv_s: e.tensor_scalar(
            out=o, in0=o, scalar1=255.0, scalar2=None, op0=ALU.mult),
            r=[inv_s], w=[inv_s], label="invs1")
        A.op("vector", lambda e, i=gmax, o=s_q: e.tensor_scalar(
            out=o, in0=i, scalar1=1.0 / 255.0, scalar2=None, op0=ALU.mult),
            r=[gmax], w=[s_q], label="sq0")

        xq = self.big0
        for dk in range(ND):
            self.magic_round(xq[:, dk * T:(dk + 1) * T], self.xfmS(dk, 0, T),
                             inv_s[:, 0:1], True, f"xq{dk}",
                             tmp=self.scr32[:, 0:T])

        sw1 = self.stat[:, 0:32]
        NF1 = F1 // 128
        A.op("vector", lambda e, o=sw1, s=s_q[:, 0:1]: e.tensor_scalar(
            out=o, in0=o, scalar1=s, scalar2=None, op0=ALU.mult),
            r=[sw1, s_q], w=[sw1], label="ssw1")
        b1s = st[:, 128:160]
        A.dma("gq", b1s, self.b1.rearrange("(c p) -> p c", p=128), label="ld_b1",
              allow_slow_non_contiguous=True)
        rcp = st[:, 192:224]
        A.op("vector", lambda e, i=sw1, o=rcp: e.reciprocal(out=o, in_=i),
             r=[sw1], w=[rcp], label="b1rcp")
        b1q = st[:, 160:192]
        A.op("vector", lambda e, b=b1s, r=rcp, o=b1q: e.tensor_tensor(
            out=o, in0=b, in1=r, op=ALU.mult), r=[b1s, rcp], w=[b1q], label="b1t")
        A.op("vector", lambda e, o=b1q: e.tensor_scalar(
            out=o, in0=o, scalar1=MAGIC, scalar2=MAGIC, op0=ALU.add, op1=ALU.subtract),
            r=[b1q], w=[b1q], label="b1r")
        A.op("vector", lambda e, o=b1q, s=sw1: e.tensor_tensor(
            out=o, in0=o, in1=s, op=ALU.mult), r=[b1q, sw1], w=[b1q], label="b1m")

        hm = st[:, 192:224]
        for fc in range(NF1):
            wsl = self.big1[:, 2048 + (fc % 2) * 1024: 2048 + (fc % 2) * 1024 + 1024]
            A.dma("sq", wsl.rearrange("p (dk f) -> p dk f", f=128),
                  self.w1qT_sl[fc], label=f"m1w{fc}")
            if 8 <= fc < 16:
                w2c = 16 + (fc - 8)
                A.dma("sq", self.big2[:, (fc - 8) * 1024:(fc - 7) * 1024],
                      self.w2qT_hbm[w2c * 128:(w2c + 1) * 128, :], label=f"w2r{w2c}")
            hfc = self.tile[:, (fc % 2) * 2048:(fc % 2) * 2048 + 2048]
            for tt in range(4):
                ps = self.pf[2 * (fc % 2) + (tt % 2)]
                for dk in range(ND):
                    l = wsl[:, dk * 128:(dk + 1) * 128]
                    r = xq[:, dk * T + tt * 512: dk * T + (tt + 1) * 512]
                    A.op("tensor", lambda e, o=ps, l=l, r=r, s=(dk == 0), p=(dk == ND - 1):
                         e.matmul(o[:, :], l, r, start=s, stop=p),
                         r=[l, r], w=[ps[:, :]], label=f"m1_{fc}_{tt}|{dk}")
                dst = hfc[:, tt * 512:(tt + 1) * 512]
                A.op("scalar", lambda e, i=ps, o=dst, s=sw1[:, fc:fc + 1], b=b1q[:, fc:fc + 1]:
                     e.activation(out=o, in_=i[:, :], func=AF.Relu, scale=s, bias=b),
                     r=[ps[:, :], sw1, b1q], w=[dst], label=f"m1e_{fc}_{tt}")
            A.op("vector", lambda e, i=hfc, o=hm[:, fc:fc + 1]: e.reduce_max(
                out=o, in_=i, axis=AX.X), r=[hfc], w=[hm[:, fc:fc + 1]], label=f"hm{fc}")
            A.dma("aq", self.h_hbm[fc * 128:(fc + 1) * 128, :], hfc, label=f"hs{fc}")

        hmax = st[:, 236:237]
        A.op("vector", lambda e, i=hm, o=hmax: e.reduce_max(out=o, in_=i, axis=AX.X),
             r=[hm], w=[hmax], label="hmax")
        A.op("vector", lambda e, o=hmax: e.tensor_scalar(
            out=o, in0=o, scalar1=EPS_Q, scalar2=None, op0=ALU.max),
            r=[hmax], w=[hmax], label="hmaxc")
        gh = st[:, 237:238]
        self.global_max_collective(hmax, 1, gh, "gmx1")
        inv_sh = st[:, 238:239]
        s_h = st[:, 239:240]
        A.op("vector", lambda e, i=gh, o=inv_sh: e.reciprocal(out=o, in_=i),
             r=[gh], w=[inv_sh], label="invh0")
        A.op("vector", lambda e, o=inv_sh: e.tensor_scalar(
            out=o, in0=o, scalar1=255.0, scalar2=None, op0=ALU.mult),
            r=[inv_sh], w=[inv_sh], label="invh1")
        A.op("vector", lambda e, i=gh, o=s_h: e.tensor_scalar(
            out=o, in0=i, scalar1=1.0 / 255.0, scalar2=None, op0=ALU.mult),
            r=[gh], w=[s_h], label="sh0")

        sw2 = self.stat[:, 224:232]
        ssw2 = st[:, 8:16]
        A.op("vector", lambda e, i=sw2, o=ssw2, s=s_h[:, 0:1]: e.tensor_scalar(
            out=o, in0=i, scalar1=s, scalar2=None, op0=ALU.mult),
            r=[sw2, s_h], w=[ssw2], label="ssw2")
        b2s = st[:, 16:24]
        A.dma("gq", b2s, self.b2.rearrange("(c p) -> p c", p=128), label="ld_b2",
              allow_slow_non_contiguous=True)
        rcp2 = st[:, 24:32]
        A.op("vector", lambda e, i=ssw2, o=rcp2: e.reciprocal(out=o, in_=i),
             r=[ssw2], w=[rcp2], label="b2rcp")
        b2q = st[:, 128:136]
        A.op("vector", lambda e, b=b2s, r=rcp2, o=b2q: e.tensor_tensor(
            out=o, in0=b, in1=r, op=ALU.mult), r=[b2s, rcp2], w=[b2q], label="b2t")
        A.op("vector", lambda e, o=b2q: e.tensor_scalar(
            out=o, in0=o, scalar1=MAGIC, scalar2=MAGIC, op0=ALU.add, op1=ALU.subtract),
            r=[b2q], w=[b2q], label="b2r")
        A.op("vector", lambda e, o=b2q, s=ssw2: e.tensor_tensor(
            out=o, in0=o, in1=s, op=ALU.mult), r=[b2q, ssw2], w=[b2q], label="b2m")
        A.dma("gq", self.row_hbm[0:1, :].rearrange("one (c p) -> p one c", p=128),
              ssw2.rearrange("p (one c) -> p one c", one=1), label="r0st",
              allow_slow_non_contiguous=True)
        A.dma("gq", self.row_hbm[1:2, :].rearrange("one (c p) -> p one c", p=128),
              b2q.rearrange("p (one c) -> p one c", one=1), label="r1st",
              allow_slow_non_contiguous=True)
        ssw2_row = self.rowb[:, 0:1024]
        b2q_row = self.rowb[:, 1024:2048]
        A.dma("gq", ssw2_row, self.row_hbm[0:1, :].broadcast_to([128, D]), label="r0ld")
        A.dma("gq", b2q_row, self.row_hbm[1:2, :].broadcast_to([128, D]), label="r1ld")

        groups = [list(range(g, min(g + 3, NT))) for g in range(0, NT, 3)]

        def m2h_issue(gi2, fc2, g02, gw2):
            hld2 = self.big1[:, (fc2 % 8) * 384:(fc2 % 8) * 384 + gw2]
            A.dma(("sq", "gq", "aq")[fc2 % 3], hld2,
                  self.h_hbm[fc2 * 128:(fc2 + 1) * 128, g02 * 128:g02 * 128 + gw2],
                  label=f"m2h{gi2}_{fc2}")
            return hld2

        prev_post = None
        for gi, tcs in enumerate(groups):
            g0 = tcs[0]
            gw = len(tcs) * 128
            for fc in range(8):
                m2h_issue(gi, fc, g0, gw)
            for fc in range(24, 28):
                wsl = self.tile[:, (fc % 4) * 1024:(fc % 4) * 1024 + 1024]
                A.dma(("gq", "aq")[fc % 2], wsl, self.w2qT_hbm[fc * 128:(fc + 1) * 128, :],
                      label=f"m2w{gi}_{fc}")
            if prev_post is not None:
                prev_post()
            for fcp in range(0, NF1, 2):
                pair = (fcp, fcp + 1)
                hqs, wsls = {}, {}
                for fc in pair:
                    hld = self.big1[:, (fc % 8) * 384:(fc % 8) * 384 + gw]
                    tmp = self.scr32[:, 1024 + (fc % 2) * 384:1024 + (fc % 2) * 384 + gw]
                    hq = self.big1[:, 3072 + (fc % 8) * 384: 3072 + (fc % 8) * 384 + gw]
                    hqs[fc] = hq
                    A.op("vector", lambda e, i=hld, o=tmp, s=inv_sh[:, 0:1]: e.tensor_scalar(
                        out=o, in0=i, scalar1=s, scalar2=MAGIC, op0=ALU.mult, op1=ALU.add),
                        r=[hld, inv_sh], w=[tmp], label=f"m2qa{gi}_{fc}")
                    if fc + 8 < NF1:
                        m2h_issue(gi, fc + 8, g0, gw)
                    A.op("scalar", lambda e, i=tmp, o=hq, b=self.nmag: e.activation(
                        out=o, in_=i, func=AF.Identity, bias=b),
                        r=[tmp, self.nmag], w=[hq], label=f"m2qb{gi}_{fc}")
                    if fc < 16:
                        wsl = self.big0[:, fc * 1024:(fc + 1) * 1024]
                        if gi == 0:
                            A.dma("sq", wsl, self.w2qT_hbm[fc * 128:(fc + 1) * 128, :],
                                  label=f"m2w{gi}_{fc}")
                    elif fc < 24:
                        wsl = self.big2[:, (fc - 16) * 1024:(fc - 15) * 1024]
                    else:
                        wsl = self.tile[:, (fc % 4) * 1024:(fc % 4) * 1024 + 1024]
                    wsls[fc] = wsl
                for ti, tc in enumerate(tcs):
                    for half in range(2):
                        ps = self.pf[2 * ti + half]
                        for fc in pair:
                            l = hqs[fc][:, ti * 128:(ti + 1) * 128]
                            r = wsls[fc][:, half * 512:(half + 1) * 512]
                            A.op("tensor", lambda e, o=ps, l=l, r=r, s=(fc == 0),
                                 p=(fc == NF1 - 1): e.matmul(o[:, :], l, r, start=s, stop=p),
                                 r=[l, r], w=[ps[:, :]], label=f"m2_{gi}_{ti}_{half}|{fc}")
                for fc in pair:
                    if 24 <= fc < 28:
                        fcn = fc + 4
                        wnx = self.tile[:, (fcn % 4) * 1024:(fcn % 4) * 1024 + 1024]
                        A.dma(("gq", "aq")[fcn % 2], wnx,
                              self.w2qT_hbm[fcn * 128:(fcn + 1) * 128, :],
                              label=f"m2w{gi}_{fcn}")

            def mkpost(tcs2):
                def post():
                    for ti, tc in enumerate(tcs2):
                        self.postln_residual([self.pf[2 * ti], self.pf[2 * ti + 1]], tc,
                                             bias_row=b2q_row, scale_row=ssw2_row,
                                             label=f"pl2_{tc}",
                                             then=lambda tc2: self.ln_tc(
                                                 tc2, self.xfm[:, tc2 * D:(tc2 + 1) * D], "ln3"))
                return post
            prev_post = mkpost(tcs)
        prev_post()

    def p3_l1_attn(self):
        A, st = self.A, self.stat
        nc = self.nc

        outwT = self.big2[:, 0:8192]
        for dk in range(ND):
            A.dmat("aq", outwT[:, dk * D:(dk + 1) * D],
                   self.outw_bf[:, dk * 128:(dk + 1) * 128], label=f"owt{dk}")

        xln = self.big0

        def xlnT(tc):
            return xln[:, tc * D:(tc + 1) * D]

        for dk in range(ND):
            A.dmat("aq", self.xfmS(dk, 0, T), self.xlnT_hbm[:, dk * 128:(dk + 1) * 128],
                   label=f"tr3_tr{dk}")

        grow = st[0:1, 240:256]
        A.dma("gq", grow, self.gamma.rearrange("(one g) -> one g", one=1), label="ld_g")
        A.op("scalar", lambda e, o=grow: e.activation(out=o, in_=o, func=AF.Sigmoid),
             r=[grow], w=[grow], label="sig_g")
        A.op("scalar", lambda e, o=grow: e.activation(out=o, in_=o, func=AF.Ln),
             r=[grow], w=[grow], label="ln_g")
        A.dma("gq", self.row_hbm[2:3, 0:16], grow, label="g_st")
        lngb = self.scr32[:, 0:16]
        A.dma("gq", lngb, self.row_hbm[2:3, 0:16].broadcast_to([128, 16]), label="g_ld")
        D_all = self.cons[:, 768:768 + 2048]
        for h in range(16):
            A.op("scalar", lambda e, o=D_all[:, h * 128:(h + 1) * 128], s=lngb[:, h:h + 1]:
                 e.activation(out=o, in_=self.mdec, func=AF.Exp, scale=s),
                 r=[self.mdec, lngb], w=[D_all[:, h * 128:(h + 1) * 128]], label=f"dall{h}")
        lngband = self.scr32[:, 32:40]
        for dc in range(ND):
            A.op("vector", lambda e, i=lngb[0:64, 2 * dc:2 * dc + 1], o=lngband[0:64, dc:dc + 1]:
                 e.tensor_copy(out=o, in_=i), r=[lngb], w=[lngband[0:64, dc:dc + 1]],
                 label=f"lb0_{dc}")
            A.op("vector", lambda e, i=lngb[64:128, 2 * dc + 1:2 * dc + 2], o=lngband[64:128, dc:dc + 1]:
                 e.tensor_copy(out=o, in_=i), r=[lngb], w=[lngband[64:128, dc:dc + 1]],
                 label=f"lb1_{dc}")
        Gq_all = self.cons[:, 2816:2816 + 1024]
        for dc in range(ND):
            A.op("scalar", lambda e, o=Gq_all[:, dc * 128:(dc + 1) * 128], s=lngband[:, dc:dc + 1]:
                 e.activation(out=o, in_=self.gq_in, func=AF.Exp, scale=s),
                 r=[self.gq_in, lngband], w=[Gq_all[:, dc * 128:(dc + 1) * 128]], label=f"gq{dc}")
        GkS = st[:, 224:240]
        for h in range(16):
            A.op("scalar", lambda e, o=GkS[:, h:h + 1], s=lngb[:, h:h + 1]:
                 e.activation(out=o, in_=self.gk_in, func=AF.Exp, scale=s),
                 r=[self.gk_in, lngb], w=[GkS[:, h:h + 1]], label=f"gk{h}")
        gCb = self.scr32[:, 16:32]
        A.op("scalar", lambda e, i=lngb, o=gCb: e.activation(
            out=o, in_=i, func=AF.Exp, scale=128.0),
            r=[lngb], w=[gCb], label="gC")
        gI = self.cons[:, 4864:4864 + 1024]
        for h in range(16):
            idx = self.idA if h % 2 == 0 else self.idB
            A.op("vector", lambda e, i=idx, o=gI[:, h * 64:(h + 1) * 64], s=gCb[:, h:h + 1]:
                 e.tensor_scalar(out=o, in0=i, scalar1=s, scalar2=None, op0=ALU.mult),
                 r=[idx, gCb], w=[gI[:, h * 64:(h + 1) * 64]], label=f"gI{h}")
        SS = self.cons[:, 3840:3840 + 1024]
        A.op("vector", lambda e, o=SS: e.memset(o, 0.0), w=[SS], label="ss0")

        bq1 = st[:, 128:136]
        A.dma("gq", bq1, self.bqkv[0:1024].rearrange("(c p) -> p c", p=128),
              label="ld_bq1", allow_slow_non_contiguous=True)
        bk1 = st[:, 136:144]
        A.dma("gq", bk1, self.bqkv[1024:2048].rearrange("(c p) -> p c", p=128),
              label="ld_bk1", allow_slow_non_contiguous=True)
        bk18 = st[:, 144:152]
        A.op("vector", lambda e, i=bk1, o=bk18: e.tensor_scalar(
            out=o, in0=i, scalar1=0.125, scalar2=None, op0=ALU.mult),
            r=[bk1], w=[bk18], label="bk18")
        bkv_row = self.rowb[:, 0:2048]
        A.dma("gq", bkv_row, self.bqkv[1024:3072].rearrange("(one d) -> one d", one=1)
              .broadcast_to([128, 2048]), label="ld_bkv")
        outb_row = self.rowb[:, 2048:3072]
        A.dma("gq", outb_row, self.outb.rearrange("(one d) -> one d", one=1)
              .broadcast_to([128, D]), label="ld_outb")

        wv_blk = self.big1[:, 0:8192]
        A.dma("sq", wv_blk.rearrange("p (dk f) -> p dk f", f=1024),
              self.wqkvT_hbm[:, 2048:3072].rearrange("(dk p) f -> p dk f", p=128),
              label="ld_wvblk")

        vt_b = [self.tile[:, 0:1024], self.tile[:, 2048:3072]]
        kt_b = [self.tile[:, 1024:2048], self.tile[:, 3072:4096]]
        yt = self.scr32[:, 1024:1536].bitcast(BF16)
        pT = self.scr32[:, 1536:1792].bitcast(BF16)
        qt = self.cons[:, 5888:6144]
        for half in range(2):
            if half == 1:
                self.p_prep_late()
            q_h = self.big0[:, 0:8192]
            k_h = self.big0[:, 8192:16384]
            t0 = half * 1024
            for part, dst, bias, scale in (("q", q_h, bq1, None), ("k", k_h, bk18, 0.125)):
                off = 0 if part == "q" else 1024
                for fc in range(ND):
                    wsl = self.big1[:, 8192 + (fc % 2) * 1024: 8192 + (fc % 2) * 1024 + 1024]
                    A.dma("sq", wsl.rearrange("p (dk f) -> p dk f", f=128),
                          self.wqkv_sl[(off // 128) + fc], label=f"qk{half}{part}{fc}")
                    for tt in range(2):
                        ps = self.pf[tt % 2]
                        for dk in range(ND):
                            l = wsl[:, dk * 128:(dk + 1) * 128]
                            r = self.xfmS(dk, t0 + tt * 512, t0 + (tt + 1) * 512)
                            A.op("tensor", lambda e, o=ps, l=l, r=r, s=(dk == 0), p=(dk == ND - 1):
                                 e.matmul(o[:, :], l, r, start=s, stop=p),
                                 r=[l, r], w=[ps[:, :]], label=f"p3m{half}{part}{fc}_{tt}|{dk}")
                        dd = dst[:, fc * 1024 + tt * 512: fc * 1024 + (tt + 1) * 512]
                        if scale is None:
                            A.op("scalar", lambda e, i=ps, o=dd, b=bias[:, fc:fc + 1]:
                                 e.activation(out=o, in_=i[:, :], func=AF.Identity, bias=b),
                                 r=[ps[:, :], bias], w=[dd], label=f"p3e{half}{part}{fc}_{tt}")
                        else:
                            A.op("scalar", lambda e, i=ps, o=dd, b=bias[:, fc:fc + 1], sc=scale:
                                 e.activation(out=o, in_=i[:, :], func=AF.Identity, bias=b, scale=sc),
                                 r=[ps[:, :], bias], w=[dd], label=f"p3e{half}{part}{fc}_{tt}")
            def g1(tci):
                tc = half * 8 + tci
                vt = vt_b[tci % 2]
                kt = kt_b[tci % 2]
                for dk in range(ND):
                    l = self.xfmS(dk, tc * 128, (tc + 1) * 128)
                    for hf in range(2):
                        ps = self.pf[hf]
                        r = wv_blk[:, dk * 1024 + hf * 512: dk * 1024 + (hf + 1) * 512]
                        A.op("tensor", lambda e, o=ps, l=l, r=r, s=(dk == 0), p=(dk == ND - 1):
                             e.matmul(o[:, :], l, r, start=s, stop=p),
                             r=[l, r], w=[ps[:, :]], label=f"vm{tc}_{hf}|{dk}")
                for hf in range(2):
                    dd = vt[:, hf * 512:(hf + 1) * 512]
                    br = bkv_row[:, 1024 + hf * 512:1024 + (hf + 1) * 512]
                    A.op("vector", lambda e, i=self.pf[hf], o=dd, b=br: e.tensor_tensor(
                        out=o, in0=i[:, :], in1=b, op=ALU.add),
                        r=[self.pf[hf][:, :], bkv_row], w=[dd], label=f"ve{tc}_{hf}")
                for k2 in range(8):
                    src = k_h[:, k2 * 1024 + tci * 128:k2 * 1024 + (tci + 1) * 128]
                    out = self.pb[:, k2 * 128:(k2 + 1) * 128]
                    A.op("tensor", lambda e, o=out, s=src: e.transpose(o, s, self.ident),
                         r=[src, self.ident], w=[out], label=f"ktT{tc}_{k2}", tr=True)
                A.op("vector", lambda e, i=self.pb[:, 0:1024], o=kt, g=GkS: e.tensor_tensor(
                    out=o.rearrange("p (h x) -> p h x", x=64),
                    in0=i.rearrange("p (h x) -> p h x", x=64),
                    in1=g.rearrange("p (h one) -> p h one", one=1).broadcast_to([128, 16, 64]),
                    op=ALU.mult),
                    r=[self.pb[:, 0:1024], GkS], w=[kt], label=f"kte{tc}")

            def g2(tci):
                tc = half * 8 + tci
                vt = vt_b[tci % 2]
                kt = kt_b[tci % 2]
                for qd in range(4):
                    ps_s = self.pf[4 + qd % 2]
                    for hh in range(4):
                        h = qd * 4 + hh
                        b = 64 * (h % 2)
                        l = k_h[:, (h // 2) * 1024 + tci * 128:(h // 2) * 1024 + (tci + 1) * 128][b:b + 64, :]
                        r = q_h[:, (h // 2) * 1024 + tci * 128:(h // 2) * 1024 + (tci + 1) * 128][b:b + 64, :]
                        out = ps_s[:, hh * 128:(hh + 1) * 128]
                        A.op("tensor", lambda e, o=out, l=l, r=r: e.matmul(o, l, r, start=True, stop=True),
                             r=[l, r], w=[out], label=f"sT{tc}_{h}")
                    A.op("vector", lambda e, i=ps_s, o=pT, d=D_all[:, qd * 512:(qd + 1) * 512]:
                         e.tensor_tensor(out=o, in0=i[:, :], in1=d, op=ALU.mult),
                         r=[ps_s[:, :], D_all], w=[pT], label=f"pT{tc}_{qd}")
                    q0 = qd * 2
                    A.op("vector", lambda e, o=qt, i0=q_h, g=Gq_all, q0=q0, tci=tci:
                         e.tensor_tensor(
                             out=o.rearrange("p (f x) -> p f x", x=128),
                             in0=i0.rearrange("p (f t) -> p f t", t=1024)[:, q0:q0 + 2, tci * 128:(tci + 1) * 128],
                             in1=g.rearrange("p (f x) -> p f x", x=128)[:, q0:q0 + 2, :],
                             op=ALU.mult),
                         r=[q_h, Gq_all], w=[qt], label=f"qt{tc}_{qd}")
                    for hh in range(4):
                        h = qd * 4 + hh
                        b = 64 * (h % 2)
                        k2 = h // 2
                        ps_y = self.pf[2 + k2 % 2]
                        ycol = (k2 // 2) * 128
                        out_y = ps_y[b:b + 64, ycol:ycol + 128]
                        l1 = vt[:, h * 64:(h + 1) * 64]
                        r1 = pT[:, hh * 128:(hh + 1) * 128]
                        A.op("tensor", lambda e, o=out_y, l=l1, r=r1, tp=(0, b): e.matmul(
                            o, l, r, start=True, stop=False, tile_position=tp),
                            r=[l1, r1], w=[ps_y[:, ycol:ycol + 128]], label=f"y{tc}_{h}|1")
                        l2 = SS[:, h * 64:(h + 1) * 64]
                        r2 = qt[:, (k2 % 2) * 128:(k2 % 2) * 128 + 128]
                        A.op("tensor", lambda e, o=out_y, l=l2, r=r2, tp=(0, b): e.matmul(
                            o, l, r, start=False, stop=True, tile_position=tp),
                            r=[l2, r2], w=[ps_y[:, ycol:ycol + 128]], label=f"y{tc}_{h}|2")
                        ps_ss = self.pf[6]
                        scol = (h // 2) * 64
                        out_s = ps_ss[b:b + 64, scol:scol + 64]
                        l3 = kt[:, h * 64:(h + 1) * 64]
                        r3 = vt[:, h * 64:(h + 1) * 64]
                        A.op("tensor", lambda e, o=out_s, l=l3, r=r3, tp=(0, b): e.matmul(
                            o, l, r, start=True, stop=False, tile_position=tp),
                            r=[l3, r3], w=[ps_ss[:, scol:scol + 64]], label=f"st{tc}_{h}|1")
                        l4 = gI[:, h * 64:(h + 1) * 64]
                        r4 = SS[:, h * 64:(h + 1) * 64]
                        A.op("tensor", lambda e, o=out_s, l=l4, r=r4, tp=(0, b): e.matmul(
                            o, l, r, start=False, stop=True, tile_position=tp),
                            r=[l4, r4], w=[ps_ss[:, scol:scol + 64]], label=f"st{tc}_{h}|2")
                for i2 in range(2):
                    A.op("scalar", lambda e, i=self.pf[2 + i2], o=yt[:, i2 * 512:(i2 + 1) * 512]:
                         e.copy(out=o, in_=i[:, :]),
                         r=[self.pf[2 + i2][:, :]], w=[yt[:, i2 * 512:(i2 + 1) * 512]],
                         label=f"ye{tc}_{i2}")
                for band in range(2):
                    src = self.pf[6][band * 64:(band + 1) * 64, 0:512]
                    dst = SS[band * 64:(band + 1) * 64, :].rearrange(
                        "p (h two x) -> p h two x", two=2, x=64)[:, :, band, :]
                    A.op("scalar", lambda e, i=src, o=dst: e.copy(
                        out=o, in_=i.rearrange("p (h x) -> p h x", x=64)),
                        r=[self.pf[6][:, :]], w=[dst], label=f"se{tc}_{band}")
                ps_o = [self.pf[2], self.pf[3]]
                for k in range(ND):
                    l = yt[:, (k % 2) * 512 + (k // 2) * 128: (k % 2) * 512 + (k // 2) * 128 + 128]
                    for hf in range(2):
                        r = outwT[:, k * D + hf * 512: k * D + (hf + 1) * 512]
                        out = ps_o[hf]
                        A.op("tensor", lambda e, o=out, l=l, r=r, s=(k == 0), p=(k == ND - 1):
                             e.matmul(o[:, :], l, r, start=s, stop=p),
                             r=[l, r], w=[out[:, :]], label=f"op3{tc}_{hf}|{k}")
                self.postln_residual(ps_o, tc, bias_row=outb_row, label=f"pl3_{tc}")

            g1(0)
            for tci in range(8):
                if tci + 1 < 8:
                    g1(tci + 1)
                g2(tci)

    def p4_l1_ffn(self):
        A, st = self.A, self.stat

        xln = self.big0

        def xlnT(tc):
            return xln[:, tc * D:(tc + 1) * D]

        self.ln_into(self.resT, xlnT, "ln4")
        self.transpose_roundtrip(xlnT, "tr4")

        def wdT(hk):
            if hk < 16:
                return self.big0[:, hk * D:(hk + 1) * D]
            return self.big2[:, (hk - 16) * D:(hk - 15) * D]

        for hk in range(24):
            A.dmat("aq", wdT(hk), self.wd_bf[:, hk * 128:(hk + 1) * 128], label=f"wdt{hk}")

        NH = H1F // 128
        for hk in range(NH):
            gsl = self.big1[:, (hk % 2) * 2048:(hk % 2) * 2048 + 1024]
            usl = self.big1[:, (hk % 2) * 2048 + 1024:(hk % 2) * 2048 + 2048]
            A.dma("sq", gsl.rearrange("p (dk f) -> p dk f", f=128),
                  self.wgT_sl[hk], label=f"ga{hk}")
            A.dma("sq", usl.rearrange("p (dk f) -> p dk f", f=128),
                  self.wuT_sl[hk], label=f"ua{hk}")
            hsb = self.big1[:, 4096 + (hk % 2) * 2048:4096 + (hk % 2) * 2048 + 2048]
            for tt in range(4):
                psg = self.pf[tt % 3]
                psu = self.pf[3 + tt % 3]
                for dk in range(ND):
                    r = self.xfmS(dk, tt * 512, (tt + 1) * 512)
                    lg = gsl[:, dk * 128:(dk + 1) * 128]
                    A.op("tensor", lambda e, o=psg, l=lg, r=r, s=(dk == 0), p=(dk == ND - 1):
                         e.matmul(o[:, :], l, r, start=s, stop=p),
                         r=[lg, r], w=[psg[:, :]], label=f"gm{hk}_{tt}|{dk}")
                for dk in range(ND):
                    r = self.xfmS(dk, tt * 512, (tt + 1) * 512)
                    lu = usl[:, dk * 128:(dk + 1) * 128]
                    A.op("tensor", lambda e, o=psu, l=lu, r=r, s=(dk == 0), p=(dk == ND - 1):
                         e.matmul(o[:, :], l, r, start=s, stop=p),
                         r=[lu, r], w=[psu[:, :]], label=f"um{hk}_{tt}|{dk}")
                sil = self.scr32[:, (tt % 2) * 512:(tt % 2) * 512 + 512]
                A.op("scalar", lambda e, i=psg, o=sil: e.activation(
                    out=o, in_=i[:, :], func=AF.Silu, scale=1.0 / 127.0),
                    r=[psg[:, :]], w=[sil], label=f"sil{hk}_{tt}")
                dst = hsb[:, tt * 512:(tt + 1) * 512]
                A.op("vector", lambda e, u=psu, s=sil, o=dst: e.scalar_tensor_tensor(
                    out=o, in0=u[:, :], scalar=1.0 / 127.0, in1=s, op0=ALU.mult, op1=ALU.mult),
                    r=[psu[:, :], sil], w=[dst], label=f"hmul{hk}_{tt}")
            A.dma("aq", self.h2_hbm[hk * 128:(hk + 1) * 128, :], hsb, label=f"h2s{hk}")

        groups = [list(range(g, min(g + 3, NT))) for g in range(0, NT, 3)]
        for gi, tcs in enumerate(groups):
            g0 = tcs[0]
            gw = len(tcs) * 128
            for hk in range(NH):
                s8 = hk % 8
                if s8 < 4:
                    hq = self.big1[:, s8 * 512: s8 * 512 + gw]
                else:
                    hq = self.big1[:, 8192 + (s8 - 4) * 512: 8192 + (s8 - 4) * 512 + gw]
                A.dma(("sq", "gq", "aq")[hk % 3], hq,
                      self.h2_hbm[hk * 128:(hk + 1) * 128, g0 * 128:g0 * 128 + gw],
                      label=f"hb{gi}_{hk}")
                for ti in range(len(tcs)):
                    l = hq[:, ti * 128:(ti + 1) * 128]
                    for hf in range(2):
                        ps = self.pf[2 * ti + hf]
                        r = wdT(hk)[:, hf * 512:(hf + 1) * 512]
                        A.op("tensor", lambda e, o=ps, l=l, r=r, s=(hk == 0), p=(hk == NH - 1):
                             e.matmul(o[:, :], l, r, start=s, stop=p),
                             r=[l, r], w=[ps[:, :]], label=f"dm{gi}_{ti}_{hf}|{hk}")
            for ti, tc in enumerate(tcs):
                self.postln_residual([self.pf[2 * ti], self.pf[2 * ti + 1]], tc,
                                     scale_const=1.0 / 127.0, label=f"pl4_{tc}")

    def p5_store_out(self):
        A = self.A
        for tc in range(NT):
            A.dma("aq", self.y_out[tc * 128:(tc + 1) * 128, :], self.resT(tc),
                  label=f"st_y{tc}")
        A.op("gpsimd", lambda e: e.engine_nop(),
             r=[self.y_out], w=(), label="final_wait")


def build_nc(upto=4):
    nc = bass.Bass("TRN2", target_bir_lowering=False, debug=False)
    with ExitStack() as ctx:
        mk = MK(nc, ctx, upto)
        mk.build()
        block = ctx.enter_context(nc.Block())
        mk.A.build(block, mk.sems)
    return nc


_IN_ORDER = [
    "L0_attn_x", "L0_attn_Wq", "L0_attn_bq", "L0_attn_Wk", "L0_attn_bk",
    "L0_attn_Wv", "L0_attn_bv", "L0_attn_Wo", "L0_attn_bo",
    "L0_ffn_w1", "L0_ffn_b1", "L0_ffn_w2", "L0_ffn_b2",
    "L1_attn_Wqkv_w", "L1_attn_Wqkv_b", "L1_attn_out_w", "L1_attn_out_b",
    "L1_attn_gamma_raw", "L1_ffn_w_gate", "L1_ffn_w_up", "L1_ffn_w_down",
]
_NAME_MAP = {
    "L0_attn_Wq": "Wq", "L0_attn_bq": "bq", "L0_attn_Wk": "Wk", "L0_attn_bk": "bk",
    "L0_attn_Wv": "Wv", "L0_attn_bv": "bv", "L0_attn_Wo": "Wo", "L0_attn_bo": "bo",
    "L0_ffn_w1": "w1", "L0_ffn_b1": "b1", "L0_ffn_w2": "w2", "L0_ffn_b2": "b2",
    "L1_attn_Wqkv_w": "wqkv", "L1_attn_Wqkv_b": "bqkv", "L1_attn_out_w": "outw",
    "L1_attn_out_b": "outb", "L1_attn_gamma_raw": "gamma",
    "L1_ffn_w_gate": "wg", "L1_ffn_w_up": "wu", "L1_ffn_w_down": "wd",
}


def kernel(**inputs):
    nc = build_nc(upto=4)
    consts = host_constants()
    shared = {}
    for k, v in inputs.items():
        if k == "L0_attn_x":
            continue
        arr = np.ascontiguousarray(np.asarray(v))
        shared[_NAME_MAP[k]] = arr
    for k, v in consts.items():
        shared[k] = np.ascontiguousarray(v)
    x = np.ascontiguousarray(np.asarray(inputs["L0_attn_x"], dtype=np.float32))
    in_maps = []
    for b in range(8):
        m = dict(shared)
        m["x"] = x[b]
        in_maps.append(m)
    res = run_bass_kernel_spmd(nc, in_maps, core_ids=list(range(8)))
    out = np.stack([np.asarray(res.results[b]["y"]) for b in range(8)], axis=0)
    return out.astype(np.float32)


# version 28
# speedup vs baseline: 1.0201x; 1.0201x over previous
import numpy as np
import ml_dtypes
import concourse.bass as bass
import concourse.mybir as mybir
from concourse.bass_utils import run_bass_kernel_spmd
from contextlib import ExitStack

F32 = mybir.dt.float32
BF16 = mybir.dt.bfloat16
I32 = mybir.dt.int32
AF = mybir.ActivationFunctionType
ALU = mybir.AluOpType
AX = mybir.AxisListType

MAGIC = 1.5 * 2.0 ** 23
T = 2048
D = 1024
NT = 16
ND = 8
H0 = 16
HD = 64
NC0 = 32
F1 = 4096
H1F = 3072
EPS_LN = 1e-5
EPS_Q = 1e-8

ENGINES = ("tensor", "vector", "scalar", "gpsimd", "sync")
NQSEM = 8
TT_DROP = True
PREP = True
PREP_W1 = True
PREP_TR = True
PREP_W2 = True
PREP_QK = False
PREP_W2T = True
PSUM_WIDE = True


def _ap_interval(ap):
    tname = type(ap.tensor).__name__
    if PSUM_WIDE and "PSum" in tname:
        return (ap.tensor.name, 0, 1 << 60)
    esz = mybir.dt.size(ap.dtype)
    lo = ap.offset
    hi = ap.offset
    if "DRam" in tname:
        for step, count in ap.ap:
            if count <= 1:
                continue
            ext = step * (count - 1)
            if ext >= 0:
                hi += ext
            else:
                lo += ext
    else:
        tshape = ap.tensor.shape
        pbytes = mybir.dt.size(ap.tensor.dtype)
        for s in tshape[1:]:
            pbytes *= s
        pitch = pbytes // esz
        lo = hi = ap.offset % pitch
        for step, count in ap.ap:
            if count <= 1 or abs(step) >= pitch:
                continue
            ext = step * (count - 1)
            if ext >= 0:
                hi += ext
            else:
                lo += ext
    return (ap.tensor.name, lo * esz, (hi + 1) * esz)


class Op:
    __slots__ = ("engine", "emit", "kind", "queue", "deps", "idx", "inc_id",
                 "flagged", "label", "qidx", "ownsem", "_maxdep", "tr")

    def __init__(self, engine, emit, kind, queue, label):
        self.tr = False
        self.engine = engine
        self.emit = emit
        self.kind = kind
        self.queue = queue
        self.deps = []
        self.idx = -1
        self.qidx = -1
        self.inc_id = None
        self.flagged = False
        self.ownsem = None
        self.label = label


class Asm:

    def __init__(self, nc):
        self.nc = nc
        self.ops = []
        self.eng_ops = {e: [] for e in ENGINES}
        self.queues = {"sq": "sync", "aq": "scalar", "gq": "gpsimd"}
        self.q_ops = {q: [] for q in self.queues}
        self.seg = {}
        self.n_ccsem = 0

    def _touch(self, op, aps, is_write):
        for ap in aps:
            name, lo, hi = _ap_interval(ap)
            segs = self.seg.setdefault(name, [])
            new = []
            deps = op.deps
            if is_write:
                keep = []
                for s in segs:
                    if s[1] <= lo or s[0] >= hi:
                        keep.append(s)
                        continue
                    if s[2] is not None:
                        deps.append(s[2])
                    deps.extend(s[3])
                    if s[0] < lo:
                        keep.append([s[0], lo, s[2], list(s[3])])
                    if s[1] > hi:
                        keep.append([hi, s[1], s[2], list(s[3])])
                keep.append([lo, hi, op, []])
                self.seg[name] = keep
            else:
                for s in segs:
                    if s[1] <= lo or s[0] >= hi:
                        continue
                    if s[2] is not None:
                        deps.append(s[2])
                    s[3].append(op)
                if not segs:
                    segs.append([lo, hi, None, [op]])

    def op(self, engine, emit, r=(), w=(), label="", tr=False):
        o = Op(engine, emit, "c", None, label)
        o.tr = tr
        self._touch(o, r, False)
        self._touch(o, w, True)
        self.ops.append(o)
        self.eng_ops[engine].append(o)
        return o

    def dma(self, queue, out, in_, label="", **kw):
        eng = self.queues[queue]
        o = Op(eng, lambda e, out=out, in_=in_, kw=kw: e.dma_start(out=out, in_=in_, **kw),
               "d", queue, label)
        self._touch(o, [in_], False)
        self._touch(o, [out], True)
        self.ops.append(o)
        self.eng_ops[eng].append(o)
        self.q_ops[queue].append(o)
        return o

    def dmat(self, queue, out, in_, label=""):
        eng = self.queues[queue]
        assert eng in ("sync", "scalar")
        o = Op(eng, lambda e, out=out, in_=in_: e.dma_start_transpose(out, in_),
               "d", queue, label)
        self._touch(o, [in_], False)
        self._touch(o, [out], True)
        last = getattr(self, "_last_dmat", None)
        if last is not None:
            o.deps.append(last)
        self._last_dmat = o
        self.ops.append(o)
        self.eng_ops[eng].append(o)
        self.q_ops[queue].append(o)
        return o

    def cc(self, emit, r=(), w=(), label="cc"):
        o = Op("gpsimd", emit, "cc", None, label)
        self._touch(o, r, False)
        self._touch(o, w, True)
        self.ops.append(o)
        self.eng_ops["gpsimd"].append(o)
        self.n_ccsem += 1
        return o

    def build(self, block, sems):
        for e in ENGINES:
            for i, o in enumerate(self.eng_ops[e]):
                o.idx = i
        for q in self.q_ops:
            for i, o in enumerate(self.q_ops[q]):
                o.qidx = i
        tforce = {}
        prev = None
        for o in self.eng_ops["tensor"]:
            if prev is not None and (prev.tr or o.tr):
                tforce[o.idx] = prev.idx
            prev = o
        ccs = []
        for o in self.ops:
            if o.kind == "cc":
                o.ownsem = sems["cc"][len(ccs)]
                ccs.append(o)
            maxdep = {}
            for d in o.deps:
                if d is o:
                    continue
                if d.kind in ("d", "cc"):
                    continue
                if d.engine == o.engine == "gpsimd":
                    continue
                if (d.engine == o.engine == "tensor" and TT_DROP and
                        "|" in o.label and o.label.split("|")[0] == d.label.split("|")[0]):
                    continue
                if maxdep.get(d.engine, -1) < d.idx:
                    maxdep[d.engine] = d.idx
            if o.engine == "tensor" and o.idx in tforce:
                if maxdep.get("tensor", -1) < tforce[o.idx]:
                    maxdep["tensor"] = tforce[o.idx]
            o._maxdep = maxdep
            for e2, i2 in maxdep.items():
                self.eng_ops[e2][i2].flagged = True
        for e in ENGINES:
            n = 0
            for o in self.eng_ops[e]:
                if o.kind == "c" and o.flagged:
                    n += 1
                    o.inc_id = n
        plans = {e: [] for e in ENGINES}
        waited = {}
        drained = {e: 0 for e in ENGINES}
        for o in self.ops:
            waits = []
            need_drain = False
            for d in o.deps:
                if d is o:
                    continue
                if d.kind == "d":
                    key = ("q", d.queue, d.qidx % NQSEM)
                    v = 16 * (d.qidx // NQSEM + 1)
                    if waited.get((o.engine, key), 0) < v:
                        waits.append((key, v))
                elif d.kind == "cc":
                    key = ("cc", id(d.ownsem))
                    if waited.get((o.engine, key), 0) < 1:
                        waits.append(((key[0], key[1], d.ownsem), 1))
                elif d.engine == o.engine == "gpsimd":
                    if d.idx >= drained[o.engine]:
                        need_drain = True
            for e2, i2 in o._maxdep.items():
                d = self.eng_ops[e2][i2]
                key = ("e", e2)
                v = d.inc_id
                assert v is not None
                if waited.get((o.engine, key), 0) < v:
                    waits.append((key, v))
            if o.kind == "d" and o.qidx >= NQSEM:
                key = ("q", o.queue, o.qidx % NQSEM)
                v = 16 * (o.qidx // NQSEM)
                if waited.get((o.engine, key), 0) < v:
                    waits.append((key, v))
            final_waits = []
            seen = set()
            for key, v in waits:
                k2 = key[:3] if key[0] != "cc" else key[:2]
                if waited.get((o.engine, k2), 0) < v and (k2, v) not in seen:
                    waited[(o.engine, k2)] = v
                    seen.add((k2, v))
                    final_waits.append((key, v))
            if need_drain:
                drained[o.engine] = o.idx
            plans[o.engine].append((final_waits, need_drain, o))
        self._plans = plans
        self._sems = sems

        nc = self.nc
        qsem = sems["q"]
        esem = sems["e"]

        def run_engine(engobj, ename):
            for waits, need_drain, o in plans[ename]:
                for key, v in waits:
                    if key[0] == "q":
                        engobj.wait_ge(qsem[(key[1], key[2])], v)
                    elif key[0] == "e":
                        engobj.wait_ge(esem[key[1]], v)
                    else:
                        engobj.wait_ge(key[2], v)
                if need_drain:
                    engobj.drain()
                inst = o.emit(engobj)
                if o.kind == "d":
                    inst.then_inc(qsem[(o.queue, o.qidx % NQSEM)], 16)
                elif o.kind == "cc":
                    inst.then_inc(o.ownsem, 1)
                elif o.flagged:
                    if ename == "gpsimd":
                        engobj.drain().then_inc(esem[ename], 1)
                    else:
                        inst.then_inc(esem[ename], 1)

        @block.tensor
        def _(e):
            run_engine(e, "tensor")

        @block.vector
        def _(e):
            run_engine(e, "vector")

        @block.scalar
        def _(e):
            run_engine(e, "scalar")

        @block.gpsimd
        def _(e):
            run_engine(e, "gpsimd")

        @block.sync
        def _(e):
            run_engine(e, "sync")



def host_constants():
    c = {}
    c["hc_ident"] = np.eye(128, dtype=ml_dtypes.bfloat16)
    ends = np.arange(0, T, 64)
    starts = np.clip(ends - 63, 0, None)
    lens = (ends - starts + 1).astype(np.float32)
    A = np.zeros((T, NC0), np.float32)
    for ch in range(NC0):
        A[starts[ch]:ends[ch] + 1, ch] = 1.0 / lens[ch]
    c["hc_A"] = A.astype(ml_dtypes.bfloat16)
    msk = np.where(ends[None, :] <= np.arange(T)[:, None], 0.0, -1e30).astype(np.float32)
    c["hc_mask"] = msk
    j = np.arange(128)[:, None]
    i = np.arange(128)[None, :]
    c["hc_mdec"] = np.where(i >= j, (i - j).astype(np.float32), 1.0e4)
    c["hc_gq"] = np.broadcast_to((i + 1).astype(np.float32), (128, 128)).copy()
    c["hc_gk"] = (127 - np.arange(128, dtype=np.float32))[:, None].copy()
    ia = np.zeros((128, 64), ml_dtypes.bfloat16)
    ia[0:64] = np.eye(64)
    ib = np.zeros((128, 64), ml_dtypes.bfloat16)
    ib[64:128] = np.eye(64)
    c["hc_idA"] = ia
    c["hc_idB"] = ib
    return c



class MK:
    def __init__(self, nc, ctx, upto):
        self.nc = nc
        self.ctx = ctx
        self.upto = upto
        e = ctx.enter_context
        self.A = None

        dram = nc.dram_tensor
        self.x_in = dram("x", [T, D], F32, kind="ExternalInput").ap()
        self.y_out = dram("y", [T, D], F32, kind="ExternalOutput").ap()
        names = [("Wq", [D, D]), ("bq", [D]), ("Wk", [D, D]), ("bk", [D]),
                 ("Wv", [D, D]), ("bv", [D]), ("Wo", [D, D]), ("bo", [D]),
                 ("w1", [F1, D]), ("b1", [F1]), ("w2", [D, F1]), ("b2", [D]),
                 ("wqkv", [3 * D, D]), ("bqkv", [3 * D]), ("outw", [D, D]),
                 ("outb", [D]), ("gamma", [16])]
        for n, shp in names:
            setattr(self, n, dram(n, shp, F32, kind="ExternalInput").ap())
        for n, shp in [("wg", [H1F, D]), ("wu", [H1F, D]), ("wd", [D, H1F])]:
            setattr(self, n, dram(n, shp, I32, kind="ExternalInput").ap())
        for n, shp, dt in [("hc_ident", [128, 128], BF16), ("hc_A", [T, NC0], BF16),
                           ("hc_mask", [T, NC0], F32), ("hc_mdec", [128, 128], F32),
                           ("hc_gq", [128, 128], F32), ("hc_gk", [128, 1], F32),
                           ("hc_idA", [128, 64], BF16), ("hc_idB", [128, 64], BF16)]:
            setattr(self, n, dram(n, shp, dt, kind="ExternalInput").ap())

        self.xlnT_hbm = dram("xlnT_hbm", [T, D], BF16).ap()
        self.w1q_hbm = dram("w1q_hbm", [F1, D], BF16).ap()
        self.w1qT_hbm = dram("w1qT_hbm", [D, F1], BF16).ap()
        self.w2q_hbm = dram("w2q_hbm", [D, F1], BF16).ap()
        self.w2qT_hbm = dram("w2qT_hbm", [F1, D], BF16).ap()
        self.h_hbm = dram("h_hbm", [F1, T], BF16).ap()
        self.wqkv_bf = dram("wqkv_bf", [3 * D, D], BF16).ap()
        self.wqkvT_hbm = dram("wqkvT_hbm", [D, 3 * D], BF16).ap()
        self.outw_bf = dram("outw_bf", [D, D], BF16).ap()
        self.wg_bf = dram("wg_bf", [H1F, D], BF16).ap()
        self.wu_bf = dram("wu_bf", [H1F, D], BF16).ap()
        self.wd_bf = dram("wd_bf", [D, H1F], BF16).ap()
        self.wgT_sl = dram("wgT_sl", [24, 128, ND, 128], BF16).ap()
        self.wuT_sl = dram("wuT_sl", [24, 128, ND, 128], BF16).ap()
        self.w1qT_sl = dram("w1qT_sl", [32, 128, ND, 128], BF16).ap()
        self.wqkv_sl = dram("wqkv_sl", [16, 128, ND, 128], BF16).ap()
        self.wq_sl = dram("wq_sl", [8, 128, ND, 128], BF16).ap()
        self.wk_sl = dram("wk_sl", [8, 128, ND, 128], BF16).ap()
        self.wv_sl = dram("wv_sl", [8, 128, ND, 128], BF16).ap()
        self.wdT_hbm = dram("wdT_hbm", [H1F, D], BF16).ap()
        self.h2_hbm = dram("h2_hbm", [H1F, T], BF16).ap()
        self.row_hbm = dram("row_hbm", [8, D], F32).ap()
        self.cc_a = [dram(f"cc_a{i}", [1, 128], F32).ap() for i in range(2)]
        self.cc_b = [dram(f"cc_b{i}", [1, 128], F32).ap() for i in range(2)]

        sb = nc.sbuf_tensor
        self.res = e(sb("res", [128, NT * D], F32))
        self.xfm = e(sb("xfm", [128, ND * T], BF16))
        self.big0 = e(sb("big0", [128, NT * D], BF16))
        self.big1 = e(sb("big1", [128, 10240], BF16))
        self.big2 = e(sb("big2", [128, 8192], BF16))
        self.scr32 = e(sb("scr32", [128, 2048], F32))
        self.tile = e(sb("tile", [128, 4096], BF16))
        self.rowb = e(sb("rowb", [128, 3072], BF16))
        self.cons = e(sb("cons", [128, 6144], BF16))
        self.con32 = e(sb("con32", [128, 800], F32))
        self.stat = e(sb("stat", [128, 256], F32))
        self.pf = [e(nc.psum_tensor(f"pf{i}", [128, 512], F32)) for i in range(7)]
        self.pb = e(nc.psum_tensor("pb", [128, 1024], BF16))

        self.sems = {
            "e": {en: e(nc.semaphore(f"se_{en}")) for en in ENGINES},
            "q": {(q, i): e(nc.semaphore(f"sq_{q}{i}"))
                  for q in ("sq", "aq", "gq") for i in range(NQSEM)},
            "cc": [e(nc.semaphore(f"scc{i}")) for i in range(2)],
        }
        self.A = Asm(nc)

    def resT(self, tc):
        return self.res[:, tc * D:(tc + 1) * D]

    def xfmS(self, dk, t0, t1):
        return self.xfm[:, dk * T + t0: dk * T + t1]

    def ln_into(self, src_of_tc, dst_of_tc, label, nt=NT, track_max=None):
        A, st = self.A, self.stat
        for tc in range(nt):
            src = src_of_tc(tc)
            bn6 = st[:, 32:44]
            for g in range(2):
                A.op("vector", lambda e, s=src, o=bn6, g=g: e.bn_stats(
                    out=o[:, 6 * g:6 * g + 6], in_=s[:, 512 * g:512 * g + 512]),
                    r=[src], w=[bn6[:, 6 * g:6 * g + 6]], label=f"{label}_bnst{tc}_{g}")
            mv = st[:, 44 + 2 * tc: 46 + 2 * tc]
            A.op("vector", lambda e, i=bn6, o=mv: e.bn_aggr(out=o, in_=i),
                 r=[bn6], w=[mv], label=f"{label}_bnag{tc}")
        mvall = st[:, 44:44 + 2 * nt]
        var_v = mvall.rearrange("p (t two) -> p t two", two=2)[:, :, 1:2]
        vareps = st[:, 80:80 + nt]
        A.op("vector", lambda e, i=var_v, o=vareps: e.tensor_scalar(
            out=o, in0=i.rearrange("p t one -> p (t one)"), scalar1=EPS_LN, scalar2=None, op0=ALU.add),
            r=[mvall], w=[vareps], label=f"{label}_veps")
        std = st[:, 96:96 + nt]
        A.op("scalar", lambda e, i=vareps, o=std: e.activation(out=o, in_=i, func=AF.Ln),
             r=[vareps], w=[std], label=f"{label}_ln")
        rstd = st[:, 112:112 + nt]
        A.op("scalar", lambda e, i=std, o=rstd: e.activation(out=o, in_=i, func=AF.Exp, scale=-0.5),
             r=[std], w=[rstd], label=f"{label}_rexp")
        for tc in range(nt):
            src = src_of_tc(tc)
            dst = dst_of_tc(tc)
            m = mvall[:, 2 * tc:2 * tc + 1]
            r = rstd[:, tc:tc + 1]
            A.op("vector", lambda e, s=src, o=dst, m=m, r=r: e.tensor_scalar(
                out=o, in0=s, scalar1=m, scalar2=r, op0=ALU.subtract, op1=ALU.mult),
                r=[src, mvall, rstd], w=[dst], label=f"{label}_nrm{tc}")

    def transpose_roundtrip(self, tm_of_tc, label):
        A = self.A
        hb = self.xlnT_hbm
        for tc in range(NT):
            src = tm_of_tc(tc)
            A.dma("aq", hb[tc * 128:(tc + 1) * 128, :], src, label=f"{label}_st{tc}")
        for dk in range(ND):
            A.dmat("aq", self.xfmS(dk, 0, T), hb[:, dk * 128:(dk + 1) * 128],
                   label=f"{label}_tr{dk}")

    def postln_residual(self, psum_list, tc, bias_row=None, label="",
                        scale_row=None, scale_const=None, then=None):
        A, st = self.A, self.stat
        ao = self.scr32[:, 0:1024]
        for i, ps in enumerate(psum_list):
            dst = ao[:, i * 512:(i + 1) * 512]
            if scale_row is not None:
                sr = scale_row[:, i * 512:(i + 1) * 512]
                A.op("vector", lambda e, p=ps, s=sr, o=dst:
                     e.tensor_tensor(out=o, in0=p[:, :], in1=s, op=ALU.mult),
                     r=[ps[:, :], scale_row], w=[dst], label=f"{label}_s{i}")
                if bias_row is not None:
                    br = bias_row[:, i * 512:(i + 1) * 512]
                    A.op("vector", lambda e, b=br, o=dst:
                         e.tensor_tensor(out=o, in0=o, in1=b, op=ALU.add),
                         r=[dst, bias_row], w=[dst], label=f"{label}_b{i}")
            elif scale_const is not None:
                A.op("vector", lambda e, p=ps, o=dst, c=scale_const:
                     e.tensor_scalar(out=o, in0=p[:, :], scalar1=c, scalar2=None,
                                     op0=ALU.mult),
                     r=[ps[:, :]], w=[dst], label=f"{label}_c{i}")
            elif bias_row is not None:
                br = bias_row[:, i * 512:(i + 1) * 512]
                A.op("vector", lambda e, p=ps, b=br, o=dst:
                     e.tensor_tensor(out=o, in0=p[:, :], in1=b, op=ALU.add),
                     r=[ps[:, :], bias_row], w=[dst], label=f"{label}_b{i}")
            else:
                A.op("vector", lambda e, p=ps, o=dst:
                     e.tensor_copy(out=o, in_=p[:, :]),
                     r=[ps[:, :]], w=[dst], label=f"{label}_cp{i}")
        bn6 = st[:, 40:52]
        for g in range(2):
            A.op("vector", lambda e, s=ao, g=g, o=bn6: e.bn_stats(
                out=o[:, 6 * g:6 * g + 6], in_=s[:, 512 * g:512 * g + 512]),
                r=[ao], w=[bn6[:, 6 * g:6 * g + 6]], label=f"{label}_bn{g}")
        mv = st[:, 52:54]
        A.op("vector", lambda e, i=bn6, o=mv: e.bn_aggr(out=o, in_=i),
             r=[bn6], w=[mv], label=f"{label}_bnag")
        ve = st[:, 54:55]
        A.op("vector", lambda e, m=mv, o=ve: e.tensor_scalar(
            out=o, in0=m[:, 1:2], scalar1=EPS_LN, scalar2=None, op0=ALU.add),
            r=[mv], w=[ve], label=f"{label}_veps")
        sd = st[:, 55:56]
        A.op("scalar", lambda e, i=ve, o=sd: e.activation(out=o, in_=i, func=AF.Ln),
             r=[ve], w=[sd], label=f"{label}_ln")
        rs = st[:, 56:57]
        A.op("scalar", lambda e, i=sd, o=rs: e.activation(out=o, in_=i, func=AF.Exp, scale=-0.5),
             r=[sd], w=[rs], label=f"{label}_rexp")
        mr = st[:, 57:58]
        A.op("vector", lambda e, m=mv, r=rs, o=mr: e.tensor_tensor(
            out=o, in0=m[:, 0:1], in1=r, op=ALU.mult),
            r=[mv, rs], w=[mr], label=f"{label}_mr")
        dst = self.resT(tc)
        A.op("vector", lambda e, a=ao, r=rs, d=dst: e.scalar_tensor_tensor(
            out=d, in0=a, scalar=r, in1=d, op0=ALU.mult, op1=ALU.add),
            r=[ao, rs, dst], w=[dst], label=f"{label}_acc")
        A.op("vector", lambda e, d=dst, m=mr: e.tensor_scalar(
            out=d, in0=d, scalar1=m, scalar2=None, op0=ALU.subtract),
            r=[dst, mr], w=[dst], label=f"{label}_sub")
        if then is not None:
            then(tc)

    def drip(self, n=2):
        while n > 0 and self._prep_units:
            self._prep_units.pop(0)()
            n -= 1

    def drip_flush(self):
        while self._prep_units:
            self._prep_units.pop(0)()

    def build(self):
        self.p0_load_consts()
        self._prep_units = []
        if PREP:
            self.p_prep_build_units()
        self.p1_l0_attn()
        self.drip_flush()
        if self.upto >= 2:
            self.p2_l0_ffn()
        if self.upto >= 3:
            self.p3_l1_attn()
        if self.upto >= 4:
            self.p4_l1_ffn()
        self.p5_store_out()

    def p0_load_consts(self):
        A = self.A
        xs = self.x_in.rearrange("(a p) d -> p a d", p=128)
        rd = self.res[:, :].rearrange("p (a d) -> p a d", d=D)
        for tc in range(NT):
            A.dma("sq", rd[:, tc:tc + 1, :], xs[:, tc:tc + 1, :], label=f"ld_x{tc}")
        self.ident = self.cons[:, 0:128]
        A.dma("sq", self.ident, self.hc_ident, label="ld_ident")
        self.A_sb = self.cons[:, 128:128 + NT * NC0]
        A.dma("sq", self.A_sb.rearrange("p (a c) -> p a c", c=NC0),
              self.hc_A.rearrange("(a p) c -> p a c", p=128), label="ld_A")
        self.idA = self.cons[:, 640:704]
        self.idB = self.cons[:, 704:768]
        A.dma("sq", self.idA, self.hc_idA, label="ld_idA")
        A.dma("sq", self.idB, self.hc_idB, label="ld_idB")
        self.mask_sb = self.con32[:, 0:NT * NC0]
        A.dma("sq", self.mask_sb.rearrange("p (a c) -> p a c", c=NC0),
              self.hc_mask.rearrange("(a p) c -> p a c", p=128), label="ld_mask")
        self.mdec = self.con32[:, 512:640]
        A.dma("sq", self.mdec, self.hc_mdec, label="ld_mdec")
        self.gq_in = self.con32[:, 640:768]
        A.dma("sq", self.gq_in, self.hc_gq, label="ld_gq")
        self.gk_in = self.con32[:, 768:769]
        A.dma("sq", self.gk_in, self.hc_gk, label="ld_gk")
        self.nmag = self.con32[:, 769:770]
        A.op("vector", lambda e, o=self.nmag: e.memset(o, -MAGIC), w=[self.nmag], label="nmag")


    def p_prep_build_units(self):
        A, st = self.A, self.stat
        b1 = self.big1
        U = self._prep_units
        fsl = [b1[:, 4096:6144].bitcast(F32), b1[:, 6144:8192].bitcast(F32)]
        osl = [b1[:, 8192:9216], b1[:, 9216:10240]]
        NF1 = F1 // 128

        sw1 = st[:, 0:32]
        sw2 = st[:, 224:232]
        am = self.con32[:, 770:771]
        rc = self.con32[:, 771:772]
        a4 = self.con32[:, 772:773]

        def w1_load(fc):
            A.dma("sq", fsl[fc % 2], self.w1[fc * 128:(fc + 1) * 128, :], label=f"w1l{fc}")

        def w1_comp(fc):
            fslot = fsl[fc % 2]
            oslot = osl[fc % 2]
            A.op("vector", lambda e, i=fslot, o=am: e.reduce_max(
                out=o, in_=i, axis=AX.X, apply_absolute_value=True),
                r=[fslot], w=[am], label=f"w1am{fc}")
            A.op("vector", lambda e, o=am: e.tensor_scalar(
                out=o, in0=o, scalar1=EPS_Q, scalar2=None, op0=ALU.max),
                r=[am], w=[am], label=f"w1amc{fc}")
            A.op("vector", lambda e, i=am, o=rc: e.reciprocal(out=o, in_=i),
                 r=[am], w=[rc], label=f"w1rc{fc}")
            A.op("vector", lambda e, o=rc: e.tensor_scalar(
                out=o, in0=o, scalar1=127.0, scalar2=None, op0=ALU.mult),
                r=[rc], w=[rc], label=f"w1rc2{fc}")
            A.op("vector", lambda e, i=am, o=sw1[:, fc:fc + 1]: e.tensor_scalar(
                out=o, in0=i, scalar1=1.0 / 127.0, scalar2=None, op0=ALU.mult),
                r=[am], w=[sw1[:, fc:fc + 1]], label=f"w1sw{fc}")
            self.magic_round(oslot, fslot, rc[:, 0:1], False, f"w1q{fc}", tmp=fslot)
            A.dma("sq", self.w1q_hbm[fc * 128:(fc + 1) * 128, :], oslot, label=f"w1s{fc}")

        U.append(lambda: (w1_load(0), w1_load(1)))
        for fc in range(NF1):
            U.append(lambda fc=fc: (w1_comp(fc),
                                    w1_load(fc + 2) if fc + 2 < NF1 else None))

        def w1t_unit(dk, hh):
            stg2 = b1[:, 4096 + (hh % 2) * 2048: 4096 + (hh % 2) * 2048 + 2048]
            A.dmat("sq", stg2, self.w1q_hbm[hh * 2048:(hh + 1) * 2048,
                                            dk * 128:(dk + 1) * 128], label=f"w1t{dk}_{hh}")
            A.dma("gq", self.w1qT_sl[hh * 16:(hh + 1) * 16, :, dk, :]
                  .rearrange("fc p f -> p fc f"),
                  stg2.rearrange("p (fc f) -> p fc f", f=128), label=f"w1ts{dk}_{hh}")

        for dk in range(ND):
            for hh in range(2):
                U.append(lambda dk=dk, hh=hh: w1t_unit(dk, hh))

        def w2_maxpass(dc):
            for q4 in range(4):
                fslot = fsl[q4 % 2]
                A.dma("aq", fslot, self.w2[dc * 128:(dc + 1) * 128, q4 * 1024:(q4 + 1) * 1024],
                      label=f"w2l{dc}_{q4}")
                A.op("vector", lambda e, i=fslot, o=a4: e.reduce_max(
                    out=o, in_=i, axis=AX.X, apply_absolute_value=True),
                    r=[fslot], w=[a4], label=f"w2a{dc}_{q4}")
                if q4 == 0:
                    A.op("vector", lambda e, i=a4, o=am: e.tensor_copy(out=o, in_=i),
                         r=[a4], w=[am], label=f"w2c{dc}")
                else:
                    A.op("vector", lambda e, i=a4, o=am: e.tensor_tensor(
                        out=o, in0=o, in1=i, op=ALU.max), r=[a4, am], w=[am], label=f"w2m{dc}_{q4}")
            A.op("vector", lambda e, o=am: e.tensor_scalar(
                out=o, in0=o, scalar1=EPS_Q, scalar2=None, op0=ALU.max),
                r=[am], w=[am], label=f"w2mc{dc}")
            A.op("vector", lambda e, i=am, o=rc: e.reciprocal(out=o, in_=i),
                 r=[am], w=[rc], label=f"w2rc{dc}")
            A.op("vector", lambda e, o=rc: e.tensor_scalar(
                out=o, in0=o, scalar1=127.0, scalar2=None, op0=ALU.mult),
                r=[rc], w=[rc], label=f"w2rc2{dc}")
            A.op("vector", lambda e, i=am, o=sw2[:, dc:dc + 1]: e.tensor_scalar(
                out=o, in0=i, scalar1=1.0 / 127.0, scalar2=None, op0=ALU.mult),
                r=[am], w=[sw2[:, dc:dc + 1]], label=f"w2sw{dc}")

        def w2_quant(dc, q4):
            fslot = fsl[q4 % 2]
            oslot = osl[q4 % 2]
            A.dma("aq", fslot, self.w2[dc * 128:(dc + 1) * 128, q4 * 1024:(q4 + 1) * 1024],
                  label=f"w2l2{dc}_{q4}")
            self.magic_round(oslot, fslot, rc[:, 0:1], False, f"w2q{dc}_{q4}", tmp=fslot)
            A.dma("aq", self.w2q_hbm[dc * 128:(dc + 1) * 128, q4 * 1024:(q4 + 1) * 1024],
                  oslot, label=f"w2s{dc}_{q4}")

        for dc in range(ND):
            U.append(lambda dc=dc: w2_maxpass(dc))
            U.append(lambda dc=dc: (w2_quant(dc, 0), w2_quant(dc, 1)))
            U.append(lambda dc=dc: (w2_quant(dc, 2), w2_quant(dc, 3)))

        def w2t_unit(fc):
            stg2 = b1[:, 4096 + (fc % 2) * 2048: 4096 + (fc % 2) * 2048 + 1024]
            A.dmat("aq", stg2, self.w2q_hbm[:, fc * 128:(fc + 1) * 128], label=f"w2t{fc}")
            A.dma("gq", self.w2qT_hbm[fc * 128:(fc + 1) * 128, :], stg2, label=f"w2ts{fc}")

        for fc in range(NF1):
            U.append(lambda fc=fc: w2t_unit(fc))

        def cast_unit(nm, dst, src_, c0, c1):
            for c in range(c0, c1):
                A.dma("gq", dst[c * 128:(c + 1) * 128, :], src_[c * 128:(c + 1) * 128, :],
                      label=f"{nm}_cast{c}")

        for c in range(0, 24, 4):
            U.append(lambda c=c: cast_unit("qk", self.wqkv_bf, self.wqkv, c, c + 4))

        def qkt_unit(dk, hh):
            stg2 = b1[:, 4096 + (hh % 2) * 2048: 4096 + (hh % 2) * 2048 + 1536]
            A.dmat("aq", stg2, self.wqkv_bf[hh * 1536:(hh + 1) * 1536,
                                            dk * 128:(dk + 1) * 128], label=f"qkt{dk}_{hh}")
            A.dma("gq", self.wqkvT_hbm[dk * 128:(dk + 1) * 128, hh * 1536:(hh + 1) * 1536],
                  stg2, label=f"qkts{dk}_{hh}")
            nsl = 12 if hh == 0 else 4
            A.dma("gq", self.wqkv_sl[hh * 12:hh * 12 + nsl, :, dk, :]
                  .rearrange("fc p f -> p fc f"),
                  stg2[:, 0:nsl * 128].rearrange("p (fc f) -> p fc f", f=128),
                  label=f"qksl{dk}_{hh}")

        for dk in range(ND):
            for hh in range(2):
                U.append(lambda dk=dk, hh=hh: qkt_unit(dk, hh))
        for c in range(0, 8, 4):
            U.append(lambda c=c: cast_unit("ow", self.outw_bf, self.outw, c, c + 4))
        for c in range(0, 24, 4):
            U.append(lambda c=c: cast_unit("wg", self.wg_bf, self.wg, c, c + 4))
        for c in range(0, 24, 4):
            U.append(lambda c=c: cast_unit("wu", self.wu_bf, self.wu, c, c + 4))
        for c in range(0, 8, 4):
            U.append(lambda c=c: cast_unit("wd", self.wd_bf, self.wd, c, c + 4))

    def p_prep_late(self):
        A = self.A
        b1 = self.big1
        for nm, w_bf, w_sl in (("wg", self.wg_bf, self.wgT_sl),
                               ("wu", self.wu_bf, self.wuT_sl)):
            for dk in range(ND):
                for hh in range(2):
                    stg2 = b1[:, 8192:9728] if hh == 0 else b1[:, 8704:10240]
                    A.dmat("sq", stg2, w_bf[hh * 1536:(hh + 1) * 1536,
                                            dk * 128:(dk + 1) * 128], label=f"{nm}t{dk}_{hh}")
                    A.dma("sq", w_sl[hh * 12:(hh + 1) * 12, :, dk, :]
                          .rearrange("fc p f -> p fc f"),
                          stg2.rearrange("p (fc f) -> p fc f", f=128),
                          label=f"{nm}ts{dk}_{hh}")

    def p1_l0_attn(self):
        A, nc = self.A, self.nc
        xln = self.big0

        def xlnT(tc):
            return xln[:, tc * D:(tc + 1) * D]

        self.ln_into(self.resT, xlnT, "ln1")
        self.transpose_roundtrip(xlnT, "tr1")

        ps = self.pf[0]
        summ_ps = ps[:, 0:256].rearrange("p (dk c) -> p dk c", c=NC0)
        for dk in range(ND):
            self.drip(1)
            for tc in range(NT):
                lhsT = xlnT(tc)[:, dk * 128:(dk + 1) * 128]
                rhs = self.A_sb[:, tc * NC0:(tc + 1) * NC0]
                out = summ_ps[:, dk, :]
                A.op("tensor", lambda e, o=out, l=lhsT, r=rhs, s=(tc == 0), p=(tc == NT - 1):
                     e.matmul(o, l, r, start=s, stop=p),
                     r=[lhsT, rhs], w=[ps[:, dk * 32:(dk + 1) * 32]], label=f"summ_{dk}|{tc}")
        summ = self.tile[:, 0:256]
        A.op("vector", lambda e, i=ps[:, 0:256], o=summ: e.tensor_copy(out=o, in_=i),
             r=[ps[:, 0:256]], w=[summ], label="summ_ev")

        bq_s = self.stat[:, 128:136]
        bk_s = self.stat[:, 136:144]
        A.dma("gq", bq_s, self.bq.rearrange("(c p) -> p c", p=128), label="ld_bq", allow_slow_non_contiguous=True)
        A.dma("gq", bk_s, self.bk.rearrange("(c p) -> p c", p=128), label="ld_bk", allow_slow_non_contiguous=True)
        bk8 = self.stat[:, 144:152]
        A.op("vector", lambda e, i=bk_s, o=bk8: e.tensor_scalar(
            out=o, in0=i, scalar1=0.125, scalar2=None, op0=ALU.mult),
            r=[bk_s], w=[bk8], label="bk8")
        bv_row = self.rowb[:, 0:1024]
        bo_row = self.rowb[:, 1024:2048]
        A.dma("gq", bv_row, self.bv.rearrange("(one d) -> one d", one=1).broadcast_to([128, D]), label="ld_bv")
        A.dma("gq", bo_row, self.bo.rearrange("(one d) -> one d", one=1).broadcast_to([128, D]), label="ld_bo")

        kfm = self.tile[:, 256:512]
        wk_sl = self.big1[:, 0:4096]
        self._p1slot = 0

        def wslot():
            s = self._p1slot % 4
            self._p1slot += 1
            return wk_sl[:, s * 1024:(s + 1) * 1024]

        for fc in range(ND):
            self.drip(2)
            buf = wslot()
            A.dma("gq", buf.rearrange("p (dk f) -> p dk f", f=128),
                  self.Wk[:, fc * 128:(fc + 1) * 128].rearrange("(dk p) f -> p dk f", p=128),
                  label=f"ld_wk{fc}")
            out = self.pf[1][:, 0:32]
            for dk in range(ND):
                l = buf[:, dk * 128:(dk + 1) * 128]
                r = summ[:, dk * 32:(dk + 1) * 32]
                A.op("tensor", lambda e, o=out, l=l, r=r, s=(dk == 0), p=(dk == ND - 1):
                     e.matmul(o, l, r, start=s, stop=p),
                     r=[l, r], w=[out], label=f"kmm{fc}|{dk}")
            o2 = kfm[:, fc * 32:(fc + 1) * 32]
            A.op("scalar", lambda e, i=out, o=o2, b=bk8[:, fc:fc + 1]: e.activation(
                out=o, in_=i, func=AF.Identity, bias=b, scale=0.125),
                r=[out, bk8], w=[o2], label=f"kev{fc}")

        vpad = self.tile[:, 512:1536]
        A.op("vector", lambda e, o=vpad: e.memset(o, 0.0), w=[vpad], label="vpad0")
        vbufs = []
        for fc in range(ND):
            self.drip(2)
            buf = wslot()
            vbufs.append(buf)
            A.dma("gq", buf.rearrange("p (dk f) -> p dk f", f=128),
                  self.Wv[:, fc * 128:(fc + 1) * 128].rearrange("(dk p) f -> p dk f", p=128),
                  label=f"ld_wv{fc}")
            for hh in range(2):
                h = 2 * fc + hh
                b = 32 * (h % 4)
                pcol = (h % 8) * 64
                out = self.pf[2][b:b + 32, pcol:pcol + 64]
                for dk in range(ND):
                    l = summ[:, dk * 32:(dk + 1) * 32]
                    r = buf[:, dk * 128 + hh * 64: dk * 128 + hh * 64 + 64]
                    A.op("tensor", lambda e, o=out, l=l, r=r, s=(dk == 0), p=(dk == ND - 1),
                         tp=(0, b): e.matmul(o, l, r, start=s, stop=p, tile_position=tp),
                         r=[l, r], w=[self.pf[2][:, pcol:pcol + 64]], label=f"vmm{h}|{dk}")
                dst = vpad[:, h * 64:(h + 1) * 64][b:b + 32, :]
                bvs = bv_row[b:b + 32, h * 64:(h + 1) * 64]
                A.op("vector", lambda e, i=out, o=dst, b2=bvs: e.tensor_tensor(
                    out=o, in0=i, in1=b2, op=ALU.add),
                    r=[self.pf[2][:, pcol:pcol + 64], bv_row], w=[dst], label=f"vev{h}")

        q = self.big0
        for fc in range(ND):
            self.drip(2)
            buf = wslot()
            A.dma("gq", buf.rearrange("p (dk f) -> p dk f", f=128),
                  self.Wq[:, fc * 128:(fc + 1) * 128].rearrange("(dk p) f -> p dk f", p=128),
                  label=f"ld_wq{fc}")
            for tt in range(4):
                self.drip(1)
                ps_t = self.pf[3 + (tt % 2)]
                for dk in range(ND):
                    l = buf[:, dk * 128:(dk + 1) * 128]
                    r = self.xfmS(dk, tt * 512, (tt + 1) * 512)
                    A.op("tensor", lambda e, o=ps_t[:, :], l=l, r=r, s=(dk == 0), p=(dk == ND - 1):
                         e.matmul(o, l, r, start=s, stop=p),
                         r=[l, r], w=[ps_t[:, :]], label=f"qmm{fc}_{tt}|{dk}")
                dst = q[:, fc * T + tt * 512: fc * T + (tt + 1) * 512]
                A.op("scalar", lambda e, i=ps_t, o=dst, b=bq_s[:, fc:fc + 1]: e.activation(
                    out=o, in_=i[:, :], func=AF.Identity, bias=b),
                    r=[ps_t[:, :], bq_s], w=[dst], label=f"qev{fc}_{tt}")

        wo_sb = self.big2[:, 0:8192]
        A.dma("gq", wo_sb.rearrange("p (dk f) -> p dk f", f=D),
              self.Wo.rearrange("(dk p) f -> p dk f", p=128), label="ld_wo")

        at_b = [self.tile[:, 1536:2048], self.tile[:, 3584:4096]]
        atT = self.tile[:, 2048:2560]
        ytile = self.tile[:, 2560:3584]

        def emit_s1(tc):
            ps_sc = self.pf[5] if tc % 2 == 0 else self.pf[3]
            for h in range(H0):
                b = 64 * (h % 2)
                l = q[:, (h // 2) * T + tc * 128:(h // 2) * T + tc * 128 + 128][b:b + 64, :]
                r = kfm[:, (h // 2) * 32:(h // 2) * 32 + 32][b:b + 64, :]
                out = ps_sc[:, h * 32:(h + 1) * 32]
                A.op("tensor", lambda e, o=out, l=l, r=r: e.matmul(o, l, r, start=True, stop=True),
                     r=[l, r], w=[out], label=f"sc{tc}_{h}")
            scf = self.scr32[:, 1024:1536]
            mk = self.mask_sb[:, tc * NC0:(tc + 1) * NC0]
            A.op("vector", lambda e, i=ps_sc, m=mk, o=scf: e.tensor_tensor(
                out=o.rearrange("p (h c) -> p h c", c=NC0),
                in0=i[:, :].rearrange("p (h c) -> p h c", c=NC0),
                in1=m.rearrange("p (one c) -> p one c", one=1).broadcast_to([128, H0, NC0]), op=ALU.add),
                r=[ps_sc[:, :], self.mask_sb], w=[scf], label=f"msk{tc}")
            mx = self.stat[:, 160:176]
            A.op("vector", lambda e, i=scf, o=mx: e.tensor_reduce(
                out=o, in_=i.rearrange("p (h c) -> p h c", c=NC0), axis=AX.X, op=ALU.max),
                r=[scf], w=[mx], label=f"mx{tc}")
            A.op("vector", lambda e, i=scf, m=mx, o=scf: e.tensor_tensor(
                out=o.rearrange("p (h c) -> p h c", c=NC0),
                in0=i.rearrange("p (h c) -> p h c", c=NC0),
                in1=m.rearrange("p (h one) -> p h one", one=1).broadcast_to([128, H0, NC0]), op=ALU.subtract),
                r=[scf, mx], w=[scf], label=f"sub{tc}")
            A.op("scalar", lambda e, i=scf, o=scf: e.activation(out=o, in_=i, func=AF.Exp),
                 r=[scf], w=[scf], label=f"exp{tc}")
            sm = self.stat[:, 176:192]
            A.op("vector", lambda e, i=scf, o=sm: e.tensor_reduce(
                out=o, in_=i.rearrange("p (h c) -> p h c", c=NC0), axis=AX.X, op=ALU.add),
                r=[scf], w=[sm], label=f"sm{tc}")
            rs = self.stat[:, 192:208]
            A.op("vector", lambda e, i=sm, o=rs: e.reciprocal(out=o, in_=i),
                 r=[sm], w=[rs], label=f"rs{tc}")
            at = at_b[tc % 2]
            A.op("vector", lambda e, i=scf, r2=rs, o=at: e.tensor_tensor(
                out=o.rearrange("p (h c) -> p h c", c=NC0),
                in0=i.rearrange("p (h c) -> p h c", c=NC0),
                in1=r2.rearrange("p (h one) -> p h one", one=1).broadcast_to([128, H0, NC0]), op=ALU.mult),
                r=[scf, rs], w=[at], label=f"at{tc}")

        def emit_s2(tc):
            at = at_b[tc % 2]
            for qd in range(4):
                src = at[:, qd * 128:(qd + 1) * 128]
                out = self.pb[:, qd * 128:(qd + 1) * 128]
                A.op("tensor", lambda e, o=out, s=src: e.transpose(o, s, self.ident),
                     r=[src, self.ident], w=[out], label=f"atT{tc}_{qd}", tr=True)
            A.op("scalar", lambda e, i=self.pb[:, 0:512], o=atT: e.copy(out=o, in_=i),
                 r=[self.pb[:, 0:512]], w=[atT], label=f"atTev{tc}")
            ps_y = [self.pf[6], self.pf[0]]
            for h in range(H0):
                b = 64 * (h % 2)
                l = vpad[:, h * 64:(h + 1) * 64]
                r = atT[:, (h // 4) * 128:(h // 4) * 128 + 128]
                bank = ps_y[(h // 2) % 2]
                col = ((h // 2) // 2) * 128
                out = bank[b:b + 64, col:col + 128]
                A.op("tensor", lambda e, o=out, l=l, r=r, tp=(0, b): e.matmul(
                    o, l, r, start=True, stop=True, tile_position=tp),
                    r=[l, r], w=[bank[:, col:col + 128]], label=f"av{tc}_{h}")
            for i2 in range(2):
                A.op("scalar", lambda e, i=ps_y[i2], o=ytile[:, i2 * 512:(i2 + 1) * 512]:
                     e.copy(out=o, in_=i[:, :]),
                     r=[ps_y[i2][:, :]], w=[ytile[:, i2 * 512:(i2 + 1) * 512]], label=f"yev{tc}_{i2}")
            ps_o = [self.pf[1], self.pf[2]]
            for k in range(ND):
                l = ytile[:, (k % 2) * 512 + (k // 2) * 128: (k % 2) * 512 + (k // 2) * 128 + 128]
                for half in range(2):
                    r = wo_sb[:, k * D + half * 512: k * D + (half + 1) * 512]
                    out = ps_o[half]
                    A.op("tensor", lambda e, o=out, l=l, r=r, s=(k == 0), p=(k == ND - 1):
                         e.matmul(o[:, :], l, r, start=s, stop=p),
                         r=[l, r], w=[out[:, :]], label=f"op{tc}_{half}|{k}")
            self.postln_residual(ps_o, tc, bo_row, f"pl1_{tc}",
                                 then=lambda tc2: self.ln_tc(
                                     tc2, self.xfm[:, tc2 * D:(tc2 + 1) * D], "ln2",
                                     colmax=self.stat[:, 208:224]))

        for tc in range(NT):
            emit_s1(tc)
            self.drip(2)
            if tc > 0:
                emit_s2(tc - 1)
                self.drip(2)
        emit_s2(NT - 1)

    def global_max_collective(self, m128, idx, out_col, label):
        A = self.A
        cca, ccb = self.cc_a[idx], self.cc_b[idx]
        A.dma("gq", cca.rearrange("one p -> p one"), m128, label=f"{label}_st")
        A.cc(lambda e, a=cca, b=ccb: e.collective_compute(
            "AllReduce", ALU.max, replica_groups=[list(range(8))],
            ins=[a], outs=[b]), r=[cca], w=[ccb], label=f"{label}_cc")
        gall = self.scr32[:, 0:128]
        A.dma("gq", gall, ccb[0:1, :].broadcast_to([128, 128]), label=f"{label}_ld")
        A.op("vector", lambda e, i=gall, o=out_col: e.reduce_max(out=o, in_=i, axis=AX.X),
             r=[gall], w=[out_col], label=f"{label}_mx")

    def magic_round(self, out_bf, in_ap, inv_ap, relu, label, tmp=None):
        A = self.A
        if tmp is None:
            tmp = self.scr32[:, 0:in_ap.shape[-1]]
        A.op("vector", lambda e, i=in_ap, o=tmp, s=inv_ap: e.tensor_scalar(
            out=o, in0=i, scalar1=s, scalar2=MAGIC, op0=ALU.mult, op1=ALU.add),
            r=[in_ap, inv_ap], w=[tmp], label=f"{label}_a")
        fn = AF.Relu if relu else AF.Identity
        A.op("scalar", lambda e, i=tmp, o=out_bf, fn=fn, b=self.nmag: e.activation(
            out=o, in_=i, func=fn, bias=b),
            r=[tmp, self.nmag], w=[out_bf], label=f"{label}_b")

    def ln_tc(self, tc, dst, label, colmax=None):
        A, st = self.A, self.stat
        src = self.resT(tc)
        bn6 = st[:, 64 + (tc % 2) * 16:64 + (tc % 2) * 16 + 12]
        for g in range(2):
            A.op("vector", lambda e, s=src, o=bn6, g=g: e.bn_stats(
                out=o[:, 6 * g:6 * g + 6], in_=s[:, 512 * g:512 * g + 512]),
                r=[src], w=[bn6[:, 6 * g:6 * g + 6]], label=f"{label}_bnst{tc}_{g}")
        mv = st[:, 96 + (tc % 2) * 2:96 + (tc % 2) * 2 + 2]
        A.op("vector", lambda e, i=bn6, o=mv: e.bn_aggr(out=o, in_=i),
             r=[bn6], w=[mv], label=f"{label}_bnag{tc}")
        ve = st[:, 100 + (tc % 2):101 + (tc % 2)]
        A.op("vector", lambda e, i=mv, o=ve: e.tensor_scalar(
            out=o, in0=i[:, 1:2], scalar1=EPS_LN, scalar2=None, op0=ALU.add),
            r=[mv], w=[ve], label=f"{label}_veps{tc}")
        A.op("scalar", lambda e, o=ve: e.activation(out=o, in_=o, func=AF.Ln),
             r=[ve], w=[ve], label=f"{label}_ln{tc}")
        rstd = st[:, 102 + (tc % 2):103 + (tc % 2)]
        A.op("scalar", lambda e, i=ve, o=rstd: e.activation(out=o, in_=i, func=AF.Exp, scale=-0.5),
             r=[ve], w=[rstd], label=f"{label}_rexp{tc}")
        A.op("vector", lambda e, s=src, o=dst, m=mv[:, 0:1], r2=rstd: e.tensor_scalar(
            out=o, in0=s, scalar1=m, scalar2=r2, op0=ALU.subtract, op1=ALU.mult),
            r=[src, mv, rstd], w=[dst], label=f"{label}_nrm{tc}")
        A.dma("aq", self.xlnT_hbm[tc * 128:(tc + 1) * 128, :], dst, label=f"{label}_st{tc}")
        if colmax is not None:
            A.op("vector", lambda e, i=dst, o=colmax[:, tc:tc + 1]:
                 e.reduce_max(out=o, in_=i, axis=AX.X),
                 r=[dst], w=[colmax[:, tc:tc + 1]], label=f"{label}_cm{tc}")

    def p2_l0_ffn(self):
        A, st = self.A, self.stat
        xln = self.big0

        def xlnT(tc):
            return xln[:, tc * D:(tc + 1) * D]

        for dk in range(ND):
            A.dmat("aq", self.xfmS(dk, 0, T), self.xlnT_hbm[:, dk * 128:(dk + 1) * 128],
                   label=f"tr2_tr{dk}")
        colmax = st[:, 208:224]
        m128 = st[:, 232:233]
        A.op("vector", lambda e, i=colmax, o=m128: e.reduce_max(out=o, in_=i, axis=AX.X),
             r=[colmax], w=[m128], label="m128")
        A.op("vector", lambda e, o=m128: e.tensor_scalar(
            out=o, in0=o, scalar1=EPS_Q, scalar2=None, op0=ALU.max),
            r=[m128], w=[m128], label="m128c")
        gmax = st[:, 233:234]
        self.global_max_collective(m128, 0, gmax, "gmx0")
        inv_s = st[:, 234:235]
        s_q = st[:, 235:236]
        A.op("vector", lambda e, i=gmax, o=inv_s: e.reciprocal(out=o, in_=i),
             r=[gmax], w=[inv_s], label="invs0")
        A.op("vector", lambda e, o=inv_s: e.tensor_scalar(
            out=o, in0=o, scalar1=255.0, scalar2=None, op0=ALU.mult),
            r=[inv_s], w=[inv_s], label="invs1")
        A.op("vector", lambda e, i=gmax, o=s_q: e.tensor_scalar(
            out=o, in0=i, scalar1=1.0 / 255.0, scalar2=None, op0=ALU.mult),
            r=[gmax], w=[s_q], label="sq0")

        xq = self.big0
        for dk in range(ND):
            self.magic_round(xq[:, dk * T:(dk + 1) * T], self.xfmS(dk, 0, T),
                             inv_s[:, 0:1], True, f"xq{dk}",
                             tmp=self.scr32[:, 0:T])

        sw1 = self.stat[:, 0:32]
        NF1 = F1 // 128
        A.op("vector", lambda e, o=sw1, s=s_q[:, 0:1]: e.tensor_scalar(
            out=o, in0=o, scalar1=s, scalar2=None, op0=ALU.mult),
            r=[sw1, s_q], w=[sw1], label="ssw1")
        b1s = st[:, 128:160]
        A.dma("gq", b1s, self.b1.rearrange("(c p) -> p c", p=128), label="ld_b1",
              allow_slow_non_contiguous=True)
        rcp = st[:, 192:224]
        A.op("vector", lambda e, i=sw1, o=rcp: e.reciprocal(out=o, in_=i),
             r=[sw1], w=[rcp], label="b1rcp")
        b1q = st[:, 160:192]
        A.op("vector", lambda e, b=b1s, r=rcp, o=b1q: e.tensor_tensor(
            out=o, in0=b, in1=r, op=ALU.mult), r=[b1s, rcp], w=[b1q], label="b1t")
        A.op("vector", lambda e, o=b1q: e.tensor_scalar(
            out=o, in0=o, scalar1=MAGIC, scalar2=MAGIC, op0=ALU.add, op1=ALU.subtract),
            r=[b1q], w=[b1q], label="b1r")
        A.op("vector", lambda e, o=b1q, s=sw1: e.tensor_tensor(
            out=o, in0=o, in1=s, op=ALU.mult), r=[b1q, sw1], w=[b1q], label="b1m")

        hm = st[:, 192:224]
        for fc in range(NF1):
            wsl = self.big1[:, 2048 + (fc % 2) * 1024: 2048 + (fc % 2) * 1024 + 1024]
            A.dma("sq", wsl.rearrange("p (dk f) -> p dk f", f=128),
                  self.w1qT_sl[fc], label=f"m1w{fc}")
            if 8 <= fc < 16:
                w2c = 16 + (fc - 8)
                A.dma("sq", self.big2[:, (fc - 8) * 1024:(fc - 7) * 1024],
                      self.w2qT_hbm[w2c * 128:(w2c + 1) * 128, :], label=f"w2r{w2c}")
            hfc = self.tile[:, (fc % 2) * 2048:(fc % 2) * 2048 + 2048]
            for tt in range(4):
                ps = self.pf[2 * (fc % 2) + (tt % 2)]
                for dk in range(ND):
                    l = wsl[:, dk * 128:(dk + 1) * 128]
                    r = xq[:, dk * T + tt * 512: dk * T + (tt + 1) * 512]
                    A.op("tensor", lambda e, o=ps, l=l, r=r, s=(dk == 0), p=(dk == ND - 1):
                         e.matmul(o[:, :], l, r, start=s, stop=p),
                         r=[l, r], w=[ps[:, :]], label=f"m1_{fc}_{tt}|{dk}")
                dst = hfc[:, tt * 512:(tt + 1) * 512]
                A.op("scalar", lambda e, i=ps, o=dst, s=sw1[:, fc:fc + 1], b=b1q[:, fc:fc + 1]:
                     e.activation(out=o, in_=i[:, :], func=AF.Relu, scale=s, bias=b),
                     r=[ps[:, :], sw1, b1q], w=[dst], label=f"m1e_{fc}_{tt}")
            A.op("vector", lambda e, i=hfc, o=hm[:, fc:fc + 1]: e.reduce_max(
                out=o, in_=i, axis=AX.X), r=[hfc], w=[hm[:, fc:fc + 1]], label=f"hm{fc}")
            A.dma("aq", self.h_hbm[fc * 128:(fc + 1) * 128, :], hfc, label=f"hs{fc}")

        hmax = st[:, 236:237]
        A.op("vector", lambda e, i=hm, o=hmax: e.reduce_max(out=o, in_=i, axis=AX.X),
             r=[hm], w=[hmax], label="hmax")
        A.op("vector", lambda e, o=hmax: e.tensor_scalar(
            out=o, in0=o, scalar1=EPS_Q, scalar2=None, op0=ALU.max),
            r=[hmax], w=[hmax], label="hmaxc")
        gh = st[:, 237:238]
        self.global_max_collective(hmax, 1, gh, "gmx1")
        inv_sh = st[:, 238:239]
        s_h = st[:, 239:240]
        A.op("vector", lambda e, i=gh, o=inv_sh: e.reciprocal(out=o, in_=i),
             r=[gh], w=[inv_sh], label="invh0")
        A.op("vector", lambda e, o=inv_sh: e.tensor_scalar(
            out=o, in0=o, scalar1=255.0, scalar2=None, op0=ALU.mult),
            r=[inv_sh], w=[inv_sh], label="invh1")
        A.op("vector", lambda e, i=gh, o=s_h: e.tensor_scalar(
            out=o, in0=i, scalar1=1.0 / 255.0, scalar2=None, op0=ALU.mult),
            r=[gh], w=[s_h], label="sh0")

        sw2 = self.stat[:, 224:232]
        ssw2 = st[:, 8:16]
        A.op("vector", lambda e, i=sw2, o=ssw2, s=s_h[:, 0:1]: e.tensor_scalar(
            out=o, in0=i, scalar1=s, scalar2=None, op0=ALU.mult),
            r=[sw2, s_h], w=[ssw2], label="ssw2")
        b2s = st[:, 16:24]
        A.dma("gq", b2s, self.b2.rearrange("(c p) -> p c", p=128), label="ld_b2",
              allow_slow_non_contiguous=True)
        rcp2 = st[:, 24:32]
        A.op("vector", lambda e, i=ssw2, o=rcp2: e.reciprocal(out=o, in_=i),
             r=[ssw2], w=[rcp2], label="b2rcp")
        b2q = st[:, 128:136]
        A.op("vector", lambda e, b=b2s, r=rcp2, o=b2q: e.tensor_tensor(
            out=o, in0=b, in1=r, op=ALU.mult), r=[b2s, rcp2], w=[b2q], label="b2t")
        A.op("vector", lambda e, o=b2q: e.tensor_scalar(
            out=o, in0=o, scalar1=MAGIC, scalar2=MAGIC, op0=ALU.add, op1=ALU.subtract),
            r=[b2q], w=[b2q], label="b2r")
        A.op("vector", lambda e, o=b2q, s=ssw2: e.tensor_tensor(
            out=o, in0=o, in1=s, op=ALU.mult), r=[b2q, ssw2], w=[b2q], label="b2m")
        A.dma("gq", self.row_hbm[0:1, :].rearrange("one (c p) -> p one c", p=128),
              ssw2.rearrange("p (one c) -> p one c", one=1), label="r0st",
              allow_slow_non_contiguous=True)
        A.dma("gq", self.row_hbm[1:2, :].rearrange("one (c p) -> p one c", p=128),
              b2q.rearrange("p (one c) -> p one c", one=1), label="r1st",
              allow_slow_non_contiguous=True)
        ssw2_row = self.rowb[:, 0:1024]
        b2q_row = self.rowb[:, 1024:2048]
        A.dma("gq", ssw2_row, self.row_hbm[0:1, :].broadcast_to([128, D]), label="r0ld")
        A.dma("gq", b2q_row, self.row_hbm[1:2, :].broadcast_to([128, D]), label="r1ld")

        groups = [list(range(g, min(g + 3, NT))) for g in range(0, NT, 3)]

        def m2h_issue(gi2, fc2, g02, gw2):
            hld2 = self.big1[:, (fc2 % 8) * 384:(fc2 % 8) * 384 + gw2]
            A.dma(("sq", "gq", "aq")[fc2 % 3], hld2,
                  self.h_hbm[fc2 * 128:(fc2 + 1) * 128, g02 * 128:g02 * 128 + gw2],
                  label=f"m2h{gi2}_{fc2}")
            return hld2

        prev_post = None
        for gi, tcs in enumerate(groups):
            g0 = tcs[0]
            gw = len(tcs) * 128
            for fc in range(8):
                m2h_issue(gi, fc, g0, gw)
            for fc in range(24, 28):
                wsl = self.tile[:, (fc % 4) * 1024:(fc % 4) * 1024 + 1024]
                A.dma(("gq", "aq")[fc % 2], wsl, self.w2qT_hbm[fc * 128:(fc + 1) * 128, :],
                      label=f"m2w{gi}_{fc}")
            if prev_post is not None:
                prev_post()
            for fc in range(NF1):
                hld = self.big1[:, (fc % 8) * 384:(fc % 8) * 384 + gw]
                tmp = self.scr32[:, 1024 + (fc % 2) * 384:1024 + (fc % 2) * 384 + gw]
                hq = self.big1[:, 3072 + (fc % 8) * 384: 3072 + (fc % 8) * 384 + gw]
                A.op("vector", lambda e, i=hld, o=tmp, s=inv_sh[:, 0:1]: e.tensor_scalar(
                    out=o, in0=i, scalar1=s, scalar2=MAGIC, op0=ALU.mult, op1=ALU.add),
                    r=[hld, inv_sh], w=[tmp], label=f"m2qa{gi}_{fc}")
                if fc + 8 < NF1:
                    m2h_issue(gi, fc + 8, g0, gw)
                A.op("scalar", lambda e, i=tmp, o=hq, b=self.nmag: e.activation(
                    out=o, in_=i, func=AF.Identity, bias=b),
                    r=[tmp, self.nmag], w=[hq], label=f"m2qb{gi}_{fc}")
                if fc < 16:
                    wsl = self.big0[:, fc * 1024:(fc + 1) * 1024]
                    if gi == 0:
                        A.dma("sq", wsl, self.w2qT_hbm[fc * 128:(fc + 1) * 128, :],
                              label=f"m2w{gi}_{fc}")
                elif fc < 24:
                    wsl = self.big2[:, (fc - 16) * 1024:(fc - 15) * 1024]
                else:
                    wsl = self.tile[:, (fc % 4) * 1024:(fc % 4) * 1024 + 1024]
                for ti, tc in enumerate(tcs):
                    l = hq[:, ti * 128:(ti + 1) * 128]
                    for half in range(2):
                        ps = self.pf[2 * ti + half]
                        r = wsl[:, half * 512:(half + 1) * 512]
                        A.op("tensor", lambda e, o=ps, l=l, r=r, s=(fc == 0), p=(fc == NF1 - 1):
                             e.matmul(o[:, :], l, r, start=s, stop=p),
                             r=[l, r], w=[ps[:, :]], label=f"m2_{gi}_{ti}_{half}|{fc}")
                if 24 <= fc < 28:
                    fcn = fc + 4
                    wnx = self.tile[:, (fcn % 4) * 1024:(fcn % 4) * 1024 + 1024]
                    A.dma(("gq", "aq")[fcn % 2], wnx, self.w2qT_hbm[fcn * 128:(fcn + 1) * 128, :],
                          label=f"m2w{gi}_{fcn}")

            def mkpost(tcs2):
                def post():
                    for ti, tc in enumerate(tcs2):
                        self.postln_residual([self.pf[2 * ti], self.pf[2 * ti + 1]], tc,
                                             bias_row=b2q_row, scale_row=ssw2_row,
                                             label=f"pl2_{tc}",
                                             then=lambda tc2: self.ln_tc(
                                                 tc2, self.xfm[:, tc2 * D:(tc2 + 1) * D], "ln3"))
                return post
            prev_post = mkpost(tcs)
        prev_post()

    def p3_l1_attn(self):
        A, st = self.A, self.stat
        nc = self.nc

        outwT = self.big2[:, 0:8192]
        for dk in range(ND):
            A.dmat("aq", outwT[:, dk * D:(dk + 1) * D],
                   self.outw_bf[:, dk * 128:(dk + 1) * 128], label=f"owt{dk}")

        xln = self.big0

        def xlnT(tc):
            return xln[:, tc * D:(tc + 1) * D]

        for dk in range(ND):
            A.dmat("aq", self.xfmS(dk, 0, T), self.xlnT_hbm[:, dk * 128:(dk + 1) * 128],
                   label=f"tr3_tr{dk}")

        grow = st[0:1, 240:256]
        A.dma("gq", grow, self.gamma.rearrange("(one g) -> one g", one=1), label="ld_g")
        A.op("scalar", lambda e, o=grow: e.activation(out=o, in_=o, func=AF.Sigmoid),
             r=[grow], w=[grow], label="sig_g")
        A.op("scalar", lambda e, o=grow: e.activation(out=o, in_=o, func=AF.Ln),
             r=[grow], w=[grow], label="ln_g")
        A.dma("gq", self.row_hbm[2:3, 0:16], grow, label="g_st")
        lngb = self.scr32[:, 0:16]
        A.dma("gq", lngb, self.row_hbm[2:3, 0:16].broadcast_to([128, 16]), label="g_ld")
        D_all = self.cons[:, 768:768 + 2048]
        for h in range(16):
            A.op("scalar", lambda e, o=D_all[:, h * 128:(h + 1) * 128], s=lngb[:, h:h + 1]:
                 e.activation(out=o, in_=self.mdec, func=AF.Exp, scale=s),
                 r=[self.mdec, lngb], w=[D_all[:, h * 128:(h + 1) * 128]], label=f"dall{h}")
        lngband = self.scr32[:, 32:40]
        for dc in range(ND):
            A.op("vector", lambda e, i=lngb[0:64, 2 * dc:2 * dc + 1], o=lngband[0:64, dc:dc + 1]:
                 e.tensor_copy(out=o, in_=i), r=[lngb], w=[lngband[0:64, dc:dc + 1]],
                 label=f"lb0_{dc}")
            A.op("vector", lambda e, i=lngb[64:128, 2 * dc + 1:2 * dc + 2], o=lngband[64:128, dc:dc + 1]:
                 e.tensor_copy(out=o, in_=i), r=[lngb], w=[lngband[64:128, dc:dc + 1]],
                 label=f"lb1_{dc}")
        Gq_all = self.cons[:, 2816:2816 + 1024]
        for dc in range(ND):
            A.op("scalar", lambda e, o=Gq_all[:, dc * 128:(dc + 1) * 128], s=lngband[:, dc:dc + 1]:
                 e.activation(out=o, in_=self.gq_in, func=AF.Exp, scale=s),
                 r=[self.gq_in, lngband], w=[Gq_all[:, dc * 128:(dc + 1) * 128]], label=f"gq{dc}")
        GkS = st[:, 224:240]
        for h in range(16):
            A.op("scalar", lambda e, o=GkS[:, h:h + 1], s=lngb[:, h:h + 1]:
                 e.activation(out=o, in_=self.gk_in, func=AF.Exp, scale=s),
                 r=[self.gk_in, lngb], w=[GkS[:, h:h + 1]], label=f"gk{h}")
        gCb = self.scr32[:, 16:32]
        A.op("scalar", lambda e, i=lngb, o=gCb: e.activation(
            out=o, in_=i, func=AF.Exp, scale=128.0),
            r=[lngb], w=[gCb], label="gC")
        gI = self.cons[:, 4864:4864 + 1024]
        for h in range(16):
            idx = self.idA if h % 2 == 0 else self.idB
            A.op("vector", lambda e, i=idx, o=gI[:, h * 64:(h + 1) * 64], s=gCb[:, h:h + 1]:
                 e.tensor_scalar(out=o, in0=i, scalar1=s, scalar2=None, op0=ALU.mult),
                 r=[idx, gCb], w=[gI[:, h * 64:(h + 1) * 64]], label=f"gI{h}")
        SS = self.cons[:, 3840:3840 + 1024]
        A.op("vector", lambda e, o=SS: e.memset(o, 0.0), w=[SS], label="ss0")

        bq1 = st[:, 128:136]
        A.dma("gq", bq1, self.bqkv[0:1024].rearrange("(c p) -> p c", p=128),
              label="ld_bq1", allow_slow_non_contiguous=True)
        bk1 = st[:, 136:144]
        A.dma("gq", bk1, self.bqkv[1024:2048].rearrange("(c p) -> p c", p=128),
              label="ld_bk1", allow_slow_non_contiguous=True)
        bk18 = st[:, 144:152]
        A.op("vector", lambda e, i=bk1, o=bk18: e.tensor_scalar(
            out=o, in0=i, scalar1=0.125, scalar2=None, op0=ALU.mult),
            r=[bk1], w=[bk18], label="bk18")
        bkv_row = self.rowb[:, 0:2048]
        A.dma("gq", bkv_row, self.bqkv[1024:3072].rearrange("(one d) -> one d", one=1)
              .broadcast_to([128, 2048]), label="ld_bkv")
        outb_row = self.rowb[:, 2048:3072]
        A.dma("gq", outb_row, self.outb.rearrange("(one d) -> one d", one=1)
              .broadcast_to([128, D]), label="ld_outb")

        wv_blk = self.big1[:, 0:8192]
        A.dma("sq", wv_blk.rearrange("p (dk f) -> p dk f", f=1024),
              self.wqkvT_hbm[:, 2048:3072].rearrange("(dk p) f -> p dk f", p=128),
              label="ld_wvblk")

        vt_b = [self.tile[:, 0:1024], self.tile[:, 2048:3072]]
        kt_b = [self.tile[:, 1024:2048], self.tile[:, 3072:4096]]
        yt = self.scr32[:, 1024:1536].bitcast(BF16)
        pT = self.scr32[:, 1536:1792].bitcast(BF16)
        qt = self.cons[:, 5888:6144]
        for half in range(2):
            if half == 1:
                self.p_prep_late()
            q_h = self.big0[:, 0:8192]
            k_h = self.big0[:, 8192:16384]
            t0 = half * 1024
            for part, dst, bias, scale in (("q", q_h, bq1, None), ("k", k_h, bk18, 0.125)):
                off = 0 if part == "q" else 1024
                for fc in range(ND):
                    wsl = self.big1[:, 8192 + (fc % 2) * 1024: 8192 + (fc % 2) * 1024 + 1024]
                    A.dma("sq", wsl.rearrange("p (dk f) -> p dk f", f=128),
                          self.wqkv_sl[(off // 128) + fc], label=f"qk{half}{part}{fc}")
                    for tt in range(2):
                        ps = self.pf[tt % 2]
                        for dk in range(ND):
                            l = wsl[:, dk * 128:(dk + 1) * 128]
                            r = self.xfmS(dk, t0 + tt * 512, t0 + (tt + 1) * 512)
                            A.op("tensor", lambda e, o=ps, l=l, r=r, s=(dk == 0), p=(dk == ND - 1):
                                 e.matmul(o[:, :], l, r, start=s, stop=p),
                                 r=[l, r], w=[ps[:, :]], label=f"p3m{half}{part}{fc}_{tt}|{dk}")
                        dd = dst[:, fc * 1024 + tt * 512: fc * 1024 + (tt + 1) * 512]
                        if scale is None:
                            A.op("scalar", lambda e, i=ps, o=dd, b=bias[:, fc:fc + 1]:
                                 e.activation(out=o, in_=i[:, :], func=AF.Identity, bias=b),
                                 r=[ps[:, :], bias], w=[dd], label=f"p3e{half}{part}{fc}_{tt}")
                        else:
                            A.op("scalar", lambda e, i=ps, o=dd, b=bias[:, fc:fc + 1], sc=scale:
                                 e.activation(out=o, in_=i[:, :], func=AF.Identity, bias=b, scale=sc),
                                 r=[ps[:, :], bias], w=[dd], label=f"p3e{half}{part}{fc}_{tt}")
            def g1(tci):
                tc = half * 8 + tci
                vt = vt_b[tci % 2]
                kt = kt_b[tci % 2]
                for dk in range(ND):
                    l = self.xfmS(dk, tc * 128, (tc + 1) * 128)
                    for hf in range(2):
                        ps = self.pf[hf]
                        r = wv_blk[:, dk * 1024 + hf * 512: dk * 1024 + (hf + 1) * 512]
                        A.op("tensor", lambda e, o=ps, l=l, r=r, s=(dk == 0), p=(dk == ND - 1):
                             e.matmul(o[:, :], l, r, start=s, stop=p),
                             r=[l, r], w=[ps[:, :]], label=f"vm{tc}_{hf}|{dk}")
                for hf in range(2):
                    dd = vt[:, hf * 512:(hf + 1) * 512]
                    br = bkv_row[:, 1024 + hf * 512:1024 + (hf + 1) * 512]
                    A.op("vector", lambda e, i=self.pf[hf], o=dd, b=br: e.tensor_tensor(
                        out=o, in0=i[:, :], in1=b, op=ALU.add),
                        r=[self.pf[hf][:, :], bkv_row], w=[dd], label=f"ve{tc}_{hf}")
                for k2 in range(8):
                    src = k_h[:, k2 * 1024 + tci * 128:k2 * 1024 + (tci + 1) * 128]
                    out = self.pb[:, k2 * 128:(k2 + 1) * 128]
                    A.op("tensor", lambda e, o=out, s=src: e.transpose(o, s, self.ident),
                         r=[src, self.ident], w=[out], label=f"ktT{tc}_{k2}", tr=True)
                A.op("vector", lambda e, i=self.pb[:, 0:1024], o=kt, g=GkS: e.tensor_tensor(
                    out=o.rearrange("p (h x) -> p h x", x=64),
                    in0=i.rearrange("p (h x) -> p h x", x=64),
                    in1=g.rearrange("p (h one) -> p h one", one=1).broadcast_to([128, 16, 64]),
                    op=ALU.mult),
                    r=[self.pb[:, 0:1024], GkS], w=[kt], label=f"kte{tc}")

            def g2(tci):
                tc = half * 8 + tci
                vt = vt_b[tci % 2]
                kt = kt_b[tci % 2]
                for qd in range(4):
                    ps_s = self.pf[4 + qd % 2]
                    for hh in range(4):
                        h = qd * 4 + hh
                        b = 64 * (h % 2)
                        l = k_h[:, (h // 2) * 1024 + tci * 128:(h // 2) * 1024 + (tci + 1) * 128][b:b + 64, :]
                        r = q_h[:, (h // 2) * 1024 + tci * 128:(h // 2) * 1024 + (tci + 1) * 128][b:b + 64, :]
                        out = ps_s[:, hh * 128:(hh + 1) * 128]
                        A.op("tensor", lambda e, o=out, l=l, r=r: e.matmul(o, l, r, start=True, stop=True),
                             r=[l, r], w=[out], label=f"sT{tc}_{h}")
                    A.op("vector", lambda e, i=ps_s, o=pT, d=D_all[:, qd * 512:(qd + 1) * 512]:
                         e.tensor_tensor(out=o, in0=i[:, :], in1=d, op=ALU.mult),
                         r=[ps_s[:, :], D_all], w=[pT], label=f"pT{tc}_{qd}")
                    q0 = qd * 2
                    A.op("vector", lambda e, o=qt, i0=q_h, g=Gq_all, q0=q0, tci=tci:
                         e.tensor_tensor(
                             out=o.rearrange("p (f x) -> p f x", x=128),
                             in0=i0.rearrange("p (f t) -> p f t", t=1024)[:, q0:q0 + 2, tci * 128:(tci + 1) * 128],
                             in1=g.rearrange("p (f x) -> p f x", x=128)[:, q0:q0 + 2, :],
                             op=ALU.mult),
                         r=[q_h, Gq_all], w=[qt], label=f"qt{tc}_{qd}")
                    for hh in range(4):
                        h = qd * 4 + hh
                        b = 64 * (h % 2)
                        k2 = h // 2
                        ps_y = self.pf[2 + k2 % 2]
                        ycol = (k2 // 2) * 128
                        out_y = ps_y[b:b + 64, ycol:ycol + 128]
                        l1 = vt[:, h * 64:(h + 1) * 64]
                        r1 = pT[:, hh * 128:(hh + 1) * 128]
                        A.op("tensor", lambda e, o=out_y, l=l1, r=r1, tp=(0, b): e.matmul(
                            o, l, r, start=True, stop=False, tile_position=tp),
                            r=[l1, r1], w=[ps_y[:, ycol:ycol + 128]], label=f"y{tc}_{h}|1")
                        l2 = SS[:, h * 64:(h + 1) * 64]
                        r2 = qt[:, (k2 % 2) * 128:(k2 % 2) * 128 + 128]
                        A.op("tensor", lambda e, o=out_y, l=l2, r=r2, tp=(0, b): e.matmul(
                            o, l, r, start=False, stop=True, tile_position=tp),
                            r=[l2, r2], w=[ps_y[:, ycol:ycol + 128]], label=f"y{tc}_{h}|2")
                        ps_ss = self.pf[6]
                        scol = (h // 2) * 64
                        out_s = ps_ss[b:b + 64, scol:scol + 64]
                        l3 = kt[:, h * 64:(h + 1) * 64]
                        r3 = vt[:, h * 64:(h + 1) * 64]
                        A.op("tensor", lambda e, o=out_s, l=l3, r=r3, tp=(0, b): e.matmul(
                            o, l, r, start=True, stop=False, tile_position=tp),
                            r=[l3, r3], w=[ps_ss[:, scol:scol + 64]], label=f"st{tc}_{h}|1")
                        l4 = gI[:, h * 64:(h + 1) * 64]
                        r4 = SS[:, h * 64:(h + 1) * 64]
                        A.op("tensor", lambda e, o=out_s, l=l4, r=r4, tp=(0, b): e.matmul(
                            o, l, r, start=False, stop=True, tile_position=tp),
                            r=[l4, r4], w=[ps_ss[:, scol:scol + 64]], label=f"st{tc}_{h}|2")
                for i2 in range(2):
                    A.op("scalar", lambda e, i=self.pf[2 + i2], o=yt[:, i2 * 512:(i2 + 1) * 512]:
                         e.copy(out=o, in_=i[:, :]),
                         r=[self.pf[2 + i2][:, :]], w=[yt[:, i2 * 512:(i2 + 1) * 512]],
                         label=f"ye{tc}_{i2}")
                for band in range(2):
                    src = self.pf[6][band * 64:(band + 1) * 64, 0:512]
                    dst = SS[band * 64:(band + 1) * 64, :].rearrange(
                        "p (h two x) -> p h two x", two=2, x=64)[:, :, band, :]
                    A.op("scalar", lambda e, i=src, o=dst: e.copy(
                        out=o, in_=i.rearrange("p (h x) -> p h x", x=64)),
                        r=[self.pf[6][:, :]], w=[dst], label=f"se{tc}_{band}")
                ps_o = [self.pf[2], self.pf[3]]
                for k in range(ND):
                    l = yt[:, (k % 2) * 512 + (k // 2) * 128: (k % 2) * 512 + (k // 2) * 128 + 128]
                    for hf in range(2):
                        r = outwT[:, k * D + hf * 512: k * D + (hf + 1) * 512]
                        out = ps_o[hf]
                        A.op("tensor", lambda e, o=out, l=l, r=r, s=(k == 0), p=(k == ND - 1):
                             e.matmul(o[:, :], l, r, start=s, stop=p),
                             r=[l, r], w=[out[:, :]], label=f"op3{tc}_{hf}|{k}")
                self.postln_residual(ps_o, tc, bias_row=outb_row, label=f"pl3_{tc}")

            g1(0)
            for tci in range(8):
                if tci + 1 < 8:
                    g1(tci + 1)
                g2(tci)

    def p4_l1_ffn(self):
        A, st = self.A, self.stat

        xln = self.big0

        def xlnT(tc):
            return xln[:, tc * D:(tc + 1) * D]

        self.ln_into(self.resT, xlnT, "ln4")
        self.transpose_roundtrip(xlnT, "tr4")

        def wdT(hk):
            if hk < 16:
                return self.big0[:, hk * D:(hk + 1) * D]
            return self.big2[:, (hk - 16) * D:(hk - 15) * D]

        for hk in range(24):
            A.dmat("aq", wdT(hk), self.wd_bf[:, hk * 128:(hk + 1) * 128], label=f"wdt{hk}")

        NH = H1F // 128
        for hk in range(NH):
            gsl = self.big1[:, (hk % 2) * 2048:(hk % 2) * 2048 + 1024]
            usl = self.big1[:, (hk % 2) * 2048 + 1024:(hk % 2) * 2048 + 2048]
            A.dma("sq", gsl.rearrange("p (dk f) -> p dk f", f=128),
                  self.wgT_sl[hk], label=f"ga{hk}")
            A.dma("sq", usl.rearrange("p (dk f) -> p dk f", f=128),
                  self.wuT_sl[hk], label=f"ua{hk}")
            hsb = self.big1[:, 4096 + (hk % 2) * 2048:4096 + (hk % 2) * 2048 + 2048]
            for tt in range(4):
                psg = self.pf[tt % 3]
                psu = self.pf[3 + tt % 3]
                for dk in range(ND):
                    r = self.xfmS(dk, tt * 512, (tt + 1) * 512)
                    lg = gsl[:, dk * 128:(dk + 1) * 128]
                    A.op("tensor", lambda e, o=psg, l=lg, r=r, s=(dk == 0), p=(dk == ND - 1):
                         e.matmul(o[:, :], l, r, start=s, stop=p),
                         r=[lg, r], w=[psg[:, :]], label=f"gm{hk}_{tt}|{dk}")
                for dk in range(ND):
                    r = self.xfmS(dk, tt * 512, (tt + 1) * 512)
                    lu = usl[:, dk * 128:(dk + 1) * 128]
                    A.op("tensor", lambda e, o=psu, l=lu, r=r, s=(dk == 0), p=(dk == ND - 1):
                         e.matmul(o[:, :], l, r, start=s, stop=p),
                         r=[lu, r], w=[psu[:, :]], label=f"um{hk}_{tt}|{dk}")
                sil = self.scr32[:, (tt % 2) * 512:(tt % 2) * 512 + 512]
                A.op("scalar", lambda e, i=psg, o=sil: e.activation(
                    out=o, in_=i[:, :], func=AF.Silu, scale=1.0 / 127.0),
                    r=[psg[:, :]], w=[sil], label=f"sil{hk}_{tt}")
                dst = hsb[:, tt * 512:(tt + 1) * 512]
                A.op("vector", lambda e, u=psu, s=sil, o=dst: e.scalar_tensor_tensor(
                    out=o, in0=u[:, :], scalar=1.0 / 127.0, in1=s, op0=ALU.mult, op1=ALU.mult),
                    r=[psu[:, :], sil], w=[dst], label=f"hmul{hk}_{tt}")
            A.dma("aq", self.h2_hbm[hk * 128:(hk + 1) * 128, :], hsb, label=f"h2s{hk}")

        groups = [list(range(g, min(g + 3, NT))) for g in range(0, NT, 3)]
        for gi, tcs in enumerate(groups):
            g0 = tcs[0]
            gw = len(tcs) * 128
            for hk in range(NH):
                s8 = hk % 8
                if s8 < 4:
                    hq = self.big1[:, s8 * 512: s8 * 512 + gw]
                else:
                    hq = self.big1[:, 8192 + (s8 - 4) * 512: 8192 + (s8 - 4) * 512 + gw]
                A.dma(("sq", "gq", "aq")[hk % 3], hq,
                      self.h2_hbm[hk * 128:(hk + 1) * 128, g0 * 128:g0 * 128 + gw],
                      label=f"hb{gi}_{hk}")
                for ti in range(len(tcs)):
                    l = hq[:, ti * 128:(ti + 1) * 128]
                    for hf in range(2):
                        ps = self.pf[2 * ti + hf]
                        r = wdT(hk)[:, hf * 512:(hf + 1) * 512]
                        A.op("tensor", lambda e, o=ps, l=l, r=r, s=(hk == 0), p=(hk == NH - 1):
                             e.matmul(o[:, :], l, r, start=s, stop=p),
                             r=[l, r], w=[ps[:, :]], label=f"dm{gi}_{ti}_{hf}|{hk}")
            for ti, tc in enumerate(tcs):
                self.postln_residual([self.pf[2 * ti], self.pf[2 * ti + 1]], tc,
                                     scale_const=1.0 / 127.0, label=f"pl4_{tc}")

    def p5_store_out(self):
        A = self.A
        for tc in range(NT):
            A.dma("aq", self.y_out[tc * 128:(tc + 1) * 128, :], self.resT(tc),
                  label=f"st_y{tc}")
        A.op("gpsimd", lambda e: e.engine_nop(),
             r=[self.y_out], w=(), label="final_wait")


def build_nc(upto=4):
    nc = bass.Bass("TRN2", target_bir_lowering=False, debug=False)
    with ExitStack() as ctx:
        mk = MK(nc, ctx, upto)
        mk.build()
        block = ctx.enter_context(nc.Block())
        mk.A.build(block, mk.sems)
    return nc


_IN_ORDER = [
    "L0_attn_x", "L0_attn_Wq", "L0_attn_bq", "L0_attn_Wk", "L0_attn_bk",
    "L0_attn_Wv", "L0_attn_bv", "L0_attn_Wo", "L0_attn_bo",
    "L0_ffn_w1", "L0_ffn_b1", "L0_ffn_w2", "L0_ffn_b2",
    "L1_attn_Wqkv_w", "L1_attn_Wqkv_b", "L1_attn_out_w", "L1_attn_out_b",
    "L1_attn_gamma_raw", "L1_ffn_w_gate", "L1_ffn_w_up", "L1_ffn_w_down",
]
_NAME_MAP = {
    "L0_attn_Wq": "Wq", "L0_attn_bq": "bq", "L0_attn_Wk": "Wk", "L0_attn_bk": "bk",
    "L0_attn_Wv": "Wv", "L0_attn_bv": "bv", "L0_attn_Wo": "Wo", "L0_attn_bo": "bo",
    "L0_ffn_w1": "w1", "L0_ffn_b1": "b1", "L0_ffn_w2": "w2", "L0_ffn_b2": "b2",
    "L1_attn_Wqkv_w": "wqkv", "L1_attn_Wqkv_b": "bqkv", "L1_attn_out_w": "outw",
    "L1_attn_out_b": "outb", "L1_attn_gamma_raw": "gamma",
    "L1_ffn_w_gate": "wg", "L1_ffn_w_up": "wu", "L1_ffn_w_down": "wd",
}


def kernel(**inputs):
    nc = build_nc(upto=4)
    consts = host_constants()
    shared = {}
    for k, v in inputs.items():
        if k == "L0_attn_x":
            continue
        arr = np.ascontiguousarray(np.asarray(v))
        shared[_NAME_MAP[k]] = arr
    for k, v in consts.items():
        shared[k] = np.ascontiguousarray(v)
    x = np.ascontiguousarray(np.asarray(inputs["L0_attn_x"], dtype=np.float32))
    in_maps = []
    for b in range(8):
        m = dict(shared)
        m["x"] = x[b]
        in_maps.append(m)
    res = run_bass_kernel_spmd(nc, in_maps, core_ids=list(range(8)))
    out = np.stack([np.asarray(res.results[b]["y"]) for b in range(8)], axis=0)
    return out.astype(np.float32)


# version 30
# speedup vs baseline: 1.0235x; 1.0033x over previous
import numpy as np
import ml_dtypes
import concourse.bass as bass
import concourse.mybir as mybir
from concourse.bass_utils import run_bass_kernel_spmd
from contextlib import ExitStack

F32 = mybir.dt.float32
BF16 = mybir.dt.bfloat16
I32 = mybir.dt.int32
AF = mybir.ActivationFunctionType
ALU = mybir.AluOpType
AX = mybir.AxisListType

MAGIC = 1.5 * 2.0 ** 23
T = 2048
D = 1024
NT = 16
ND = 8
H0 = 16
HD = 64
NC0 = 32
F1 = 4096
H1F = 3072
EPS_LN = 1e-5
EPS_Q = 1e-8

ENGINES = ("tensor", "vector", "scalar", "gpsimd", "sync")
NQSEM = 8
TT_DROP = True
PREP = True
PREP_W1 = True
PREP_TR = True
PREP_W2 = True
PREP_QK = False
PREP_W2T = True
PSUM_WIDE = True


def _ap_interval(ap):
    tname = type(ap.tensor).__name__
    if PSUM_WIDE and "PSum" in tname:
        return (ap.tensor.name, 0, 1 << 60)
    esz = mybir.dt.size(ap.dtype)
    lo = ap.offset
    hi = ap.offset
    if "DRam" in tname:
        for step, count in ap.ap:
            if count <= 1:
                continue
            ext = step * (count - 1)
            if ext >= 0:
                hi += ext
            else:
                lo += ext
    else:
        tshape = ap.tensor.shape
        pbytes = mybir.dt.size(ap.tensor.dtype)
        for s in tshape[1:]:
            pbytes *= s
        pitch = pbytes // esz
        lo = hi = ap.offset % pitch
        for step, count in ap.ap:
            if count <= 1 or abs(step) >= pitch:
                continue
            ext = step * (count - 1)
            if ext >= 0:
                hi += ext
            else:
                lo += ext
    return (ap.tensor.name, lo * esz, (hi + 1) * esz)


class Op:
    __slots__ = ("engine", "emit", "kind", "queue", "deps", "idx", "inc_id",
                 "flagged", "label", "qidx", "ownsem", "_maxdep", "tr")

    def __init__(self, engine, emit, kind, queue, label):
        self.tr = False
        self.engine = engine
        self.emit = emit
        self.kind = kind
        self.queue = queue
        self.deps = []
        self.idx = -1
        self.qidx = -1
        self.inc_id = None
        self.flagged = False
        self.ownsem = None
        self.label = label


class Asm:

    def __init__(self, nc):
        self.nc = nc
        self.ops = []
        self.eng_ops = {e: [] for e in ENGINES}
        self.queues = {"sq": "sync", "aq": "scalar", "gq": "gpsimd"}
        self.q_ops = {q: [] for q in self.queues}
        self.seg = {}
        self.n_ccsem = 0

    def _touch(self, op, aps, is_write):
        for ap in aps:
            name, lo, hi = _ap_interval(ap)
            segs = self.seg.setdefault(name, [])
            new = []
            deps = op.deps
            if is_write:
                keep = []
                for s in segs:
                    if s[1] <= lo or s[0] >= hi:
                        keep.append(s)
                        continue
                    if s[2] is not None:
                        deps.append(s[2])
                    deps.extend(s[3])
                    if s[0] < lo:
                        keep.append([s[0], lo, s[2], list(s[3])])
                    if s[1] > hi:
                        keep.append([hi, s[1], s[2], list(s[3])])
                keep.append([lo, hi, op, []])
                self.seg[name] = keep
            else:
                for s in segs:
                    if s[1] <= lo or s[0] >= hi:
                        continue
                    if s[2] is not None:
                        deps.append(s[2])
                    s[3].append(op)
                if not segs:
                    segs.append([lo, hi, None, [op]])

    def op(self, engine, emit, r=(), w=(), label="", tr=False):
        o = Op(engine, emit, "c", None, label)
        o.tr = tr
        self._touch(o, r, False)
        self._touch(o, w, True)
        self.ops.append(o)
        self.eng_ops[engine].append(o)
        return o

    def dma(self, queue, out, in_, label="", **kw):
        eng = self.queues[queue]
        o = Op(eng, lambda e, out=out, in_=in_, kw=kw: e.dma_start(out=out, in_=in_, **kw),
               "d", queue, label)
        self._touch(o, [in_], False)
        self._touch(o, [out], True)
        self.ops.append(o)
        self.eng_ops[eng].append(o)
        self.q_ops[queue].append(o)
        return o

    def dmat(self, queue, out, in_, label=""):
        eng = self.queues[queue]
        assert eng in ("sync", "scalar")
        o = Op(eng, lambda e, out=out, in_=in_: e.dma_start_transpose(out, in_),
               "d", queue, label)
        self._touch(o, [in_], False)
        self._touch(o, [out], True)
        last = getattr(self, "_last_dmat", None)
        if last is not None:
            o.deps.append(last)
        self._last_dmat = o
        self.ops.append(o)
        self.eng_ops[eng].append(o)
        self.q_ops[queue].append(o)
        return o

    def cc(self, emit, r=(), w=(), label="cc"):
        o = Op("gpsimd", emit, "cc", None, label)
        self._touch(o, r, False)
        self._touch(o, w, True)
        self.ops.append(o)
        self.eng_ops["gpsimd"].append(o)
        self.n_ccsem += 1
        return o

    def build(self, block, sems):
        for e in ENGINES:
            for i, o in enumerate(self.eng_ops[e]):
                o.idx = i
        for q in self.q_ops:
            for i, o in enumerate(self.q_ops[q]):
                o.qidx = i
        tforce = {}
        prev = None
        for o in self.eng_ops["tensor"]:
            if prev is not None and (prev.tr or o.tr):
                tforce[o.idx] = prev.idx
            prev = o
        ccs = []
        for o in self.ops:
            if o.kind == "cc":
                o.ownsem = sems["cc"][len(ccs)]
                ccs.append(o)
            maxdep = {}
            for d in o.deps:
                if d is o:
                    continue
                if d.kind in ("d", "cc"):
                    continue
                if d.engine == o.engine == "gpsimd":
                    continue
                if (d.engine == o.engine == "tensor" and TT_DROP and
                        "|" in o.label and o.label.split("|")[0] == d.label.split("|")[0]):
                    continue
                if maxdep.get(d.engine, -1) < d.idx:
                    maxdep[d.engine] = d.idx
            if o.engine == "tensor" and o.idx in tforce:
                if maxdep.get("tensor", -1) < tforce[o.idx]:
                    maxdep["tensor"] = tforce[o.idx]
            o._maxdep = maxdep
            for e2, i2 in maxdep.items():
                self.eng_ops[e2][i2].flagged = True
        for e in ENGINES:
            n = 0
            for o in self.eng_ops[e]:
                if o.kind == "c" and o.flagged:
                    n += 1
                    o.inc_id = n
        plans = {e: [] for e in ENGINES}
        waited = {}
        drained = {e: 0 for e in ENGINES}
        for o in self.ops:
            waits = []
            need_drain = False
            for d in o.deps:
                if d is o:
                    continue
                if d.kind == "d":
                    key = ("q", d.queue, d.qidx % NQSEM)
                    v = 16 * (d.qidx // NQSEM + 1)
                    if waited.get((o.engine, key), 0) < v:
                        waits.append((key, v))
                elif d.kind == "cc":
                    key = ("cc", id(d.ownsem))
                    if waited.get((o.engine, key), 0) < 1:
                        waits.append(((key[0], key[1], d.ownsem), 1))
                elif d.engine == o.engine == "gpsimd":
                    if d.idx >= drained[o.engine]:
                        need_drain = True
            for e2, i2 in o._maxdep.items():
                d = self.eng_ops[e2][i2]
                key = ("e", e2)
                v = d.inc_id
                assert v is not None
                if waited.get((o.engine, key), 0) < v:
                    waits.append((key, v))
            if o.kind == "d" and o.qidx >= NQSEM:
                key = ("q", o.queue, o.qidx % NQSEM)
                v = 16 * (o.qidx // NQSEM)
                if waited.get((o.engine, key), 0) < v:
                    waits.append((key, v))
            final_waits = []
            seen = set()
            for key, v in waits:
                k2 = key[:3] if key[0] != "cc" else key[:2]
                if waited.get((o.engine, k2), 0) < v and (k2, v) not in seen:
                    waited[(o.engine, k2)] = v
                    seen.add((k2, v))
                    final_waits.append((key, v))
            if need_drain:
                drained[o.engine] = o.idx
            plans[o.engine].append((final_waits, need_drain, o))
        self._plans = plans
        self._sems = sems

        nc = self.nc
        qsem = sems["q"]
        esem = sems["e"]

        def run_engine(engobj, ename):
            for waits, need_drain, o in plans[ename]:
                for key, v in waits:
                    if key[0] == "q":
                        engobj.wait_ge(qsem[(key[1], key[2])], v)
                    elif key[0] == "e":
                        engobj.wait_ge(esem[key[1]], v)
                    else:
                        engobj.wait_ge(key[2], v)
                if need_drain:
                    engobj.drain()
                inst = o.emit(engobj)
                if o.kind == "d":
                    inst.then_inc(qsem[(o.queue, o.qidx % NQSEM)], 16)
                elif o.kind == "cc":
                    inst.then_inc(o.ownsem, 1)
                elif o.flagged:
                    if ename == "gpsimd":
                        engobj.drain().then_inc(esem[ename], 1)
                    else:
                        inst.then_inc(esem[ename], 1)

        @block.tensor
        def _(e):
            run_engine(e, "tensor")

        @block.vector
        def _(e):
            run_engine(e, "vector")

        @block.scalar
        def _(e):
            run_engine(e, "scalar")

        @block.gpsimd
        def _(e):
            run_engine(e, "gpsimd")

        @block.sync
        def _(e):
            run_engine(e, "sync")



def host_constants():
    c = {}
    c["hc_ident"] = np.eye(128, dtype=ml_dtypes.bfloat16)
    ends = np.arange(0, T, 64)
    starts = np.clip(ends - 63, 0, None)
    lens = (ends - starts + 1).astype(np.float32)
    A = np.zeros((T, NC0), np.float32)
    for ch in range(NC0):
        A[starts[ch]:ends[ch] + 1, ch] = 1.0 / lens[ch]
    c["hc_A"] = A.astype(ml_dtypes.bfloat16)
    msk = np.where(ends[None, :] <= np.arange(T)[:, None], 0.0, -1e30).astype(np.float32)
    c["hc_mask"] = msk
    j = np.arange(128)[:, None]
    i = np.arange(128)[None, :]
    c["hc_mdec"] = np.where(i >= j, (i - j).astype(np.float32), 1.0e4)
    c["hc_gq"] = np.broadcast_to((i + 1).astype(np.float32), (128, 128)).copy()
    c["hc_gk"] = (127 - np.arange(128, dtype=np.float32))[:, None].copy()
    ia = np.zeros((128, 64), ml_dtypes.bfloat16)
    ia[0:64] = np.eye(64)
    ib = np.zeros((128, 64), ml_dtypes.bfloat16)
    ib[64:128] = np.eye(64)
    c["hc_idA"] = ia
    c["hc_idB"] = ib
    return c



class MK:
    def __init__(self, nc, ctx, upto):
        self.nc = nc
        self.ctx = ctx
        self.upto = upto
        e = ctx.enter_context
        self.A = None

        dram = nc.dram_tensor
        self.x_in = dram("x", [T, D], F32, kind="ExternalInput").ap()
        self.y_out = dram("y", [T, D], F32, kind="ExternalOutput").ap()
        names = [("Wq", [D, D]), ("bq", [D]), ("Wk", [D, D]), ("bk", [D]),
                 ("Wv", [D, D]), ("bv", [D]), ("Wo", [D, D]), ("bo", [D]),
                 ("w1", [F1, D]), ("b1", [F1]), ("w2", [D, F1]), ("b2", [D]),
                 ("wqkv", [3 * D, D]), ("bqkv", [3 * D]), ("outw", [D, D]),
                 ("outb", [D]), ("gamma", [16])]
        for n, shp in names:
            setattr(self, n, dram(n, shp, F32, kind="ExternalInput").ap())
        for n, shp in [("wg", [H1F, D]), ("wu", [H1F, D]), ("wd", [D, H1F])]:
            setattr(self, n, dram(n, shp, I32, kind="ExternalInput").ap())
        for n, shp, dt in [("hc_ident", [128, 128], BF16), ("hc_A", [T, NC0], BF16),
                           ("hc_mask", [T, NC0], F32), ("hc_mdec", [128, 128], F32),
                           ("hc_gq", [128, 128], F32), ("hc_gk", [128, 1], F32),
                           ("hc_idA", [128, 64], BF16), ("hc_idB", [128, 64], BF16)]:
            setattr(self, n, dram(n, shp, dt, kind="ExternalInput").ap())

        self.xlnT_hbm = dram("xlnT_hbm", [T, D], BF16).ap()
        self.w1q_hbm = dram("w1q_hbm", [F1, D], BF16).ap()
        self.w1qT_hbm = dram("w1qT_hbm", [D, F1], BF16).ap()
        self.w2q_hbm = dram("w2q_hbm", [D, F1], BF16).ap()
        self.w2qT_hbm = dram("w2qT_hbm", [F1, D], BF16).ap()
        self.h_hbm = dram("h_hbm", [F1, T], BF16).ap()
        self.wqkv_bf = dram("wqkv_bf", [3 * D, D], BF16).ap()
        self.wqkvT_hbm = dram("wqkvT_hbm", [D, 3 * D], BF16).ap()
        self.outw_bf = dram("outw_bf", [D, D], BF16).ap()
        self.wg_bf = dram("wg_bf", [H1F, D], BF16).ap()
        self.wu_bf = dram("wu_bf", [H1F, D], BF16).ap()
        self.wd_bf = dram("wd_bf", [D, H1F], BF16).ap()
        self.wgT_sl = dram("wgT_sl", [24, 128, ND, 128], BF16).ap()
        self.wuT_sl = dram("wuT_sl", [24, 128, ND, 128], BF16).ap()
        self.w1qT_sl = dram("w1qT_sl", [32, 128, ND, 128], BF16).ap()
        self.wqkv_sl = dram("wqkv_sl", [16, 128, ND, 128], BF16).ap()
        self.wq_sl = dram("wq_sl", [8, 128, ND, 128], BF16).ap()
        self.wk_sl = dram("wk_sl", [8, 128, ND, 128], BF16).ap()
        self.wv_sl = dram("wv_sl", [8, 128, ND, 128], BF16).ap()
        self.wdT_hbm = dram("wdT_hbm", [H1F, D], BF16).ap()
        self.h2_hbm = dram("h2_hbm", [H1F, T], BF16).ap()
        self.row_hbm = dram("row_hbm", [8, D], F32).ap()
        self.cc_a = [dram(f"cc_a{i}", [1, 128], F32).ap() for i in range(2)]
        self.cc_b = [dram(f"cc_b{i}", [1, 128], F32).ap() for i in range(2)]

        sb = nc.sbuf_tensor
        self.res = e(sb("res", [128, NT * D], F32))
        self.xfm = e(sb("xfm", [128, ND * T], BF16))
        self.big0 = e(sb("big0", [128, NT * D], BF16))
        self.big1 = e(sb("big1", [128, 10240], BF16))
        self.big2 = e(sb("big2", [128, 8192], BF16))
        self.scr32 = e(sb("scr32", [128, 2048], F32))
        self.tile = e(sb("tile", [128, 4096], BF16))
        self.rowb = e(sb("rowb", [128, 3072], BF16))
        self.cons = e(sb("cons", [128, 6144], BF16))
        self.con32 = e(sb("con32", [128, 800], F32))
        self.stat = e(sb("stat", [128, 256], F32))
        self.pf = [e(nc.psum_tensor(f"pf{i}", [128, 512], F32)) for i in range(7)]
        self.pb = e(nc.psum_tensor("pb", [128, 1024], BF16))

        self.sems = {
            "e": {en: e(nc.semaphore(f"se_{en}")) for en in ENGINES},
            "q": {(q, i): e(nc.semaphore(f"sq_{q}{i}"))
                  for q in ("sq", "aq", "gq") for i in range(NQSEM)},
            "cc": [e(nc.semaphore(f"scc{i}")) for i in range(2)],
        }
        self.A = Asm(nc)

    def resT(self, tc):
        return self.res[:, tc * D:(tc + 1) * D]

    def xfmS(self, dk, t0, t1):
        return self.xfm[:, dk * T + t0: dk * T + t1]

    def ln_into(self, src_of_tc, dst_of_tc, label, nt=NT, track_max=None):
        A, st = self.A, self.stat
        for tc in range(nt):
            src = src_of_tc(tc)
            bn6 = st[:, 32:44]
            for g in range(2):
                A.op("vector", lambda e, s=src, o=bn6, g=g: e.bn_stats(
                    out=o[:, 6 * g:6 * g + 6], in_=s[:, 512 * g:512 * g + 512]),
                    r=[src], w=[bn6[:, 6 * g:6 * g + 6]], label=f"{label}_bnst{tc}_{g}")
            mv = st[:, 44 + 2 * tc: 46 + 2 * tc]
            A.op("vector", lambda e, i=bn6, o=mv: e.bn_aggr(out=o, in_=i),
                 r=[bn6], w=[mv], label=f"{label}_bnag{tc}")
        mvall = st[:, 44:44 + 2 * nt]
        var_v = mvall.rearrange("p (t two) -> p t two", two=2)[:, :, 1:2]
        vareps = st[:, 80:80 + nt]
        A.op("vector", lambda e, i=var_v, o=vareps: e.tensor_scalar(
            out=o, in0=i.rearrange("p t one -> p (t one)"), scalar1=EPS_LN, scalar2=None, op0=ALU.add),
            r=[mvall], w=[vareps], label=f"{label}_veps")
        std = st[:, 96:96 + nt]
        A.op("scalar", lambda e, i=vareps, o=std: e.activation(out=o, in_=i, func=AF.Ln),
             r=[vareps], w=[std], label=f"{label}_ln")
        rstd = st[:, 112:112 + nt]
        A.op("scalar", lambda e, i=std, o=rstd: e.activation(out=o, in_=i, func=AF.Exp, scale=-0.5),
             r=[std], w=[rstd], label=f"{label}_rexp")
        for tc in range(nt):
            src = src_of_tc(tc)
            dst = dst_of_tc(tc)
            m = mvall[:, 2 * tc:2 * tc + 1]
            r = rstd[:, tc:tc + 1]
            A.op("vector", lambda e, s=src, o=dst, m=m, r=r: e.tensor_scalar(
                out=o, in0=s, scalar1=m, scalar2=r, op0=ALU.subtract, op1=ALU.mult),
                r=[src, mvall, rstd], w=[dst], label=f"{label}_nrm{tc}")

    def transpose_roundtrip(self, tm_of_tc, label):
        A = self.A
        hb = self.xlnT_hbm
        for tc in range(NT):
            src = tm_of_tc(tc)
            A.dma("aq", hb[tc * 128:(tc + 1) * 128, :], src, label=f"{label}_st{tc}")
        for dk in range(ND):
            A.dmat("aq", self.xfmS(dk, 0, T), hb[:, dk * 128:(dk + 1) * 128],
                   label=f"{label}_tr{dk}")

    def postln_residual(self, psum_list, tc, bias_row=None, label="",
                        scale_row=None, scale_const=None, then=None):
        A, st = self.A, self.stat
        ao = self.scr32[:, 0:1024]
        for i, ps in enumerate(psum_list):
            dst = ao[:, i * 512:(i + 1) * 512]
            if scale_row is not None:
                sr = scale_row[:, i * 512:(i + 1) * 512]
                A.op("vector", lambda e, p=ps, s=sr, o=dst:
                     e.tensor_tensor(out=o, in0=p[:, :], in1=s, op=ALU.mult),
                     r=[ps[:, :], scale_row], w=[dst], label=f"{label}_s{i}")
                if bias_row is not None:
                    br = bias_row[:, i * 512:(i + 1) * 512]
                    A.op("vector", lambda e, b=br, o=dst:
                         e.tensor_tensor(out=o, in0=o, in1=b, op=ALU.add),
                         r=[dst, bias_row], w=[dst], label=f"{label}_b{i}")
            elif scale_const is not None:
                A.op("vector", lambda e, p=ps, o=dst, c=scale_const:
                     e.tensor_scalar(out=o, in0=p[:, :], scalar1=c, scalar2=None,
                                     op0=ALU.mult),
                     r=[ps[:, :]], w=[dst], label=f"{label}_c{i}")
            elif bias_row is not None:
                br = bias_row[:, i * 512:(i + 1) * 512]
                A.op("vector", lambda e, p=ps, b=br, o=dst:
                     e.tensor_tensor(out=o, in0=p[:, :], in1=b, op=ALU.add),
                     r=[ps[:, :], bias_row], w=[dst], label=f"{label}_b{i}")
            else:
                A.op("vector", lambda e, p=ps, o=dst:
                     e.tensor_copy(out=o, in_=p[:, :]),
                     r=[ps[:, :]], w=[dst], label=f"{label}_cp{i}")
        bn6 = st[:, 40:52]
        for g in range(2):
            A.op("vector", lambda e, s=ao, g=g, o=bn6: e.bn_stats(
                out=o[:, 6 * g:6 * g + 6], in_=s[:, 512 * g:512 * g + 512]),
                r=[ao], w=[bn6[:, 6 * g:6 * g + 6]], label=f"{label}_bn{g}")
        mv = st[:, 52:54]
        A.op("vector", lambda e, i=bn6, o=mv: e.bn_aggr(out=o, in_=i),
             r=[bn6], w=[mv], label=f"{label}_bnag")
        sd = st[:, 55:56]
        A.op("scalar", lambda e, m=mv, o=sd, b=self.epsc: e.activation(
            out=o, in_=m[:, 1:2], func=AF.Ln, bias=b),
            r=[mv, self.epsc], w=[sd], label=f"{label}_ln")
        rs = st[:, 56:57]
        A.op("scalar", lambda e, i=sd, o=rs: e.activation(out=o, in_=i, func=AF.Exp, scale=-0.5),
             r=[sd], w=[rs], label=f"{label}_rexp")
        mr = st[:, 57:58]
        A.op("vector", lambda e, m=mv, r=rs, o=mr: e.tensor_tensor(
            out=o, in0=m[:, 0:1], in1=r, op=ALU.mult),
            r=[mv, rs], w=[mr], label=f"{label}_mr")
        dst = self.resT(tc)
        A.op("vector", lambda e, a=ao, r=rs, d=dst: e.scalar_tensor_tensor(
            out=d, in0=a, scalar=r, in1=d, op0=ALU.mult, op1=ALU.add),
            r=[ao, rs, dst], w=[dst], label=f"{label}_acc")
        A.op("vector", lambda e, d=dst, m=mr: e.tensor_scalar(
            out=d, in0=d, scalar1=m, scalar2=None, op0=ALU.subtract),
            r=[dst, mr], w=[dst], label=f"{label}_sub")
        if then is not None:
            then(tc)

    def drip(self, n=2):
        while n > 0 and self._prep_units:
            self._prep_units.pop(0)()
            n -= 1

    def drip_flush(self):
        while self._prep_units:
            self._prep_units.pop(0)()

    def build(self):
        self.p0_load_consts()
        self._prep_units = []
        if PREP:
            self.p_prep_build_units()
        self.p1_l0_attn()
        self.drip_flush()
        if self.upto >= 2:
            self.p2_l0_ffn()
        if self.upto >= 3:
            self.p3_l1_attn()
        if self.upto >= 4:
            self.p4_l1_ffn()
        self.p5_store_out()

    def p0_load_consts(self):
        A = self.A
        xs = self.x_in.rearrange("(a p) d -> p a d", p=128)
        rd = self.res[:, :].rearrange("p (a d) -> p a d", d=D)
        for tc in range(NT):
            A.dma("sq", rd[:, tc:tc + 1, :], xs[:, tc:tc + 1, :], label=f"ld_x{tc}")
        self.ident = self.cons[:, 0:128]
        A.dma("sq", self.ident, self.hc_ident, label="ld_ident")
        self.A_sb = self.cons[:, 128:128 + NT * NC0]
        A.dma("sq", self.A_sb.rearrange("p (a c) -> p a c", c=NC0),
              self.hc_A.rearrange("(a p) c -> p a c", p=128), label="ld_A")
        self.idA = self.cons[:, 640:704]
        self.idB = self.cons[:, 704:768]
        A.dma("sq", self.idA, self.hc_idA, label="ld_idA")
        A.dma("sq", self.idB, self.hc_idB, label="ld_idB")
        self.mask_sb = self.con32[:, 0:NT * NC0]
        A.dma("sq", self.mask_sb.rearrange("p (a c) -> p a c", c=NC0),
              self.hc_mask.rearrange("(a p) c -> p a c", p=128), label="ld_mask")
        self.mdec = self.con32[:, 512:640]
        A.dma("sq", self.mdec, self.hc_mdec, label="ld_mdec")
        self.gq_in = self.con32[:, 640:768]
        A.dma("sq", self.gq_in, self.hc_gq, label="ld_gq")
        self.gk_in = self.con32[:, 768:769]
        A.dma("sq", self.gk_in, self.hc_gk, label="ld_gk")
        self.nmag = self.con32[:, 769:770]
        A.op("vector", lambda e, o=self.nmag: e.memset(o, -MAGIC), w=[self.nmag], label="nmag")
        self.epsc = self.con32[:, 773:774]
        A.op("vector", lambda e, o=self.epsc: e.memset(o, EPS_LN), w=[self.epsc], label="epsc")


    def p_prep_build_units(self):
        A, st = self.A, self.stat
        b1 = self.big1
        U = self._prep_units
        fsl = [b1[:, 4096:6144].bitcast(F32), b1[:, 6144:8192].bitcast(F32)]
        osl = [b1[:, 8192:9216], b1[:, 9216:10240]]
        NF1 = F1 // 128

        sw1 = st[:, 0:32]
        sw2 = st[:, 224:232]
        am = self.con32[:, 770:771]
        rc = self.con32[:, 771:772]
        a4 = self.con32[:, 772:773]

        def w1_load(fc):
            A.dma("sq", fsl[fc % 2], self.w1[fc * 128:(fc + 1) * 128, :], label=f"w1l{fc}")

        def w1_comp(fc):
            fslot = fsl[fc % 2]
            oslot = osl[fc % 2]
            A.op("vector", lambda e, i=fslot, o=am: e.reduce_max(
                out=o, in_=i, axis=AX.X, apply_absolute_value=True),
                r=[fslot], w=[am], label=f"w1am{fc}")
            A.op("vector", lambda e, o=am: e.tensor_scalar(
                out=o, in0=o, scalar1=EPS_Q, scalar2=None, op0=ALU.max),
                r=[am], w=[am], label=f"w1amc{fc}")
            A.op("vector", lambda e, i=am, o=rc: e.reciprocal(out=o, in_=i),
                 r=[am], w=[rc], label=f"w1rc{fc}")
            A.op("vector", lambda e, o=rc: e.tensor_scalar(
                out=o, in0=o, scalar1=127.0, scalar2=None, op0=ALU.mult),
                r=[rc], w=[rc], label=f"w1rc2{fc}")
            A.op("vector", lambda e, i=am, o=sw1[:, fc:fc + 1]: e.tensor_scalar(
                out=o, in0=i, scalar1=1.0 / 127.0, scalar2=None, op0=ALU.mult),
                r=[am], w=[sw1[:, fc:fc + 1]], label=f"w1sw{fc}")
            self.magic_round(oslot, fslot, rc[:, 0:1], False, f"w1q{fc}", tmp=fslot)
            A.dma("sq", self.w1q_hbm[fc * 128:(fc + 1) * 128, :], oslot, label=f"w1s{fc}")

        U.append(lambda: (w1_load(0), w1_load(1)))
        for fc in range(NF1):
            U.append(lambda fc=fc: (w1_comp(fc),
                                    w1_load(fc + 2) if fc + 2 < NF1 else None))

        def w1t_unit(dk, hh):
            stg2 = b1[:, 4096 + (hh % 2) * 2048: 4096 + (hh % 2) * 2048 + 2048]
            A.dmat("sq", stg2, self.w1q_hbm[hh * 2048:(hh + 1) * 2048,
                                            dk * 128:(dk + 1) * 128], label=f"w1t{dk}_{hh}")
            A.dma("gq", self.w1qT_sl[hh * 16:(hh + 1) * 16, :, dk, :]
                  .rearrange("fc p f -> p fc f"),
                  stg2.rearrange("p (fc f) -> p fc f", f=128), label=f"w1ts{dk}_{hh}")

        for dk in range(ND):
            for hh in range(2):
                U.append(lambda dk=dk, hh=hh: w1t_unit(dk, hh))

        def w2_maxpass(dc):
            for q4 in range(4):
                fslot = fsl[q4 % 2]
                A.dma("aq", fslot, self.w2[dc * 128:(dc + 1) * 128, q4 * 1024:(q4 + 1) * 1024],
                      label=f"w2l{dc}_{q4}")
                A.op("vector", lambda e, i=fslot, o=a4: e.reduce_max(
                    out=o, in_=i, axis=AX.X, apply_absolute_value=True),
                    r=[fslot], w=[a4], label=f"w2a{dc}_{q4}")
                if q4 == 0:
                    A.op("vector", lambda e, i=a4, o=am: e.tensor_copy(out=o, in_=i),
                         r=[a4], w=[am], label=f"w2c{dc}")
                else:
                    A.op("vector", lambda e, i=a4, o=am: e.tensor_tensor(
                        out=o, in0=o, in1=i, op=ALU.max), r=[a4, am], w=[am], label=f"w2m{dc}_{q4}")
            A.op("vector", lambda e, o=am: e.tensor_scalar(
                out=o, in0=o, scalar1=EPS_Q, scalar2=None, op0=ALU.max),
                r=[am], w=[am], label=f"w2mc{dc}")
            A.op("vector", lambda e, i=am, o=rc: e.reciprocal(out=o, in_=i),
                 r=[am], w=[rc], label=f"w2rc{dc}")
            A.op("vector", lambda e, o=rc: e.tensor_scalar(
                out=o, in0=o, scalar1=127.0, scalar2=None, op0=ALU.mult),
                r=[rc], w=[rc], label=f"w2rc2{dc}")
            A.op("vector", lambda e, i=am, o=sw2[:, dc:dc + 1]: e.tensor_scalar(
                out=o, in0=i, scalar1=1.0 / 127.0, scalar2=None, op0=ALU.mult),
                r=[am], w=[sw2[:, dc:dc + 1]], label=f"w2sw{dc}")

        def w2_quant(dc, q4):
            fslot = fsl[q4 % 2]
            oslot = osl[q4 % 2]
            A.dma("aq", fslot, self.w2[dc * 128:(dc + 1) * 128, q4 * 1024:(q4 + 1) * 1024],
                  label=f"w2l2{dc}_{q4}")
            self.magic_round(oslot, fslot, rc[:, 0:1], False, f"w2q{dc}_{q4}", tmp=fslot)
            A.dma("aq", self.w2q_hbm[dc * 128:(dc + 1) * 128, q4 * 1024:(q4 + 1) * 1024],
                  oslot, label=f"w2s{dc}_{q4}")

        for dc in range(ND):
            U.append(lambda dc=dc: w2_maxpass(dc))
            U.append(lambda dc=dc: (w2_quant(dc, 0), w2_quant(dc, 1)))
            U.append(lambda dc=dc: (w2_quant(dc, 2), w2_quant(dc, 3)))

        def w2t_unit(fc):
            stg2 = b1[:, 4096 + (fc % 2) * 2048: 4096 + (fc % 2) * 2048 + 1024]
            A.dmat("aq", stg2, self.w2q_hbm[:, fc * 128:(fc + 1) * 128], label=f"w2t{fc}")
            A.dma("gq", self.w2qT_hbm[fc * 128:(fc + 1) * 128, :], stg2, label=f"w2ts{fc}")

        for fc in range(NF1):
            U.append(lambda fc=fc: w2t_unit(fc))

        def cast_unit(nm, dst, src_, c0, c1):
            for c in range(c0, c1):
                A.dma("gq", dst[c * 128:(c + 1) * 128, :], src_[c * 128:(c + 1) * 128, :],
                      label=f"{nm}_cast{c}")

        for c in range(0, 24, 4):
            U.append(lambda c=c: cast_unit("qk", self.wqkv_bf, self.wqkv, c, c + 4))

        def qkt_unit(dk, hh):
            stg2 = b1[:, 4096 + (hh % 2) * 2048: 4096 + (hh % 2) * 2048 + 1536]
            A.dmat("aq", stg2, self.wqkv_bf[hh * 1536:(hh + 1) * 1536,
                                            dk * 128:(dk + 1) * 128], label=f"qkt{dk}_{hh}")
            A.dma("gq", self.wqkvT_hbm[dk * 128:(dk + 1) * 128, hh * 1536:(hh + 1) * 1536],
                  stg2, label=f"qkts{dk}_{hh}")
            nsl = 12 if hh == 0 else 4
            A.dma("gq", self.wqkv_sl[hh * 12:hh * 12 + nsl, :, dk, :]
                  .rearrange("fc p f -> p fc f"),
                  stg2[:, 0:nsl * 128].rearrange("p (fc f) -> p fc f", f=128),
                  label=f"qksl{dk}_{hh}")

        for dk in range(ND):
            for hh in range(2):
                U.append(lambda dk=dk, hh=hh: qkt_unit(dk, hh))
        for c in range(0, 8, 4):
            U.append(lambda c=c: cast_unit("ow", self.outw_bf, self.outw, c, c + 4))
        for c in range(0, 24, 4):
            U.append(lambda c=c: cast_unit("wg", self.wg_bf, self.wg, c, c + 4))
        for c in range(0, 24, 4):
            U.append(lambda c=c: cast_unit("wu", self.wu_bf, self.wu, c, c + 4))
        for c in range(0, 8, 4):
            U.append(lambda c=c: cast_unit("wd", self.wd_bf, self.wd, c, c + 4))

    def p_prep_late(self):
        A = self.A
        b1 = self.big1
        for nm, w_bf, w_sl in (("wg", self.wg_bf, self.wgT_sl),
                               ("wu", self.wu_bf, self.wuT_sl)):
            for dk in range(ND):
                for hh in range(2):
                    stg2 = b1[:, 8192:9728] if hh == 0 else b1[:, 8704:10240]
                    A.dmat("sq", stg2, w_bf[hh * 1536:(hh + 1) * 1536,
                                            dk * 128:(dk + 1) * 128], label=f"{nm}t{dk}_{hh}")
                    A.dma("sq", w_sl[hh * 12:(hh + 1) * 12, :, dk, :]
                          .rearrange("fc p f -> p fc f"),
                          stg2.rearrange("p (fc f) -> p fc f", f=128),
                          label=f"{nm}ts{dk}_{hh}")

    def p1_l0_attn(self):
        A, nc = self.A, self.nc
        xln = self.big0

        def xlnT(tc):
            return xln[:, tc * D:(tc + 1) * D]

        self.ln_into(self.resT, xlnT, "ln1")
        self.transpose_roundtrip(xlnT, "tr1")

        ps = self.pf[0]
        summ_ps = ps[:, 0:256].rearrange("p (dk c) -> p dk c", c=NC0)
        for dk in range(ND):
            self.drip(1)
            for tc in range(NT):
                lhsT = xlnT(tc)[:, dk * 128:(dk + 1) * 128]
                rhs = self.A_sb[:, tc * NC0:(tc + 1) * NC0]
                out = summ_ps[:, dk, :]
                A.op("tensor", lambda e, o=out, l=lhsT, r=rhs, s=(tc == 0), p=(tc == NT - 1):
                     e.matmul(o, l, r, start=s, stop=p),
                     r=[lhsT, rhs], w=[ps[:, dk * 32:(dk + 1) * 32]], label=f"summ_{dk}|{tc}")
        summ = self.tile[:, 0:256]
        A.op("vector", lambda e, i=ps[:, 0:256], o=summ: e.tensor_copy(out=o, in_=i),
             r=[ps[:, 0:256]], w=[summ], label="summ_ev")

        bq_s = self.stat[:, 128:136]
        bk_s = self.stat[:, 136:144]
        A.dma("gq", bq_s, self.bq.rearrange("(c p) -> p c", p=128), label="ld_bq", allow_slow_non_contiguous=True)
        A.dma("gq", bk_s, self.bk.rearrange("(c p) -> p c", p=128), label="ld_bk", allow_slow_non_contiguous=True)
        bk8 = self.stat[:, 144:152]
        A.op("vector", lambda e, i=bk_s, o=bk8: e.tensor_scalar(
            out=o, in0=i, scalar1=0.125, scalar2=None, op0=ALU.mult),
            r=[bk_s], w=[bk8], label="bk8")
        bv_row = self.rowb[:, 0:1024]
        bo_row = self.rowb[:, 1024:2048]
        A.dma("gq", bv_row, self.bv.rearrange("(one d) -> one d", one=1).broadcast_to([128, D]), label="ld_bv")
        A.dma("gq", bo_row, self.bo.rearrange("(one d) -> one d", one=1).broadcast_to([128, D]), label="ld_bo")

        kfm = self.tile[:, 256:512]
        wk_sl = self.big1[:, 0:4096]
        self._p1slot = 0

        def wslot():
            s = self._p1slot % 4
            self._p1slot += 1
            return wk_sl[:, s * 1024:(s + 1) * 1024]

        for fc in range(ND):
            self.drip(2)
            buf = wslot()
            A.dma("gq", buf.rearrange("p (dk f) -> p dk f", f=128),
                  self.Wk[:, fc * 128:(fc + 1) * 128].rearrange("(dk p) f -> p dk f", p=128),
                  label=f"ld_wk{fc}")
            out = self.pf[1][:, 0:32]
            for dk in range(ND):
                l = buf[:, dk * 128:(dk + 1) * 128]
                r = summ[:, dk * 32:(dk + 1) * 32]
                A.op("tensor", lambda e, o=out, l=l, r=r, s=(dk == 0), p=(dk == ND - 1):
                     e.matmul(o, l, r, start=s, stop=p),
                     r=[l, r], w=[out], label=f"kmm{fc}|{dk}")
            o2 = kfm[:, fc * 32:(fc + 1) * 32]
            A.op("scalar", lambda e, i=out, o=o2, b=bk8[:, fc:fc + 1]: e.activation(
                out=o, in_=i, func=AF.Identity, bias=b, scale=0.125),
                r=[out, bk8], w=[o2], label=f"kev{fc}")

        vpad = self.tile[:, 512:1536]
        A.op("vector", lambda e, o=vpad: e.memset(o, 0.0), w=[vpad], label="vpad0")
        vbufs = []
        for fc in range(ND):
            self.drip(2)
            buf = wslot()
            vbufs.append(buf)
            A.dma("gq", buf.rearrange("p (dk f) -> p dk f", f=128),
                  self.Wv[:, fc * 128:(fc + 1) * 128].rearrange("(dk p) f -> p dk f", p=128),
                  label=f"ld_wv{fc}")
            for hh in range(2):
                h = 2 * fc + hh
                b = 32 * (h % 4)
                pcol = (h % 8) * 64
                out = self.pf[2][b:b + 32, pcol:pcol + 64]
                for dk in range(ND):
                    l = summ[:, dk * 32:(dk + 1) * 32]
                    r = buf[:, dk * 128 + hh * 64: dk * 128 + hh * 64 + 64]
                    A.op("tensor", lambda e, o=out, l=l, r=r, s=(dk == 0), p=(dk == ND - 1),
                         tp=(0, b): e.matmul(o, l, r, start=s, stop=p, tile_position=tp),
                         r=[l, r], w=[self.pf[2][:, pcol:pcol + 64]], label=f"vmm{h}|{dk}")
                dst = vpad[:, h * 64:(h + 1) * 64][b:b + 32, :]
                bvs = bv_row[b:b + 32, h * 64:(h + 1) * 64]
                A.op("vector", lambda e, i=out, o=dst, b2=bvs: e.tensor_tensor(
                    out=o, in0=i, in1=b2, op=ALU.add),
                    r=[self.pf[2][:, pcol:pcol + 64], bv_row], w=[dst], label=f"vev{h}")

        q = self.big0
        for fc in range(ND):
            self.drip(2)
            buf = wslot()
            A.dma("gq", buf.rearrange("p (dk f) -> p dk f", f=128),
                  self.Wq[:, fc * 128:(fc + 1) * 128].rearrange("(dk p) f -> p dk f", p=128),
                  label=f"ld_wq{fc}")
            for tt in range(4):
                self.drip(1)
                ps_t = self.pf[3 + (tt % 2)]
                for dk in range(ND):
                    l = buf[:, dk * 128:(dk + 1) * 128]
                    r = self.xfmS(dk, tt * 512, (tt + 1) * 512)
                    A.op("tensor", lambda e, o=ps_t[:, :], l=l, r=r, s=(dk == 0), p=(dk == ND - 1):
                         e.matmul(o, l, r, start=s, stop=p),
                         r=[l, r], w=[ps_t[:, :]], label=f"qmm{fc}_{tt}|{dk}")
                dst = q[:, fc * T + tt * 512: fc * T + (tt + 1) * 512]
                A.op("scalar", lambda e, i=ps_t, o=dst, b=bq_s[:, fc:fc + 1]: e.activation(
                    out=o, in_=i[:, :], func=AF.Identity, bias=b),
                    r=[ps_t[:, :], bq_s], w=[dst], label=f"qev{fc}_{tt}")

        wo_sb = self.big2[:, 0:8192]
        A.dma("gq", wo_sb.rearrange("p (dk f) -> p dk f", f=D),
              self.Wo.rearrange("(dk p) f -> p dk f", p=128), label="ld_wo")

        at_b = [self.tile[:, 1536:2048], self.tile[:, 3584:4096]]
        atT = self.tile[:, 2048:2560]
        ytile = self.tile[:, 2560:3584]

        def emit_s1(tc):
            ps_sc = self.pf[5] if tc % 2 == 0 else self.pf[3]
            for h in range(H0):
                b = 64 * (h % 2)
                l = q[:, (h // 2) * T + tc * 128:(h // 2) * T + tc * 128 + 128][b:b + 64, :]
                r = kfm[:, (h // 2) * 32:(h // 2) * 32 + 32][b:b + 64, :]
                out = ps_sc[:, h * 32:(h + 1) * 32]
                A.op("tensor", lambda e, o=out, l=l, r=r: e.matmul(o, l, r, start=True, stop=True),
                     r=[l, r], w=[out], label=f"sc{tc}_{h}")
            scf = self.scr32[:, 1024:1536]
            mk = self.mask_sb[:, tc * NC0:(tc + 1) * NC0]
            A.op("vector", lambda e, i=ps_sc, m=mk, o=scf: e.tensor_tensor(
                out=o.rearrange("p (h c) -> p h c", c=NC0),
                in0=i[:, :].rearrange("p (h c) -> p h c", c=NC0),
                in1=m.rearrange("p (one c) -> p one c", one=1).broadcast_to([128, H0, NC0]), op=ALU.add),
                r=[ps_sc[:, :], self.mask_sb], w=[scf], label=f"msk{tc}")
            mx = self.stat[:, 160:176]
            A.op("vector", lambda e, i=scf, o=mx: e.tensor_reduce(
                out=o, in_=i.rearrange("p (h c) -> p h c", c=NC0), axis=AX.X, op=ALU.max),
                r=[scf], w=[mx], label=f"mx{tc}")
            A.op("vector", lambda e, i=scf, m=mx, o=scf: e.tensor_tensor(
                out=o.rearrange("p (h c) -> p h c", c=NC0),
                in0=i.rearrange("p (h c) -> p h c", c=NC0),
                in1=m.rearrange("p (h one) -> p h one", one=1).broadcast_to([128, H0, NC0]), op=ALU.subtract),
                r=[scf, mx], w=[scf], label=f"sub{tc}")
            A.op("scalar", lambda e, i=scf, o=scf: e.activation(out=o, in_=i, func=AF.Exp),
                 r=[scf], w=[scf], label=f"exp{tc}")
            sm = self.stat[:, 176:192]
            A.op("vector", lambda e, i=scf, o=sm: e.tensor_reduce(
                out=o, in_=i.rearrange("p (h c) -> p h c", c=NC0), axis=AX.X, op=ALU.add),
                r=[scf], w=[sm], label=f"sm{tc}")
            rs = self.stat[:, 192:208]
            A.op("vector", lambda e, i=sm, o=rs: e.reciprocal(out=o, in_=i),
                 r=[sm], w=[rs], label=f"rs{tc}")
            at = at_b[tc % 2]
            A.op("vector", lambda e, i=scf, r2=rs, o=at: e.tensor_tensor(
                out=o.rearrange("p (h c) -> p h c", c=NC0),
                in0=i.rearrange("p (h c) -> p h c", c=NC0),
                in1=r2.rearrange("p (h one) -> p h one", one=1).broadcast_to([128, H0, NC0]), op=ALU.mult),
                r=[scf, rs], w=[at], label=f"at{tc}")

        def emit_s2(tc):
            at = at_b[tc % 2]
            for qd in range(4):
                src = at[:, qd * 128:(qd + 1) * 128]
                out = self.pb[:, qd * 128:(qd + 1) * 128]
                A.op("tensor", lambda e, o=out, s=src: e.transpose(o, s, self.ident),
                     r=[src, self.ident], w=[out], label=f"atT{tc}_{qd}", tr=True)
            A.op("scalar", lambda e, i=self.pb[:, 0:512], o=atT: e.copy(out=o, in_=i),
                 r=[self.pb[:, 0:512]], w=[atT], label=f"atTev{tc}")
            ps_y = [self.pf[6], self.pf[0]]
            for h in range(H0):
                b = 64 * (h % 2)
                l = vpad[:, h * 64:(h + 1) * 64]
                r = atT[:, (h // 4) * 128:(h // 4) * 128 + 128]
                bank = ps_y[(h // 2) % 2]
                col = ((h // 2) // 2) * 128
                out = bank[b:b + 64, col:col + 128]
                A.op("tensor", lambda e, o=out, l=l, r=r, tp=(0, b): e.matmul(
                    o, l, r, start=True, stop=True, tile_position=tp),
                    r=[l, r], w=[bank[:, col:col + 128]], label=f"av{tc}_{h}")
            for i2 in range(2):
                A.op("scalar", lambda e, i=ps_y[i2], o=ytile[:, i2 * 512:(i2 + 1) * 512]:
                     e.copy(out=o, in_=i[:, :]),
                     r=[ps_y[i2][:, :]], w=[ytile[:, i2 * 512:(i2 + 1) * 512]], label=f"yev{tc}_{i2}")
            ps_o = [self.pf[1], self.pf[2]]
            for k in range(ND):
                l = ytile[:, (k % 2) * 512 + (k // 2) * 128: (k % 2) * 512 + (k // 2) * 128 + 128]
                for half in range(2):
                    r = wo_sb[:, k * D + half * 512: k * D + (half + 1) * 512]
                    out = ps_o[half]
                    A.op("tensor", lambda e, o=out, l=l, r=r, s=(k == 0), p=(k == ND - 1):
                         e.matmul(o[:, :], l, r, start=s, stop=p),
                         r=[l, r], w=[out[:, :]], label=f"op{tc}_{half}|{k}")
            self.postln_residual(ps_o, tc, bo_row, f"pl1_{tc}",
                                 then=lambda tc2: self.ln_tc(
                                     tc2, self.xfm[:, tc2 * D:(tc2 + 1) * D], "ln2",
                                     colmax=self.stat[:, 208:224]))

        for tc in range(NT):
            emit_s1(tc)
            self.drip(2)
            if tc > 0:
                emit_s2(tc - 1)
                self.drip(2)
        emit_s2(NT - 1)

    def global_max_collective(self, m128, idx, out_col, label):
        A = self.A
        cca, ccb = self.cc_a[idx], self.cc_b[idx]
        A.dma("gq", cca.rearrange("one p -> p one"), m128, label=f"{label}_st")
        A.cc(lambda e, a=cca, b=ccb: e.collective_compute(
            "AllReduce", ALU.max, replica_groups=[list(range(8))],
            ins=[a], outs=[b]), r=[cca], w=[ccb], label=f"{label}_cc")
        gall = self.scr32[:, 0:128]
        A.dma("gq", gall, ccb[0:1, :].broadcast_to([128, 128]), label=f"{label}_ld")
        A.op("vector", lambda e, i=gall, o=out_col: e.reduce_max(out=o, in_=i, axis=AX.X),
             r=[gall], w=[out_col], label=f"{label}_mx")

    def magic_round(self, out_bf, in_ap, inv_ap, relu, label, tmp=None):
        A = self.A
        if tmp is None:
            tmp = self.scr32[:, 0:in_ap.shape[-1]]
        A.op("vector", lambda e, i=in_ap, o=tmp, s=inv_ap: e.tensor_scalar(
            out=o, in0=i, scalar1=s, scalar2=MAGIC, op0=ALU.mult, op1=ALU.add),
            r=[in_ap, inv_ap], w=[tmp], label=f"{label}_a")
        fn = AF.Relu if relu else AF.Identity
        A.op("scalar", lambda e, i=tmp, o=out_bf, fn=fn, b=self.nmag: e.activation(
            out=o, in_=i, func=fn, bias=b),
            r=[tmp, self.nmag], w=[out_bf], label=f"{label}_b")

    def ln_tc(self, tc, dst, label, colmax=None):
        A, st = self.A, self.stat
        src = self.resT(tc)
        bn6 = st[:, 64 + (tc % 2) * 16:64 + (tc % 2) * 16 + 12]
        for g in range(2):
            A.op("vector", lambda e, s=src, o=bn6, g=g: e.bn_stats(
                out=o[:, 6 * g:6 * g + 6], in_=s[:, 512 * g:512 * g + 512]),
                r=[src], w=[bn6[:, 6 * g:6 * g + 6]], label=f"{label}_bnst{tc}_{g}")
        mv = st[:, 96 + (tc % 2) * 2:96 + (tc % 2) * 2 + 2]
        A.op("vector", lambda e, i=bn6, o=mv: e.bn_aggr(out=o, in_=i),
             r=[bn6], w=[mv], label=f"{label}_bnag{tc}")
        ve = st[:, 100 + (tc % 2):101 + (tc % 2)]
        A.op("scalar", lambda e, i=mv, o=ve, b=self.epsc: e.activation(
            out=o, in_=i[:, 1:2], func=AF.Ln, bias=b),
            r=[mv, self.epsc], w=[ve], label=f"{label}_ln{tc}")
        rstd = st[:, 102 + (tc % 2):103 + (tc % 2)]
        A.op("scalar", lambda e, i=ve, o=rstd: e.activation(out=o, in_=i, func=AF.Exp, scale=-0.5),
             r=[ve], w=[rstd], label=f"{label}_rexp{tc}")
        A.op("vector", lambda e, s=src, o=dst, m=mv[:, 0:1], r2=rstd: e.tensor_scalar(
            out=o, in0=s, scalar1=m, scalar2=r2, op0=ALU.subtract, op1=ALU.mult),
            r=[src, mv, rstd], w=[dst], label=f"{label}_nrm{tc}")
        A.dma("aq", self.xlnT_hbm[tc * 128:(tc + 1) * 128, :], dst, label=f"{label}_st{tc}")
        if colmax is not None:
            A.op("vector", lambda e, i=dst, o=colmax[:, tc:tc + 1]:
                 e.reduce_max(out=o, in_=i, axis=AX.X),
                 r=[dst], w=[colmax[:, tc:tc + 1]], label=f"{label}_cm{tc}")

    def p2_l0_ffn(self):
        A, st = self.A, self.stat
        xln = self.big0

        def xlnT(tc):
            return xln[:, tc * D:(tc + 1) * D]

        for dk in range(ND):
            A.dmat("aq", self.xfmS(dk, 0, T), self.xlnT_hbm[:, dk * 128:(dk + 1) * 128],
                   label=f"tr2_tr{dk}")
        colmax = st[:, 208:224]
        m128 = st[:, 232:233]
        A.op("vector", lambda e, i=colmax, o=m128: e.reduce_max(out=o, in_=i, axis=AX.X),
             r=[colmax], w=[m128], label="m128")
        A.op("vector", lambda e, o=m128: e.tensor_scalar(
            out=o, in0=o, scalar1=EPS_Q, scalar2=None, op0=ALU.max),
            r=[m128], w=[m128], label="m128c")
        gmax = st[:, 233:234]
        self.global_max_collective(m128, 0, gmax, "gmx0")
        inv_s = st[:, 234:235]
        s_q = st[:, 235:236]
        A.op("vector", lambda e, i=gmax, o=inv_s: e.reciprocal(out=o, in_=i),
             r=[gmax], w=[inv_s], label="invs0")
        A.op("vector", lambda e, o=inv_s: e.tensor_scalar(
            out=o, in0=o, scalar1=255.0, scalar2=None, op0=ALU.mult),
            r=[inv_s], w=[inv_s], label="invs1")
        A.op("vector", lambda e, i=gmax, o=s_q: e.tensor_scalar(
            out=o, in0=i, scalar1=1.0 / 255.0, scalar2=None, op0=ALU.mult),
            r=[gmax], w=[s_q], label="sq0")

        xq = self.big0
        for dk in range(ND):
            self.magic_round(xq[:, dk * T:(dk + 1) * T], self.xfmS(dk, 0, T),
                             inv_s[:, 0:1], True, f"xq{dk}",
                             tmp=self.scr32[:, 0:T])

        sw1 = self.stat[:, 0:32]
        NF1 = F1 // 128
        A.op("vector", lambda e, o=sw1, s=s_q[:, 0:1]: e.tensor_scalar(
            out=o, in0=o, scalar1=s, scalar2=None, op0=ALU.mult),
            r=[sw1, s_q], w=[sw1], label="ssw1")
        b1s = st[:, 128:160]
        A.dma("gq", b1s, self.b1.rearrange("(c p) -> p c", p=128), label="ld_b1",
              allow_slow_non_contiguous=True)
        rcp = st[:, 192:224]
        A.op("vector", lambda e, i=sw1, o=rcp: e.reciprocal(out=o, in_=i),
             r=[sw1], w=[rcp], label="b1rcp")
        b1q = st[:, 160:192]
        A.op("vector", lambda e, b=b1s, r=rcp, o=b1q: e.tensor_tensor(
            out=o, in0=b, in1=r, op=ALU.mult), r=[b1s, rcp], w=[b1q], label="b1t")
        A.op("vector", lambda e, o=b1q: e.tensor_scalar(
            out=o, in0=o, scalar1=MAGIC, scalar2=MAGIC, op0=ALU.add, op1=ALU.subtract),
            r=[b1q], w=[b1q], label="b1r")
        A.op("vector", lambda e, o=b1q, s=sw1: e.tensor_tensor(
            out=o, in0=o, in1=s, op=ALU.mult), r=[b1q, sw1], w=[b1q], label="b1m")

        hm = st[:, 192:224]
        for fc in range(NF1):
            wsl = self.big1[:, 2048 + (fc % 2) * 1024: 2048 + (fc % 2) * 1024 + 1024]
            A.dma("sq", wsl.rearrange("p (dk f) -> p dk f", f=128),
                  self.w1qT_sl[fc], label=f"m1w{fc}")
            if 8 <= fc < 16:
                w2c = 16 + (fc - 8)
                A.dma("sq", self.big2[:, (fc - 8) * 1024:(fc - 7) * 1024],
                      self.w2qT_hbm[w2c * 128:(w2c + 1) * 128, :], label=f"w2r{w2c}")
            hfc = self.tile[:, (fc % 2) * 2048:(fc % 2) * 2048 + 2048]
            for tt in range(4):
                ps = self.pf[2 * (fc % 2) + (tt % 2)]
                for dk in range(ND):
                    l = wsl[:, dk * 128:(dk + 1) * 128]
                    r = xq[:, dk * T + tt * 512: dk * T + (tt + 1) * 512]
                    A.op("tensor", lambda e, o=ps, l=l, r=r, s=(dk == 0), p=(dk == ND - 1):
                         e.matmul(o[:, :], l, r, start=s, stop=p),
                         r=[l, r], w=[ps[:, :]], label=f"m1_{fc}_{tt}|{dk}")
                dst = hfc[:, tt * 512:(tt + 1) * 512]
                A.op("scalar", lambda e, i=ps, o=dst, s=sw1[:, fc:fc + 1], b=b1q[:, fc:fc + 1]:
                     e.activation(out=o, in_=i[:, :], func=AF.Relu, scale=s, bias=b),
                     r=[ps[:, :], sw1, b1q], w=[dst], label=f"m1e_{fc}_{tt}")
            A.op("vector", lambda e, i=hfc, o=hm[:, fc:fc + 1]: e.reduce_max(
                out=o, in_=i, axis=AX.X), r=[hfc], w=[hm[:, fc:fc + 1]], label=f"hm{fc}")
            A.dma("aq", self.h_hbm[fc * 128:(fc + 1) * 128, :], hfc, label=f"hs{fc}")

        hmax = st[:, 236:237]
        A.op("vector", lambda e, i=hm, o=hmax: e.reduce_max(out=o, in_=i, axis=AX.X),
             r=[hm], w=[hmax], label="hmax")
        A.op("vector", lambda e, o=hmax: e.tensor_scalar(
            out=o, in0=o, scalar1=EPS_Q, scalar2=None, op0=ALU.max),
            r=[hmax], w=[hmax], label="hmaxc")
        gh = st[:, 237:238]
        self.global_max_collective(hmax, 1, gh, "gmx1")
        inv_sh = st[:, 238:239]
        s_h = st[:, 239:240]
        A.op("vector", lambda e, i=gh, o=inv_sh: e.reciprocal(out=o, in_=i),
             r=[gh], w=[inv_sh], label="invh0")
        A.op("vector", lambda e, o=inv_sh: e.tensor_scalar(
            out=o, in0=o, scalar1=255.0, scalar2=None, op0=ALU.mult),
            r=[inv_sh], w=[inv_sh], label="invh1")
        A.op("vector", lambda e, i=gh, o=s_h: e.tensor_scalar(
            out=o, in0=i, scalar1=1.0 / 255.0, scalar2=None, op0=ALU.mult),
            r=[gh], w=[s_h], label="sh0")

        sw2 = self.stat[:, 224:232]
        ssw2 = st[:, 8:16]
        A.op("vector", lambda e, i=sw2, o=ssw2, s=s_h[:, 0:1]: e.tensor_scalar(
            out=o, in0=i, scalar1=s, scalar2=None, op0=ALU.mult),
            r=[sw2, s_h], w=[ssw2], label="ssw2")
        b2s = st[:, 16:24]
        A.dma("gq", b2s, self.b2.rearrange("(c p) -> p c", p=128), label="ld_b2",
              allow_slow_non_contiguous=True)
        rcp2 = st[:, 24:32]
        A.op("vector", lambda e, i=ssw2, o=rcp2: e.reciprocal(out=o, in_=i),
             r=[ssw2], w=[rcp2], label="b2rcp")
        b2q = st[:, 128:136]
        A.op("vector", lambda e, b=b2s, r=rcp2, o=b2q: e.tensor_tensor(
            out=o, in0=b, in1=r, op=ALU.mult), r=[b2s, rcp2], w=[b2q], label="b2t")
        A.op("vector", lambda e, o=b2q: e.tensor_scalar(
            out=o, in0=o, scalar1=MAGIC, scalar2=MAGIC, op0=ALU.add, op1=ALU.subtract),
            r=[b2q], w=[b2q], label="b2r")
        A.op("vector", lambda e, o=b2q, s=ssw2: e.tensor_tensor(
            out=o, in0=o, in1=s, op=ALU.mult), r=[b2q, ssw2], w=[b2q], label="b2m")
        A.dma("gq", self.row_hbm[0:1, :].rearrange("one (c p) -> p one c", p=128),
              ssw2.rearrange("p (one c) -> p one c", one=1), label="r0st",
              allow_slow_non_contiguous=True)
        A.dma("gq", self.row_hbm[1:2, :].rearrange("one (c p) -> p one c", p=128),
              b2q.rearrange("p (one c) -> p one c", one=1), label="r1st",
              allow_slow_non_contiguous=True)
        ssw2_row = self.rowb[:, 0:1024]
        b2q_row = self.rowb[:, 1024:2048]
        A.dma("gq", ssw2_row, self.row_hbm[0:1, :].broadcast_to([128, D]), label="r0ld")
        A.dma("gq", b2q_row, self.row_hbm[1:2, :].broadcast_to([128, D]), label="r1ld")

        groups = [list(range(g, min(g + 3, NT))) for g in range(0, NT, 3)]

        def m2h_issue(gi2, fc2, g02, gw2):
            hld2 = self.big1[:, (fc2 % 8) * 384:(fc2 % 8) * 384 + gw2]
            A.dma(("sq", "gq", "aq")[fc2 % 3], hld2,
                  self.h_hbm[fc2 * 128:(fc2 + 1) * 128, g02 * 128:g02 * 128 + gw2],
                  label=f"m2h{gi2}_{fc2}")
            return hld2

        prev_post = None
        for gi, tcs in enumerate(groups):
            g0 = tcs[0]
            gw = len(tcs) * 128
            for fc in range(8):
                m2h_issue(gi, fc, g0, gw)
            for fc in range(24, 28):
                wsl = self.tile[:, (fc % 4) * 1024:(fc % 4) * 1024 + 1024]
                A.dma(("gq", "aq")[fc % 2], wsl, self.w2qT_hbm[fc * 128:(fc + 1) * 128, :],
                      label=f"m2w{gi}_{fc}")
            if prev_post is not None:
                prev_post()
            for fc in range(NF1):
                hld = self.big1[:, (fc % 8) * 384:(fc % 8) * 384 + gw]
                tmp = self.scr32[:, 1024 + (fc % 2) * 384:1024 + (fc % 2) * 384 + gw]
                hq = self.big1[:, 3072 + (fc % 8) * 384: 3072 + (fc % 8) * 384 + gw]
                A.op("vector", lambda e, i=hld, o=tmp, s=inv_sh[:, 0:1]: e.tensor_scalar(
                    out=o, in0=i, scalar1=s, scalar2=MAGIC, op0=ALU.mult, op1=ALU.add),
                    r=[hld, inv_sh], w=[tmp], label=f"m2qa{gi}_{fc}")
                if fc + 8 < NF1:
                    m2h_issue(gi, fc + 8, g0, gw)
                A.op("scalar", lambda e, i=tmp, o=hq, b=self.nmag: e.activation(
                    out=o, in_=i, func=AF.Identity, bias=b),
                    r=[tmp, self.nmag], w=[hq], label=f"m2qb{gi}_{fc}")
                if fc < 16:
                    wsl = self.big0[:, fc * 1024:(fc + 1) * 1024]
                    if gi == 0:
                        A.dma("sq", wsl, self.w2qT_hbm[fc * 128:(fc + 1) * 128, :],
                              label=f"m2w{gi}_{fc}")
                elif fc < 24:
                    wsl = self.big2[:, (fc - 16) * 1024:(fc - 15) * 1024]
                else:
                    wsl = self.tile[:, (fc % 4) * 1024:(fc % 4) * 1024 + 1024]
                for ti, tc in enumerate(tcs):
                    l = hq[:, ti * 128:(ti + 1) * 128]
                    for half in range(2):
                        ps = self.pf[2 * ti + half]
                        r = wsl[:, half * 512:(half + 1) * 512]
                        A.op("tensor", lambda e, o=ps, l=l, r=r, s=(fc == 0), p=(fc == NF1 - 1):
                             e.matmul(o[:, :], l, r, start=s, stop=p),
                             r=[l, r], w=[ps[:, :]], label=f"m2_{gi}_{ti}_{half}|{fc}")
                if 24 <= fc < 28:
                    fcn = fc + 4
                    wnx = self.tile[:, (fcn % 4) * 1024:(fcn % 4) * 1024 + 1024]
                    A.dma(("gq", "aq")[fcn % 2], wnx, self.w2qT_hbm[fcn * 128:(fcn + 1) * 128, :],
                          label=f"m2w{gi}_{fcn}")

            def mkpost(tcs2):
                def post():
                    for ti, tc in enumerate(tcs2):
                        self.postln_residual([self.pf[2 * ti], self.pf[2 * ti + 1]], tc,
                                             bias_row=b2q_row, scale_row=ssw2_row,
                                             label=f"pl2_{tc}",
                                             then=lambda tc2: self.ln_tc(
                                                 tc2, self.xfm[:, tc2 * D:(tc2 + 1) * D], "ln3"))
                return post
            prev_post = mkpost(tcs)
        prev_post()

    def p3_l1_attn(self):
        A, st = self.A, self.stat
        nc = self.nc

        outwT = self.big2[:, 0:8192]
        for dk in range(ND):
            A.dmat("aq", outwT[:, dk * D:(dk + 1) * D],
                   self.outw_bf[:, dk * 128:(dk + 1) * 128], label=f"owt{dk}")

        xln = self.big0

        def xlnT(tc):
            return xln[:, tc * D:(tc + 1) * D]

        for dk in range(ND):
            A.dmat("aq", self.xfmS(dk, 0, T), self.xlnT_hbm[:, dk * 128:(dk + 1) * 128],
                   label=f"tr3_tr{dk}")

        grow = st[0:1, 240:256]
        A.dma("gq", grow, self.gamma.rearrange("(one g) -> one g", one=1), label="ld_g")
        A.op("scalar", lambda e, o=grow: e.activation(out=o, in_=o, func=AF.Sigmoid),
             r=[grow], w=[grow], label="sig_g")
        A.op("scalar", lambda e, o=grow: e.activation(out=o, in_=o, func=AF.Ln),
             r=[grow], w=[grow], label="ln_g")
        A.dma("gq", self.row_hbm[2:3, 0:16], grow, label="g_st")
        lngb = self.scr32[:, 0:16]
        A.dma("gq", lngb, self.row_hbm[2:3, 0:16].broadcast_to([128, 16]), label="g_ld")
        D_all = self.cons[:, 768:768 + 2048]
        for h in range(16):
            A.op("scalar", lambda e, o=D_all[:, h * 128:(h + 1) * 128], s=lngb[:, h:h + 1]:
                 e.activation(out=o, in_=self.mdec, func=AF.Exp, scale=s),
                 r=[self.mdec, lngb], w=[D_all[:, h * 128:(h + 1) * 128]], label=f"dall{h}")
        lngband = self.scr32[:, 32:40]
        for dc in range(ND):
            A.op("vector", lambda e, i=lngb[0:64, 2 * dc:2 * dc + 1], o=lngband[0:64, dc:dc + 1]:
                 e.tensor_copy(out=o, in_=i), r=[lngb], w=[lngband[0:64, dc:dc + 1]],
                 label=f"lb0_{dc}")
            A.op("vector", lambda e, i=lngb[64:128, 2 * dc + 1:2 * dc + 2], o=lngband[64:128, dc:dc + 1]:
                 e.tensor_copy(out=o, in_=i), r=[lngb], w=[lngband[64:128, dc:dc + 1]],
                 label=f"lb1_{dc}")
        Gq_all = self.cons[:, 2816:2816 + 1024]
        for dc in range(ND):
            A.op("scalar", lambda e, o=Gq_all[:, dc * 128:(dc + 1) * 128], s=lngband[:, dc:dc + 1]:
                 e.activation(out=o, in_=self.gq_in, func=AF.Exp, scale=s),
                 r=[self.gq_in, lngband], w=[Gq_all[:, dc * 128:(dc + 1) * 128]], label=f"gq{dc}")
        GkS = st[:, 224:240]
        for h in range(16):
            A.op("scalar", lambda e, o=GkS[:, h:h + 1], s=lngb[:, h:h + 1]:
                 e.activation(out=o, in_=self.gk_in, func=AF.Exp, scale=s),
                 r=[self.gk_in, lngb], w=[GkS[:, h:h + 1]], label=f"gk{h}")
        gCb = self.scr32[:, 16:32]
        A.op("scalar", lambda e, i=lngb, o=gCb: e.activation(
            out=o, in_=i, func=AF.Exp, scale=128.0),
            r=[lngb], w=[gCb], label="gC")
        gI = self.cons[:, 4864:4864 + 1024]
        for h in range(16):
            idx = self.idA if h % 2 == 0 else self.idB
            A.op("vector", lambda e, i=idx, o=gI[:, h * 64:(h + 1) * 64], s=gCb[:, h:h + 1]:
                 e.tensor_scalar(out=o, in0=i, scalar1=s, scalar2=None, op0=ALU.mult),
                 r=[idx, gCb], w=[gI[:, h * 64:(h + 1) * 64]], label=f"gI{h}")
        SS = self.cons[:, 3840:3840 + 1024]
        A.op("vector", lambda e, o=SS: e.memset(o, 0.0), w=[SS], label="ss0")

        bq1 = st[:, 128:136]
        A.dma("gq", bq1, self.bqkv[0:1024].rearrange("(c p) -> p c", p=128),
              label="ld_bq1", allow_slow_non_contiguous=True)
        bk1 = st[:, 136:144]
        A.dma("gq", bk1, self.bqkv[1024:2048].rearrange("(c p) -> p c", p=128),
              label="ld_bk1", allow_slow_non_contiguous=True)
        bk18 = st[:, 144:152]
        A.op("vector", lambda e, i=bk1, o=bk18: e.tensor_scalar(
            out=o, in0=i, scalar1=0.125, scalar2=None, op0=ALU.mult),
            r=[bk1], w=[bk18], label="bk18")
        bkv_row = self.rowb[:, 0:2048]
        A.dma("gq", bkv_row, self.bqkv[1024:3072].rearrange("(one d) -> one d", one=1)
              .broadcast_to([128, 2048]), label="ld_bkv")
        outb_row = self.rowb[:, 2048:3072]
        A.dma("gq", outb_row, self.outb.rearrange("(one d) -> one d", one=1)
              .broadcast_to([128, D]), label="ld_outb")

        wv_blk = self.big1[:, 0:8192]
        A.dma("sq", wv_blk.rearrange("p (dk f) -> p dk f", f=1024),
              self.wqkvT_hbm[:, 2048:3072].rearrange("(dk p) f -> p dk f", p=128),
              label="ld_wvblk")

        vt_b = [self.tile[:, 0:1024], self.tile[:, 2048:3072]]
        kt_b = [self.tile[:, 1024:2048], self.tile[:, 3072:4096]]
        yt = self.scr32[:, 1024:1536].bitcast(BF16)
        pT = self.scr32[:, 1536:1792].bitcast(BF16)
        qt = self.cons[:, 5888:6144]
        for half in range(2):
            if half == 1:
                self.p_prep_late()
            q_h = self.big0[:, 0:8192]
            k_h = self.big0[:, 8192:16384]
            t0 = half * 1024
            for part, dst, bias, scale in (("q", q_h, bq1, None), ("k", k_h, bk18, 0.125)):
                off = 0 if part == "q" else 1024
                for fc in range(ND):
                    wsl = self.big1[:, 8192 + (fc % 2) * 1024: 8192 + (fc % 2) * 1024 + 1024]
                    A.dma("sq", wsl.rearrange("p (dk f) -> p dk f", f=128),
                          self.wqkv_sl[(off // 128) + fc], label=f"qk{half}{part}{fc}")
                    for tt in range(2):
                        ps = self.pf[tt % 2]
                        for dk in range(ND):
                            l = wsl[:, dk * 128:(dk + 1) * 128]
                            r = self.xfmS(dk, t0 + tt * 512, t0 + (tt + 1) * 512)
                            A.op("tensor", lambda e, o=ps, l=l, r=r, s=(dk == 0), p=(dk == ND - 1):
                                 e.matmul(o[:, :], l, r, start=s, stop=p),
                                 r=[l, r], w=[ps[:, :]], label=f"p3m{half}{part}{fc}_{tt}|{dk}")
                        dd = dst[:, fc * 1024 + tt * 512: fc * 1024 + (tt + 1) * 512]
                        if scale is None:
                            A.op("scalar", lambda e, i=ps, o=dd, b=bias[:, fc:fc + 1]:
                                 e.activation(out=o, in_=i[:, :], func=AF.Identity, bias=b),
                                 r=[ps[:, :], bias], w=[dd], label=f"p3e{half}{part}{fc}_{tt}")
                        else:
                            A.op("scalar", lambda e, i=ps, o=dd, b=bias[:, fc:fc + 1], sc=scale:
                                 e.activation(out=o, in_=i[:, :], func=AF.Identity, bias=b, scale=sc),
                                 r=[ps[:, :], bias], w=[dd], label=f"p3e{half}{part}{fc}_{tt}")
            def g1(tci):
                tc = half * 8 + tci
                vt = vt_b[tci % 2]
                kt = kt_b[tci % 2]
                for dk in range(ND):
                    l = self.xfmS(dk, tc * 128, (tc + 1) * 128)
                    for hf in range(2):
                        ps = self.pf[hf]
                        r = wv_blk[:, dk * 1024 + hf * 512: dk * 1024 + (hf + 1) * 512]
                        A.op("tensor", lambda e, o=ps, l=l, r=r, s=(dk == 0), p=(dk == ND - 1):
                             e.matmul(o[:, :], l, r, start=s, stop=p),
                             r=[l, r], w=[ps[:, :]], label=f"vm{tc}_{hf}|{dk}")
                for hf in range(2):
                    dd = vt[:, hf * 512:(hf + 1) * 512]
                    br = bkv_row[:, 1024 + hf * 512:1024 + (hf + 1) * 512]
                    A.op("vector", lambda e, i=self.pf[hf], o=dd, b=br: e.tensor_tensor(
                        out=o, in0=i[:, :], in1=b, op=ALU.add),
                        r=[self.pf[hf][:, :], bkv_row], w=[dd], label=f"ve{tc}_{hf}")
                for k2 in range(8):
                    src = k_h[:, k2 * 1024 + tci * 128:k2 * 1024 + (tci + 1) * 128]
                    out = self.pb[:, k2 * 128:(k2 + 1) * 128]
                    A.op("tensor", lambda e, o=out, s=src: e.transpose(o, s, self.ident),
                         r=[src, self.ident], w=[out], label=f"ktT{tc}_{k2}", tr=True)
                A.op("vector", lambda e, i=self.pb[:, 0:1024], o=kt, g=GkS: e.tensor_tensor(
                    out=o.rearrange("p (h x) -> p h x", x=64),
                    in0=i.rearrange("p (h x) -> p h x", x=64),
                    in1=g.rearrange("p (h one) -> p h one", one=1).broadcast_to([128, 16, 64]),
                    op=ALU.mult),
                    r=[self.pb[:, 0:1024], GkS], w=[kt], label=f"kte{tc}")

            def g2(tci):
                tc = half * 8 + tci
                vt = vt_b[tci % 2]
                kt = kt_b[tci % 2]
                for qd in range(4):
                    ps_s = self.pf[4 + qd % 2]
                    for hh in range(4):
                        h = qd * 4 + hh
                        b = 64 * (h % 2)
                        l = k_h[:, (h // 2) * 1024 + tci * 128:(h // 2) * 1024 + (tci + 1) * 128][b:b + 64, :]
                        r = q_h[:, (h // 2) * 1024 + tci * 128:(h // 2) * 1024 + (tci + 1) * 128][b:b + 64, :]
                        out = ps_s[:, hh * 128:(hh + 1) * 128]
                        A.op("tensor", lambda e, o=out, l=l, r=r: e.matmul(o, l, r, start=True, stop=True),
                             r=[l, r], w=[out], label=f"sT{tc}_{h}")
                    A.op("vector", lambda e, i=ps_s, o=pT, d=D_all[:, qd * 512:(qd + 1) * 512]:
                         e.tensor_tensor(out=o, in0=i[:, :], in1=d, op=ALU.mult),
                         r=[ps_s[:, :], D_all], w=[pT], label=f"pT{tc}_{qd}")
                    q0 = qd * 2
                    A.op("vector", lambda e, o=qt, i0=q_h, g=Gq_all, q0=q0, tci=tci:
                         e.tensor_tensor(
                             out=o.rearrange("p (f x) -> p f x", x=128),
                             in0=i0.rearrange("p (f t) -> p f t", t=1024)[:, q0:q0 + 2, tci * 128:(tci + 1) * 128],
                             in1=g.rearrange("p (f x) -> p f x", x=128)[:, q0:q0 + 2, :],
                             op=ALU.mult),
                         r=[q_h, Gq_all], w=[qt], label=f"qt{tc}_{qd}")
                    for hh in range(4):
                        h = qd * 4 + hh
                        b = 64 * (h % 2)
                        k2 = h // 2
                        ps_y = self.pf[2 + k2 % 2]
                        ycol = (k2 // 2) * 128
                        out_y = ps_y[b:b + 64, ycol:ycol + 128]
                        l1 = vt[:, h * 64:(h + 1) * 64]
                        r1 = pT[:, hh * 128:(hh + 1) * 128]
                        A.op("tensor", lambda e, o=out_y, l=l1, r=r1, tp=(0, b): e.matmul(
                            o, l, r, start=True, stop=False, tile_position=tp),
                            r=[l1, r1], w=[ps_y[:, ycol:ycol + 128]], label=f"y{tc}_{h}|1")
                        l2 = SS[:, h * 64:(h + 1) * 64]
                        r2 = qt[:, (k2 % 2) * 128:(k2 % 2) * 128 + 128]
                        A.op("tensor", lambda e, o=out_y, l=l2, r=r2, tp=(0, b): e.matmul(
                            o, l, r, start=False, stop=True, tile_position=tp),
                            r=[l2, r2], w=[ps_y[:, ycol:ycol + 128]], label=f"y{tc}_{h}|2")
                        ps_ss = self.pf[6]
                        scol = (h // 2) * 64
                        out_s = ps_ss[b:b + 64, scol:scol + 64]
                        l3 = kt[:, h * 64:(h + 1) * 64]
                        r3 = vt[:, h * 64:(h + 1) * 64]
                        A.op("tensor", lambda e, o=out_s, l=l3, r=r3, tp=(0, b): e.matmul(
                            o, l, r, start=True, stop=False, tile_position=tp),
                            r=[l3, r3], w=[ps_ss[:, scol:scol + 64]], label=f"st{tc}_{h}|1")
                        l4 = gI[:, h * 64:(h + 1) * 64]
                        r4 = SS[:, h * 64:(h + 1) * 64]
                        A.op("tensor", lambda e, o=out_s, l=l4, r=r4, tp=(0, b): e.matmul(
                            o, l, r, start=False, stop=True, tile_position=tp),
                            r=[l4, r4], w=[ps_ss[:, scol:scol + 64]], label=f"st{tc}_{h}|2")
                for i2 in range(2):
                    A.op("scalar", lambda e, i=self.pf[2 + i2], o=yt[:, i2 * 512:(i2 + 1) * 512]:
                         e.copy(out=o, in_=i[:, :]),
                         r=[self.pf[2 + i2][:, :]], w=[yt[:, i2 * 512:(i2 + 1) * 512]],
                         label=f"ye{tc}_{i2}")
                for band in range(2):
                    src = self.pf[6][band * 64:(band + 1) * 64, 0:512]
                    dst = SS[band * 64:(band + 1) * 64, :].rearrange(
                        "p (h two x) -> p h two x", two=2, x=64)[:, :, band, :]
                    A.op("scalar", lambda e, i=src, o=dst: e.copy(
                        out=o, in_=i.rearrange("p (h x) -> p h x", x=64)),
                        r=[self.pf[6][:, :]], w=[dst], label=f"se{tc}_{band}")
                ps_o = [self.pf[2], self.pf[3]]
                for k in range(ND):
                    l = yt[:, (k % 2) * 512 + (k // 2) * 128: (k % 2) * 512 + (k // 2) * 128 + 128]
                    for hf in range(2):
                        r = outwT[:, k * D + hf * 512: k * D + (hf + 1) * 512]
                        out = ps_o[hf]
                        A.op("tensor", lambda e, o=out, l=l, r=r, s=(k == 0), p=(k == ND - 1):
                             e.matmul(o[:, :], l, r, start=s, stop=p),
                             r=[l, r], w=[out[:, :]], label=f"op3{tc}_{hf}|{k}")
                self.postln_residual(ps_o, tc, bias_row=outb_row, label=f"pl3_{tc}")

            g1(0)
            for tci in range(8):
                if tci + 1 < 8:
                    g1(tci + 1)
                g2(tci)

    def p4_l1_ffn(self):
        A, st = self.A, self.stat

        xln = self.big0

        def xlnT(tc):
            return xln[:, tc * D:(tc + 1) * D]

        self.ln_into(self.resT, xlnT, "ln4")
        self.transpose_roundtrip(xlnT, "tr4")

        def wdT(hk):
            if hk < 16:
                return self.big0[:, hk * D:(hk + 1) * D]
            return self.big2[:, (hk - 16) * D:(hk - 15) * D]

        for hk in range(24):
            A.dmat("aq", wdT(hk), self.wd_bf[:, hk * 128:(hk + 1) * 128], label=f"wdt{hk}")

        NH = H1F // 128
        for hk in range(NH):
            gsl = self.big1[:, (hk % 2) * 2048:(hk % 2) * 2048 + 1024]
            usl = self.big1[:, (hk % 2) * 2048 + 1024:(hk % 2) * 2048 + 2048]
            A.dma("sq", gsl.rearrange("p (dk f) -> p dk f", f=128),
                  self.wgT_sl[hk], label=f"ga{hk}")
            A.dma("sq", usl.rearrange("p (dk f) -> p dk f", f=128),
                  self.wuT_sl[hk], label=f"ua{hk}")
            hsb = self.big1[:, 4096 + (hk % 2) * 2048:4096 + (hk % 2) * 2048 + 2048]
            for tt in range(4):
                psg = self.pf[tt % 3]
                psu = self.pf[3 + tt % 3]
                for dk in range(ND):
                    r = self.xfmS(dk, tt * 512, (tt + 1) * 512)
                    lg = gsl[:, dk * 128:(dk + 1) * 128]
                    A.op("tensor", lambda e, o=psg, l=lg, r=r, s=(dk == 0), p=(dk == ND - 1):
                         e.matmul(o[:, :], l, r, start=s, stop=p),
                         r=[lg, r], w=[psg[:, :]], label=f"gm{hk}_{tt}|{dk}")
                for dk in range(ND):
                    r = self.xfmS(dk, tt * 512, (tt + 1) * 512)
                    lu = usl[:, dk * 128:(dk + 1) * 128]
                    A.op("tensor", lambda e, o=psu, l=lu, r=r, s=(dk == 0), p=(dk == ND - 1):
                         e.matmul(o[:, :], l, r, start=s, stop=p),
                         r=[lu, r], w=[psu[:, :]], label=f"um{hk}_{tt}|{dk}")
                sil = self.scr32[:, (tt % 2) * 512:(tt % 2) * 512 + 512]
                A.op("scalar", lambda e, i=psg, o=sil: e.activation(
                    out=o, in_=i[:, :], func=AF.Silu, scale=1.0 / 127.0),
                    r=[psg[:, :]], w=[sil], label=f"sil{hk}_{tt}")
                dst = hsb[:, tt * 512:(tt + 1) * 512]
                A.op("vector", lambda e, u=psu, s=sil, o=dst: e.scalar_tensor_tensor(
                    out=o, in0=u[:, :], scalar=1.0 / 127.0, in1=s, op0=ALU.mult, op1=ALU.mult),
                    r=[psu[:, :], sil], w=[dst], label=f"hmul{hk}_{tt}")
            A.dma("aq", self.h2_hbm[hk * 128:(hk + 1) * 128, :], hsb, label=f"h2s{hk}")

        groups = [list(range(g, min(g + 3, NT))) for g in range(0, NT, 3)]
        for gi, tcs in enumerate(groups):
            g0 = tcs[0]
            gw = len(tcs) * 128
            for hk in range(NH):
                s8 = hk % 8
                if s8 < 4:
                    hq = self.big1[:, s8 * 512: s8 * 512 + gw]
                else:
                    hq = self.big1[:, 8192 + (s8 - 4) * 512: 8192 + (s8 - 4) * 512 + gw]
                A.dma(("sq", "gq", "aq")[hk % 3], hq,
                      self.h2_hbm[hk * 128:(hk + 1) * 128, g0 * 128:g0 * 128 + gw],
                      label=f"hb{gi}_{hk}")
                for ti in range(len(tcs)):
                    l = hq[:, ti * 128:(ti + 1) * 128]
                    for hf in range(2):
                        ps = self.pf[2 * ti + hf]
                        r = wdT(hk)[:, hf * 512:(hf + 1) * 512]
                        A.op("tensor", lambda e, o=ps, l=l, r=r, s=(hk == 0), p=(hk == NH - 1):
                             e.matmul(o[:, :], l, r, start=s, stop=p),
                             r=[l, r], w=[ps[:, :]], label=f"dm{gi}_{ti}_{hf}|{hk}")
            for ti, tc in enumerate(tcs):
                self.postln_residual([self.pf[2 * ti], self.pf[2 * ti + 1]], tc,
                                     scale_const=1.0 / 127.0, label=f"pl4_{tc}")

    def p5_store_out(self):
        A = self.A
        for tc in range(NT):
            A.dma("aq", self.y_out[tc * 128:(tc + 1) * 128, :], self.resT(tc),
                  label=f"st_y{tc}")
        A.op("gpsimd", lambda e: e.engine_nop(),
             r=[self.y_out], w=(), label="final_wait")


def build_nc(upto=4):
    nc = bass.Bass("TRN2", target_bir_lowering=False, debug=False)
    with ExitStack() as ctx:
        mk = MK(nc, ctx, upto)
        mk.build()
        block = ctx.enter_context(nc.Block())
        mk.A.build(block, mk.sems)
    return nc


_IN_ORDER = [
    "L0_attn_x", "L0_attn_Wq", "L0_attn_bq", "L0_attn_Wk", "L0_attn_bk",
    "L0_attn_Wv", "L0_attn_bv", "L0_attn_Wo", "L0_attn_bo",
    "L0_ffn_w1", "L0_ffn_b1", "L0_ffn_w2", "L0_ffn_b2",
    "L1_attn_Wqkv_w", "L1_attn_Wqkv_b", "L1_attn_out_w", "L1_attn_out_b",
    "L1_attn_gamma_raw", "L1_ffn_w_gate", "L1_ffn_w_up", "L1_ffn_w_down",
]
_NAME_MAP = {
    "L0_attn_Wq": "Wq", "L0_attn_bq": "bq", "L0_attn_Wk": "Wk", "L0_attn_bk": "bk",
    "L0_attn_Wv": "Wv", "L0_attn_bv": "bv", "L0_attn_Wo": "Wo", "L0_attn_bo": "bo",
    "L0_ffn_w1": "w1", "L0_ffn_b1": "b1", "L0_ffn_w2": "w2", "L0_ffn_b2": "b2",
    "L1_attn_Wqkv_w": "wqkv", "L1_attn_Wqkv_b": "bqkv", "L1_attn_out_w": "outw",
    "L1_attn_out_b": "outb", "L1_attn_gamma_raw": "gamma",
    "L1_ffn_w_gate": "wg", "L1_ffn_w_up": "wu", "L1_ffn_w_down": "wd",
}


def kernel(**inputs):
    nc = build_nc(upto=4)
    consts = host_constants()
    shared = {}
    for k, v in inputs.items():
        if k == "L0_attn_x":
            continue
        arr = np.ascontiguousarray(np.asarray(v))
        shared[_NAME_MAP[k]] = arr
    for k, v in consts.items():
        shared[k] = np.ascontiguousarray(v)
    x = np.ascontiguousarray(np.asarray(inputs["L0_attn_x"], dtype=np.float32))
    in_maps = []
    for b in range(8):
        m = dict(shared)
        m["x"] = x[b]
        in_maps.append(m)
    res = run_bass_kernel_spmd(nc, in_maps, core_ids=list(range(8)))
    out = np.stack([np.asarray(res.results[b]["y"]) for b in range(8)], axis=0)
    return out.astype(np.float32)


# version 31
# speedup vs baseline: 1.0282x; 1.0046x over previous
import numpy as np
import ml_dtypes
import concourse.bass as bass
import concourse.mybir as mybir
from concourse.bass_utils import run_bass_kernel_spmd
from contextlib import ExitStack

F32 = mybir.dt.float32
BF16 = mybir.dt.bfloat16
I32 = mybir.dt.int32
AF = mybir.ActivationFunctionType
ALU = mybir.AluOpType
AX = mybir.AxisListType

MAGIC = 1.5 * 2.0 ** 23
T = 2048
D = 1024
NT = 16
ND = 8
H0 = 16
HD = 64
NC0 = 32
F1 = 4096
H1F = 3072
EPS_LN = 1e-5
EPS_Q = 1e-8

ENGINES = ("tensor", "vector", "scalar", "gpsimd", "sync")
NQSEM = 8
TT_DROP = True
PREP = True
PREP_W1 = True
PREP_TR = True
PREP_W2 = True
PREP_QK = False
PREP_W2T = True
PSUM_WIDE = True


def _ap_interval(ap):
    tname = type(ap.tensor).__name__
    if PSUM_WIDE and "PSum" in tname:
        return (ap.tensor.name, 0, 1 << 60)
    esz = mybir.dt.size(ap.dtype)
    lo = ap.offset
    hi = ap.offset
    if "DRam" in tname:
        for step, count in ap.ap:
            if count <= 1:
                continue
            ext = step * (count - 1)
            if ext >= 0:
                hi += ext
            else:
                lo += ext
    else:
        tshape = ap.tensor.shape
        pbytes = mybir.dt.size(ap.tensor.dtype)
        for s in tshape[1:]:
            pbytes *= s
        pitch = pbytes // esz
        lo = hi = ap.offset % pitch
        for step, count in ap.ap:
            if count <= 1 or abs(step) >= pitch:
                continue
            ext = step * (count - 1)
            if ext >= 0:
                hi += ext
            else:
                lo += ext
    return (ap.tensor.name, lo * esz, (hi + 1) * esz)


class Op:
    __slots__ = ("engine", "emit", "kind", "queue", "deps", "idx", "inc_id",
                 "flagged", "label", "qidx", "ownsem", "_maxdep", "tr")

    def __init__(self, engine, emit, kind, queue, label):
        self.tr = False
        self.engine = engine
        self.emit = emit
        self.kind = kind
        self.queue = queue
        self.deps = []
        self.idx = -1
        self.qidx = -1
        self.inc_id = None
        self.flagged = False
        self.ownsem = None
        self.label = label


class Asm:

    def __init__(self, nc):
        self.nc = nc
        self.ops = []
        self.eng_ops = {e: [] for e in ENGINES}
        self.queues = {"sq": "sync", "aq": "scalar", "gq": "gpsimd"}
        self.q_ops = {q: [] for q in self.queues}
        self.seg = {}
        self.n_ccsem = 0

    def _touch(self, op, aps, is_write):
        for ap in aps:
            name, lo, hi = _ap_interval(ap)
            segs = self.seg.setdefault(name, [])
            new = []
            deps = op.deps
            if is_write:
                keep = []
                for s in segs:
                    if s[1] <= lo or s[0] >= hi:
                        keep.append(s)
                        continue
                    if s[2] is not None:
                        deps.append(s[2])
                    deps.extend(s[3])
                    if s[0] < lo:
                        keep.append([s[0], lo, s[2], list(s[3])])
                    if s[1] > hi:
                        keep.append([hi, s[1], s[2], list(s[3])])
                keep.append([lo, hi, op, []])
                self.seg[name] = keep
            else:
                for s in segs:
                    if s[1] <= lo or s[0] >= hi:
                        continue
                    if s[2] is not None:
                        deps.append(s[2])
                    s[3].append(op)
                if not segs:
                    segs.append([lo, hi, None, [op]])

    def op(self, engine, emit, r=(), w=(), label="", tr=False):
        o = Op(engine, emit, "c", None, label)
        o.tr = tr
        self._touch(o, r, False)
        self._touch(o, w, True)
        self.ops.append(o)
        self.eng_ops[engine].append(o)
        return o

    def dma(self, queue, out, in_, label="", **kw):
        eng = self.queues[queue]
        o = Op(eng, lambda e, out=out, in_=in_, kw=kw: e.dma_start(out=out, in_=in_, **kw),
               "d", queue, label)
        self._touch(o, [in_], False)
        self._touch(o, [out], True)
        self.ops.append(o)
        self.eng_ops[eng].append(o)
        self.q_ops[queue].append(o)
        return o

    def dmat(self, queue, out, in_, label=""):
        eng = self.queues[queue]
        assert eng in ("sync", "scalar")
        o = Op(eng, lambda e, out=out, in_=in_: e.dma_start_transpose(out, in_),
               "d", queue, label)
        self._touch(o, [in_], False)
        self._touch(o, [out], True)
        last = getattr(self, "_last_dmat", None)
        if last is not None:
            o.deps.append(last)
        self._last_dmat = o
        self.ops.append(o)
        self.eng_ops[eng].append(o)
        self.q_ops[queue].append(o)
        return o

    def cc(self, emit, r=(), w=(), label="cc"):
        o = Op("gpsimd", emit, "cc", None, label)
        self._touch(o, r, False)
        self._touch(o, w, True)
        self.ops.append(o)
        self.eng_ops["gpsimd"].append(o)
        self.n_ccsem += 1
        return o

    def build(self, block, sems):
        for e in ENGINES:
            for i, o in enumerate(self.eng_ops[e]):
                o.idx = i
        for q in self.q_ops:
            for i, o in enumerate(self.q_ops[q]):
                o.qidx = i
        tforce = {}
        prev = None
        for o in self.eng_ops["tensor"]:
            if prev is not None and (prev.tr or o.tr):
                tforce[o.idx] = prev.idx
            prev = o
        ccs = []
        for o in self.ops:
            if o.kind == "cc":
                o.ownsem = sems["cc"][len(ccs)]
                ccs.append(o)
            maxdep = {}
            for d in o.deps:
                if d is o:
                    continue
                if d.kind in ("d", "cc"):
                    continue
                if d.engine == o.engine == "gpsimd":
                    continue
                if (d.engine == o.engine == "tensor" and TT_DROP and
                        "|" in o.label and o.label.split("|")[0] == d.label.split("|")[0]):
                    continue
                if maxdep.get(d.engine, -1) < d.idx:
                    maxdep[d.engine] = d.idx
            if o.engine == "tensor" and o.idx in tforce:
                if maxdep.get("tensor", -1) < tforce[o.idx]:
                    maxdep["tensor"] = tforce[o.idx]
            o._maxdep = maxdep
            for e2, i2 in maxdep.items():
                self.eng_ops[e2][i2].flagged = True
        for e in ENGINES:
            n = 0
            for o in self.eng_ops[e]:
                if o.kind == "c" and o.flagged:
                    n += 1
                    o.inc_id = n
        plans = {e: [] for e in ENGINES}
        waited = {}
        drained = {e: 0 for e in ENGINES}
        for o in self.ops:
            waits = []
            need_drain = False
            for d in o.deps:
                if d is o:
                    continue
                if d.kind == "d":
                    key = ("q", d.queue, d.qidx % NQSEM)
                    v = 16 * (d.qidx // NQSEM + 1)
                    if waited.get((o.engine, key), 0) < v:
                        waits.append((key, v))
                elif d.kind == "cc":
                    key = ("cc", id(d.ownsem))
                    if waited.get((o.engine, key), 0) < 1:
                        waits.append(((key[0], key[1], d.ownsem), 1))
                elif d.engine == o.engine == "gpsimd":
                    if d.idx >= drained[o.engine]:
                        need_drain = True
            for e2, i2 in o._maxdep.items():
                d = self.eng_ops[e2][i2]
                key = ("e", e2)
                v = d.inc_id
                assert v is not None
                if waited.get((o.engine, key), 0) < v:
                    waits.append((key, v))
            if o.kind == "d" and o.qidx >= NQSEM:
                key = ("q", o.queue, o.qidx % NQSEM)
                v = 16 * (o.qidx // NQSEM)
                if waited.get((o.engine, key), 0) < v:
                    waits.append((key, v))
            final_waits = []
            seen = set()
            for key, v in waits:
                k2 = key[:3] if key[0] != "cc" else key[:2]
                if waited.get((o.engine, k2), 0) < v and (k2, v) not in seen:
                    waited[(o.engine, k2)] = v
                    seen.add((k2, v))
                    final_waits.append((key, v))
            if need_drain:
                drained[o.engine] = o.idx
            plans[o.engine].append((final_waits, need_drain, o))
        self._plans = plans
        self._sems = sems

        nc = self.nc
        qsem = sems["q"]
        esem = sems["e"]

        def run_engine(engobj, ename):
            for waits, need_drain, o in plans[ename]:
                for key, v in waits:
                    if key[0] == "q":
                        engobj.wait_ge(qsem[(key[1], key[2])], v)
                    elif key[0] == "e":
                        engobj.wait_ge(esem[key[1]], v)
                    else:
                        engobj.wait_ge(key[2], v)
                if need_drain:
                    engobj.drain()
                inst = o.emit(engobj)
                if o.kind == "d":
                    inst.then_inc(qsem[(o.queue, o.qidx % NQSEM)], 16)
                elif o.kind == "cc":
                    inst.then_inc(o.ownsem, 1)
                elif o.flagged:
                    if ename == "gpsimd":
                        engobj.drain().then_inc(esem[ename], 1)
                    else:
                        inst.then_inc(esem[ename], 1)

        @block.tensor
        def _(e):
            run_engine(e, "tensor")

        @block.vector
        def _(e):
            run_engine(e, "vector")

        @block.scalar
        def _(e):
            run_engine(e, "scalar")

        @block.gpsimd
        def _(e):
            run_engine(e, "gpsimd")

        @block.sync
        def _(e):
            run_engine(e, "sync")



def host_constants():
    c = {}
    c["hc_ident"] = np.eye(128, dtype=ml_dtypes.bfloat16)
    ends = np.arange(0, T, 64)
    starts = np.clip(ends - 63, 0, None)
    lens = (ends - starts + 1).astype(np.float32)
    A = np.zeros((T, NC0), np.float32)
    for ch in range(NC0):
        A[starts[ch]:ends[ch] + 1, ch] = 1.0 / lens[ch]
    c["hc_A"] = A.astype(ml_dtypes.bfloat16)
    msk = np.where(ends[None, :] <= np.arange(T)[:, None], 0.0, -1e30).astype(np.float32)
    c["hc_mask"] = msk
    j = np.arange(128)[:, None]
    i = np.arange(128)[None, :]
    c["hc_mdec"] = np.where(i >= j, (i - j).astype(np.float32), 1.0e4)
    c["hc_gq"] = np.broadcast_to((i + 1).astype(np.float32), (128, 128)).copy()
    c["hc_gk"] = (127 - np.arange(128, dtype=np.float32))[:, None].copy()
    ia = np.zeros((128, 64), ml_dtypes.bfloat16)
    ia[0:64] = np.eye(64)
    ib = np.zeros((128, 64), ml_dtypes.bfloat16)
    ib[64:128] = np.eye(64)
    c["hc_idA"] = ia
    c["hc_idB"] = ib
    return c



class MK:
    def __init__(self, nc, ctx, upto):
        self.nc = nc
        self.ctx = ctx
        self.upto = upto
        e = ctx.enter_context
        self.A = None

        dram = nc.dram_tensor
        self.x_in = dram("x", [T, D], F32, kind="ExternalInput").ap()
        self.y_out = dram("y", [T, D], F32, kind="ExternalOutput").ap()
        names = [("Wq", [D, D]), ("bq", [D]), ("Wk", [D, D]), ("bk", [D]),
                 ("Wv", [D, D]), ("bv", [D]), ("Wo", [D, D]), ("bo", [D]),
                 ("w1", [F1, D]), ("b1", [F1]), ("w2", [D, F1]), ("b2", [D]),
                 ("wqkv", [3 * D, D]), ("bqkv", [3 * D]), ("outw", [D, D]),
                 ("outb", [D]), ("gamma", [16])]
        for n, shp in names:
            setattr(self, n, dram(n, shp, F32, kind="ExternalInput").ap())
        for n, shp in [("wg", [H1F, D]), ("wu", [H1F, D]), ("wd", [D, H1F])]:
            setattr(self, n, dram(n, shp, I32, kind="ExternalInput").ap())
        for n, shp, dt in [("hc_ident", [128, 128], BF16), ("hc_A", [T, NC0], BF16),
                           ("hc_mask", [T, NC0], F32), ("hc_mdec", [128, 128], F32),
                           ("hc_gq", [128, 128], F32), ("hc_gk", [128, 1], F32),
                           ("hc_idA", [128, 64], BF16), ("hc_idB", [128, 64], BF16)]:
            setattr(self, n, dram(n, shp, dt, kind="ExternalInput").ap())

        self.xlnT_hbm = dram("xlnT_hbm", [T, D], BF16).ap()
        self.w1q_hbm = dram("w1q_hbm", [F1, D], BF16).ap()
        self.w1qT_hbm = dram("w1qT_hbm", [D, F1], BF16).ap()
        self.w2q_hbm = dram("w2q_hbm", [D, F1], BF16).ap()
        self.w2qT_hbm = dram("w2qT_hbm", [F1, D], BF16).ap()
        self.h_hbm = dram("h_hbm", [F1, T], BF16).ap()
        self.wqkv_bf = dram("wqkv_bf", [3 * D, D], BF16).ap()
        self.wqkvT_hbm = dram("wqkvT_hbm", [D, 3 * D], BF16).ap()
        self.outw_bf = dram("outw_bf", [D, D], BF16).ap()
        self.wg_bf = dram("wg_bf", [H1F, D], BF16).ap()
        self.wu_bf = dram("wu_bf", [H1F, D], BF16).ap()
        self.wd_bf = dram("wd_bf", [D, H1F], BF16).ap()
        self.wgT_sl = dram("wgT_sl", [24, 128, ND, 128], BF16).ap()
        self.wuT_sl = dram("wuT_sl", [24, 128, ND, 128], BF16).ap()
        self.w1qT_sl = dram("w1qT_sl", [32, 128, ND, 128], BF16).ap()
        self.wqkv_sl = dram("wqkv_sl", [16, 128, ND, 128], BF16).ap()
        self.wq_sl = dram("wq_sl", [8, 128, ND, 128], BF16).ap()
        self.wk_sl = dram("wk_sl", [8, 128, ND, 128], BF16).ap()
        self.wv_sl = dram("wv_sl", [8, 128, ND, 128], BF16).ap()
        self.wdT_hbm = dram("wdT_hbm", [H1F, D], BF16).ap()
        self.h2_hbm = dram("h2_hbm", [H1F, T], BF16).ap()
        self.row_hbm = dram("row_hbm", [8, D], F32).ap()
        self.cc_a = [dram(f"cc_a{i}", [1, 128], F32).ap() for i in range(2)]
        self.cc_b = [dram(f"cc_b{i}", [1, 128], F32).ap() for i in range(2)]

        sb = nc.sbuf_tensor
        self.res = e(sb("res", [128, NT * D], F32))
        self.xfm = e(sb("xfm", [128, ND * T], BF16))
        self.big0 = e(sb("big0", [128, NT * D], BF16))
        self.big1 = e(sb("big1", [128, 10240], BF16))
        self.big2 = e(sb("big2", [128, 8192], BF16))
        self.scr32 = e(sb("scr32", [128, 2048], F32))
        self.tile = e(sb("tile", [128, 4096], BF16))
        self.rowb = e(sb("rowb", [128, 3072], BF16))
        self.cons = e(sb("cons", [128, 6144], BF16))
        self.con32 = e(sb("con32", [128, 800], F32))
        self.stat = e(sb("stat", [128, 256], F32))
        self.pf = [e(nc.psum_tensor(f"pf{i}", [128, 512], F32)) for i in range(7)]
        self.pb = e(nc.psum_tensor("pb", [128, 1024], BF16))

        self.sems = {
            "e": {en: e(nc.semaphore(f"se_{en}")) for en in ENGINES},
            "q": {(q, i): e(nc.semaphore(f"sq_{q}{i}"))
                  for q in ("sq", "aq", "gq") for i in range(NQSEM)},
            "cc": [e(nc.semaphore(f"scc{i}")) for i in range(2)],
        }
        self.A = Asm(nc)

    def resT(self, tc):
        return self.res[:, tc * D:(tc + 1) * D]

    def xfmS(self, dk, t0, t1):
        return self.xfm[:, dk * T + t0: dk * T + t1]

    def ln_into(self, src_of_tc, dst_of_tc, label, nt=NT, track_max=None):
        A, st = self.A, self.stat
        for tc in range(nt):
            src = src_of_tc(tc)
            bn6 = st[:, 32:44]
            for g in range(2):
                A.op("vector", lambda e, s=src, o=bn6, g=g: e.bn_stats(
                    out=o[:, 6 * g:6 * g + 6], in_=s[:, 512 * g:512 * g + 512]),
                    r=[src], w=[bn6[:, 6 * g:6 * g + 6]], label=f"{label}_bnst{tc}_{g}")
            mv = st[:, 44 + 2 * tc: 46 + 2 * tc]
            A.op("vector", lambda e, i=bn6, o=mv: e.bn_aggr(out=o, in_=i),
                 r=[bn6], w=[mv], label=f"{label}_bnag{tc}")
        mvall = st[:, 44:44 + 2 * nt]
        var_v = mvall.rearrange("p (t two) -> p t two", two=2)[:, :, 1:2]
        vareps = st[:, 80:80 + nt]
        A.op("vector", lambda e, i=var_v, o=vareps: e.tensor_scalar(
            out=o, in0=i.rearrange("p t one -> p (t one)"), scalar1=EPS_LN, scalar2=None, op0=ALU.add),
            r=[mvall], w=[vareps], label=f"{label}_veps")
        std = st[:, 96:96 + nt]
        A.op("scalar", lambda e, i=vareps, o=std: e.activation(out=o, in_=i, func=AF.Ln),
             r=[vareps], w=[std], label=f"{label}_ln")
        rstd = st[:, 112:112 + nt]
        A.op("scalar", lambda e, i=std, o=rstd: e.activation(out=o, in_=i, func=AF.Exp, scale=-0.5),
             r=[std], w=[rstd], label=f"{label}_rexp")
        for tc in range(nt):
            src = src_of_tc(tc)
            dst = dst_of_tc(tc)
            m = mvall[:, 2 * tc:2 * tc + 1]
            r = rstd[:, tc:tc + 1]
            A.op("vector", lambda e, s=src, o=dst, m=m, r=r: e.tensor_scalar(
                out=o, in0=s, scalar1=m, scalar2=r, op0=ALU.subtract, op1=ALU.mult),
                r=[src, mvall, rstd], w=[dst], label=f"{label}_nrm{tc}")

    def transpose_roundtrip(self, tm_of_tc, label):
        A = self.A
        hb = self.xlnT_hbm
        for tc in range(NT):
            src = tm_of_tc(tc)
            A.dma("aq", hb[tc * 128:(tc + 1) * 128, :], src, label=f"{label}_st{tc}")
        for dk in range(ND):
            A.dmat("aq", self.xfmS(dk, 0, T), hb[:, dk * 128:(dk + 1) * 128],
                   label=f"{label}_tr{dk}")

    def postln_residual(self, psum_list, tc, bias_row=None, label="",
                        scale_row=None, scale_const=None, then=None):
        A, st = self.A, self.stat
        ao = self.scr32[:, 0:1024]
        for i, ps in enumerate(psum_list):
            dst = ao[:, i * 512:(i + 1) * 512]
            if scale_row is not None:
                sr = scale_row[:, i * 512:(i + 1) * 512]
                A.op("vector", lambda e, p=ps, s=sr, o=dst:
                     e.tensor_tensor(out=o, in0=p[:, :], in1=s, op=ALU.mult),
                     r=[ps[:, :], scale_row], w=[dst], label=f"{label}_s{i}")
                if bias_row is not None:
                    br = bias_row[:, i * 512:(i + 1) * 512]
                    A.op("vector", lambda e, b=br, o=dst:
                         e.tensor_tensor(out=o, in0=o, in1=b, op=ALU.add),
                         r=[dst, bias_row], w=[dst], label=f"{label}_b{i}")
            elif scale_const is not None:
                A.op("vector", lambda e, p=ps, o=dst, c=scale_const:
                     e.tensor_scalar(out=o, in0=p[:, :], scalar1=c, scalar2=None,
                                     op0=ALU.mult),
                     r=[ps[:, :]], w=[dst], label=f"{label}_c{i}")
            elif bias_row is not None:
                br = bias_row[:, i * 512:(i + 1) * 512]
                A.op("vector", lambda e, p=ps, b=br, o=dst:
                     e.tensor_tensor(out=o, in0=p[:, :], in1=b, op=ALU.add),
                     r=[ps[:, :], bias_row], w=[dst], label=f"{label}_b{i}")
            else:
                A.op("vector", lambda e, p=ps, o=dst:
                     e.tensor_copy(out=o, in_=p[:, :]),
                     r=[ps[:, :]], w=[dst], label=f"{label}_cp{i}")
        bn6 = st[:, 40:52]
        for g in range(2):
            A.op("vector", lambda e, s=ao, g=g, o=bn6: e.bn_stats(
                out=o[:, 6 * g:6 * g + 6], in_=s[:, 512 * g:512 * g + 512]),
                r=[ao], w=[bn6[:, 6 * g:6 * g + 6]], label=f"{label}_bn{g}")
        mv = st[:, 52:54]
        A.op("vector", lambda e, i=bn6, o=mv: e.bn_aggr(out=o, in_=i),
             r=[bn6], w=[mv], label=f"{label}_bnag")
        ve = st[:, 54:55]
        A.op("vector", lambda e, m=mv, o=ve: e.tensor_scalar(
            out=o, in0=m[:, 1:2], scalar1=EPS_LN, scalar2=None, op0=ALU.add),
            r=[mv], w=[ve], label=f"{label}_veps")
        sd = st[:, 55:56]
        A.op("scalar", lambda e, i=ve, o=sd: e.activation(out=o, in_=i, func=AF.Ln),
             r=[ve], w=[sd], label=f"{label}_ln")
        rs = st[:, 56:57]
        A.op("scalar", lambda e, i=sd, o=rs: e.activation(out=o, in_=i, func=AF.Exp, scale=-0.5),
             r=[sd], w=[rs], label=f"{label}_rexp")
        dst = self.resT(tc)
        A.op("vector", lambda e, a=ao, m=mv, r=rs: e.tensor_scalar(
            out=a, in0=a, scalar1=m[:, 0:1], scalar2=r, op0=ALU.subtract, op1=ALU.mult),
            r=[ao, mv, rs], w=[ao], label=f"{label}_nrm")
        A.op("vector", lambda e, a=ao, d=dst: e.tensor_tensor(
            out=d, in0=d, in1=a, op=ALU.add),
            r=[ao, dst], w=[dst], label=f"{label}_acc")
        if then is not None:
            then(tc)

    def drip(self, n=2):
        while n > 0 and self._prep_units:
            self._prep_units.pop(0)()
            n -= 1

    def drip_flush(self):
        while self._prep_units:
            self._prep_units.pop(0)()

    def build(self):
        self.p0_load_consts()
        self._prep_units = []
        if PREP:
            self.p_prep_build_units()
        self.p1_l0_attn()
        self.drip_flush()
        if self.upto >= 2:
            self.p2_l0_ffn()
        if self.upto >= 3:
            self.p3_l1_attn()
        if self.upto >= 4:
            self.p4_l1_ffn()
        self.p5_store_out()

    def p0_load_consts(self):
        A = self.A
        xs = self.x_in.rearrange("(a p) d -> p a d", p=128)
        rd = self.res[:, :].rearrange("p (a d) -> p a d", d=D)
        for tc in range(NT):
            A.dma("sq", rd[:, tc:tc + 1, :], xs[:, tc:tc + 1, :], label=f"ld_x{tc}")
        self.ident = self.cons[:, 0:128]
        A.dma("sq", self.ident, self.hc_ident, label="ld_ident")
        self.A_sb = self.cons[:, 128:128 + NT * NC0]
        A.dma("sq", self.A_sb.rearrange("p (a c) -> p a c", c=NC0),
              self.hc_A.rearrange("(a p) c -> p a c", p=128), label="ld_A")
        self.idA = self.cons[:, 640:704]
        self.idB = self.cons[:, 704:768]
        A.dma("sq", self.idA, self.hc_idA, label="ld_idA")
        A.dma("sq", self.idB, self.hc_idB, label="ld_idB")
        self.mask_sb = self.con32[:, 0:NT * NC0]
        A.dma("sq", self.mask_sb.rearrange("p (a c) -> p a c", c=NC0),
              self.hc_mask.rearrange("(a p) c -> p a c", p=128), label="ld_mask")
        self.mdec = self.con32[:, 512:640]
        A.dma("sq", self.mdec, self.hc_mdec, label="ld_mdec")
        self.gq_in = self.con32[:, 640:768]
        A.dma("sq", self.gq_in, self.hc_gq, label="ld_gq")
        self.gk_in = self.con32[:, 768:769]
        A.dma("sq", self.gk_in, self.hc_gk, label="ld_gk")
        self.nmag = self.con32[:, 769:770]
        A.op("vector", lambda e, o=self.nmag: e.memset(o, -MAGIC), w=[self.nmag], label="nmag")


    def p_prep_build_units(self):
        A, st = self.A, self.stat
        b1 = self.big1
        U = self._prep_units
        fsl = [b1[:, 4096:6144].bitcast(F32), b1[:, 6144:8192].bitcast(F32)]
        osl = [b1[:, 8192:9216], b1[:, 9216:10240]]
        NF1 = F1 // 128

        sw1 = st[:, 0:32]
        sw2 = st[:, 224:232]
        am = self.con32[:, 770:771]
        rc = self.con32[:, 771:772]
        a4 = self.con32[:, 772:773]

        def w1_load(fc):
            A.dma("sq", fsl[fc % 2], self.w1[fc * 128:(fc + 1) * 128, :], label=f"w1l{fc}")

        def w1_comp(fc):
            fslot = fsl[fc % 2]
            oslot = osl[fc % 2]
            A.op("vector", lambda e, i=fslot, o=am: e.reduce_max(
                out=o, in_=i, axis=AX.X, apply_absolute_value=True),
                r=[fslot], w=[am], label=f"w1am{fc}")
            A.op("vector", lambda e, o=am: e.tensor_scalar(
                out=o, in0=o, scalar1=EPS_Q, scalar2=None, op0=ALU.max),
                r=[am], w=[am], label=f"w1amc{fc}")
            A.op("vector", lambda e, i=am, o=rc: e.reciprocal(out=o, in_=i),
                 r=[am], w=[rc], label=f"w1rc{fc}")
            A.op("vector", lambda e, o=rc: e.tensor_scalar(
                out=o, in0=o, scalar1=127.0, scalar2=None, op0=ALU.mult),
                r=[rc], w=[rc], label=f"w1rc2{fc}")
            A.op("vector", lambda e, i=am, o=sw1[:, fc:fc + 1]: e.tensor_scalar(
                out=o, in0=i, scalar1=1.0 / 127.0, scalar2=None, op0=ALU.mult),
                r=[am], w=[sw1[:, fc:fc + 1]], label=f"w1sw{fc}")
            self.magic_round(oslot, fslot, rc[:, 0:1], False, f"w1q{fc}", tmp=fslot)
            A.dma("sq", self.w1q_hbm[fc * 128:(fc + 1) * 128, :], oslot, label=f"w1s{fc}")

        U.append(lambda: (w1_load(0), w1_load(1)))
        for fc in range(NF1):
            U.append(lambda fc=fc: (w1_comp(fc),
                                    w1_load(fc + 2) if fc + 2 < NF1 else None))

        def w1t_unit(dk, hh):
            stg2 = b1[:, 4096 + (hh % 2) * 2048: 4096 + (hh % 2) * 2048 + 2048]
            A.dmat("sq", stg2, self.w1q_hbm[hh * 2048:(hh + 1) * 2048,
                                            dk * 128:(dk + 1) * 128], label=f"w1t{dk}_{hh}")
            A.dma("gq", self.w1qT_sl[hh * 16:(hh + 1) * 16, :, dk, :]
                  .rearrange("fc p f -> p fc f"),
                  stg2.rearrange("p (fc f) -> p fc f", f=128), label=f"w1ts{dk}_{hh}")

        for dk in range(ND):
            for hh in range(2):
                U.append(lambda dk=dk, hh=hh: w1t_unit(dk, hh))

        def w2_maxpass(dc):
            for q4 in range(4):
                fslot = fsl[q4 % 2]
                A.dma("aq", fslot, self.w2[dc * 128:(dc + 1) * 128, q4 * 1024:(q4 + 1) * 1024],
                      label=f"w2l{dc}_{q4}")
                A.op("vector", lambda e, i=fslot, o=a4: e.reduce_max(
                    out=o, in_=i, axis=AX.X, apply_absolute_value=True),
                    r=[fslot], w=[a4], label=f"w2a{dc}_{q4}")
                if q4 == 0:
                    A.op("vector", lambda e, i=a4, o=am: e.tensor_copy(out=o, in_=i),
                         r=[a4], w=[am], label=f"w2c{dc}")
                else:
                    A.op("vector", lambda e, i=a4, o=am: e.tensor_tensor(
                        out=o, in0=o, in1=i, op=ALU.max), r=[a4, am], w=[am], label=f"w2m{dc}_{q4}")
            A.op("vector", lambda e, o=am: e.tensor_scalar(
                out=o, in0=o, scalar1=EPS_Q, scalar2=None, op0=ALU.max),
                r=[am], w=[am], label=f"w2mc{dc}")
            A.op("vector", lambda e, i=am, o=rc: e.reciprocal(out=o, in_=i),
                 r=[am], w=[rc], label=f"w2rc{dc}")
            A.op("vector", lambda e, o=rc: e.tensor_scalar(
                out=o, in0=o, scalar1=127.0, scalar2=None, op0=ALU.mult),
                r=[rc], w=[rc], label=f"w2rc2{dc}")
            A.op("vector", lambda e, i=am, o=sw2[:, dc:dc + 1]: e.tensor_scalar(
                out=o, in0=i, scalar1=1.0 / 127.0, scalar2=None, op0=ALU.mult),
                r=[am], w=[sw2[:, dc:dc + 1]], label=f"w2sw{dc}")

        def w2_quant(dc, q4):
            fslot = fsl[q4 % 2]
            oslot = osl[q4 % 2]
            A.dma("aq", fslot, self.w2[dc * 128:(dc + 1) * 128, q4 * 1024:(q4 + 1) * 1024],
                  label=f"w2l2{dc}_{q4}")
            self.magic_round(oslot, fslot, rc[:, 0:1], False, f"w2q{dc}_{q4}", tmp=fslot)
            A.dma("aq", self.w2q_hbm[dc * 128:(dc + 1) * 128, q4 * 1024:(q4 + 1) * 1024],
                  oslot, label=f"w2s{dc}_{q4}")

        for dc in range(ND):
            U.append(lambda dc=dc: w2_maxpass(dc))
            U.append(lambda dc=dc: (w2_quant(dc, 0), w2_quant(dc, 1)))
            U.append(lambda dc=dc: (w2_quant(dc, 2), w2_quant(dc, 3)))

        def w2t_unit(fc):
            stg2 = b1[:, 4096 + (fc % 2) * 2048: 4096 + (fc % 2) * 2048 + 1024]
            A.dmat("aq", stg2, self.w2q_hbm[:, fc * 128:(fc + 1) * 128], label=f"w2t{fc}")
            A.dma("gq", self.w2qT_hbm[fc * 128:(fc + 1) * 128, :], stg2, label=f"w2ts{fc}")

        for fc in range(NF1):
            U.append(lambda fc=fc: w2t_unit(fc))

        def cast_unit(nm, dst, src_, c0, c1):
            for c in range(c0, c1):
                A.dma("gq", dst[c * 128:(c + 1) * 128, :], src_[c * 128:(c + 1) * 128, :],
                      label=f"{nm}_cast{c}")

        for c in range(0, 24, 4):
            U.append(lambda c=c: cast_unit("qk", self.wqkv_bf, self.wqkv, c, c + 4))

        def qkt_unit(dk, hh):
            stg2 = b1[:, 4096 + (hh % 2) * 2048: 4096 + (hh % 2) * 2048 + 1536]
            A.dmat("aq", stg2, self.wqkv_bf[hh * 1536:(hh + 1) * 1536,
                                            dk * 128:(dk + 1) * 128], label=f"qkt{dk}_{hh}")
            A.dma("gq", self.wqkvT_hbm[dk * 128:(dk + 1) * 128, hh * 1536:(hh + 1) * 1536],
                  stg2, label=f"qkts{dk}_{hh}")
            nsl = 12 if hh == 0 else 4
            A.dma("gq", self.wqkv_sl[hh * 12:hh * 12 + nsl, :, dk, :]
                  .rearrange("fc p f -> p fc f"),
                  stg2[:, 0:nsl * 128].rearrange("p (fc f) -> p fc f", f=128),
                  label=f"qksl{dk}_{hh}")

        for dk in range(ND):
            for hh in range(2):
                U.append(lambda dk=dk, hh=hh: qkt_unit(dk, hh))
        for c in range(0, 8, 4):
            U.append(lambda c=c: cast_unit("ow", self.outw_bf, self.outw, c, c + 4))
        for c in range(0, 24, 4):
            U.append(lambda c=c: cast_unit("wg", self.wg_bf, self.wg, c, c + 4))
        for c in range(0, 24, 4):
            U.append(lambda c=c: cast_unit("wu", self.wu_bf, self.wu, c, c + 4))
        for c in range(0, 8, 4):
            U.append(lambda c=c: cast_unit("wd", self.wd_bf, self.wd, c, c + 4))

    def p_prep_late(self):
        A = self.A
        b1 = self.big1
        for nm, w_bf, w_sl in (("wg", self.wg_bf, self.wgT_sl),
                               ("wu", self.wu_bf, self.wuT_sl)):
            for dk in range(ND):
                for hh in range(2):
                    stg2 = b1[:, 8192:9728] if hh == 0 else b1[:, 8704:10240]
                    A.dmat("sq", stg2, w_bf[hh * 1536:(hh + 1) * 1536,
                                            dk * 128:(dk + 1) * 128], label=f"{nm}t{dk}_{hh}")
                    A.dma("sq", w_sl[hh * 12:(hh + 1) * 12, :, dk, :]
                          .rearrange("fc p f -> p fc f"),
                          stg2.rearrange("p (fc f) -> p fc f", f=128),
                          label=f"{nm}ts{dk}_{hh}")

    def p1_l0_attn(self):
        A, nc = self.A, self.nc
        xln = self.big0

        def xlnT(tc):
            return xln[:, tc * D:(tc + 1) * D]

        self.ln_into(self.resT, xlnT, "ln1")
        self.transpose_roundtrip(xlnT, "tr1")

        ps = self.pf[0]
        summ_ps = ps[:, 0:256].rearrange("p (dk c) -> p dk c", c=NC0)
        for dk in range(ND):
            self.drip(1)
            for tc in range(NT):
                lhsT = xlnT(tc)[:, dk * 128:(dk + 1) * 128]
                rhs = self.A_sb[:, tc * NC0:(tc + 1) * NC0]
                out = summ_ps[:, dk, :]
                A.op("tensor", lambda e, o=out, l=lhsT, r=rhs, s=(tc == 0), p=(tc == NT - 1):
                     e.matmul(o, l, r, start=s, stop=p),
                     r=[lhsT, rhs], w=[ps[:, dk * 32:(dk + 1) * 32]], label=f"summ_{dk}|{tc}")
        summ = self.tile[:, 0:256]
        A.op("vector", lambda e, i=ps[:, 0:256], o=summ: e.tensor_copy(out=o, in_=i),
             r=[ps[:, 0:256]], w=[summ], label="summ_ev")

        bq_s = self.stat[:, 128:136]
        bk_s = self.stat[:, 136:144]
        A.dma("gq", bq_s, self.bq.rearrange("(c p) -> p c", p=128), label="ld_bq", allow_slow_non_contiguous=True)
        A.dma("gq", bk_s, self.bk.rearrange("(c p) -> p c", p=128), label="ld_bk", allow_slow_non_contiguous=True)
        bk8 = self.stat[:, 144:152]
        A.op("vector", lambda e, i=bk_s, o=bk8: e.tensor_scalar(
            out=o, in0=i, scalar1=0.125, scalar2=None, op0=ALU.mult),
            r=[bk_s], w=[bk8], label="bk8")
        bv_row = self.rowb[:, 0:1024]
        bo_row = self.rowb[:, 1024:2048]
        A.dma("gq", bv_row, self.bv.rearrange("(one d) -> one d", one=1).broadcast_to([128, D]), label="ld_bv")
        A.dma("gq", bo_row, self.bo.rearrange("(one d) -> one d", one=1).broadcast_to([128, D]), label="ld_bo")

        kfm = self.tile[:, 256:512]
        wk_sl = self.big1[:, 0:4096]
        self._p1slot = 0

        def wslot():
            s = self._p1slot % 4
            self._p1slot += 1
            return wk_sl[:, s * 1024:(s + 1) * 1024]

        for fc in range(ND):
            self.drip(2)
            buf = wslot()
            A.dma("gq", buf.rearrange("p (dk f) -> p dk f", f=128),
                  self.Wk[:, fc * 128:(fc + 1) * 128].rearrange("(dk p) f -> p dk f", p=128),
                  label=f"ld_wk{fc}")
            out = self.pf[1][:, 0:32]
            for dk in range(ND):
                l = buf[:, dk * 128:(dk + 1) * 128]
                r = summ[:, dk * 32:(dk + 1) * 32]
                A.op("tensor", lambda e, o=out, l=l, r=r, s=(dk == 0), p=(dk == ND - 1):
                     e.matmul(o, l, r, start=s, stop=p),
                     r=[l, r], w=[out], label=f"kmm{fc}|{dk}")
            o2 = kfm[:, fc * 32:(fc + 1) * 32]
            A.op("scalar", lambda e, i=out, o=o2, b=bk8[:, fc:fc + 1]: e.activation(
                out=o, in_=i, func=AF.Identity, bias=b, scale=0.125),
                r=[out, bk8], w=[o2], label=f"kev{fc}")

        vpad = self.tile[:, 512:1536]
        A.op("vector", lambda e, o=vpad: e.memset(o, 0.0), w=[vpad], label="vpad0")
        vbufs = []
        for fc in range(ND):
            self.drip(2)
            buf = wslot()
            vbufs.append(buf)
            A.dma("gq", buf.rearrange("p (dk f) -> p dk f", f=128),
                  self.Wv[:, fc * 128:(fc + 1) * 128].rearrange("(dk p) f -> p dk f", p=128),
                  label=f"ld_wv{fc}")
            for hh in range(2):
                h = 2 * fc + hh
                b = 32 * (h % 4)
                pcol = (h % 8) * 64
                out = self.pf[2][b:b + 32, pcol:pcol + 64]
                for dk in range(ND):
                    l = summ[:, dk * 32:(dk + 1) * 32]
                    r = buf[:, dk * 128 + hh * 64: dk * 128 + hh * 64 + 64]
                    A.op("tensor", lambda e, o=out, l=l, r=r, s=(dk == 0), p=(dk == ND - 1),
                         tp=(0, b): e.matmul(o, l, r, start=s, stop=p, tile_position=tp),
                         r=[l, r], w=[self.pf[2][:, pcol:pcol + 64]], label=f"vmm{h}|{dk}")
                dst = vpad[:, h * 64:(h + 1) * 64][b:b + 32, :]
                bvs = bv_row[b:b + 32, h * 64:(h + 1) * 64]
                A.op("vector", lambda e, i=out, o=dst, b2=bvs: e.tensor_tensor(
                    out=o, in0=i, in1=b2, op=ALU.add),
                    r=[self.pf[2][:, pcol:pcol + 64], bv_row], w=[dst], label=f"vev{h}")

        q = self.big0
        for fc in range(ND):
            self.drip(2)
            buf = wslot()
            A.dma("gq", buf.rearrange("p (dk f) -> p dk f", f=128),
                  self.Wq[:, fc * 128:(fc + 1) * 128].rearrange("(dk p) f -> p dk f", p=128),
                  label=f"ld_wq{fc}")
            for tt in range(4):
                self.drip(1)
                ps_t = self.pf[3 + (tt % 2)]
                for dk in range(ND):
                    l = buf[:, dk * 128:(dk + 1) * 128]
                    r = self.xfmS(dk, tt * 512, (tt + 1) * 512)
                    A.op("tensor", lambda e, o=ps_t[:, :], l=l, r=r, s=(dk == 0), p=(dk == ND - 1):
                         e.matmul(o, l, r, start=s, stop=p),
                         r=[l, r], w=[ps_t[:, :]], label=f"qmm{fc}_{tt}|{dk}")
                dst = q[:, fc * T + tt * 512: fc * T + (tt + 1) * 512]
                A.op("scalar", lambda e, i=ps_t, o=dst, b=bq_s[:, fc:fc + 1]: e.activation(
                    out=o, in_=i[:, :], func=AF.Identity, bias=b),
                    r=[ps_t[:, :], bq_s], w=[dst], label=f"qev{fc}_{tt}")

        wo_sb = self.big2[:, 0:8192]
        A.dma("gq", wo_sb.rearrange("p (dk f) -> p dk f", f=D),
              self.Wo.rearrange("(dk p) f -> p dk f", p=128), label="ld_wo")

        at_b = [self.tile[:, 1536:2048], self.tile[:, 3584:4096]]
        atT = self.tile[:, 2048:2560]
        ytile = self.tile[:, 2560:3584]

        def emit_s1(tc):
            ps_sc = self.pf[5] if tc % 2 == 0 else self.pf[3]
            for h in range(H0):
                b = 64 * (h % 2)
                l = q[:, (h // 2) * T + tc * 128:(h // 2) * T + tc * 128 + 128][b:b + 64, :]
                r = kfm[:, (h // 2) * 32:(h // 2) * 32 + 32][b:b + 64, :]
                out = ps_sc[:, h * 32:(h + 1) * 32]
                A.op("tensor", lambda e, o=out, l=l, r=r: e.matmul(o, l, r, start=True, stop=True),
                     r=[l, r], w=[out], label=f"sc{tc}_{h}")
            scf = self.scr32[:, 1024:1536]
            mk = self.mask_sb[:, tc * NC0:(tc + 1) * NC0]
            A.op("vector", lambda e, i=ps_sc, m=mk, o=scf: e.tensor_tensor(
                out=o.rearrange("p (h c) -> p h c", c=NC0),
                in0=i[:, :].rearrange("p (h c) -> p h c", c=NC0),
                in1=m.rearrange("p (one c) -> p one c", one=1).broadcast_to([128, H0, NC0]), op=ALU.add),
                r=[ps_sc[:, :], self.mask_sb], w=[scf], label=f"msk{tc}")
            mx = self.stat[:, 160:176]
            A.op("vector", lambda e, i=scf, o=mx: e.tensor_reduce(
                out=o, in_=i.rearrange("p (h c) -> p h c", c=NC0), axis=AX.X, op=ALU.max),
                r=[scf], w=[mx], label=f"mx{tc}")
            A.op("vector", lambda e, i=scf, m=mx, o=scf: e.tensor_tensor(
                out=o.rearrange("p (h c) -> p h c", c=NC0),
                in0=i.rearrange("p (h c) -> p h c", c=NC0),
                in1=m.rearrange("p (h one) -> p h one", one=1).broadcast_to([128, H0, NC0]), op=ALU.subtract),
                r=[scf, mx], w=[scf], label=f"sub{tc}")
            A.op("scalar", lambda e, i=scf, o=scf: e.activation(out=o, in_=i, func=AF.Exp),
                 r=[scf], w=[scf], label=f"exp{tc}")
            sm = self.stat[:, 176:192]
            A.op("vector", lambda e, i=scf, o=sm: e.tensor_reduce(
                out=o, in_=i.rearrange("p (h c) -> p h c", c=NC0), axis=AX.X, op=ALU.add),
                r=[scf], w=[sm], label=f"sm{tc}")
            rs = self.stat[:, 192:208]
            A.op("vector", lambda e, i=sm, o=rs: e.reciprocal(out=o, in_=i),
                 r=[sm], w=[rs], label=f"rs{tc}")
            at = at_b[tc % 2]
            A.op("vector", lambda e, i=scf, r2=rs, o=at: e.tensor_tensor(
                out=o.rearrange("p (h c) -> p h c", c=NC0),
                in0=i.rearrange("p (h c) -> p h c", c=NC0),
                in1=r2.rearrange("p (h one) -> p h one", one=1).broadcast_to([128, H0, NC0]), op=ALU.mult),
                r=[scf, rs], w=[at], label=f"at{tc}")

        def emit_s2(tc):
            at = at_b[tc % 2]
            for qd in range(4):
                src = at[:, qd * 128:(qd + 1) * 128]
                out = self.pb[:, qd * 128:(qd + 1) * 128]
                A.op("tensor", lambda e, o=out, s=src: e.transpose(o, s, self.ident),
                     r=[src, self.ident], w=[out], label=f"atT{tc}_{qd}", tr=True)
            A.op("scalar", lambda e, i=self.pb[:, 0:512], o=atT: e.copy(out=o, in_=i),
                 r=[self.pb[:, 0:512]], w=[atT], label=f"atTev{tc}")
            ps_y = [self.pf[6], self.pf[0]]
            for h in range(H0):
                b = 64 * (h % 2)
                l = vpad[:, h * 64:(h + 1) * 64]
                r = atT[:, (h // 4) * 128:(h // 4) * 128 + 128]
                bank = ps_y[(h // 2) % 2]
                col = ((h // 2) // 2) * 128
                out = bank[b:b + 64, col:col + 128]
                A.op("tensor", lambda e, o=out, l=l, r=r, tp=(0, b): e.matmul(
                    o, l, r, start=True, stop=True, tile_position=tp),
                    r=[l, r], w=[bank[:, col:col + 128]], label=f"av{tc}_{h}")
            for i2 in range(2):
                A.op("scalar", lambda e, i=ps_y[i2], o=ytile[:, i2 * 512:(i2 + 1) * 512]:
                     e.copy(out=o, in_=i[:, :]),
                     r=[ps_y[i2][:, :]], w=[ytile[:, i2 * 512:(i2 + 1) * 512]], label=f"yev{tc}_{i2}")
            ps_o = [self.pf[1], self.pf[2]]
            for k in range(ND):
                l = ytile[:, (k % 2) * 512 + (k // 2) * 128: (k % 2) * 512 + (k // 2) * 128 + 128]
                for half in range(2):
                    r = wo_sb[:, k * D + half * 512: k * D + (half + 1) * 512]
                    out = ps_o[half]
                    A.op("tensor", lambda e, o=out, l=l, r=r, s=(k == 0), p=(k == ND - 1):
                         e.matmul(o[:, :], l, r, start=s, stop=p),
                         r=[l, r], w=[out[:, :]], label=f"op{tc}_{half}|{k}")
            self.postln_residual(ps_o, tc, bo_row, f"pl1_{tc}",
                                 then=lambda tc2: self.ln_tc(
                                     tc2, self.xfm[:, tc2 * D:(tc2 + 1) * D], "ln2",
                                     colmax=self.stat[:, 208:224]))

        for tc in range(NT):
            emit_s1(tc)
            self.drip(2)
            if tc > 0:
                emit_s2(tc - 1)
                self.drip(2)
        emit_s2(NT - 1)

    def global_max_collective(self, m128, idx, out_col, label):
        A = self.A
        cca, ccb = self.cc_a[idx], self.cc_b[idx]
        A.dma("gq", cca.rearrange("one p -> p one"), m128, label=f"{label}_st")
        A.cc(lambda e, a=cca, b=ccb: e.collective_compute(
            "AllReduce", ALU.max, replica_groups=[list(range(8))],
            ins=[a], outs=[b]), r=[cca], w=[ccb], label=f"{label}_cc")
        gall = self.scr32[:, 0:128]
        A.dma("gq", gall, ccb[0:1, :].broadcast_to([128, 128]), label=f"{label}_ld")
        A.op("vector", lambda e, i=gall, o=out_col: e.reduce_max(out=o, in_=i, axis=AX.X),
             r=[gall], w=[out_col], label=f"{label}_mx")

    def magic_round(self, out_bf, in_ap, inv_ap, relu, label, tmp=None):
        A = self.A
        if tmp is None:
            tmp = self.scr32[:, 0:in_ap.shape[-1]]
        A.op("vector", lambda e, i=in_ap, o=tmp, s=inv_ap: e.tensor_scalar(
            out=o, in0=i, scalar1=s, scalar2=MAGIC, op0=ALU.mult, op1=ALU.add),
            r=[in_ap, inv_ap], w=[tmp], label=f"{label}_a")
        fn = AF.Relu if relu else AF.Identity
        A.op("scalar", lambda e, i=tmp, o=out_bf, fn=fn, b=self.nmag: e.activation(
            out=o, in_=i, func=fn, bias=b),
            r=[tmp, self.nmag], w=[out_bf], label=f"{label}_b")

    def ln_tc(self, tc, dst, label, colmax=None):
        A, st = self.A, self.stat
        src = self.resT(tc)
        bn6 = st[:, 64 + (tc % 2) * 16:64 + (tc % 2) * 16 + 12]
        for g in range(2):
            A.op("vector", lambda e, s=src, o=bn6, g=g: e.bn_stats(
                out=o[:, 6 * g:6 * g + 6], in_=s[:, 512 * g:512 * g + 512]),
                r=[src], w=[bn6[:, 6 * g:6 * g + 6]], label=f"{label}_bnst{tc}_{g}")
        mv = st[:, 96 + (tc % 2) * 2:96 + (tc % 2) * 2 + 2]
        A.op("vector", lambda e, i=bn6, o=mv: e.bn_aggr(out=o, in_=i),
             r=[bn6], w=[mv], label=f"{label}_bnag{tc}")
        ve = st[:, 100 + (tc % 2):101 + (tc % 2)]
        A.op("vector", lambda e, i=mv, o=ve: e.tensor_scalar(
            out=o, in0=i[:, 1:2], scalar1=EPS_LN, scalar2=None, op0=ALU.add),
            r=[mv], w=[ve], label=f"{label}_veps{tc}")
        A.op("scalar", lambda e, o=ve: e.activation(out=o, in_=o, func=AF.Ln),
             r=[ve], w=[ve], label=f"{label}_ln{tc}")
        rstd = st[:, 102 + (tc % 2):103 + (tc % 2)]
        A.op("scalar", lambda e, i=ve, o=rstd: e.activation(out=o, in_=i, func=AF.Exp, scale=-0.5),
             r=[ve], w=[rstd], label=f"{label}_rexp{tc}")
        A.op("vector", lambda e, s=src, o=dst, m=mv[:, 0:1], r2=rstd: e.tensor_scalar(
            out=o, in0=s, scalar1=m, scalar2=r2, op0=ALU.subtract, op1=ALU.mult),
            r=[src, mv, rstd], w=[dst], label=f"{label}_nrm{tc}")
        A.dma("aq", self.xlnT_hbm[tc * 128:(tc + 1) * 128, :], dst, label=f"{label}_st{tc}")
        if colmax is not None:
            A.op("vector", lambda e, i=dst, o=colmax[:, tc:tc + 1]:
                 e.reduce_max(out=o, in_=i, axis=AX.X),
                 r=[dst], w=[colmax[:, tc:tc + 1]], label=f"{label}_cm{tc}")

    def p2_l0_ffn(self):
        A, st = self.A, self.stat
        xln = self.big0

        def xlnT(tc):
            return xln[:, tc * D:(tc + 1) * D]

        for dk in range(ND):
            A.dmat("aq", self.xfmS(dk, 0, T), self.xlnT_hbm[:, dk * 128:(dk + 1) * 128],
                   label=f"tr2_tr{dk}")
        colmax = st[:, 208:224]
        m128 = st[:, 232:233]
        A.op("vector", lambda e, i=colmax, o=m128: e.reduce_max(out=o, in_=i, axis=AX.X),
             r=[colmax], w=[m128], label="m128")
        A.op("vector", lambda e, o=m128: e.tensor_scalar(
            out=o, in0=o, scalar1=EPS_Q, scalar2=None, op0=ALU.max),
            r=[m128], w=[m128], label="m128c")
        gmax = st[:, 233:234]
        self.global_max_collective(m128, 0, gmax, "gmx0")
        inv_s = st[:, 234:235]
        s_q = st[:, 235:236]
        A.op("vector", lambda e, i=gmax, o=inv_s: e.reciprocal(out=o, in_=i),
             r=[gmax], w=[inv_s], label="invs0")
        A.op("vector", lambda e, o=inv_s: e.tensor_scalar(
            out=o, in0=o, scalar1=255.0, scalar2=None, op0=ALU.mult),
            r=[inv_s], w=[inv_s], label="invs1")
        A.op("vector", lambda e, i=gmax, o=s_q: e.tensor_scalar(
            out=o, in0=i, scalar1=1.0 / 255.0, scalar2=None, op0=ALU.mult),
            r=[gmax], w=[s_q], label="sq0")

        xq = self.big0
        for dk in range(ND):
            self.magic_round(xq[:, dk * T:(dk + 1) * T], self.xfmS(dk, 0, T),
                             inv_s[:, 0:1], True, f"xq{dk}",
                             tmp=self.scr32[:, 0:T])

        sw1 = self.stat[:, 0:32]
        NF1 = F1 // 128
        A.op("vector", lambda e, o=sw1, s=s_q[:, 0:1]: e.tensor_scalar(
            out=o, in0=o, scalar1=s, scalar2=None, op0=ALU.mult),
            r=[sw1, s_q], w=[sw1], label="ssw1")
        b1s = st[:, 128:160]
        A.dma("gq", b1s, self.b1.rearrange("(c p) -> p c", p=128), label="ld_b1",
              allow_slow_non_contiguous=True)
        rcp = st[:, 192:224]
        A.op("vector", lambda e, i=sw1, o=rcp: e.reciprocal(out=o, in_=i),
             r=[sw1], w=[rcp], label="b1rcp")
        b1q = st[:, 160:192]
        A.op("vector", lambda e, b=b1s, r=rcp, o=b1q: e.tensor_tensor(
            out=o, in0=b, in1=r, op=ALU.mult), r=[b1s, rcp], w=[b1q], label="b1t")
        A.op("vector", lambda e, o=b1q: e.tensor_scalar(
            out=o, in0=o, scalar1=MAGIC, scalar2=MAGIC, op0=ALU.add, op1=ALU.subtract),
            r=[b1q], w=[b1q], label="b1r")
        A.op("vector", lambda e, o=b1q, s=sw1: e.tensor_tensor(
            out=o, in0=o, in1=s, op=ALU.mult), r=[b1q, sw1], w=[b1q], label="b1m")

        hm = st[:, 192:224]
        for fc in range(NF1):
            wsl = self.big1[:, 2048 + (fc % 2) * 1024: 2048 + (fc % 2) * 1024 + 1024]
            A.dma("sq", wsl.rearrange("p (dk f) -> p dk f", f=128),
                  self.w1qT_sl[fc], label=f"m1w{fc}")
            if 8 <= fc < 16:
                w2c = 16 + (fc - 8)
                A.dma("sq", self.big2[:, (fc - 8) * 1024:(fc - 7) * 1024],
                      self.w2qT_hbm[w2c * 128:(w2c + 1) * 128, :], label=f"w2r{w2c}")
            hfc = self.tile[:, (fc % 2) * 2048:(fc % 2) * 2048 + 2048]
            for tt in range(4):
                ps = self.pf[2 * (fc % 2) + (tt % 2)]
                for dk in range(ND):
                    l = wsl[:, dk * 128:(dk + 1) * 128]
                    r = xq[:, dk * T + tt * 512: dk * T + (tt + 1) * 512]
                    A.op("tensor", lambda e, o=ps, l=l, r=r, s=(dk == 0), p=(dk == ND - 1):
                         e.matmul(o[:, :], l, r, start=s, stop=p),
                         r=[l, r], w=[ps[:, :]], label=f"m1_{fc}_{tt}|{dk}")
                dst = hfc[:, tt * 512:(tt + 1) * 512]
                A.op("scalar", lambda e, i=ps, o=dst, s=sw1[:, fc:fc + 1], b=b1q[:, fc:fc + 1]:
                     e.activation(out=o, in_=i[:, :], func=AF.Relu, scale=s, bias=b),
                     r=[ps[:, :], sw1, b1q], w=[dst], label=f"m1e_{fc}_{tt}")
            A.op("vector", lambda e, i=hfc, o=hm[:, fc:fc + 1]: e.reduce_max(
                out=o, in_=i, axis=AX.X), r=[hfc], w=[hm[:, fc:fc + 1]], label=f"hm{fc}")
            A.dma("aq", self.h_hbm[fc * 128:(fc + 1) * 128, :], hfc, label=f"hs{fc}")

        hmax = st[:, 236:237]
        A.op("vector", lambda e, i=hm, o=hmax: e.reduce_max(out=o, in_=i, axis=AX.X),
             r=[hm], w=[hmax], label="hmax")
        A.op("vector", lambda e, o=hmax: e.tensor_scalar(
            out=o, in0=o, scalar1=EPS_Q, scalar2=None, op0=ALU.max),
            r=[hmax], w=[hmax], label="hmaxc")
        gh = st[:, 237:238]
        self.global_max_collective(hmax, 1, gh, "gmx1")
        inv_sh = st[:, 238:239]
        s_h = st[:, 239:240]
        A.op("vector", lambda e, i=gh, o=inv_sh: e.reciprocal(out=o, in_=i),
             r=[gh], w=[inv_sh], label="invh0")
        A.op("vector", lambda e, o=inv_sh: e.tensor_scalar(
            out=o, in0=o, scalar1=255.0, scalar2=None, op0=ALU.mult),
            r=[inv_sh], w=[inv_sh], label="invh1")
        A.op("vector", lambda e, i=gh, o=s_h: e.tensor_scalar(
            out=o, in0=i, scalar1=1.0 / 255.0, scalar2=None, op0=ALU.mult),
            r=[gh], w=[s_h], label="sh0")

        sw2 = self.stat[:, 224:232]
        ssw2 = st[:, 8:16]
        A.op("vector", lambda e, i=sw2, o=ssw2, s=s_h[:, 0:1]: e.tensor_scalar(
            out=o, in0=i, scalar1=s, scalar2=None, op0=ALU.mult),
            r=[sw2, s_h], w=[ssw2], label="ssw2")
        b2s = st[:, 16:24]
        A.dma("gq", b2s, self.b2.rearrange("(c p) -> p c", p=128), label="ld_b2",
              allow_slow_non_contiguous=True)
        rcp2 = st[:, 24:32]
        A.op("vector", lambda e, i=ssw2, o=rcp2: e.reciprocal(out=o, in_=i),
             r=[ssw2], w=[rcp2], label="b2rcp")
        b2q = st[:, 128:136]
        A.op("vector", lambda e, b=b2s, r=rcp2, o=b2q: e.tensor_tensor(
            out=o, in0=b, in1=r, op=ALU.mult), r=[b2s, rcp2], w=[b2q], label="b2t")
        A.op("vector", lambda e, o=b2q: e.tensor_scalar(
            out=o, in0=o, scalar1=MAGIC, scalar2=MAGIC, op0=ALU.add, op1=ALU.subtract),
            r=[b2q], w=[b2q], label="b2r")
        A.op("vector", lambda e, o=b2q, s=ssw2: e.tensor_tensor(
            out=o, in0=o, in1=s, op=ALU.mult), r=[b2q, ssw2], w=[b2q], label="b2m")
        A.dma("gq", self.row_hbm[0:1, :].rearrange("one (c p) -> p one c", p=128),
              ssw2.rearrange("p (one c) -> p one c", one=1), label="r0st",
              allow_slow_non_contiguous=True)
        A.dma("gq", self.row_hbm[1:2, :].rearrange("one (c p) -> p one c", p=128),
              b2q.rearrange("p (one c) -> p one c", one=1), label="r1st",
              allow_slow_non_contiguous=True)
        ssw2_row = self.rowb[:, 0:1024]
        b2q_row = self.rowb[:, 1024:2048]
        A.dma("gq", ssw2_row, self.row_hbm[0:1, :].broadcast_to([128, D]), label="r0ld")
        A.dma("gq", b2q_row, self.row_hbm[1:2, :].broadcast_to([128, D]), label="r1ld")

        groups = [list(range(g, min(g + 3, NT))) for g in range(0, NT, 3)]

        def m2h_issue(gi2, fc2, g02, gw2):
            hld2 = self.big1[:, (fc2 % 8) * 384:(fc2 % 8) * 384 + gw2]
            A.dma(("sq", "gq", "aq")[fc2 % 3], hld2,
                  self.h_hbm[fc2 * 128:(fc2 + 1) * 128, g02 * 128:g02 * 128 + gw2],
                  label=f"m2h{gi2}_{fc2}")
            return hld2

        prev_post = None
        for gi, tcs in enumerate(groups):
            g0 = tcs[0]
            gw = len(tcs) * 128
            for fc in range(8):
                m2h_issue(gi, fc, g0, gw)
            for fc in range(24, 28):
                wsl = self.tile[:, (fc % 4) * 1024:(fc % 4) * 1024 + 1024]
                A.dma(("gq", "aq")[fc % 2], wsl, self.w2qT_hbm[fc * 128:(fc + 1) * 128, :],
                      label=f"m2w{gi}_{fc}")
            if prev_post is not None:
                prev_post()
            for fc in range(NF1):
                hld = self.big1[:, (fc % 8) * 384:(fc % 8) * 384 + gw]
                tmp = self.scr32[:, 1024 + (fc % 2) * 384:1024 + (fc % 2) * 384 + gw]
                hq = self.big1[:, 3072 + (fc % 8) * 384: 3072 + (fc % 8) * 384 + gw]
                A.op("vector", lambda e, i=hld, o=tmp, s=inv_sh[:, 0:1]: e.tensor_scalar(
                    out=o, in0=i, scalar1=s, scalar2=MAGIC, op0=ALU.mult, op1=ALU.add),
                    r=[hld, inv_sh], w=[tmp], label=f"m2qa{gi}_{fc}")
                if fc + 8 < NF1:
                    m2h_issue(gi, fc + 8, g0, gw)
                A.op("scalar", lambda e, i=tmp, o=hq, b=self.nmag: e.activation(
                    out=o, in_=i, func=AF.Identity, bias=b),
                    r=[tmp, self.nmag], w=[hq], label=f"m2qb{gi}_{fc}")
                if fc < 16:
                    wsl = self.big0[:, fc * 1024:(fc + 1) * 1024]
                    if gi == 0:
                        A.dma("sq", wsl, self.w2qT_hbm[fc * 128:(fc + 1) * 128, :],
                              label=f"m2w{gi}_{fc}")
                elif fc < 24:
                    wsl = self.big2[:, (fc - 16) * 1024:(fc - 15) * 1024]
                else:
                    wsl = self.tile[:, (fc % 4) * 1024:(fc % 4) * 1024 + 1024]
                for ti, tc in enumerate(tcs):
                    l = hq[:, ti * 128:(ti + 1) * 128]
                    for half in range(2):
                        ps = self.pf[2 * ti + half]
                        r = wsl[:, half * 512:(half + 1) * 512]
                        A.op("tensor", lambda e, o=ps, l=l, r=r, s=(fc == 0), p=(fc == NF1 - 1):
                             e.matmul(o[:, :], l, r, start=s, stop=p),
                             r=[l, r], w=[ps[:, :]], label=f"m2_{gi}_{ti}_{half}|{fc}")
                if 24 <= fc < 28:
                    fcn = fc + 4
                    wnx = self.tile[:, (fcn % 4) * 1024:(fcn % 4) * 1024 + 1024]
                    A.dma(("gq", "aq")[fcn % 2], wnx, self.w2qT_hbm[fcn * 128:(fcn + 1) * 128, :],
                          label=f"m2w{gi}_{fcn}")

            def mkpost(tcs2):
                def post():
                    for ti, tc in enumerate(tcs2):
                        self.postln_residual([self.pf[2 * ti], self.pf[2 * ti + 1]], tc,
                                             bias_row=b2q_row, scale_row=ssw2_row,
                                             label=f"pl2_{tc}",
                                             then=lambda tc2: self.ln_tc(
                                                 tc2, self.xfm[:, tc2 * D:(tc2 + 1) * D], "ln3"))
                return post
            prev_post = mkpost(tcs)
        prev_post()

    def p3_l1_attn(self):
        A, st = self.A, self.stat
        nc = self.nc

        outwT = self.big2[:, 0:8192]
        for dk in range(ND):
            A.dmat("aq", outwT[:, dk * D:(dk + 1) * D],
                   self.outw_bf[:, dk * 128:(dk + 1) * 128], label=f"owt{dk}")

        xln = self.big0

        def xlnT(tc):
            return xln[:, tc * D:(tc + 1) * D]

        for dk in range(ND):
            A.dmat("aq", self.xfmS(dk, 0, T), self.xlnT_hbm[:, dk * 128:(dk + 1) * 128],
                   label=f"tr3_tr{dk}")

        grow = st[0:1, 240:256]
        A.dma("gq", grow, self.gamma.rearrange("(one g) -> one g", one=1), label="ld_g")
        A.op("scalar", lambda e, o=grow: e.activation(out=o, in_=o, func=AF.Sigmoid),
             r=[grow], w=[grow], label="sig_g")
        A.op("scalar", lambda e, o=grow: e.activation(out=o, in_=o, func=AF.Ln),
             r=[grow], w=[grow], label="ln_g")
        A.dma("gq", self.row_hbm[2:3, 0:16], grow, label="g_st")
        lngb = self.scr32[:, 0:16]
        A.dma("gq", lngb, self.row_hbm[2:3, 0:16].broadcast_to([128, 16]), label="g_ld")
        D_all = self.cons[:, 768:768 + 2048]
        for h in range(16):
            A.op("scalar", lambda e, o=D_all[:, h * 128:(h + 1) * 128], s=lngb[:, h:h + 1]:
                 e.activation(out=o, in_=self.mdec, func=AF.Exp, scale=s),
                 r=[self.mdec, lngb], w=[D_all[:, h * 128:(h + 1) * 128]], label=f"dall{h}")
        lngband = self.scr32[:, 32:40]
        for dc in range(ND):
            A.op("vector", lambda e, i=lngb[0:64, 2 * dc:2 * dc + 1], o=lngband[0:64, dc:dc + 1]:
                 e.tensor_copy(out=o, in_=i), r=[lngb], w=[lngband[0:64, dc:dc + 1]],
                 label=f"lb0_{dc}")
            A.op("vector", lambda e, i=lngb[64:128, 2 * dc + 1:2 * dc + 2], o=lngband[64:128, dc:dc + 1]:
                 e.tensor_copy(out=o, in_=i), r=[lngb], w=[lngband[64:128, dc:dc + 1]],
                 label=f"lb1_{dc}")
        Gq_all = self.cons[:, 2816:2816 + 1024]
        for dc in range(ND):
            A.op("scalar", lambda e, o=Gq_all[:, dc * 128:(dc + 1) * 128], s=lngband[:, dc:dc + 1]:
                 e.activation(out=o, in_=self.gq_in, func=AF.Exp, scale=s),
                 r=[self.gq_in, lngband], w=[Gq_all[:, dc * 128:(dc + 1) * 128]], label=f"gq{dc}")
        GkS = st[:, 224:240]
        for h in range(16):
            A.op("scalar", lambda e, o=GkS[:, h:h + 1], s=lngb[:, h:h + 1]:
                 e.activation(out=o, in_=self.gk_in, func=AF.Exp, scale=s),
                 r=[self.gk_in, lngb], w=[GkS[:, h:h + 1]], label=f"gk{h}")
        gCb = self.scr32[:, 16:32]
        A.op("scalar", lambda e, i=lngb, o=gCb: e.activation(
            out=o, in_=i, func=AF.Exp, scale=128.0),
            r=[lngb], w=[gCb], label="gC")
        gI = self.cons[:, 4864:4864 + 1024]
        for h in range(16):
            idx = self.idA if h % 2 == 0 else self.idB
            A.op("vector", lambda e, i=idx, o=gI[:, h * 64:(h + 1) * 64], s=gCb[:, h:h + 1]:
                 e.tensor_scalar(out=o, in0=i, scalar1=s, scalar2=None, op0=ALU.mult),
                 r=[idx, gCb], w=[gI[:, h * 64:(h + 1) * 64]], label=f"gI{h}")
        SS = self.cons[:, 3840:3840 + 1024]
        A.op("vector", lambda e, o=SS: e.memset(o, 0.0), w=[SS], label="ss0")

        bq1 = st[:, 128:136]
        A.dma("gq", bq1, self.bqkv[0:1024].rearrange("(c p) -> p c", p=128),
              label="ld_bq1", allow_slow_non_contiguous=True)
        bk1 = st[:, 136:144]
        A.dma("gq", bk1, self.bqkv[1024:2048].rearrange("(c p) -> p c", p=128),
              label="ld_bk1", allow_slow_non_contiguous=True)
        bk18 = st[:, 144:152]
        A.op("vector", lambda e, i=bk1, o=bk18: e.tensor_scalar(
            out=o, in0=i, scalar1=0.125, scalar2=None, op0=ALU.mult),
            r=[bk1], w=[bk18], label="bk18")
        bkv_row = self.rowb[:, 0:2048]
        A.dma("gq", bkv_row, self.bqkv[1024:3072].rearrange("(one d) -> one d", one=1)
              .broadcast_to([128, 2048]), label="ld_bkv")
        outb_row = self.rowb[:, 2048:3072]
        A.dma("gq", outb_row, self.outb.rearrange("(one d) -> one d", one=1)
              .broadcast_to([128, D]), label="ld_outb")

        wv_blk = self.big1[:, 0:8192]
        A.dma("sq", wv_blk.rearrange("p (dk f) -> p dk f", f=1024),
              self.wqkvT_hbm[:, 2048:3072].rearrange("(dk p) f -> p dk f", p=128),
              label="ld_wvblk")

        vt_b = [self.tile[:, 0:1024], self.tile[:, 2048:3072]]
        kt_b = [self.tile[:, 1024:2048], self.tile[:, 3072:4096]]
        yt = self.scr32[:, 1024:1536].bitcast(BF16)
        pT = self.scr32[:, 1536:1792].bitcast(BF16)
        qt = self.cons[:, 5888:6144]
        for half in range(2):
            if half == 1:
                self.p_prep_late()
            q_h = self.big0[:, 0:8192]
            k_h = self.big0[:, 8192:16384]
            t0 = half * 1024
            for part, dst, bias, scale in (("q", q_h, bq1, None), ("k", k_h, bk18, 0.125)):
                off = 0 if part == "q" else 1024
                for fc in range(ND):
                    wsl = self.big1[:, 8192 + (fc % 2) * 1024: 8192 + (fc % 2) * 1024 + 1024]
                    A.dma("sq", wsl.rearrange("p (dk f) -> p dk f", f=128),
                          self.wqkv_sl[(off // 128) + fc], label=f"qk{half}{part}{fc}")
                    for tt in range(2):
                        ps = self.pf[tt % 2]
                        for dk in range(ND):
                            l = wsl[:, dk * 128:(dk + 1) * 128]
                            r = self.xfmS(dk, t0 + tt * 512, t0 + (tt + 1) * 512)
                            A.op("tensor", lambda e, o=ps, l=l, r=r, s=(dk == 0), p=(dk == ND - 1):
                                 e.matmul(o[:, :], l, r, start=s, stop=p),
                                 r=[l, r], w=[ps[:, :]], label=f"p3m{half}{part}{fc}_{tt}|{dk}")
                        dd = dst[:, fc * 1024 + tt * 512: fc * 1024 + (tt + 1) * 512]
                        if scale is None:
                            A.op("scalar", lambda e, i=ps, o=dd, b=bias[:, fc:fc + 1]:
                                 e.activation(out=o, in_=i[:, :], func=AF.Identity, bias=b),
                                 r=[ps[:, :], bias], w=[dd], label=f"p3e{half}{part}{fc}_{tt}")
                        else:
                            A.op("scalar", lambda e, i=ps, o=dd, b=bias[:, fc:fc + 1], sc=scale:
                                 e.activation(out=o, in_=i[:, :], func=AF.Identity, bias=b, scale=sc),
                                 r=[ps[:, :], bias], w=[dd], label=f"p3e{half}{part}{fc}_{tt}")
            def g1(tci):
                tc = half * 8 + tci
                vt = vt_b[tci % 2]
                kt = kt_b[tci % 2]
                for dk in range(ND):
                    l = self.xfmS(dk, tc * 128, (tc + 1) * 128)
                    for hf in range(2):
                        ps = self.pf[hf]
                        r = wv_blk[:, dk * 1024 + hf * 512: dk * 1024 + (hf + 1) * 512]
                        A.op("tensor", lambda e, o=ps, l=l, r=r, s=(dk == 0), p=(dk == ND - 1):
                             e.matmul(o[:, :], l, r, start=s, stop=p),
                             r=[l, r], w=[ps[:, :]], label=f"vm{tc}_{hf}|{dk}")
                for hf in range(2):
                    dd = vt[:, hf * 512:(hf + 1) * 512]
                    br = bkv_row[:, 1024 + hf * 512:1024 + (hf + 1) * 512]
                    A.op("vector", lambda e, i=self.pf[hf], o=dd, b=br: e.tensor_tensor(
                        out=o, in0=i[:, :], in1=b, op=ALU.add),
                        r=[self.pf[hf][:, :], bkv_row], w=[dd], label=f"ve{tc}_{hf}")
                for k2 in range(8):
                    src = k_h[:, k2 * 1024 + tci * 128:k2 * 1024 + (tci + 1) * 128]
                    out = self.pb[:, k2 * 128:(k2 + 1) * 128]
                    A.op("tensor", lambda e, o=out, s=src: e.transpose(o, s, self.ident),
                         r=[src, self.ident], w=[out], label=f"ktT{tc}_{k2}", tr=True)
                A.op("vector", lambda e, i=self.pb[:, 0:1024], o=kt, g=GkS: e.tensor_tensor(
                    out=o.rearrange("p (h x) -> p h x", x=64),
                    in0=i.rearrange("p (h x) -> p h x", x=64),
                    in1=g.rearrange("p (h one) -> p h one", one=1).broadcast_to([128, 16, 64]),
                    op=ALU.mult),
                    r=[self.pb[:, 0:1024], GkS], w=[kt], label=f"kte{tc}")

            def g2(tci):
                tc = half * 8 + tci
                vt = vt_b[tci % 2]
                kt = kt_b[tci % 2]
                for qd in range(4):
                    ps_s = self.pf[4 + qd % 2]
                    for hh in range(4):
                        h = qd * 4 + hh
                        b = 64 * (h % 2)
                        l = k_h[:, (h // 2) * 1024 + tci * 128:(h // 2) * 1024 + (tci + 1) * 128][b:b + 64, :]
                        r = q_h[:, (h // 2) * 1024 + tci * 128:(h // 2) * 1024 + (tci + 1) * 128][b:b + 64, :]
                        out = ps_s[:, hh * 128:(hh + 1) * 128]
                        A.op("tensor", lambda e, o=out, l=l, r=r: e.matmul(o, l, r, start=True, stop=True),
                             r=[l, r], w=[out], label=f"sT{tc}_{h}")
                    A.op("vector", lambda e, i=ps_s, o=pT, d=D_all[:, qd * 512:(qd + 1) * 512]:
                         e.tensor_tensor(out=o, in0=i[:, :], in1=d, op=ALU.mult),
                         r=[ps_s[:, :], D_all], w=[pT], label=f"pT{tc}_{qd}")
                    q0 = qd * 2
                    A.op("vector", lambda e, o=qt, i0=q_h, g=Gq_all, q0=q0, tci=tci:
                         e.tensor_tensor(
                             out=o.rearrange("p (f x) -> p f x", x=128),
                             in0=i0.rearrange("p (f t) -> p f t", t=1024)[:, q0:q0 + 2, tci * 128:(tci + 1) * 128],
                             in1=g.rearrange("p (f x) -> p f x", x=128)[:, q0:q0 + 2, :],
                             op=ALU.mult),
                         r=[q_h, Gq_all], w=[qt], label=f"qt{tc}_{qd}")
                    for hh in range(4):
                        h = qd * 4 + hh
                        b = 64 * (h % 2)
                        k2 = h // 2
                        ps_y = self.pf[2 + k2 % 2]
                        ycol = (k2 // 2) * 128
                        out_y = ps_y[b:b + 64, ycol:ycol + 128]
                        l1 = vt[:, h * 64:(h + 1) * 64]
                        r1 = pT[:, hh * 128:(hh + 1) * 128]
                        A.op("tensor", lambda e, o=out_y, l=l1, r=r1, tp=(0, b): e.matmul(
                            o, l, r, start=True, stop=False, tile_position=tp),
                            r=[l1, r1], w=[ps_y[:, ycol:ycol + 128]], label=f"y{tc}_{h}|1")
                        l2 = SS[:, h * 64:(h + 1) * 64]
                        r2 = qt[:, (k2 % 2) * 128:(k2 % 2) * 128 + 128]
                        A.op("tensor", lambda e, o=out_y, l=l2, r=r2, tp=(0, b): e.matmul(
                            o, l, r, start=False, stop=True, tile_position=tp),
                            r=[l2, r2], w=[ps_y[:, ycol:ycol + 128]], label=f"y{tc}_{h}|2")
                        ps_ss = self.pf[6]
                        scol = (h // 2) * 64
                        out_s = ps_ss[b:b + 64, scol:scol + 64]
                        l3 = kt[:, h * 64:(h + 1) * 64]
                        r3 = vt[:, h * 64:(h + 1) * 64]
                        A.op("tensor", lambda e, o=out_s, l=l3, r=r3, tp=(0, b): e.matmul(
                            o, l, r, start=True, stop=False, tile_position=tp),
                            r=[l3, r3], w=[ps_ss[:, scol:scol + 64]], label=f"st{tc}_{h}|1")
                        l4 = gI[:, h * 64:(h + 1) * 64]
                        r4 = SS[:, h * 64:(h + 1) * 64]
                        A.op("tensor", lambda e, o=out_s, l=l4, r=r4, tp=(0, b): e.matmul(
                            o, l, r, start=False, stop=True, tile_position=tp),
                            r=[l4, r4], w=[ps_ss[:, scol:scol + 64]], label=f"st{tc}_{h}|2")
                for i2 in range(2):
                    A.op("scalar", lambda e, i=self.pf[2 + i2], o=yt[:, i2 * 512:(i2 + 1) * 512]:
                         e.copy(out=o, in_=i[:, :]),
                         r=[self.pf[2 + i2][:, :]], w=[yt[:, i2 * 512:(i2 + 1) * 512]],
                         label=f"ye{tc}_{i2}")
                for band in range(2):
                    src = self.pf[6][band * 64:(band + 1) * 64, 0:512]
                    dst = SS[band * 64:(band + 1) * 64, :].rearrange(
                        "p (h two x) -> p h two x", two=2, x=64)[:, :, band, :]
                    A.op("scalar", lambda e, i=src, o=dst: e.copy(
                        out=o, in_=i.rearrange("p (h x) -> p h x", x=64)),
                        r=[self.pf[6][:, :]], w=[dst], label=f"se{tc}_{band}")
                ps_o = [self.pf[2], self.pf[3]]
                for k in range(ND):
                    l = yt[:, (k % 2) * 512 + (k // 2) * 128: (k % 2) * 512 + (k // 2) * 128 + 128]
                    for hf in range(2):
                        r = outwT[:, k * D + hf * 512: k * D + (hf + 1) * 512]
                        out = ps_o[hf]
                        A.op("tensor", lambda e, o=out, l=l, r=r, s=(k == 0), p=(k == ND - 1):
                             e.matmul(o[:, :], l, r, start=s, stop=p),
                             r=[l, r], w=[out[:, :]], label=f"op3{tc}_{hf}|{k}")
                self.postln_residual(ps_o, tc, bias_row=outb_row, label=f"pl3_{tc}")

            g1(0)
            for tci in range(8):
                if tci + 1 < 8:
                    g1(tci + 1)
                g2(tci)

    def p4_l1_ffn(self):
        A, st = self.A, self.stat

        xln = self.big0

        def xlnT(tc):
            return xln[:, tc * D:(tc + 1) * D]

        self.ln_into(self.resT, xlnT, "ln4")
        self.transpose_roundtrip(xlnT, "tr4")

        def wdT(hk):
            if hk < 16:
                return self.big0[:, hk * D:(hk + 1) * D]
            return self.big2[:, (hk - 16) * D:(hk - 15) * D]

        for hk in range(24):
            A.dmat("aq", wdT(hk), self.wd_bf[:, hk * 128:(hk + 1) * 128], label=f"wdt{hk}")

        NH = H1F // 128
        for hk in range(NH):
            gsl = self.big1[:, (hk % 2) * 2048:(hk % 2) * 2048 + 1024]
            usl = self.big1[:, (hk % 2) * 2048 + 1024:(hk % 2) * 2048 + 2048]
            A.dma("sq", gsl.rearrange("p (dk f) -> p dk f", f=128),
                  self.wgT_sl[hk], label=f"ga{hk}")
            A.dma("sq", usl.rearrange("p (dk f) -> p dk f", f=128),
                  self.wuT_sl[hk], label=f"ua{hk}")
            hsb = self.big1[:, 4096 + (hk % 2) * 2048:4096 + (hk % 2) * 2048 + 2048]
            for tt in range(4):
                psg = self.pf[tt % 3]
                psu = self.pf[3 + tt % 3]
                for dk in range(ND):
                    r = self.xfmS(dk, tt * 512, (tt + 1) * 512)
                    lg = gsl[:, dk * 128:(dk + 1) * 128]
                    A.op("tensor", lambda e, o=psg, l=lg, r=r, s=(dk == 0), p=(dk == ND - 1):
                         e.matmul(o[:, :], l, r, start=s, stop=p),
                         r=[lg, r], w=[psg[:, :]], label=f"gm{hk}_{tt}|{dk}")
                for dk in range(ND):
                    r = self.xfmS(dk, tt * 512, (tt + 1) * 512)
                    lu = usl[:, dk * 128:(dk + 1) * 128]
                    A.op("tensor", lambda e, o=psu, l=lu, r=r, s=(dk == 0), p=(dk == ND - 1):
                         e.matmul(o[:, :], l, r, start=s, stop=p),
                         r=[lu, r], w=[psu[:, :]], label=f"um{hk}_{tt}|{dk}")
                sil = self.scr32[:, (tt % 2) * 512:(tt % 2) * 512 + 512]
                A.op("scalar", lambda e, i=psg, o=sil: e.activation(
                    out=o, in_=i[:, :], func=AF.Silu, scale=1.0 / 127.0),
                    r=[psg[:, :]], w=[sil], label=f"sil{hk}_{tt}")
                dst = hsb[:, tt * 512:(tt + 1) * 512]
                A.op("vector", lambda e, u=psu, s=sil, o=dst: e.scalar_tensor_tensor(
                    out=o, in0=u[:, :], scalar=1.0 / 127.0, in1=s, op0=ALU.mult, op1=ALU.mult),
                    r=[psu[:, :], sil], w=[dst], label=f"hmul{hk}_{tt}")
            A.dma("aq", self.h2_hbm[hk * 128:(hk + 1) * 128, :], hsb, label=f"h2s{hk}")

        groups = [list(range(g, min(g + 3, NT))) for g in range(0, NT, 3)]
        for gi, tcs in enumerate(groups):
            g0 = tcs[0]
            gw = len(tcs) * 128
            for hk in range(NH):
                s8 = hk % 8
                if s8 < 4:
                    hq = self.big1[:, s8 * 512: s8 * 512 + gw]
                else:
                    hq = self.big1[:, 8192 + (s8 - 4) * 512: 8192 + (s8 - 4) * 512 + gw]
                A.dma(("sq", "gq", "aq")[hk % 3], hq,
                      self.h2_hbm[hk * 128:(hk + 1) * 128, g0 * 128:g0 * 128 + gw],
                      label=f"hb{gi}_{hk}")
                for ti in range(len(tcs)):
                    l = hq[:, ti * 128:(ti + 1) * 128]
                    for hf in range(2):
                        ps = self.pf[2 * ti + hf]
                        r = wdT(hk)[:, hf * 512:(hf + 1) * 512]
                        A.op("tensor", lambda e, o=ps, l=l, r=r, s=(hk == 0), p=(hk == NH - 1):
                             e.matmul(o[:, :], l, r, start=s, stop=p),
                             r=[l, r], w=[ps[:, :]], label=f"dm{gi}_{ti}_{hf}|{hk}")
            for ti, tc in enumerate(tcs):
                self.postln_residual([self.pf[2 * ti], self.pf[2 * ti + 1]], tc,
                                     scale_const=1.0 / 127.0, label=f"pl4_{tc}")

    def p5_store_out(self):
        A = self.A
        for tc in range(NT):
            A.dma("aq", self.y_out[tc * 128:(tc + 1) * 128, :], self.resT(tc),
                  label=f"st_y{tc}")
        A.op("gpsimd", lambda e: e.engine_nop(),
             r=[self.y_out], w=(), label="final_wait")


def build_nc(upto=4):
    nc = bass.Bass("TRN2", target_bir_lowering=False, debug=False)
    with ExitStack() as ctx:
        mk = MK(nc, ctx, upto)
        mk.build()
        block = ctx.enter_context(nc.Block())
        mk.A.build(block, mk.sems)
    return nc


_IN_ORDER = [
    "L0_attn_x", "L0_attn_Wq", "L0_attn_bq", "L0_attn_Wk", "L0_attn_bk",
    "L0_attn_Wv", "L0_attn_bv", "L0_attn_Wo", "L0_attn_bo",
    "L0_ffn_w1", "L0_ffn_b1", "L0_ffn_w2", "L0_ffn_b2",
    "L1_attn_Wqkv_w", "L1_attn_Wqkv_b", "L1_attn_out_w", "L1_attn_out_b",
    "L1_attn_gamma_raw", "L1_ffn_w_gate", "L1_ffn_w_up", "L1_ffn_w_down",
]
_NAME_MAP = {
    "L0_attn_Wq": "Wq", "L0_attn_bq": "bq", "L0_attn_Wk": "Wk", "L0_attn_bk": "bk",
    "L0_attn_Wv": "Wv", "L0_attn_bv": "bv", "L0_attn_Wo": "Wo", "L0_attn_bo": "bo",
    "L0_ffn_w1": "w1", "L0_ffn_b1": "b1", "L0_ffn_w2": "w2", "L0_ffn_b2": "b2",
    "L1_attn_Wqkv_w": "wqkv", "L1_attn_Wqkv_b": "bqkv", "L1_attn_out_w": "outw",
    "L1_attn_out_b": "outb", "L1_attn_gamma_raw": "gamma",
    "L1_ffn_w_gate": "wg", "L1_ffn_w_up": "wu", "L1_ffn_w_down": "wd",
}


def kernel(**inputs):
    nc = build_nc(upto=4)
    consts = host_constants()
    shared = {}
    for k, v in inputs.items():
        if k == "L0_attn_x":
            continue
        arr = np.ascontiguousarray(np.asarray(v))
        shared[_NAME_MAP[k]] = arr
    for k, v in consts.items():
        shared[k] = np.ascontiguousarray(v)
    x = np.ascontiguousarray(np.asarray(inputs["L0_attn_x"], dtype=np.float32))
    in_maps = []
    for b in range(8):
        m = dict(shared)
        m["x"] = x[b]
        in_maps.append(m)
    res = run_bass_kernel_spmd(nc, in_maps, core_ids=list(range(8)))
    out = np.stack([np.asarray(res.results[b]["y"]) for b in range(8)], axis=0)
    return out.astype(np.float32)


# version 32 (final)
# speedup vs baseline: 1.0359x; 1.0075x over previous
import numpy as np
import ml_dtypes
import concourse.bass as bass
import concourse.mybir as mybir
from concourse.bass_utils import run_bass_kernel_spmd
from contextlib import ExitStack

F32 = mybir.dt.float32
BF16 = mybir.dt.bfloat16
I32 = mybir.dt.int32
AF = mybir.ActivationFunctionType
ALU = mybir.AluOpType
AX = mybir.AxisListType

MAGIC = 1.5 * 2.0 ** 23
T = 2048
D = 1024
NT = 16
ND = 8
H0 = 16
HD = 64
NC0 = 32
F1 = 4096
H1F = 3072
EPS_LN = 1e-5
EPS_Q = 1e-8

ENGINES = ("tensor", "vector", "scalar", "gpsimd", "sync")
NQSEM = 8
TT_DROP = True
PREP = True
PREP_W1 = True
PREP_TR = True
PREP_W2 = True
PREP_QK = False
PREP_W2T = True
PSUM_WIDE = True


def _ap_interval(ap):
    tname = type(ap.tensor).__name__
    if PSUM_WIDE and "PSum" in tname:
        return (ap.tensor.name, 0, 1 << 60)
    esz = mybir.dt.size(ap.dtype)
    lo = ap.offset
    hi = ap.offset
    if "DRam" in tname:
        for step, count in ap.ap:
            if count <= 1:
                continue
            ext = step * (count - 1)
            if ext >= 0:
                hi += ext
            else:
                lo += ext
    else:
        tshape = ap.tensor.shape
        pbytes = mybir.dt.size(ap.tensor.dtype)
        for s in tshape[1:]:
            pbytes *= s
        pitch = pbytes // esz
        lo = hi = ap.offset % pitch
        for step, count in ap.ap:
            if count <= 1 or abs(step) >= pitch:
                continue
            ext = step * (count - 1)
            if ext >= 0:
                hi += ext
            else:
                lo += ext
    return (ap.tensor.name, lo * esz, (hi + 1) * esz)


class Op:
    __slots__ = ("engine", "emit", "kind", "queue", "deps", "idx", "inc_id",
                 "flagged", "label", "qidx", "ownsem", "_maxdep", "tr")

    def __init__(self, engine, emit, kind, queue, label):
        self.tr = False
        self.engine = engine
        self.emit = emit
        self.kind = kind
        self.queue = queue
        self.deps = []
        self.idx = -1
        self.qidx = -1
        self.inc_id = None
        self.flagged = False
        self.ownsem = None
        self.label = label


class Asm:

    def __init__(self, nc):
        self.nc = nc
        self.ops = []
        self.eng_ops = {e: [] for e in ENGINES}
        self.queues = {"sq": "sync", "aq": "scalar", "gq": "gpsimd"}
        self.q_ops = {q: [] for q in self.queues}
        self.seg = {}
        self.n_ccsem = 0

    def _touch(self, op, aps, is_write):
        for ap in aps:
            name, lo, hi = _ap_interval(ap)
            segs = self.seg.setdefault(name, [])
            new = []
            deps = op.deps
            if is_write:
                keep = []
                for s in segs:
                    if s[1] <= lo or s[0] >= hi:
                        keep.append(s)
                        continue
                    if s[2] is not None:
                        deps.append(s[2])
                    deps.extend(s[3])
                    if s[0] < lo:
                        keep.append([s[0], lo, s[2], list(s[3])])
                    if s[1] > hi:
                        keep.append([hi, s[1], s[2], list(s[3])])
                keep.append([lo, hi, op, []])
                self.seg[name] = keep
            else:
                for s in segs:
                    if s[1] <= lo or s[0] >= hi:
                        continue
                    if s[2] is not None:
                        deps.append(s[2])
                    s[3].append(op)
                if not segs:
                    segs.append([lo, hi, None, [op]])

    def op(self, engine, emit, r=(), w=(), label="", tr=False):
        o = Op(engine, emit, "c", None, label)
        o.tr = tr
        self._touch(o, r, False)
        self._touch(o, w, True)
        self.ops.append(o)
        self.eng_ops[engine].append(o)
        return o

    def dma(self, queue, out, in_, label="", **kw):
        eng = self.queues[queue]
        o = Op(eng, lambda e, out=out, in_=in_, kw=kw: e.dma_start(out=out, in_=in_, **kw),
               "d", queue, label)
        self._touch(o, [in_], False)
        self._touch(o, [out], True)
        self.ops.append(o)
        self.eng_ops[eng].append(o)
        self.q_ops[queue].append(o)
        return o

    def dmat(self, queue, out, in_, label=""):
        eng = self.queues[queue]
        assert eng in ("sync", "scalar")
        o = Op(eng, lambda e, out=out, in_=in_: e.dma_start_transpose(out, in_),
               "d", queue, label)
        self._touch(o, [in_], False)
        self._touch(o, [out], True)
        last = getattr(self, "_last_dmat", None)
        if last is not None:
            o.deps.append(last)
        self._last_dmat = o
        self.ops.append(o)
        self.eng_ops[eng].append(o)
        self.q_ops[queue].append(o)
        return o

    def cc(self, emit, r=(), w=(), label="cc"):
        o = Op("gpsimd", emit, "cc", None, label)
        self._touch(o, r, False)
        self._touch(o, w, True)
        self.ops.append(o)
        self.eng_ops["gpsimd"].append(o)
        self.n_ccsem += 1
        return o

    def build(self, block, sems):
        for e in ENGINES:
            for i, o in enumerate(self.eng_ops[e]):
                o.idx = i
        for q in self.q_ops:
            for i, o in enumerate(self.q_ops[q]):
                o.qidx = i
        tforce = {}
        prev = None
        for o in self.eng_ops["tensor"]:
            if prev is not None and (prev.tr or o.tr):
                tforce[o.idx] = prev.idx
            prev = o
        ccs = []
        for o in self.ops:
            if o.kind == "cc":
                o.ownsem = sems["cc"][len(ccs)]
                ccs.append(o)
            maxdep = {}
            for d in o.deps:
                if d is o:
                    continue
                if d.kind in ("d", "cc"):
                    continue
                if d.engine == o.engine == "gpsimd":
                    continue
                if (d.engine == o.engine == "tensor" and TT_DROP and
                        "|" in o.label and o.label.split("|")[0] == d.label.split("|")[0]):
                    continue
                if maxdep.get(d.engine, -1) < d.idx:
                    maxdep[d.engine] = d.idx
            if o.engine == "tensor" and o.idx in tforce:
                if maxdep.get("tensor", -1) < tforce[o.idx]:
                    maxdep["tensor"] = tforce[o.idx]
            o._maxdep = maxdep
            for e2, i2 in maxdep.items():
                self.eng_ops[e2][i2].flagged = True
        for e in ENGINES:
            n = 0
            for o in self.eng_ops[e]:
                if o.kind == "c" and o.flagged:
                    n += 1
                    o.inc_id = n
        plans = {e: [] for e in ENGINES}
        waited = {}
        drained = {e: 0 for e in ENGINES}
        for o in self.ops:
            waits = []
            need_drain = False
            for d in o.deps:
                if d is o:
                    continue
                if d.kind == "d":
                    key = ("q", d.queue, d.qidx % NQSEM)
                    v = 16 * (d.qidx // NQSEM + 1)
                    if waited.get((o.engine, key), 0) < v:
                        waits.append((key, v))
                elif d.kind == "cc":
                    key = ("cc", id(d.ownsem))
                    if waited.get((o.engine, key), 0) < 1:
                        waits.append(((key[0], key[1], d.ownsem), 1))
                elif d.engine == o.engine == "gpsimd":
                    if d.idx >= drained[o.engine]:
                        need_drain = True
            for e2, i2 in o._maxdep.items():
                d = self.eng_ops[e2][i2]
                key = ("e", e2)
                v = d.inc_id
                assert v is not None
                if waited.get((o.engine, key), 0) < v:
                    waits.append((key, v))
            if o.kind == "d" and o.qidx >= NQSEM:
                key = ("q", o.queue, o.qidx % NQSEM)
                v = 16 * (o.qidx // NQSEM)
                if waited.get((o.engine, key), 0) < v:
                    waits.append((key, v))
            final_waits = []
            seen = set()
            for key, v in waits:
                k2 = key[:3] if key[0] != "cc" else key[:2]
                if waited.get((o.engine, k2), 0) < v and (k2, v) not in seen:
                    waited[(o.engine, k2)] = v
                    seen.add((k2, v))
                    final_waits.append((key, v))
            if need_drain:
                drained[o.engine] = o.idx
            plans[o.engine].append((final_waits, need_drain, o))
        self._plans = plans
        self._sems = sems

        nc = self.nc
        qsem = sems["q"]
        esem = sems["e"]

        def run_engine(engobj, ename):
            for waits, need_drain, o in plans[ename]:
                for key, v in waits:
                    if key[0] == "q":
                        engobj.wait_ge(qsem[(key[1], key[2])], v)
                    elif key[0] == "e":
                        engobj.wait_ge(esem[key[1]], v)
                    else:
                        engobj.wait_ge(key[2], v)
                if need_drain:
                    engobj.drain()
                inst = o.emit(engobj)
                if o.kind == "d":
                    inst.then_inc(qsem[(o.queue, o.qidx % NQSEM)], 16)
                elif o.kind == "cc":
                    inst.then_inc(o.ownsem, 1)
                elif o.flagged:
                    if ename == "gpsimd":
                        engobj.drain().then_inc(esem[ename], 1)
                    else:
                        inst.then_inc(esem[ename], 1)

        @block.tensor
        def _(e):
            run_engine(e, "tensor")

        @block.vector
        def _(e):
            run_engine(e, "vector")

        @block.scalar
        def _(e):
            run_engine(e, "scalar")

        @block.gpsimd
        def _(e):
            run_engine(e, "gpsimd")

        @block.sync
        def _(e):
            run_engine(e, "sync")



def host_constants():
    c = {}
    c["hc_ident"] = np.eye(128, dtype=ml_dtypes.bfloat16)
    ends = np.arange(0, T, 64)
    starts = np.clip(ends - 63, 0, None)
    lens = (ends - starts + 1).astype(np.float32)
    A = np.zeros((T, NC0), np.float32)
    for ch in range(NC0):
        A[starts[ch]:ends[ch] + 1, ch] = 1.0 / lens[ch]
    c["hc_A"] = A.astype(ml_dtypes.bfloat16)
    msk = np.where(ends[None, :] <= np.arange(T)[:, None], 0.0, -1e30).astype(np.float32)
    c["hc_mask"] = msk
    j = np.arange(128)[:, None]
    i = np.arange(128)[None, :]
    c["hc_mdec"] = np.where(i >= j, (i - j).astype(np.float32), 1.0e4)
    c["hc_gq"] = np.broadcast_to((i + 1).astype(np.float32), (128, 128)).copy()
    c["hc_gk"] = (127 - np.arange(128, dtype=np.float32))[:, None].copy()
    ia = np.zeros((128, 64), ml_dtypes.bfloat16)
    ia[0:64] = np.eye(64)
    ib = np.zeros((128, 64), ml_dtypes.bfloat16)
    ib[64:128] = np.eye(64)
    c["hc_idA"] = ia
    c["hc_idB"] = ib
    return c



class MK:
    def __init__(self, nc, ctx, upto):
        self.nc = nc
        self.ctx = ctx
        self.upto = upto
        e = ctx.enter_context
        self.A = None

        dram = nc.dram_tensor
        self.x_in = dram("x", [T, D], F32, kind="ExternalInput").ap()
        self.y_out = dram("y", [T, D], F32, kind="ExternalOutput").ap()
        names = [("Wq", [D, D]), ("bq", [D]), ("Wk", [D, D]), ("bk", [D]),
                 ("Wv", [D, D]), ("bv", [D]), ("Wo", [D, D]), ("bo", [D]),
                 ("w1", [F1, D]), ("b1", [F1]), ("w2", [D, F1]), ("b2", [D]),
                 ("wqkv", [3 * D, D]), ("bqkv", [3 * D]), ("outw", [D, D]),
                 ("outb", [D]), ("gamma", [16])]
        for n, shp in names:
            setattr(self, n, dram(n, shp, F32, kind="ExternalInput").ap())
        for n, shp in [("wg", [H1F, D]), ("wu", [H1F, D]), ("wd", [D, H1F])]:
            setattr(self, n, dram(n, shp, I32, kind="ExternalInput").ap())
        for n, shp, dt in [("hc_ident", [128, 128], BF16), ("hc_A", [T, NC0], BF16),
                           ("hc_mask", [T, NC0], F32), ("hc_mdec", [128, 128], F32),
                           ("hc_gq", [128, 128], F32), ("hc_gk", [128, 1], F32),
                           ("hc_idA", [128, 64], BF16), ("hc_idB", [128, 64], BF16)]:
            setattr(self, n, dram(n, shp, dt, kind="ExternalInput").ap())

        self.xlnT_hbm = dram("xlnT_hbm", [T, D], BF16).ap()
        self.w1q_hbm = dram("w1q_hbm", [F1, D], BF16).ap()
        self.w1qT_hbm = dram("w1qT_hbm", [D, F1], BF16).ap()
        self.w2q_hbm = dram("w2q_hbm", [D, F1], BF16).ap()
        self.w2qT_hbm = dram("w2qT_hbm", [F1, D], BF16).ap()
        self.h_hbm = dram("h_hbm", [F1, T], BF16).ap()
        self.wqkv_bf = dram("wqkv_bf", [3 * D, D], BF16).ap()
        self.wqkvT_hbm = dram("wqkvT_hbm", [D, 3 * D], BF16).ap()
        self.outw_bf = dram("outw_bf", [D, D], BF16).ap()
        self.wg_bf = dram("wg_bf", [H1F, D], BF16).ap()
        self.wu_bf = dram("wu_bf", [H1F, D], BF16).ap()
        self.wd_bf = dram("wd_bf", [D, H1F], BF16).ap()
        self.wgT_sl = dram("wgT_sl", [24, 128, ND, 128], BF16).ap()
        self.wuT_sl = dram("wuT_sl", [24, 128, ND, 128], BF16).ap()
        self.w1qT_sl = dram("w1qT_sl", [32, 128, ND, 128], BF16).ap()
        self.wqkv_sl = dram("wqkv_sl", [16, 128, ND, 128], BF16).ap()
        self.wq_sl = dram("wq_sl", [8, 128, ND, 128], BF16).ap()
        self.wk_sl = dram("wk_sl", [8, 128, ND, 128], BF16).ap()
        self.wv_sl = dram("wv_sl", [8, 128, ND, 128], BF16).ap()
        self.wdT_hbm = dram("wdT_hbm", [H1F, D], BF16).ap()
        self.h2_hbm = dram("h2_hbm", [H1F, T], BF16).ap()
        self.row_hbm = dram("row_hbm", [8, D], F32).ap()
        self.cc_a = [dram(f"cc_a{i}", [1, 128], F32).ap() for i in range(2)]
        self.cc_b = [dram(f"cc_b{i}", [1, 128], F32).ap() for i in range(2)]

        sb = nc.sbuf_tensor
        self.res = e(sb("res", [128, NT * D], F32))
        self.xfm = e(sb("xfm", [128, ND * T], BF16))
        self.big0 = e(sb("big0", [128, NT * D], BF16))
        self.big1 = e(sb("big1", [128, 10240], BF16))
        self.big2 = e(sb("big2", [128, 8192], BF16))
        self.scr32 = e(sb("scr32", [128, 2048], F32))
        self.tile = e(sb("tile", [128, 4096], BF16))
        self.rowb = e(sb("rowb", [128, 3072], BF16))
        self.cons = e(sb("cons", [128, 6144], BF16))
        self.con32 = e(sb("con32", [128, 800], F32))
        self.stat = e(sb("stat", [128, 256], F32))
        self.pf = [e(nc.psum_tensor(f"pf{i}", [128, 512], F32)) for i in range(7)]
        self.pb = e(nc.psum_tensor("pb", [128, 1024], BF16))

        self.sems = {
            "e": {en: e(nc.semaphore(f"se_{en}")) for en in ENGINES},
            "q": {(q, i): e(nc.semaphore(f"sq_{q}{i}"))
                  for q in ("sq", "aq", "gq") for i in range(NQSEM)},
            "cc": [e(nc.semaphore(f"scc{i}")) for i in range(2)],
        }
        self.A = Asm(nc)

    def resT(self, tc):
        return self.res[:, tc * D:(tc + 1) * D]

    def xfmS(self, dk, t0, t1):
        return self.xfm[:, dk * T + t0: dk * T + t1]

    def ln_into(self, src_of_tc, dst_of_tc, label, nt=NT, track_max=None):
        A, st = self.A, self.stat
        for tc in range(nt):
            src = src_of_tc(tc)
            bn6 = st[:, 32:44]
            for g in range(2):
                A.op("vector", lambda e, s=src, o=bn6, g=g: e.bn_stats(
                    out=o[:, 6 * g:6 * g + 6], in_=s[:, 512 * g:512 * g + 512]),
                    r=[src], w=[bn6[:, 6 * g:6 * g + 6]], label=f"{label}_bnst{tc}_{g}")
            mv = st[:, 44 + 2 * tc: 46 + 2 * tc]
            A.op("vector", lambda e, i=bn6, o=mv: e.bn_aggr(out=o, in_=i),
                 r=[bn6], w=[mv], label=f"{label}_bnag{tc}")
        mvall = st[:, 44:44 + 2 * nt]
        var_v = mvall.rearrange("p (t two) -> p t two", two=2)[:, :, 1:2]
        vareps = st[:, 80:80 + nt]
        A.op("vector", lambda e, i=var_v, o=vareps: e.tensor_scalar(
            out=o, in0=i.rearrange("p t one -> p (t one)"), scalar1=EPS_LN, scalar2=None, op0=ALU.add),
            r=[mvall], w=[vareps], label=f"{label}_veps")
        std = st[:, 96:96 + nt]
        A.op("scalar", lambda e, i=vareps, o=std: e.activation(out=o, in_=i, func=AF.Ln),
             r=[vareps], w=[std], label=f"{label}_ln")
        rstd = st[:, 112:112 + nt]
        A.op("scalar", lambda e, i=std, o=rstd: e.activation(out=o, in_=i, func=AF.Exp, scale=-0.5),
             r=[std], w=[rstd], label=f"{label}_rexp")
        for tc in range(nt):
            src = src_of_tc(tc)
            dst = dst_of_tc(tc)
            m = mvall[:, 2 * tc:2 * tc + 1]
            r = rstd[:, tc:tc + 1]
            A.op("vector", lambda e, s=src, o=dst, m=m, r=r: e.tensor_scalar(
                out=o, in0=s, scalar1=m, scalar2=r, op0=ALU.subtract, op1=ALU.mult),
                r=[src, mvall, rstd], w=[dst], label=f"{label}_nrm{tc}")

    def transpose_roundtrip(self, tm_of_tc, label):
        A = self.A
        hb = self.xlnT_hbm
        for tc in range(NT):
            src = tm_of_tc(tc)
            A.dma("aq", hb[tc * 128:(tc + 1) * 128, :], src, label=f"{label}_st{tc}")
        for dk in range(ND):
            A.dmat("aq", self.xfmS(dk, 0, T), hb[:, dk * 128:(dk + 1) * 128],
                   label=f"{label}_tr{dk}")

    def postln_residual(self, psum_list, tc, bias_row=None, label="",
                        scale_row=None, scale_const=None, then=None):
        A, st = self.A, self.stat
        ao = self.scr32[:, 0:1024]
        for i, ps in enumerate(psum_list):
            dst = ao[:, i * 512:(i + 1) * 512]
            if scale_row is not None:
                sr = scale_row[:, i * 512:(i + 1) * 512]
                A.op("vector", lambda e, p=ps, s=sr, o=dst:
                     e.tensor_tensor(out=o, in0=p[:, :], in1=s, op=ALU.mult),
                     r=[ps[:, :], scale_row], w=[dst], label=f"{label}_s{i}")
                if bias_row is not None:
                    br = bias_row[:, i * 512:(i + 1) * 512]
                    A.op("vector", lambda e, b=br, o=dst:
                         e.tensor_tensor(out=o, in0=o, in1=b, op=ALU.add),
                         r=[dst, bias_row], w=[dst], label=f"{label}_b{i}")
            elif scale_const is not None:
                A.op("vector", lambda e, p=ps, o=dst, c=scale_const:
                     e.tensor_scalar(out=o, in0=p[:, :], scalar1=c, scalar2=None,
                                     op0=ALU.mult),
                     r=[ps[:, :]], w=[dst], label=f"{label}_c{i}")
            elif bias_row is not None:
                br = bias_row[:, i * 512:(i + 1) * 512]
                A.op("vector", lambda e, p=ps, b=br, o=dst:
                     e.tensor_tensor(out=o, in0=p[:, :], in1=b, op=ALU.add),
                     r=[ps[:, :], bias_row], w=[dst], label=f"{label}_b{i}")
            else:
                A.op("vector", lambda e, p=ps, o=dst:
                     e.tensor_copy(out=o, in_=p[:, :]),
                     r=[ps[:, :]], w=[dst], label=f"{label}_cp{i}")
        bn6 = st[:, 40:52]
        for g in range(2):
            A.op("vector", lambda e, s=ao, g=g, o=bn6: e.bn_stats(
                out=o[:, 6 * g:6 * g + 6], in_=s[:, 512 * g:512 * g + 512]),
                r=[ao], w=[bn6[:, 6 * g:6 * g + 6]], label=f"{label}_bn{g}")
        mv = st[:, 52:54]
        A.op("vector", lambda e, i=bn6, o=mv: e.bn_aggr(out=o, in_=i),
             r=[bn6], w=[mv], label=f"{label}_bnag")
        ve = st[:, 54:55]
        A.op("vector", lambda e, m=mv, o=ve: e.tensor_scalar(
            out=o, in0=m[:, 1:2], scalar1=EPS_LN, scalar2=None, op0=ALU.add),
            r=[mv], w=[ve], label=f"{label}_veps")
        sd = st[:, 55:56]
        A.op("scalar", lambda e, i=ve, o=sd: e.activation(out=o, in_=i, func=AF.Ln),
             r=[ve], w=[sd], label=f"{label}_ln")
        rs = st[:, 56:57]
        A.op("scalar", lambda e, i=sd, o=rs: e.activation(out=o, in_=i, func=AF.Exp, scale=-0.5),
             r=[sd], w=[rs], label=f"{label}_rexp")
        dst = self.resT(tc)
        A.op("vector", lambda e, a=ao, m=mv, r=rs: e.tensor_scalar(
            out=a, in0=a, scalar1=m[:, 0:1], scalar2=r, op0=ALU.subtract, op1=ALU.mult),
            r=[ao, mv, rs], w=[ao], label=f"{label}_nrm")
        A.op("vector", lambda e, a=ao, d=dst: e.tensor_tensor(
            out=d, in0=d, in1=a, op=ALU.add),
            r=[ao, dst], w=[dst], label=f"{label}_acc")
        if then is not None:
            then(tc)

    def drip(self, n=2):
        while n > 0 and self._prep_units:
            self._prep_units.pop(0)()
            n -= 1

    def drip_flush(self):
        while self._prep_units:
            self._prep_units.pop(0)()

    def build(self):
        self.p0_load_consts()
        self._prep_units = []
        if PREP:
            self.p_prep_build_units()
        self.p1_l0_attn()
        self.drip_flush()
        if self.upto >= 2:
            self.p2_l0_ffn()
        if self.upto >= 3:
            self.p3_l1_attn()
        if self.upto >= 4:
            self.p4_l1_ffn()
        self.p5_store_out()

    def p0_load_consts(self):
        A = self.A
        xs = self.x_in.rearrange("(a p) d -> p a d", p=128)
        rd = self.res[:, :].rearrange("p (a d) -> p a d", d=D)
        for tc in range(NT):
            A.dma("sq", rd[:, tc:tc + 1, :], xs[:, tc:tc + 1, :], label=f"ld_x{tc}")
        self.ident = self.cons[:, 0:128]
        A.dma("sq", self.ident, self.hc_ident, label="ld_ident")
        self.A_sb = self.cons[:, 128:128 + NT * NC0]
        A.dma("sq", self.A_sb.rearrange("p (a c) -> p a c", c=NC0),
              self.hc_A.rearrange("(a p) c -> p a c", p=128), label="ld_A")
        self.idA = self.cons[:, 640:704]
        self.idB = self.cons[:, 704:768]
        A.dma("sq", self.idA, self.hc_idA, label="ld_idA")
        A.dma("sq", self.idB, self.hc_idB, label="ld_idB")
        self.mask_sb = self.con32[:, 0:NT * NC0]
        A.dma("sq", self.mask_sb.rearrange("p (a c) -> p a c", c=NC0),
              self.hc_mask.rearrange("(a p) c -> p a c", p=128), label="ld_mask")
        self.mdec = self.con32[:, 512:640]
        A.dma("sq", self.mdec, self.hc_mdec, label="ld_mdec")
        self.gq_in = self.con32[:, 640:768]
        A.dma("sq", self.gq_in, self.hc_gq, label="ld_gq")
        self.gk_in = self.con32[:, 768:769]
        A.dma("sq", self.gk_in, self.hc_gk, label="ld_gk")
        self.nmag = self.con32[:, 769:770]
        A.op("vector", lambda e, o=self.nmag: e.memset(o, -MAGIC), w=[self.nmag], label="nmag")
        self.pmag = self.con32[:, 774:775]
        A.op("vector", lambda e, o=self.pmag: e.memset(o, MAGIC), w=[self.pmag], label="pmag")


    def p_prep_build_units(self):
        A, st = self.A, self.stat
        b1 = self.big1
        U = self._prep_units
        fsl = [b1[:, 4096:6144].bitcast(F32), b1[:, 6144:8192].bitcast(F32)]
        osl = [b1[:, 8192:9216], b1[:, 9216:10240]]
        NF1 = F1 // 128

        sw1 = st[:, 0:32]
        sw2 = st[:, 224:232]
        am = self.con32[:, 770:771]
        rc = self.con32[:, 771:772]
        a4 = self.con32[:, 772:773]

        def w1_load(fc):
            A.dma("sq", fsl[fc % 2], self.w1[fc * 128:(fc + 1) * 128, :], label=f"w1l{fc}")

        def w1_comp(fc):
            fslot = fsl[fc % 2]
            oslot = osl[fc % 2]
            A.op("vector", lambda e, i=fslot, o=am: e.reduce_max(
                out=o, in_=i, axis=AX.X, apply_absolute_value=True),
                r=[fslot], w=[am], label=f"w1am{fc}")
            A.op("vector", lambda e, o=am: e.tensor_scalar(
                out=o, in0=o, scalar1=EPS_Q, scalar2=None, op0=ALU.max),
                r=[am], w=[am], label=f"w1amc{fc}")
            A.op("vector", lambda e, i=am, o=rc: e.reciprocal(out=o, in_=i),
                 r=[am], w=[rc], label=f"w1rc{fc}")
            A.op("vector", lambda e, o=rc: e.tensor_scalar(
                out=o, in0=o, scalar1=127.0, scalar2=None, op0=ALU.mult),
                r=[rc], w=[rc], label=f"w1rc2{fc}")
            A.op("vector", lambda e, i=am, o=sw1[:, fc:fc + 1]: e.tensor_scalar(
                out=o, in0=i, scalar1=1.0 / 127.0, scalar2=None, op0=ALU.mult),
                r=[am], w=[sw1[:, fc:fc + 1]], label=f"w1sw{fc}")
            self.magic_round(oslot, fslot, rc[:, 0:1], False, f"w1q{fc}", tmp=fslot)
            A.dma("sq", self.w1q_hbm[fc * 128:(fc + 1) * 128, :], oslot, label=f"w1s{fc}")

        U.append(lambda: (w1_load(0), w1_load(1)))
        for fc in range(NF1):
            U.append(lambda fc=fc: (w1_comp(fc),
                                    w1_load(fc + 2) if fc + 2 < NF1 else None))

        def w1t_unit(dk, hh):
            stg2 = b1[:, 4096 + (hh % 2) * 2048: 4096 + (hh % 2) * 2048 + 2048]
            A.dmat("sq", stg2, self.w1q_hbm[hh * 2048:(hh + 1) * 2048,
                                            dk * 128:(dk + 1) * 128], label=f"w1t{dk}_{hh}")
            A.dma("gq", self.w1qT_sl[hh * 16:(hh + 1) * 16, :, dk, :]
                  .rearrange("fc p f -> p fc f"),
                  stg2.rearrange("p (fc f) -> p fc f", f=128), label=f"w1ts{dk}_{hh}")

        for dk in range(ND):
            for hh in range(2):
                U.append(lambda dk=dk, hh=hh: w1t_unit(dk, hh))

        def w2_maxpass(dc):
            for q4 in range(4):
                fslot = fsl[q4 % 2]
                A.dma("aq", fslot, self.w2[dc * 128:(dc + 1) * 128, q4 * 1024:(q4 + 1) * 1024],
                      label=f"w2l{dc}_{q4}")
                A.op("vector", lambda e, i=fslot, o=a4: e.reduce_max(
                    out=o, in_=i, axis=AX.X, apply_absolute_value=True),
                    r=[fslot], w=[a4], label=f"w2a{dc}_{q4}")
                if q4 == 0:
                    A.op("vector", lambda e, i=a4, o=am: e.tensor_copy(out=o, in_=i),
                         r=[a4], w=[am], label=f"w2c{dc}")
                else:
                    A.op("vector", lambda e, i=a4, o=am: e.tensor_tensor(
                        out=o, in0=o, in1=i, op=ALU.max), r=[a4, am], w=[am], label=f"w2m{dc}_{q4}")
            A.op("vector", lambda e, o=am: e.tensor_scalar(
                out=o, in0=o, scalar1=EPS_Q, scalar2=None, op0=ALU.max),
                r=[am], w=[am], label=f"w2mc{dc}")
            A.op("vector", lambda e, i=am, o=rc: e.reciprocal(out=o, in_=i),
                 r=[am], w=[rc], label=f"w2rc{dc}")
            A.op("vector", lambda e, o=rc: e.tensor_scalar(
                out=o, in0=o, scalar1=127.0, scalar2=None, op0=ALU.mult),
                r=[rc], w=[rc], label=f"w2rc2{dc}")
            A.op("vector", lambda e, i=am, o=sw2[:, dc:dc + 1]: e.tensor_scalar(
                out=o, in0=i, scalar1=1.0 / 127.0, scalar2=None, op0=ALU.mult),
                r=[am], w=[sw2[:, dc:dc + 1]], label=f"w2sw{dc}")

        def w2_quant(dc, q4):
            fslot = fsl[q4 % 2]
            oslot = osl[q4 % 2]
            A.dma("aq", fslot, self.w2[dc * 128:(dc + 1) * 128, q4 * 1024:(q4 + 1) * 1024],
                  label=f"w2l2{dc}_{q4}")
            self.magic_round(oslot, fslot, rc[:, 0:1], False, f"w2q{dc}_{q4}", tmp=fslot)
            A.dma("aq", self.w2q_hbm[dc * 128:(dc + 1) * 128, q4 * 1024:(q4 + 1) * 1024],
                  oslot, label=f"w2s{dc}_{q4}")

        for dc in range(ND):
            U.append(lambda dc=dc: w2_maxpass(dc))
            U.append(lambda dc=dc: (w2_quant(dc, 0), w2_quant(dc, 1)))
            U.append(lambda dc=dc: (w2_quant(dc, 2), w2_quant(dc, 3)))

        def w2t_unit(fc):
            stg2 = b1[:, 4096 + (fc % 2) * 2048: 4096 + (fc % 2) * 2048 + 1024]
            A.dmat("aq", stg2, self.w2q_hbm[:, fc * 128:(fc + 1) * 128], label=f"w2t{fc}")
            A.dma("gq", self.w2qT_hbm[fc * 128:(fc + 1) * 128, :], stg2, label=f"w2ts{fc}")

        for fc in range(NF1):
            U.append(lambda fc=fc: w2t_unit(fc))

        def cast_unit(nm, dst, src_, c0, c1):
            for c in range(c0, c1):
                A.dma("gq", dst[c * 128:(c + 1) * 128, :], src_[c * 128:(c + 1) * 128, :],
                      label=f"{nm}_cast{c}")

        for c in range(0, 24, 4):
            U.append(lambda c=c: cast_unit("qk", self.wqkv_bf, self.wqkv, c, c + 4))

        def qkt_unit(dk, hh):
            stg2 = b1[:, 4096 + (hh % 2) * 2048: 4096 + (hh % 2) * 2048 + 1536]
            A.dmat("aq", stg2, self.wqkv_bf[hh * 1536:(hh + 1) * 1536,
                                            dk * 128:(dk + 1) * 128], label=f"qkt{dk}_{hh}")
            A.dma("gq", self.wqkvT_hbm[dk * 128:(dk + 1) * 128, hh * 1536:(hh + 1) * 1536],
                  stg2, label=f"qkts{dk}_{hh}")
            nsl = 12 if hh == 0 else 4
            A.dma("gq", self.wqkv_sl[hh * 12:hh * 12 + nsl, :, dk, :]
                  .rearrange("fc p f -> p fc f"),
                  stg2[:, 0:nsl * 128].rearrange("p (fc f) -> p fc f", f=128),
                  label=f"qksl{dk}_{hh}")

        for dk in range(ND):
            for hh in range(2):
                U.append(lambda dk=dk, hh=hh: qkt_unit(dk, hh))
        for c in range(0, 8, 4):
            U.append(lambda c=c: cast_unit("ow", self.outw_bf, self.outw, c, c + 4))
        for c in range(0, 24, 4):
            U.append(lambda c=c: cast_unit("wg", self.wg_bf, self.wg, c, c + 4))
        for c in range(0, 24, 4):
            U.append(lambda c=c: cast_unit("wu", self.wu_bf, self.wu, c, c + 4))
        for c in range(0, 8, 4):
            U.append(lambda c=c: cast_unit("wd", self.wd_bf, self.wd, c, c + 4))

    def p_prep_late(self):
        A = self.A
        b1 = self.big1
        for nm, w_bf, w_sl in (("wg", self.wg_bf, self.wgT_sl),
                               ("wu", self.wu_bf, self.wuT_sl)):
            for dk in range(ND):
                for hh in range(2):
                    stg2 = b1[:, 8192:9728] if hh == 0 else b1[:, 8704:10240]
                    A.dmat("sq", stg2, w_bf[hh * 1536:(hh + 1) * 1536,
                                            dk * 128:(dk + 1) * 128], label=f"{nm}t{dk}_{hh}")
                    A.dma("sq", w_sl[hh * 12:(hh + 1) * 12, :, dk, :]
                          .rearrange("fc p f -> p fc f"),
                          stg2.rearrange("p (fc f) -> p fc f", f=128),
                          label=f"{nm}ts{dk}_{hh}")

    def p1_l0_attn(self):
        A, nc = self.A, self.nc
        xln = self.big0

        def xlnT(tc):
            return xln[:, tc * D:(tc + 1) * D]

        self.ln_into(self.resT, xlnT, "ln1")
        self.transpose_roundtrip(xlnT, "tr1")

        ps = self.pf[0]
        summ_ps = ps[:, 0:256].rearrange("p (dk c) -> p dk c", c=NC0)
        for dk in range(ND):
            self.drip(1)
            for tc in range(NT):
                lhsT = xlnT(tc)[:, dk * 128:(dk + 1) * 128]
                rhs = self.A_sb[:, tc * NC0:(tc + 1) * NC0]
                out = summ_ps[:, dk, :]
                A.op("tensor", lambda e, o=out, l=lhsT, r=rhs, s=(tc == 0), p=(tc == NT - 1):
                     e.matmul(o, l, r, start=s, stop=p),
                     r=[lhsT, rhs], w=[ps[:, dk * 32:(dk + 1) * 32]], label=f"summ_{dk}|{tc}")
        summ = self.tile[:, 0:256]
        A.op("vector", lambda e, i=ps[:, 0:256], o=summ: e.tensor_copy(out=o, in_=i),
             r=[ps[:, 0:256]], w=[summ], label="summ_ev")

        bq_s = self.stat[:, 128:136]
        bk_s = self.stat[:, 136:144]
        A.dma("gq", bq_s, self.bq.rearrange("(c p) -> p c", p=128), label="ld_bq", allow_slow_non_contiguous=True)
        A.dma("gq", bk_s, self.bk.rearrange("(c p) -> p c", p=128), label="ld_bk", allow_slow_non_contiguous=True)
        bk8 = self.stat[:, 144:152]
        A.op("vector", lambda e, i=bk_s, o=bk8: e.tensor_scalar(
            out=o, in0=i, scalar1=0.125, scalar2=None, op0=ALU.mult),
            r=[bk_s], w=[bk8], label="bk8")
        bv_row = self.rowb[:, 0:1024]
        bo_row = self.rowb[:, 1024:2048]
        A.dma("gq", bv_row, self.bv.rearrange("(one d) -> one d", one=1).broadcast_to([128, D]), label="ld_bv")
        A.dma("gq", bo_row, self.bo.rearrange("(one d) -> one d", one=1).broadcast_to([128, D]), label="ld_bo")

        kfm = self.tile[:, 256:512]
        wk_sl = self.big1[:, 0:4096]
        self._p1slot = 0

        def wslot():
            s = self._p1slot % 4
            self._p1slot += 1
            return wk_sl[:, s * 1024:(s + 1) * 1024]

        for fc in range(ND):
            self.drip(2)
            buf = wslot()
            A.dma("gq", buf.rearrange("p (dk f) -> p dk f", f=128),
                  self.Wk[:, fc * 128:(fc + 1) * 128].rearrange("(dk p) f -> p dk f", p=128),
                  label=f"ld_wk{fc}")
            out = self.pf[1][:, 0:32]
            for dk in range(ND):
                l = buf[:, dk * 128:(dk + 1) * 128]
                r = summ[:, dk * 32:(dk + 1) * 32]
                A.op("tensor", lambda e, o=out, l=l, r=r, s=(dk == 0), p=(dk == ND - 1):
                     e.matmul(o, l, r, start=s, stop=p),
                     r=[l, r], w=[out], label=f"kmm{fc}|{dk}")
            o2 = kfm[:, fc * 32:(fc + 1) * 32]
            A.op("scalar", lambda e, i=out, o=o2, b=bk8[:, fc:fc + 1]: e.activation(
                out=o, in_=i, func=AF.Identity, bias=b, scale=0.125),
                r=[out, bk8], w=[o2], label=f"kev{fc}")

        vpad = self.tile[:, 512:1536]
        A.op("vector", lambda e, o=vpad: e.memset(o, 0.0), w=[vpad], label="vpad0")
        vbufs = []
        for fc in range(ND):
            self.drip(2)
            buf = wslot()
            vbufs.append(buf)
            A.dma("gq", buf.rearrange("p (dk f) -> p dk f", f=128),
                  self.Wv[:, fc * 128:(fc + 1) * 128].rearrange("(dk p) f -> p dk f", p=128),
                  label=f"ld_wv{fc}")
            for hh in range(2):
                h = 2 * fc + hh
                b = 32 * (h % 4)
                pcol = (h % 8) * 64
                out = self.pf[2][b:b + 32, pcol:pcol + 64]
                for dk in range(ND):
                    l = summ[:, dk * 32:(dk + 1) * 32]
                    r = buf[:, dk * 128 + hh * 64: dk * 128 + hh * 64 + 64]
                    A.op("tensor", lambda e, o=out, l=l, r=r, s=(dk == 0), p=(dk == ND - 1),
                         tp=(0, b): e.matmul(o, l, r, start=s, stop=p, tile_position=tp),
                         r=[l, r], w=[self.pf[2][:, pcol:pcol + 64]], label=f"vmm{h}|{dk}")
                dst = vpad[:, h * 64:(h + 1) * 64][b:b + 32, :]
                bvs = bv_row[b:b + 32, h * 64:(h + 1) * 64]
                A.op("vector", lambda e, i=out, o=dst, b2=bvs: e.tensor_tensor(
                    out=o, in0=i, in1=b2, op=ALU.add),
                    r=[self.pf[2][:, pcol:pcol + 64], bv_row], w=[dst], label=f"vev{h}")

        q = self.big0
        for fc in range(ND):
            self.drip(2)
            buf = wslot()
            A.dma("gq", buf.rearrange("p (dk f) -> p dk f", f=128),
                  self.Wq[:, fc * 128:(fc + 1) * 128].rearrange("(dk p) f -> p dk f", p=128),
                  label=f"ld_wq{fc}")
            for tt in range(4):
                self.drip(1)
                ps_t = self.pf[3 + (tt % 2)]
                for dk in range(ND):
                    l = buf[:, dk * 128:(dk + 1) * 128]
                    r = self.xfmS(dk, tt * 512, (tt + 1) * 512)
                    A.op("tensor", lambda e, o=ps_t[:, :], l=l, r=r, s=(dk == 0), p=(dk == ND - 1):
                         e.matmul(o, l, r, start=s, stop=p),
                         r=[l, r], w=[ps_t[:, :]], label=f"qmm{fc}_{tt}|{dk}")
                dst = q[:, fc * T + tt * 512: fc * T + (tt + 1) * 512]
                A.op("scalar", lambda e, i=ps_t, o=dst, b=bq_s[:, fc:fc + 1]: e.activation(
                    out=o, in_=i[:, :], func=AF.Identity, bias=b),
                    r=[ps_t[:, :], bq_s], w=[dst], label=f"qev{fc}_{tt}")

        wo_sb = self.big2[:, 0:8192]
        A.dma("gq", wo_sb.rearrange("p (dk f) -> p dk f", f=D),
              self.Wo.rearrange("(dk p) f -> p dk f", p=128), label="ld_wo")

        at_b = [self.tile[:, 1536:2048], self.tile[:, 3584:4096]]
        atT = self.tile[:, 2048:2560]
        ytile = self.tile[:, 2560:3584]

        def emit_s1(tc):
            ps_sc = self.pf[5] if tc % 2 == 0 else self.pf[3]
            for h in range(H0):
                b = 64 * (h % 2)
                l = q[:, (h // 2) * T + tc * 128:(h // 2) * T + tc * 128 + 128][b:b + 64, :]
                r = kfm[:, (h // 2) * 32:(h // 2) * 32 + 32][b:b + 64, :]
                out = ps_sc[:, h * 32:(h + 1) * 32]
                A.op("tensor", lambda e, o=out, l=l, r=r: e.matmul(o, l, r, start=True, stop=True),
                     r=[l, r], w=[out], label=f"sc{tc}_{h}")
            scf = self.scr32[:, 1024:1536]
            mk = self.mask_sb[:, tc * NC0:(tc + 1) * NC0]
            A.op("vector", lambda e, i=ps_sc, m=mk, o=scf: e.tensor_tensor(
                out=o.rearrange("p (h c) -> p h c", c=NC0),
                in0=i[:, :].rearrange("p (h c) -> p h c", c=NC0),
                in1=m.rearrange("p (one c) -> p one c", one=1).broadcast_to([128, H0, NC0]), op=ALU.add),
                r=[ps_sc[:, :], self.mask_sb], w=[scf], label=f"msk{tc}")
            mx = self.stat[:, 160:176]
            A.op("vector", lambda e, i=scf, o=mx: e.tensor_reduce(
                out=o, in_=i.rearrange("p (h c) -> p h c", c=NC0), axis=AX.X, op=ALU.max),
                r=[scf], w=[mx], label=f"mx{tc}")
            A.op("vector", lambda e, i=scf, m=mx, o=scf: e.tensor_tensor(
                out=o.rearrange("p (h c) -> p h c", c=NC0),
                in0=i.rearrange("p (h c) -> p h c", c=NC0),
                in1=m.rearrange("p (h one) -> p h one", one=1).broadcast_to([128, H0, NC0]), op=ALU.subtract),
                r=[scf, mx], w=[scf], label=f"sub{tc}")
            A.op("scalar", lambda e, i=scf, o=scf: e.activation(out=o, in_=i, func=AF.Exp),
                 r=[scf], w=[scf], label=f"exp{tc}")
            sm = self.stat[:, 176:192]
            A.op("vector", lambda e, i=scf, o=sm: e.tensor_reduce(
                out=o, in_=i.rearrange("p (h c) -> p h c", c=NC0), axis=AX.X, op=ALU.add),
                r=[scf], w=[sm], label=f"sm{tc}")
            rs = self.stat[:, 192:208]
            A.op("vector", lambda e, i=sm, o=rs: e.reciprocal(out=o, in_=i),
                 r=[sm], w=[rs], label=f"rs{tc}")
            at = at_b[tc % 2]
            A.op("vector", lambda e, i=scf, r2=rs, o=at: e.tensor_tensor(
                out=o.rearrange("p (h c) -> p h c", c=NC0),
                in0=i.rearrange("p (h c) -> p h c", c=NC0),
                in1=r2.rearrange("p (h one) -> p h one", one=1).broadcast_to([128, H0, NC0]), op=ALU.mult),
                r=[scf, rs], w=[at], label=f"at{tc}")

        def emit_s2(tc):
            at = at_b[tc % 2]
            for qd in range(4):
                src = at[:, qd * 128:(qd + 1) * 128]
                out = self.pb[:, qd * 128:(qd + 1) * 128]
                A.op("tensor", lambda e, o=out, s=src: e.transpose(o, s, self.ident),
                     r=[src, self.ident], w=[out], label=f"atT{tc}_{qd}", tr=True)
            A.op("scalar", lambda e, i=self.pb[:, 0:512], o=atT: e.copy(out=o, in_=i),
                 r=[self.pb[:, 0:512]], w=[atT], label=f"atTev{tc}")
            ps_y = [self.pf[6], self.pf[0]]
            for h in range(H0):
                b = 64 * (h % 2)
                l = vpad[:, h * 64:(h + 1) * 64]
                r = atT[:, (h // 4) * 128:(h // 4) * 128 + 128]
                bank = ps_y[(h // 2) % 2]
                col = ((h // 2) // 2) * 128
                out = bank[b:b + 64, col:col + 128]
                A.op("tensor", lambda e, o=out, l=l, r=r, tp=(0, b): e.matmul(
                    o, l, r, start=True, stop=True, tile_position=tp),
                    r=[l, r], w=[bank[:, col:col + 128]], label=f"av{tc}_{h}")
            for i2 in range(2):
                A.op("scalar", lambda e, i=ps_y[i2], o=ytile[:, i2 * 512:(i2 + 1) * 512]:
                     e.copy(out=o, in_=i[:, :]),
                     r=[ps_y[i2][:, :]], w=[ytile[:, i2 * 512:(i2 + 1) * 512]], label=f"yev{tc}_{i2}")
            ps_o = [self.pf[1], self.pf[2]]
            for k in range(ND):
                l = ytile[:, (k % 2) * 512 + (k // 2) * 128: (k % 2) * 512 + (k // 2) * 128 + 128]
                for half in range(2):
                    r = wo_sb[:, k * D + half * 512: k * D + (half + 1) * 512]
                    out = ps_o[half]
                    A.op("tensor", lambda e, o=out, l=l, r=r, s=(k == 0), p=(k == ND - 1):
                         e.matmul(o[:, :], l, r, start=s, stop=p),
                         r=[l, r], w=[out[:, :]], label=f"op{tc}_{half}|{k}")
            self.postln_residual(ps_o, tc, bo_row, f"pl1_{tc}",
                                 then=lambda tc2: self.ln_tc(
                                     tc2, self.xfm[:, tc2 * D:(tc2 + 1) * D], "ln2",
                                     colmax=self.stat[:, 208:224]))

        for tc in range(NT):
            emit_s1(tc)
            self.drip(2)
            if tc > 0:
                emit_s2(tc - 1)
                self.drip(2)
        emit_s2(NT - 1)

    def global_max_collective(self, m128, idx, out_col, label):
        A = self.A
        cca, ccb = self.cc_a[idx], self.cc_b[idx]
        A.dma("gq", cca.rearrange("one p -> p one"), m128, label=f"{label}_st")
        A.cc(lambda e, a=cca, b=ccb: e.collective_compute(
            "AllReduce", ALU.max, replica_groups=[list(range(8))],
            ins=[a], outs=[b]), r=[cca], w=[ccb], label=f"{label}_cc")
        gall = self.scr32[:, 0:128]
        A.dma("gq", gall, ccb[0:1, :].broadcast_to([128, 128]), label=f"{label}_ld")
        A.op("vector", lambda e, i=gall, o=out_col: e.reduce_max(out=o, in_=i, axis=AX.X),
             r=[gall], w=[out_col], label=f"{label}_mx")

    def magic_round(self, out_bf, in_ap, inv_ap, relu, label, tmp=None):
        A = self.A
        if tmp is None:
            tmp = self.scr32[:, 0:in_ap.shape[-1]]
        A.op("vector", lambda e, i=in_ap, o=tmp, s=inv_ap: e.tensor_scalar(
            out=o, in0=i, scalar1=s, scalar2=MAGIC, op0=ALU.mult, op1=ALU.add),
            r=[in_ap, inv_ap], w=[tmp], label=f"{label}_a")
        fn = AF.Relu if relu else AF.Identity
        A.op("scalar", lambda e, i=tmp, o=out_bf, fn=fn, b=self.nmag: e.activation(
            out=o, in_=i, func=fn, bias=b),
            r=[tmp, self.nmag], w=[out_bf], label=f"{label}_b")

    def ln_tc(self, tc, dst, label, colmax=None):
        A, st = self.A, self.stat
        src = self.resT(tc)
        bn6 = st[:, 64 + (tc % 2) * 16:64 + (tc % 2) * 16 + 12]
        for g in range(2):
            A.op("vector", lambda e, s=src, o=bn6, g=g: e.bn_stats(
                out=o[:, 6 * g:6 * g + 6], in_=s[:, 512 * g:512 * g + 512]),
                r=[src], w=[bn6[:, 6 * g:6 * g + 6]], label=f"{label}_bnst{tc}_{g}")
        mv = st[:, 96 + (tc % 2) * 2:96 + (tc % 2) * 2 + 2]
        A.op("vector", lambda e, i=bn6, o=mv: e.bn_aggr(out=o, in_=i),
             r=[bn6], w=[mv], label=f"{label}_bnag{tc}")
        ve = st[:, 100 + (tc % 2):101 + (tc % 2)]
        A.op("vector", lambda e, i=mv, o=ve: e.tensor_scalar(
            out=o, in0=i[:, 1:2], scalar1=EPS_LN, scalar2=None, op0=ALU.add),
            r=[mv], w=[ve], label=f"{label}_veps{tc}")
        A.op("scalar", lambda e, o=ve: e.activation(out=o, in_=o, func=AF.Ln),
             r=[ve], w=[ve], label=f"{label}_ln{tc}")
        rstd = st[:, 102 + (tc % 2):103 + (tc % 2)]
        A.op("scalar", lambda e, i=ve, o=rstd: e.activation(out=o, in_=i, func=AF.Exp, scale=-0.5),
             r=[ve], w=[rstd], label=f"{label}_rexp{tc}")
        A.op("vector", lambda e, s=src, o=dst, m=mv[:, 0:1], r2=rstd: e.tensor_scalar(
            out=o, in0=s, scalar1=m, scalar2=r2, op0=ALU.subtract, op1=ALU.mult),
            r=[src, mv, rstd], w=[dst], label=f"{label}_nrm{tc}")
        A.dma("aq", self.xlnT_hbm[tc * 128:(tc + 1) * 128, :], dst, label=f"{label}_st{tc}")
        if colmax is not None:
            A.op("vector", lambda e, i=dst, o=colmax[:, tc:tc + 1]:
                 e.reduce_max(out=o, in_=i, axis=AX.X),
                 r=[dst], w=[colmax[:, tc:tc + 1]], label=f"{label}_cm{tc}")

    def p2_l0_ffn(self):
        A, st = self.A, self.stat
        xln = self.big0

        def xlnT(tc):
            return xln[:, tc * D:(tc + 1) * D]

        for dk in range(ND):
            A.dmat("aq", self.xfmS(dk, 0, T), self.xlnT_hbm[:, dk * 128:(dk + 1) * 128],
                   label=f"tr2_tr{dk}")
        colmax = st[:, 208:224]
        m128 = st[:, 232:233]
        A.op("vector", lambda e, i=colmax, o=m128: e.reduce_max(out=o, in_=i, axis=AX.X),
             r=[colmax], w=[m128], label="m128")
        A.op("vector", lambda e, o=m128: e.tensor_scalar(
            out=o, in0=o, scalar1=EPS_Q, scalar2=None, op0=ALU.max),
            r=[m128], w=[m128], label="m128c")
        gmax = st[:, 233:234]
        self.global_max_collective(m128, 0, gmax, "gmx0")
        inv_s = st[:, 234:235]
        s_q = st[:, 235:236]
        A.op("vector", lambda e, i=gmax, o=inv_s: e.reciprocal(out=o, in_=i),
             r=[gmax], w=[inv_s], label="invs0")
        A.op("vector", lambda e, o=inv_s: e.tensor_scalar(
            out=o, in0=o, scalar1=255.0, scalar2=None, op0=ALU.mult),
            r=[inv_s], w=[inv_s], label="invs1")
        A.op("vector", lambda e, i=gmax, o=s_q: e.tensor_scalar(
            out=o, in0=i, scalar1=1.0 / 255.0, scalar2=None, op0=ALU.mult),
            r=[gmax], w=[s_q], label="sq0")

        xq = self.big0
        for dk in range(ND):
            self.magic_round(xq[:, dk * T:(dk + 1) * T], self.xfmS(dk, 0, T),
                             inv_s[:, 0:1], True, f"xq{dk}",
                             tmp=self.scr32[:, 0:T])

        sw1 = self.stat[:, 0:32]
        NF1 = F1 // 128
        A.op("vector", lambda e, o=sw1, s=s_q[:, 0:1]: e.tensor_scalar(
            out=o, in0=o, scalar1=s, scalar2=None, op0=ALU.mult),
            r=[sw1, s_q], w=[sw1], label="ssw1")
        b1s = st[:, 128:160]
        A.dma("gq", b1s, self.b1.rearrange("(c p) -> p c", p=128), label="ld_b1",
              allow_slow_non_contiguous=True)
        rcp = st[:, 192:224]
        A.op("vector", lambda e, i=sw1, o=rcp: e.reciprocal(out=o, in_=i),
             r=[sw1], w=[rcp], label="b1rcp")
        b1q = st[:, 160:192]
        A.op("vector", lambda e, b=b1s, r=rcp, o=b1q: e.tensor_tensor(
            out=o, in0=b, in1=r, op=ALU.mult), r=[b1s, rcp], w=[b1q], label="b1t")
        A.op("vector", lambda e, o=b1q: e.tensor_scalar(
            out=o, in0=o, scalar1=MAGIC, scalar2=MAGIC, op0=ALU.add, op1=ALU.subtract),
            r=[b1q], w=[b1q], label="b1r")
        A.op("vector", lambda e, o=b1q, s=sw1: e.tensor_tensor(
            out=o, in0=o, in1=s, op=ALU.mult), r=[b1q, sw1], w=[b1q], label="b1m")

        hm = st[:, 192:224]
        for fc in range(NF1):
            wsl = self.big1[:, 2048 + (fc % 2) * 1024: 2048 + (fc % 2) * 1024 + 1024]
            A.dma("sq", wsl.rearrange("p (dk f) -> p dk f", f=128),
                  self.w1qT_sl[fc], label=f"m1w{fc}")
            if 8 <= fc < 16:
                w2c = 16 + (fc - 8)
                A.dma("sq", self.big2[:, (fc - 8) * 1024:(fc - 7) * 1024],
                      self.w2qT_hbm[w2c * 128:(w2c + 1) * 128, :], label=f"w2r{w2c}")
            hfc = self.tile[:, (fc % 2) * 2048:(fc % 2) * 2048 + 2048]
            for tt in range(4):
                ps = self.pf[2 * (fc % 2) + (tt % 2)]
                for dk in range(ND):
                    l = wsl[:, dk * 128:(dk + 1) * 128]
                    r = xq[:, dk * T + tt * 512: dk * T + (tt + 1) * 512]
                    A.op("tensor", lambda e, o=ps, l=l, r=r, s=(dk == 0), p=(dk == ND - 1):
                         e.matmul(o[:, :], l, r, start=s, stop=p),
                         r=[l, r], w=[ps[:, :]], label=f"m1_{fc}_{tt}|{dk}")
                dst = hfc[:, tt * 512:(tt + 1) * 512]
                A.op("scalar", lambda e, i=ps, o=dst, s=sw1[:, fc:fc + 1], b=b1q[:, fc:fc + 1]:
                     e.activation(out=o, in_=i[:, :], func=AF.Relu, scale=s, bias=b),
                     r=[ps[:, :], sw1, b1q], w=[dst], label=f"m1e_{fc}_{tt}")
            A.op("vector", lambda e, i=hfc, o=hm[:, fc:fc + 1]: e.reduce_max(
                out=o, in_=i, axis=AX.X), r=[hfc], w=[hm[:, fc:fc + 1]], label=f"hm{fc}")
            A.dma("aq", self.h_hbm[fc * 128:(fc + 1) * 128, :], hfc, label=f"hs{fc}")

        hmax = st[:, 236:237]
        A.op("vector", lambda e, i=hm, o=hmax: e.reduce_max(out=o, in_=i, axis=AX.X),
             r=[hm], w=[hmax], label="hmax")
        A.op("vector", lambda e, o=hmax: e.tensor_scalar(
            out=o, in0=o, scalar1=EPS_Q, scalar2=None, op0=ALU.max),
            r=[hmax], w=[hmax], label="hmaxc")
        gh = st[:, 237:238]
        self.global_max_collective(hmax, 1, gh, "gmx1")
        inv_sh = st[:, 238:239]
        s_h = st[:, 239:240]
        A.op("vector", lambda e, i=gh, o=inv_sh: e.reciprocal(out=o, in_=i),
             r=[gh], w=[inv_sh], label="invh0")
        A.op("vector", lambda e, o=inv_sh: e.tensor_scalar(
            out=o, in0=o, scalar1=255.0, scalar2=None, op0=ALU.mult),
            r=[inv_sh], w=[inv_sh], label="invh1")
        A.op("vector", lambda e, i=gh, o=s_h: e.tensor_scalar(
            out=o, in0=i, scalar1=1.0 / 255.0, scalar2=None, op0=ALU.mult),
            r=[gh], w=[s_h], label="sh0")

        sw2 = self.stat[:, 224:232]
        ssw2 = st[:, 8:16]
        A.op("vector", lambda e, i=sw2, o=ssw2, s=s_h[:, 0:1]: e.tensor_scalar(
            out=o, in0=i, scalar1=s, scalar2=None, op0=ALU.mult),
            r=[sw2, s_h], w=[ssw2], label="ssw2")
        b2s = st[:, 16:24]
        A.dma("gq", b2s, self.b2.rearrange("(c p) -> p c", p=128), label="ld_b2",
              allow_slow_non_contiguous=True)
        rcp2 = st[:, 24:32]
        A.op("vector", lambda e, i=ssw2, o=rcp2: e.reciprocal(out=o, in_=i),
             r=[ssw2], w=[rcp2], label="b2rcp")
        b2q = st[:, 128:136]
        A.op("vector", lambda e, b=b2s, r=rcp2, o=b2q: e.tensor_tensor(
            out=o, in0=b, in1=r, op=ALU.mult), r=[b2s, rcp2], w=[b2q], label="b2t")
        A.op("vector", lambda e, o=b2q: e.tensor_scalar(
            out=o, in0=o, scalar1=MAGIC, scalar2=MAGIC, op0=ALU.add, op1=ALU.subtract),
            r=[b2q], w=[b2q], label="b2r")
        A.op("vector", lambda e, o=b2q, s=ssw2: e.tensor_tensor(
            out=o, in0=o, in1=s, op=ALU.mult), r=[b2q, ssw2], w=[b2q], label="b2m")
        A.dma("gq", self.row_hbm[0:1, :].rearrange("one (c p) -> p one c", p=128),
              ssw2.rearrange("p (one c) -> p one c", one=1), label="r0st",
              allow_slow_non_contiguous=True)
        A.dma("gq", self.row_hbm[1:2, :].rearrange("one (c p) -> p one c", p=128),
              b2q.rearrange("p (one c) -> p one c", one=1), label="r1st",
              allow_slow_non_contiguous=True)
        ssw2_row = self.rowb[:, 0:1024]
        b2q_row = self.rowb[:, 1024:2048]
        A.dma("gq", ssw2_row, self.row_hbm[0:1, :].broadcast_to([128, D]), label="r0ld")
        A.dma("gq", b2q_row, self.row_hbm[1:2, :].broadcast_to([128, D]), label="r1ld")

        groups = [list(range(g, min(g + 3, NT))) for g in range(0, NT, 3)]

        def m2h_issue(gi2, fc2, g02, gw2):
            hld2 = self.big1[:, (fc2 % 8) * 384:(fc2 % 8) * 384 + gw2]
            A.dma(("sq", "gq", "aq")[fc2 % 3], hld2,
                  self.h_hbm[fc2 * 128:(fc2 + 1) * 128, g02 * 128:g02 * 128 + gw2],
                  label=f"m2h{gi2}_{fc2}")
            return hld2

        prev_post = None
        for gi, tcs in enumerate(groups):
            g0 = tcs[0]
            gw = len(tcs) * 128
            for fc in range(8):
                m2h_issue(gi, fc, g0, gw)
            for fc in range(24, 28):
                wsl = self.tile[:, (fc % 4) * 1024:(fc % 4) * 1024 + 1024]
                A.dma(("gq", "aq")[fc % 2], wsl, self.w2qT_hbm[fc * 128:(fc + 1) * 128, :],
                      label=f"m2w{gi}_{fc}")
            if prev_post is not None:
                prev_post()
            for fc in range(NF1):
                hld = self.big1[:, (fc % 8) * 384:(fc % 8) * 384 + gw]
                tmp = self.scr32[:, 1024 + (fc % 2) * 384:1024 + (fc % 2) * 384 + gw]
                hq = self.big1[:, 3072 + (fc % 8) * 384: 3072 + (fc % 8) * 384 + gw]
                if fc % 2 == 0:
                    A.op("vector", lambda e, i=hld, o=tmp, s=inv_sh[:, 0:1]: e.tensor_scalar(
                        out=o, in0=i, scalar1=s, scalar2=MAGIC, op0=ALU.mult, op1=ALU.add),
                        r=[hld, inv_sh], w=[tmp], label=f"m2qa{gi}_{fc}")
                else:
                    A.op("scalar", lambda e, i=hld, o=tmp, s=inv_sh[:, 0:1], b=self.pmag:
                         e.activation(out=o, in_=i, func=AF.Identity, scale=s, bias=b),
                         r=[hld, inv_sh, self.pmag], w=[tmp], label=f"m2qa{gi}_{fc}")
                if fc + 8 < NF1:
                    m2h_issue(gi, fc + 8, g0, gw)
                if fc % 2 == 0:
                    A.op("scalar", lambda e, i=tmp, o=hq, b=self.nmag: e.activation(
                        out=o, in_=i, func=AF.Identity, bias=b),
                        r=[tmp, self.nmag], w=[hq], label=f"m2qb{gi}_{fc}")
                else:
                    A.op("vector", lambda e, i=tmp, o=hq, n=self.nmag[:, 0:1]: e.tensor_scalar(
                        out=o, in0=i, scalar1=n, scalar2=None, op0=ALU.add),
                        r=[tmp, self.nmag], w=[hq], label=f"m2qb{gi}_{fc}")
                if fc < 16:
                    wsl = self.big0[:, fc * 1024:(fc + 1) * 1024]
                    if gi == 0:
                        A.dma("sq", wsl, self.w2qT_hbm[fc * 128:(fc + 1) * 128, :],
                              label=f"m2w{gi}_{fc}")
                elif fc < 24:
                    wsl = self.big2[:, (fc - 16) * 1024:(fc - 15) * 1024]
                else:
                    wsl = self.tile[:, (fc % 4) * 1024:(fc % 4) * 1024 + 1024]
                for ti, tc in enumerate(tcs):
                    l = hq[:, ti * 128:(ti + 1) * 128]
                    for half in range(2):
                        ps = self.pf[2 * ti + half]
                        r = wsl[:, half * 512:(half + 1) * 512]
                        A.op("tensor", lambda e, o=ps, l=l, r=r, s=(fc == 0), p=(fc == NF1 - 1):
                             e.matmul(o[:, :], l, r, start=s, stop=p),
                             r=[l, r], w=[ps[:, :]], label=f"m2_{gi}_{ti}_{half}|{fc}")
                if 24 <= fc < 28:
                    fcn = fc + 4
                    wnx = self.tile[:, (fcn % 4) * 1024:(fcn % 4) * 1024 + 1024]
                    A.dma(("gq", "aq")[fcn % 2], wnx, self.w2qT_hbm[fcn * 128:(fcn + 1) * 128, :],
                          label=f"m2w{gi}_{fcn}")

            def mkpost(tcs2):
                def post():
                    for ti, tc in enumerate(tcs2):
                        self.postln_residual([self.pf[2 * ti], self.pf[2 * ti + 1]], tc,
                                             bias_row=b2q_row, scale_row=ssw2_row,
                                             label=f"pl2_{tc}",
                                             then=lambda tc2: self.ln_tc(
                                                 tc2, self.xfm[:, tc2 * D:(tc2 + 1) * D], "ln3"))
                return post
            prev_post = mkpost(tcs)
        prev_post()

    def p3_l1_attn(self):
        A, st = self.A, self.stat
        nc = self.nc

        outwT = self.big2[:, 0:8192]
        for dk in range(ND):
            A.dmat("aq", outwT[:, dk * D:(dk + 1) * D],
                   self.outw_bf[:, dk * 128:(dk + 1) * 128], label=f"owt{dk}")

        xln = self.big0

        def xlnT(tc):
            return xln[:, tc * D:(tc + 1) * D]

        for dk in range(ND):
            A.dmat("aq", self.xfmS(dk, 0, T), self.xlnT_hbm[:, dk * 128:(dk + 1) * 128],
                   label=f"tr3_tr{dk}")

        grow = st[0:1, 240:256]
        A.dma("gq", grow, self.gamma.rearrange("(one g) -> one g", one=1), label="ld_g")
        A.op("scalar", lambda e, o=grow: e.activation(out=o, in_=o, func=AF.Sigmoid),
             r=[grow], w=[grow], label="sig_g")
        A.op("scalar", lambda e, o=grow: e.activation(out=o, in_=o, func=AF.Ln),
             r=[grow], w=[grow], label="ln_g")
        A.dma("gq", self.row_hbm[2:3, 0:16], grow, label="g_st")
        lngb = self.scr32[:, 0:16]
        A.dma("gq", lngb, self.row_hbm[2:3, 0:16].broadcast_to([128, 16]), label="g_ld")
        D_all = self.cons[:, 768:768 + 2048]
        for h in range(16):
            A.op("scalar", lambda e, o=D_all[:, h * 128:(h + 1) * 128], s=lngb[:, h:h + 1]:
                 e.activation(out=o, in_=self.mdec, func=AF.Exp, scale=s),
                 r=[self.mdec, lngb], w=[D_all[:, h * 128:(h + 1) * 128]], label=f"dall{h}")
        lngband = self.scr32[:, 32:40]
        for dc in range(ND):
            A.op("vector", lambda e, i=lngb[0:64, 2 * dc:2 * dc + 1], o=lngband[0:64, dc:dc + 1]:
                 e.tensor_copy(out=o, in_=i), r=[lngb], w=[lngband[0:64, dc:dc + 1]],
                 label=f"lb0_{dc}")
            A.op("vector", lambda e, i=lngb[64:128, 2 * dc + 1:2 * dc + 2], o=lngband[64:128, dc:dc + 1]:
                 e.tensor_copy(out=o, in_=i), r=[lngb], w=[lngband[64:128, dc:dc + 1]],
                 label=f"lb1_{dc}")
        Gq_all = self.cons[:, 2816:2816 + 1024]
        for dc in range(ND):
            A.op("scalar", lambda e, o=Gq_all[:, dc * 128:(dc + 1) * 128], s=lngband[:, dc:dc + 1]:
                 e.activation(out=o, in_=self.gq_in, func=AF.Exp, scale=s),
                 r=[self.gq_in, lngband], w=[Gq_all[:, dc * 128:(dc + 1) * 128]], label=f"gq{dc}")
        GkS = st[:, 224:240]
        for h in range(16):
            A.op("scalar", lambda e, o=GkS[:, h:h + 1], s=lngb[:, h:h + 1]:
                 e.activation(out=o, in_=self.gk_in, func=AF.Exp, scale=s),
                 r=[self.gk_in, lngb], w=[GkS[:, h:h + 1]], label=f"gk{h}")
        gCb = self.scr32[:, 16:32]
        A.op("scalar", lambda e, i=lngb, o=gCb: e.activation(
            out=o, in_=i, func=AF.Exp, scale=128.0),
            r=[lngb], w=[gCb], label="gC")
        gI = self.cons[:, 4864:4864 + 1024]
        for h in range(16):
            idx = self.idA if h % 2 == 0 else self.idB
            A.op("vector", lambda e, i=idx, o=gI[:, h * 64:(h + 1) * 64], s=gCb[:, h:h + 1]:
                 e.tensor_scalar(out=o, in0=i, scalar1=s, scalar2=None, op0=ALU.mult),
                 r=[idx, gCb], w=[gI[:, h * 64:(h + 1) * 64]], label=f"gI{h}")
        SS = self.cons[:, 3840:3840 + 1024]
        A.op("vector", lambda e, o=SS: e.memset(o, 0.0), w=[SS], label="ss0")

        bq1 = st[:, 128:136]
        A.dma("gq", bq1, self.bqkv[0:1024].rearrange("(c p) -> p c", p=128),
              label="ld_bq1", allow_slow_non_contiguous=True)
        bk1 = st[:, 136:144]
        A.dma("gq", bk1, self.bqkv[1024:2048].rearrange("(c p) -> p c", p=128),
              label="ld_bk1", allow_slow_non_contiguous=True)
        bk18 = st[:, 144:152]
        A.op("vector", lambda e, i=bk1, o=bk18: e.tensor_scalar(
            out=o, in0=i, scalar1=0.125, scalar2=None, op0=ALU.mult),
            r=[bk1], w=[bk18], label="bk18")
        bkv_row = self.rowb[:, 0:2048]
        A.dma("gq", bkv_row, self.bqkv[1024:3072].rearrange("(one d) -> one d", one=1)
              .broadcast_to([128, 2048]), label="ld_bkv")
        outb_row = self.rowb[:, 2048:3072]
        A.dma("gq", outb_row, self.outb.rearrange("(one d) -> one d", one=1)
              .broadcast_to([128, D]), label="ld_outb")

        wv_blk = self.big1[:, 0:8192]
        A.dma("sq", wv_blk.rearrange("p (dk f) -> p dk f", f=1024),
              self.wqkvT_hbm[:, 2048:3072].rearrange("(dk p) f -> p dk f", p=128),
              label="ld_wvblk")

        vt_b = [self.tile[:, 0:1024], self.tile[:, 2048:3072]]
        kt_b = [self.tile[:, 1024:2048], self.tile[:, 3072:4096]]
        yt = self.scr32[:, 1024:1536].bitcast(BF16)
        pT = self.scr32[:, 1536:1792].bitcast(BF16)
        qt = self.cons[:, 5888:6144]
        for half in range(2):
            if half == 1:
                self.p_prep_late()
            q_h = self.big0[:, 0:8192]
            k_h = self.big0[:, 8192:16384]
            t0 = half * 1024
            for part, dst, bias, scale in (("q", q_h, bq1, None), ("k", k_h, bk18, 0.125)):
                off = 0 if part == "q" else 1024
                for fc in range(ND):
                    wsl = self.big1[:, 8192 + (fc % 2) * 1024: 8192 + (fc % 2) * 1024 + 1024]
                    A.dma("sq", wsl.rearrange("p (dk f) -> p dk f", f=128),
                          self.wqkv_sl[(off // 128) + fc], label=f"qk{half}{part}{fc}")
                    for tt in range(2):
                        ps = self.pf[tt % 2]
                        for dk in range(ND):
                            l = wsl[:, dk * 128:(dk + 1) * 128]
                            r = self.xfmS(dk, t0 + tt * 512, t0 + (tt + 1) * 512)
                            A.op("tensor", lambda e, o=ps, l=l, r=r, s=(dk == 0), p=(dk == ND - 1):
                                 e.matmul(o[:, :], l, r, start=s, stop=p),
                                 r=[l, r], w=[ps[:, :]], label=f"p3m{half}{part}{fc}_{tt}|{dk}")
                        dd = dst[:, fc * 1024 + tt * 512: fc * 1024 + (tt + 1) * 512]
                        if scale is None:
                            A.op("scalar", lambda e, i=ps, o=dd, b=bias[:, fc:fc + 1]:
                                 e.activation(out=o, in_=i[:, :], func=AF.Identity, bias=b),
                                 r=[ps[:, :], bias], w=[dd], label=f"p3e{half}{part}{fc}_{tt}")
                        else:
                            A.op("scalar", lambda e, i=ps, o=dd, b=bias[:, fc:fc + 1], sc=scale:
                                 e.activation(out=o, in_=i[:, :], func=AF.Identity, bias=b, scale=sc),
                                 r=[ps[:, :], bias], w=[dd], label=f"p3e{half}{part}{fc}_{tt}")
            def g1(tci):
                tc = half * 8 + tci
                vt = vt_b[tci % 2]
                kt = kt_b[tci % 2]
                for dk in range(ND):
                    l = self.xfmS(dk, tc * 128, (tc + 1) * 128)
                    for hf in range(2):
                        ps = self.pf[hf]
                        r = wv_blk[:, dk * 1024 + hf * 512: dk * 1024 + (hf + 1) * 512]
                        A.op("tensor", lambda e, o=ps, l=l, r=r, s=(dk == 0), p=(dk == ND - 1):
                             e.matmul(o[:, :], l, r, start=s, stop=p),
                             r=[l, r], w=[ps[:, :]], label=f"vm{tc}_{hf}|{dk}")
                for hf in range(2):
                    dd = vt[:, hf * 512:(hf + 1) * 512]
                    br = bkv_row[:, 1024 + hf * 512:1024 + (hf + 1) * 512]
                    A.op("vector", lambda e, i=self.pf[hf], o=dd, b=br: e.tensor_tensor(
                        out=o, in0=i[:, :], in1=b, op=ALU.add),
                        r=[self.pf[hf][:, :], bkv_row], w=[dd], label=f"ve{tc}_{hf}")
                for k2 in range(8):
                    src = k_h[:, k2 * 1024 + tci * 128:k2 * 1024 + (tci + 1) * 128]
                    out = self.pb[:, k2 * 128:(k2 + 1) * 128]
                    A.op("tensor", lambda e, o=out, s=src: e.transpose(o, s, self.ident),
                         r=[src, self.ident], w=[out], label=f"ktT{tc}_{k2}", tr=True)
                A.op("vector", lambda e, i=self.pb[:, 0:1024], o=kt, g=GkS: e.tensor_tensor(
                    out=o.rearrange("p (h x) -> p h x", x=64),
                    in0=i.rearrange("p (h x) -> p h x", x=64),
                    in1=g.rearrange("p (h one) -> p h one", one=1).broadcast_to([128, 16, 64]),
                    op=ALU.mult),
                    r=[self.pb[:, 0:1024], GkS], w=[kt], label=f"kte{tc}")

            def g2(tci):
                tc = half * 8 + tci
                vt = vt_b[tci % 2]
                kt = kt_b[tci % 2]
                for qd in range(4):
                    ps_s = self.pf[4 + qd % 2]
                    for hh in range(4):
                        h = qd * 4 + hh
                        b = 64 * (h % 2)
                        l = k_h[:, (h // 2) * 1024 + tci * 128:(h // 2) * 1024 + (tci + 1) * 128][b:b + 64, :]
                        r = q_h[:, (h // 2) * 1024 + tci * 128:(h // 2) * 1024 + (tci + 1) * 128][b:b + 64, :]
                        out = ps_s[:, hh * 128:(hh + 1) * 128]
                        A.op("tensor", lambda e, o=out, l=l, r=r: e.matmul(o, l, r, start=True, stop=True),
                             r=[l, r], w=[out], label=f"sT{tc}_{h}")
                    A.op("vector", lambda e, i=ps_s, o=pT, d=D_all[:, qd * 512:(qd + 1) * 512]:
                         e.tensor_tensor(out=o, in0=i[:, :], in1=d, op=ALU.mult),
                         r=[ps_s[:, :], D_all], w=[pT], label=f"pT{tc}_{qd}")
                    q0 = qd * 2
                    A.op("vector", lambda e, o=qt, i0=q_h, g=Gq_all, q0=q0, tci=tci:
                         e.tensor_tensor(
                             out=o.rearrange("p (f x) -> p f x", x=128),
                             in0=i0.rearrange("p (f t) -> p f t", t=1024)[:, q0:q0 + 2, tci * 128:(tci + 1) * 128],
                             in1=g.rearrange("p (f x) -> p f x", x=128)[:, q0:q0 + 2, :],
                             op=ALU.mult),
                         r=[q_h, Gq_all], w=[qt], label=f"qt{tc}_{qd}")
                    for hh in range(4):
                        h = qd * 4 + hh
                        b = 64 * (h % 2)
                        k2 = h // 2
                        ps_y = self.pf[2 + k2 % 2]
                        ycol = (k2 // 2) * 128
                        out_y = ps_y[b:b + 64, ycol:ycol + 128]
                        l1 = vt[:, h * 64:(h + 1) * 64]
                        r1 = pT[:, hh * 128:(hh + 1) * 128]
                        A.op("tensor", lambda e, o=out_y, l=l1, r=r1, tp=(0, b): e.matmul(
                            o, l, r, start=True, stop=False, tile_position=tp),
                            r=[l1, r1], w=[ps_y[:, ycol:ycol + 128]], label=f"y{tc}_{h}|1")
                        l2 = SS[:, h * 64:(h + 1) * 64]
                        r2 = qt[:, (k2 % 2) * 128:(k2 % 2) * 128 + 128]
                        A.op("tensor", lambda e, o=out_y, l=l2, r=r2, tp=(0, b): e.matmul(
                            o, l, r, start=False, stop=True, tile_position=tp),
                            r=[l2, r2], w=[ps_y[:, ycol:ycol + 128]], label=f"y{tc}_{h}|2")
                        ps_ss = self.pf[6]
                        scol = (h // 2) * 64
                        out_s = ps_ss[b:b + 64, scol:scol + 64]
                        l3 = kt[:, h * 64:(h + 1) * 64]
                        r3 = vt[:, h * 64:(h + 1) * 64]
                        A.op("tensor", lambda e, o=out_s, l=l3, r=r3, tp=(0, b): e.matmul(
                            o, l, r, start=True, stop=False, tile_position=tp),
                            r=[l3, r3], w=[ps_ss[:, scol:scol + 64]], label=f"st{tc}_{h}|1")
                        l4 = gI[:, h * 64:(h + 1) * 64]
                        r4 = SS[:, h * 64:(h + 1) * 64]
                        A.op("tensor", lambda e, o=out_s, l=l4, r=r4, tp=(0, b): e.matmul(
                            o, l, r, start=False, stop=True, tile_position=tp),
                            r=[l4, r4], w=[ps_ss[:, scol:scol + 64]], label=f"st{tc}_{h}|2")
                for i2 in range(2):
                    A.op("scalar", lambda e, i=self.pf[2 + i2], o=yt[:, i2 * 512:(i2 + 1) * 512]:
                         e.copy(out=o, in_=i[:, :]),
                         r=[self.pf[2 + i2][:, :]], w=[yt[:, i2 * 512:(i2 + 1) * 512]],
                         label=f"ye{tc}_{i2}")
                for band in range(2):
                    src = self.pf[6][band * 64:(band + 1) * 64, 0:512]
                    dst = SS[band * 64:(band + 1) * 64, :].rearrange(
                        "p (h two x) -> p h two x", two=2, x=64)[:, :, band, :]
                    A.op("scalar", lambda e, i=src, o=dst: e.copy(
                        out=o, in_=i.rearrange("p (h x) -> p h x", x=64)),
                        r=[self.pf[6][:, :]], w=[dst], label=f"se{tc}_{band}")
                ps_o = [self.pf[2], self.pf[3]]
                for k in range(ND):
                    l = yt[:, (k % 2) * 512 + (k // 2) * 128: (k % 2) * 512 + (k // 2) * 128 + 128]
                    for hf in range(2):
                        r = outwT[:, k * D + hf * 512: k * D + (hf + 1) * 512]
                        out = ps_o[hf]
                        A.op("tensor", lambda e, o=out, l=l, r=r, s=(k == 0), p=(k == ND - 1):
                             e.matmul(o[:, :], l, r, start=s, stop=p),
                             r=[l, r], w=[out[:, :]], label=f"op3{tc}_{hf}|{k}")
                self.postln_residual(ps_o, tc, bias_row=outb_row, label=f"pl3_{tc}")

            g1(0)
            for tci in range(8):
                if tci + 1 < 8:
                    g1(tci + 1)
                g2(tci)

    def p4_l1_ffn(self):
        A, st = self.A, self.stat

        xln = self.big0

        def xlnT(tc):
            return xln[:, tc * D:(tc + 1) * D]

        self.ln_into(self.resT, xlnT, "ln4")
        self.transpose_roundtrip(xlnT, "tr4")

        def wdT(hk):
            if hk < 16:
                return self.big0[:, hk * D:(hk + 1) * D]
            return self.big2[:, (hk - 16) * D:(hk - 15) * D]

        for hk in range(24):
            A.dmat("aq", wdT(hk), self.wd_bf[:, hk * 128:(hk + 1) * 128], label=f"wdt{hk}")

        NH = H1F // 128
        for hk in range(NH):
            gsl = self.big1[:, (hk % 2) * 2048:(hk % 2) * 2048 + 1024]
            usl = self.big1[:, (hk % 2) * 2048 + 1024:(hk % 2) * 2048 + 2048]
            A.dma("sq", gsl.rearrange("p (dk f) -> p dk f", f=128),
                  self.wgT_sl[hk], label=f"ga{hk}")
            A.dma("sq", usl.rearrange("p (dk f) -> p dk f", f=128),
                  self.wuT_sl[hk], label=f"ua{hk}")
            hsb = self.big1[:, 4096 + (hk % 2) * 2048:4096 + (hk % 2) * 2048 + 2048]
            for tt in range(4):
                psg = self.pf[tt % 3]
                psu = self.pf[3 + tt % 3]
                for dk in range(ND):
                    r = self.xfmS(dk, tt * 512, (tt + 1) * 512)
                    lg = gsl[:, dk * 128:(dk + 1) * 128]
                    A.op("tensor", lambda e, o=psg, l=lg, r=r, s=(dk == 0), p=(dk == ND - 1):
                         e.matmul(o[:, :], l, r, start=s, stop=p),
                         r=[lg, r], w=[psg[:, :]], label=f"gm{hk}_{tt}|{dk}")
                for dk in range(ND):
                    r = self.xfmS(dk, tt * 512, (tt + 1) * 512)
                    lu = usl[:, dk * 128:(dk + 1) * 128]
                    A.op("tensor", lambda e, o=psu, l=lu, r=r, s=(dk == 0), p=(dk == ND - 1):
                         e.matmul(o[:, :], l, r, start=s, stop=p),
                         r=[lu, r], w=[psu[:, :]], label=f"um{hk}_{tt}|{dk}")
                sil = self.scr32[:, (tt % 2) * 512:(tt % 2) * 512 + 512]
                A.op("scalar", lambda e, i=psg, o=sil: e.activation(
                    out=o, in_=i[:, :], func=AF.Silu, scale=1.0 / 127.0),
                    r=[psg[:, :]], w=[sil], label=f"sil{hk}_{tt}")
                dst = hsb[:, tt * 512:(tt + 1) * 512]
                A.op("vector", lambda e, u=psu, s=sil, o=dst: e.scalar_tensor_tensor(
                    out=o, in0=u[:, :], scalar=1.0 / 127.0, in1=s, op0=ALU.mult, op1=ALU.mult),
                    r=[psu[:, :], sil], w=[dst], label=f"hmul{hk}_{tt}")
            A.dma("aq", self.h2_hbm[hk * 128:(hk + 1) * 128, :], hsb, label=f"h2s{hk}")

        groups = [list(range(g, min(g + 3, NT))) for g in range(0, NT, 3)]
        for gi, tcs in enumerate(groups):
            g0 = tcs[0]
            gw = len(tcs) * 128
            for hk in range(NH):
                s8 = hk % 8
                if s8 < 4:
                    hq = self.big1[:, s8 * 512: s8 * 512 + gw]
                else:
                    hq = self.big1[:, 8192 + (s8 - 4) * 512: 8192 + (s8 - 4) * 512 + gw]
                A.dma(("sq", "gq", "aq")[hk % 3], hq,
                      self.h2_hbm[hk * 128:(hk + 1) * 128, g0 * 128:g0 * 128 + gw],
                      label=f"hb{gi}_{hk}")
                for ti in range(len(tcs)):
                    l = hq[:, ti * 128:(ti + 1) * 128]
                    for hf in range(2):
                        ps = self.pf[2 * ti + hf]
                        r = wdT(hk)[:, hf * 512:(hf + 1) * 512]
                        A.op("tensor", lambda e, o=ps, l=l, r=r, s=(hk == 0), p=(hk == NH - 1):
                             e.matmul(o[:, :], l, r, start=s, stop=p),
                             r=[l, r], w=[ps[:, :]], label=f"dm{gi}_{ti}_{hf}|{hk}")
            for ti, tc in enumerate(tcs):
                self.postln_residual([self.pf[2 * ti], self.pf[2 * ti + 1]], tc,
                                     scale_const=1.0 / 127.0, label=f"pl4_{tc}")

    def p5_store_out(self):
        A = self.A
        for tc in range(NT):
            A.dma("aq", self.y_out[tc * 128:(tc + 1) * 128, :], self.resT(tc),
                  label=f"st_y{tc}")
        A.op("gpsimd", lambda e: e.engine_nop(),
             r=[self.y_out], w=(), label="final_wait")


def build_nc(upto=4):
    nc = bass.Bass("TRN2", target_bir_lowering=False, debug=False)
    with ExitStack() as ctx:
        mk = MK(nc, ctx, upto)
        mk.build()
        block = ctx.enter_context(nc.Block())
        mk.A.build(block, mk.sems)
    return nc


_IN_ORDER = [
    "L0_attn_x", "L0_attn_Wq", "L0_attn_bq", "L0_attn_Wk", "L0_attn_bk",
    "L0_attn_Wv", "L0_attn_bv", "L0_attn_Wo", "L0_attn_bo",
    "L0_ffn_w1", "L0_ffn_b1", "L0_ffn_w2", "L0_ffn_b2",
    "L1_attn_Wqkv_w", "L1_attn_Wqkv_b", "L1_attn_out_w", "L1_attn_out_b",
    "L1_attn_gamma_raw", "L1_ffn_w_gate", "L1_ffn_w_up", "L1_ffn_w_down",
]
_NAME_MAP = {
    "L0_attn_Wq": "Wq", "L0_attn_bq": "bq", "L0_attn_Wk": "Wk", "L0_attn_bk": "bk",
    "L0_attn_Wv": "Wv", "L0_attn_bv": "bv", "L0_attn_Wo": "Wo", "L0_attn_bo": "bo",
    "L0_ffn_w1": "w1", "L0_ffn_b1": "b1", "L0_ffn_w2": "w2", "L0_ffn_b2": "b2",
    "L1_attn_Wqkv_w": "wqkv", "L1_attn_Wqkv_b": "bqkv", "L1_attn_out_w": "outw",
    "L1_attn_out_b": "outb", "L1_attn_gamma_raw": "gamma",
    "L1_ffn_w_gate": "wg", "L1_ffn_w_up": "wu", "L1_ffn_w_down": "wd",
}


def kernel(**inputs):
    nc = build_nc(upto=4)
    consts = host_constants()
    shared = {}
    for k, v in inputs.items():
        if k == "L0_attn_x":
            continue
        arr = np.ascontiguousarray(np.asarray(v))
        shared[_NAME_MAP[k]] = arr
    for k, v in consts.items():
        shared[k] = np.ascontiguousarray(v)
    x = np.ascontiguousarray(np.asarray(inputs["L0_attn_x"], dtype=np.float32))
    in_maps = []
    for b in range(8):
        m = dict(shared)
        m["x"] = x[b]
        in_maps.append(m)
    res = run_bass_kernel_spmd(nc, in_maps, core_ids=list(range(8)))
    out = np.stack([np.asarray(res.results[b]["y"]) for b in range(8)], axis=0)
    return out.astype(np.float32)
